# version 06
# baseline (speedup 1.0000x reference)
import numpy as np
from contextlib import ExitStack
import concourse.bass as bass
import concourse.mybir as mybir
from concourse.bass_utils import run_bass_kernel_spmd

F32 = mybir.dt.float32
BF16 = mybir.dt.bfloat16
I32 = mybir.dt.int32
AF = mybir.ActivationFunctionType
OP = mybir.AluOpType
AX = mybir.AxisListType

ENGS = ("tensor", "vector", "scalar", "gpsimd", "sync")


class Sem:
    __slots__ = ("h", "n", "name")

    def __init__(self, h, name):
        self.h = h
        self.n = 0
        self.name = name

    def inc(self, k=1):
        self.n += k
        return (self, k)

    def now(self):
        return (self, self.n)

    def at(self, v):
        return (self, v)


class Emit:

    def __init__(self, nc, es):
        self.nc = nc
        self.es = es
        self.cur_es = es
        self.ops = {e: [] for e in ENGS}
        self.last_wait = {}
        self.bank_rel = {}
        self.all_sems = []
        self.pending = {}

    def sem(self, name):
        h = self.es.enter_context(self.nc.semaphore(name))
        s = Sem(h, name)
        self.all_sems.append(s)
        return s

    def sb(self, name, shape, dtype):
        return self.cur_es.enter_context(self.nc.sbuf_tensor(name, shape, dtype))

    def ps(self, name, shape, dtype=F32):
        return self.cur_es.enter_context(self.nc.psum_tensor(name, shape, dtype))

    def barrier(self):
        specs = [s.now() for s in self.all_sems if s.n > 0]
        for e in ENGS:
            self.pending[e] = list(specs)
        self.bank_rel = {}

    AUTO_EDGE_ENGS = ("vector", "scalar", "gpsimd")
    _ASYNC_NAMES = ("dma_start", "dma_start_transpose", "collective_compute",
                    "indirect_dma_start", "wait_ge", "nop")

    def _auto_sem(self, eng):
        key = f"__auto_{eng}"
        if not hasattr(self, "_autos"):
            self._autos = {}
        if eng not in self._autos:
            self._autos[eng] = self.sem(key)
        return self._autos[eng]

    def op(self, eng, fn, waits=(), incs=()):
        if waits and isinstance(waits[0], Sem):
            waits = (waits,)
        if incs and isinstance(incs[0], Sem):
            incs = (incs,)
        waits = tuple(waits)
        if self.pending.get(eng):
            waits = tuple(self.pending.pop(eng)) + waits
        names = fn.__code__.co_names
        is_async = any(nm in names for nm in self._ASYNC_NAMES)
        auto_inc = None
        if eng in self.AUTO_EDGE_ENGS and not is_async:
            edge = getattr(self, "_edge", {}).get(eng)
            if edge is not None:
                waits = waits + (edge,)
            if not incs:
                a = self._auto_sem(eng)
                auto_inc = (a, 1, a.inc(1) and a.n)
            if not hasattr(self, "_edge"):
                self._edge = {}
            if incs:
                sem, k = incs[-1]
                self._edge[eng] = (sem, sem.n)
            else:
                self._edge[eng] = (self._autos[eng], self._autos[eng].n)
        incs = tuple((sem, k, sem.n) for (sem, k) in incs)
        if auto_inc is not None:
            incs = incs + ((auto_inc[0], 1, auto_inc[2]),)
        self.ops[eng].append((waits, fn, incs))

    def replay(self, eng, engine_obj):
        for waits, fn, incs in self.ops[eng]:
            for (sem, val) in waits:
                key = (eng, id(sem))
                if val > 0 and self.last_wait.get(key, -1) < val:
                    engine_obj.wait_ge(sem.h, val)
                    self.last_wait[key] = val
            ins = fn(engine_obj)
            for (sem, k, val_after) in incs:
                ins = ins.then_inc(sem.h, k)

    def finish(self):
        with self.nc.Block() as block:
            block.tensor(lambda t: self.replay("tensor", t))
            block.vector(lambda v: self.replay("vector", v))
            block.scalar(lambda s: self.replay("scalar", s))
            block.gpsimd(lambda g: self.replay("gpsimd", g))
            block.sync(lambda sy: self.replay("sync", sy))
        self.ops = {e: [] for e in ENGS}


def ceil_div(a, b):
    return (a + b - 1) // b



def host_wT(w):
    return np.ascontiguousarray(np.asarray(w, dtype=np.float32).T)


def host_fm(x2d):
    return np.ascontiguousarray(np.asarray(x2d, dtype=np.float32).T)


CONST_TRIL_INC = None


def host_consts():
    ident = np.eye(128, dtype=np.float32)
    trilinc = np.tril(np.ones((128, 128), np.float32))
    triuinc = np.triu(np.ones((128, 128), np.float32))
    upperneg = np.where(np.arange(128)[:, None] > np.arange(128)[None, :],
                        np.float32(-1e30), np.float32(0.0))
    tp = np.arange(128)
    same_blk = (tp[:, None] // 32) == (tp[None, :] // 32)
    diff = tp[None, :] - tp[:, None]
    diffblk = np.where(same_blk & (diff >= 0), diff.astype(np.float32), np.float32(0.0))
    offneg = np.where(same_blk & (diff >= 0), np.float32(0.0), np.float32(-1e30))
    return dict(ident=ident, trilinc=trilinc.astype(np.float32), triuinc=triuinc,
                upperneg=upperneg, diffblk=diffblk, offneg=offneg)




def emit_proj(em, psum_pair, w_sb, ktiles, act3, otiles, epilogue, s_pe,
              first_waits=(), extra_k0_waits=None, N=512, start_acc=True):
    rel = [em.bank_rel.get(id(psum_pair[0])), em.bank_rel.get(id(psum_pair[1]))]
    for ot in range(otiles):
        pb = psum_pair[ot % 2]
        for k in range(ktiles):
            wts = []
            if k == 0:
                if ot < 2:
                    wts.extend(first_waits)
                if rel[ot % 2] is not None:
                    wts.append(rel[ot % 2])
                if extra_k0_waits is not None:
                    wts.extend(extra_k0_waits(ot))
            em.op("tensor", lambda t, ot=ot, k=k, pb=pb: t.matmul(
                pb[:, 0:N], w_sb[:, k, ot * 128:(ot + 1) * 128], act3[:, k, :],
                start=(k == 0 and start_acc), stop=(k == ktiles - 1)),
                waits=tuple(wts), incs=s_pe.inc() if k == ktiles - 1 else ())
        rel[ot % 2] = epilogue(ot, pb, s_pe.now())
    em.bank_rel[id(psum_pair[0])] = rel[0]
    em.bank_rel[id(psum_pair[1])] = rel[1]


class LNKit:

    def __init__(self, em, name, N=512, psums=None, bufs=None):
        self.em = em
        self.N = N
        if bufs is None:
            self.vbf = em.sb(f"{name}_vbf", [128, 8, N], BF16)[:]
            self.v2bf = em.sb(f"{name}_v2bf", [128, 8, N], BF16)[:]
        else:
            self.vbf, self.v2bf = bufs
        self.stat = em.sb(f"{name}_stat", [1, 4, N], F32)
        self.ones128 = em.sb(f"{name}_o128", [128, 1], BF16)
        self.onesrow = em.sb(f"{name}_orow", [1, 128], F32)
        if psums is None:
            self.pstA = em.ps(f"{name}_pstA", [1, N])[:]
            self.pstB = em.ps(f"{name}_pstB", [1, N])[:]
            pbc = em.ps(f"{name}_pbc", [128, 2 * N])
            self.pbc_mean = pbc[:, 0:N]
            self.pbc_rstd = pbc[:, N:2 * N]
            self.rel_keys = ()
        else:
            self.pstA, self.pstB, self.pbc_mean, self.pbc_rstd, self.rel_keys = psums
        self.s_pe = em.sem(f"{name}_pe")
        self.s_dve = em.sem(f"{name}_dve")
        self.s_act = em.sem(f"{name}_act")
        em.op("vector", lambda v: v.memset(self.ones128[:], 1.0), incs=self.s_dve.inc())
        em.op("vector", lambda v: v.memset(self.onesrow[:], 1.0), incs=self.s_dve.inc())
        self.init_done = self.s_dve.now()
        self.prev_sums = None
        self.prev_bc_readers = None

    def emit(self, vsb, out_f32, out_bf=None, v_ready=(), out_waits=()):
        em, N = self.em, self.N
        cw = list(v_ready)
        if self.prev_sums is not None:
            cw.append(self.prev_sums)
        em.op("scalar", lambda s: s.copy(
            self.vbf.rearrange("p a t -> p (a t)"), vsb.rearrange("p a t -> p (a t)")),
            waits=tuple(cw), incs=self.s_act.inc())
        em.op("scalar", lambda s: s.activation(
            self.v2bf.rearrange("p a t -> p (a t)"), vsb.rearrange("p a t -> p (a t)"), AF.Square),
            incs=self.s_act.inc())
        casts_done = self.last_casts = self.s_act.now()
        k0w = [casts_done, self.s_dve.now()]
        for key in self.rel_keys:
            if em.bank_rel.get(id(key)) is not None:
                k0w.append(em.bank_rel[id(key)])
        for k in range(8):
            em.op("tensor", lambda t, k=k: t.matmul(self.pstA, self.ones128[:], self.vbf[:, k, :],
                                                    start=(k == 0), stop=(k == 7)),
                  waits=tuple(k0w) if k == 0 else ())
        for k in range(8):
            em.op("tensor", lambda t, k=k: t.matmul(self.pstB, self.ones128[:], self.v2bf[:, k, :],
                                                    start=(k == 0), stop=(k == 7)),
                  incs=self.s_pe.inc() if k == 7 else ())
        self.prev_sums = sums_done = self.s_pe.now()
        st = self.stat
        em.op("vector", lambda v: v.tensor_scalar_mul(st[:, 0, :], self.pstA, 1.0 / 1024),
              waits=sums_done, incs=self.s_dve.inc())
        em.op("vector", lambda v: v.tensor_scalar_mul(st[:, 1, :], self.pstB, 1.0 / 1024),
              incs=self.s_dve.inc())
        em.op("vector", lambda v: v.tensor_mul(st[:, 2, :], st[:, 0, :], st[:, 0, :]),
              incs=self.s_dve.inc())
        em.op("vector", lambda v: v.scalar_tensor_tensor(st[:, 2, :], st[:, 1, :], 1e-5, st[:, 2, :], OP.add, OP.subtract),
              incs=self.s_dve.inc())
        em.op("scalar", lambda s: s.activation(st[:, 3, :], st[:, 2, :], AF.Ln),
              waits=self.s_dve.now(), incs=self.s_act.inc())
        em.op("scalar", lambda s: s.activation(st[:, 1, :], st[:, 3, :], AF.Exp, scale=-0.5),
              incs=self.s_act.inc())
        rstd_done = self.s_act.now()
        bw = [rstd_done]
        if self.prev_bc_readers is not None:
            bw.append(self.prev_bc_readers)
        em.op("tensor", lambda t: t.matmul(self.pbc_mean, self.onesrow[:], st[:, 0, :], start=True, stop=True),
              waits=tuple(bw))
        em.op("tensor", lambda t: t.matmul(self.pbc_rstd, self.onesrow[:], st[:, 1, :], start=True, stop=True),
              incs=self.s_pe.inc())
        bc_done = self.s_pe.now()
        tgt = out_f32 if out_f32 is not None else out_bf
        for ot in range(8):
            em.op("vector", lambda v, ot=ot: v.tensor_sub(tgt[:, ot, :], vsb[:, ot, :], self.pbc_mean),
                  waits=(bc_done,) + tuple(out_waits) if ot == 0 else (), incs=self.s_dve.inc())
            em.op("vector", lambda v, ot=ot: v.tensor_mul(tgt[:, ot, :], tgt[:, ot, :], self.pbc_rstd),
                  incs=self.s_dve.inc())
        done = self.s_dve.now()
        for key in self.rel_keys:
            em.bank_rel[id(key)] = done
        ret = [done]
        if out_bf is not None and out_f32 is not None:
            em.op("scalar", lambda s: s.copy(
                out_bf.rearrange("p a t -> p (a t)"), out_f32.rearrange("p a t -> p (a t)")),
                waits=done, incs=self.s_act.inc())
            ret.append(self.s_act.now())
        self.prev_bc_readers = done
        return tuple(ret)



RG8 = [[0, 1, 2, 3, 4, 5, 6, 7]]
NH = 16


def build_mamba(T=4096):
    nc = bass.Bass("TRN2", target_bir_lowering=False, debug=False)
    aps = {"T": T}
    aps["u"] = nc.dram_tensor("u", [1024, 128 + T], F32, kind="ExternalInput").ap()
    aps["wxbd"] = nc.dram_tensor("wxbd", [1024, 2128], F32, kind="ExternalInput").ap()
    aps["wzc"] = nc.dram_tensor("wzc", [1024, 2112], F32, kind="ExternalInput").ap()
    aps["owT"] = nc.dram_tensor("owT", [2048, 1024], F32, kind="ExternalInput").ap()
    aps["cwcol"] = nc.dram_tensor("cwcol", [128, 18, 4], F32, kind="ExternalInput").ap()
    aps["cbcol"] = nc.dram_tensor("cbcol", [128, 18], F32, kind="ExternalInput").ap()
    aps["dtbcol"] = nc.dram_tensor("dtbcol", [16, 1], F32, kind="ExternalInput").ap()
    aps["alog"] = nc.dram_tensor("alog", [1, 16], F32, kind="ExternalInput").ap()
    aps["dvec"] = nc.dram_tensor("dvec", [1, 16], F32, kind="ExternalInput").ap()
    aps["rmspk"] = nc.dram_tensor("rmspk", [128, 16], F32, kind="ExternalInput").ap()
    aps["triuinc"] = nc.dram_tensor("triuinc", [128, 128], F32, kind="ExternalInput").ap()
    aps["upperneg"] = nc.dram_tensor("upperneg", [128, 128], F32, kind="ExternalInput").ap()
    aps["ident"] = nc.dram_tensor("ident", [128, 128], F32, kind="ExternalInput").ap()
    aps["m2T"] = nc.dram_tensor("m2T", [8, 8], F32, kind="ExternalInput").ap()
    aps["selcol"] = nc.dram_tensor("selcol", [8, 1], F32, kind="ExternalInput").ap()
    aps["x1"] = nc.dram_tensor("x1", [1024, T], F32, kind="ExternalOutput").ap()
    aps["sp_xtok"] = nc.dram_tensor("sp_xtok", [T, 2048], BF16).ap()
    aps["sp_btok"] = nc.dram_tensor("sp_btok", [T, 64], BF16).ap()
    aps["sp_bfeat"] = nc.dram_tensor("sp_bfeat", [64, T], BF16).ap()
    aps["sp_dt"] = nc.dram_tensor("sp_dt", [16, T], F32).ap()
    aps["w8d"] = nc.dram_tensor("w8d", [8, 16], F32).ap()
    aps["cum_dram"] = nc.dram_tensor("cum_dram", [16, 128], F32).ap()
    aps["ow_bf"] = nc.dram_tensor("ow_bf", [8, 128, 16, 128], BF16).ap()
    aps["ag_in"] = nc.dram_tensor("ag_in", [65, 2048], F32).ap()
    aps["ag_out"] = nc.dram_tensor("ag_out", [520, 2048], F32).ap()
    es = ExitStack()
    with es:
        em = Emit(nc, es)
        emit_mamba(em, aps)
        em.finish()
    return nc


def emit_mamba(em, cfg):
    T = cfg["T"]
    NSC = T // 512
    u = cfg["u"]
    urr = u.rearrange("(k p) t -> p k t", p=128)

    wbig = em.sb("a_wbig", [128, 8, 2176], BF16)
    cw = em.sb("a_cw", [128, 18, 4], F32)
    cb = em.sb("a_cb", [128, 18], F32)
    dtb = em.sb("a_dtb", [16, 1], F32)
    rmsw = em.sb("a_rmsw", [128, 16], F32)
    triu = em.sb("a_triu", [128, 128], F32)
    upneg = em.sb("a_upneg", [128, 128], F32)
    ident = em.sb("a_ident", [128, 128], F32)
    alogr = em.sb("a_alogr", [1, 16], F32)
    dvecr = em.sb("a_dvecr", [1, 16], F32)
    arow = em.sb("a_arow", [1, 16], F32)
    abc = em.sb("a_abc", [128, 16], F32)
    dial = em.sb("a_dial", [128, 16 * 128], BF16)
    usb1 = em.sb("a_u", [128, 8, 512], F32)
    usb2 = [usb1, usb1]
    ubf1 = em.sb("a_ub", [128, 8, 512], BF16)
    ubf2 = [ubf1, ubf1]
    uhalo = em.sb("a_uhalo", [128, 8, 3], BF16)
    scr = em.sb("a_scr", [128, 10304], F32)
    xcraw = scr[:, 0:8755].rearrange("p (a t) -> p a t", a=17)
    acc1 = [scr[:, 8768:9280], scr[:, 9280:9792]]
    cacc2 = scr[0:64, 9792:10304]
    sigz = scr[:, 0:8192].rearrange("p (a t) -> p a t", a=16)
    szp = scr[:, 8192:10240].rearrange("p (a t) -> p a t", a=16)
    sgld = scr[0:64, 0:2048]
    w8rep = scr[0:1, 2048:3072].bitcast(BF16)
    un2 = em.sb("a_un2", [128, 8192], BF16)
    xhat = un2[:].rearrange("p (a t) -> p a t", a=16)
    ygat = un2[:].rearrange("p (a t) -> p a t", a=16)
    lnvbf = un2[:, 0:4096].rearrange("p (a t) -> p a t", a=8)
    lnv2bf = un2[:, 4096:8192].rearrange("p (a t) -> p a t", a=8)
    un3 = em.sb("a_un3", [128, 4096], BF16)
    dtscr = un3[0:16, 0:3072].bitcast(F32).rearrange("p (a t) -> p a t", a=3)
    owsb = [un3[:, 0:2048].rearrange("p (a t) -> p a t", a=16),
            un3[:, 2048:4096].rearrange("p (a t) -> p a t", a=16)]
    ccraw = em.sb("a_ccraw", [64, 515], F32)
    bfeat = em.sb("a_bfeat", [64, 512], BF16)
    chat = em.sb("a_chat", [64, 512], BF16)
    dtraw = un3[0:16, 3072:4096].bitcast(F32)
    dtsf = em.sb("a_dtsf", [16, 4, 128], F32)
    xtok = [em.sb(f"a_xtok_{i}", [128, 2048], BF16) for i in range(2)]
    btok = [em.sb(f"a_btok_{i}", [128, 64], BF16) for i in range(2)]
    btok2 = em.sb("a_btok2", [128, 64], BF16)
    dtT = em.sb("a_dtT", [128, 16], F32)
    catok = em.sb("a_catok", [128, 16], F32)
    cafeat = em.sb("a_cafeat", [16, 128], F32)
    L2 = em.sb("a_L2", [2, 2048], F32)
    R2 = em.sb("a_R2", [2, 2048], F32)
    e1row = em.sb("a_e1row", [1, 2048], BF16)
    edec = em.sb("a_edec", [128, 16], F32)
    edl = em.sb("a_edl", [64, 16], F32)
    dacc = em.sb("a_dacc", [1, 16], F32)
    ssb = em.sb("a_ssb", [64, 16 * 128], F32)
    sbf = em.sb("a_sbf", [64, 16 * 128], BF16)
    g2sb = em.sb("a_g2sb", [128, 128], F32)
    emtmp = em.sb("a_emtmp", [128, 512], F32)
    emask = em.sb("a_emask", [128, 512], F32)
    g2all = em.sb("a_g2all", [128, 2048], BF16)
    cc2 = [em.sb(f"a_cc2_{i}", [64, 128], BF16) for i in range(2)]
    y2bf = em.sb("a_y2bf", [128, 2048], BF16)
    rstdr = em.sb("a_rstdr", [1, 3, 128], F32)
    rbc = em.sb("a_rbc", [128, 128], F32)
    w8all = em.sb("a_w8all", [1, 128], F32)
    dall = em.sb("a_dall", [8, 16], F32)
    m2t = em.sb("a_m2t", [8, 8], F32)
    selc = em.sb("a_selc", [8, 1], F32)
    onesrow = em.sb("a_onesrow", [1, 128], F32)
    onesbf = em.sb("a_onesbf", [1, 128], BF16)

    pY = em.ps("a_pY", [128, 2048])
    pD = [em.ps(f"a_pD{i}", [128, 512]) for i in range(2)]
    pq = [em.ps(f"a_pq{i}", [128, 512]) for i in range(2)]

    ln = LNKit(em, "a_ln", psums=(pD[0][0:1, 0:512], pD[1][0:1, 0:512],
                                  pD[0][:, 0:512], pD[1][:, 0:512],
                                  (pD[0], pD[1])), bufs=(lnvbf, lnv2bf))

    s_in = em.sem("a_in")
    s_w = em.sem("a_w")
    s_pe = em.sem("a_pe")
    s_dve = em.sem("a_dve")
    s_act = em.sem("a_act")
    s_tr = em.sem("a_tr")
    s_cum = em.sem("a_cum")
    s_inu = em.sem("a_inu")
    s_sp = em.sem("a_sp")
    s_ld = em.sem("a_ld")
    s_cc = em.sem("a_cc")
    s_st = em.sem("a_st")

    SR = {}

    em.op("gpsimd", lambda g: g.dma_start(
        out=wbig[:, :, 0:2128], in_=cfg["wxbd"].rearrange("(k p) o -> p k o", p=128)), incs=s_w.inc(16))
    for name_, sb_ in (("cwcol", cw), ("cbcol", cb), ("dtbcol", dtb), ("rmspk", rmsw),
                       ("triuinc", triu), ("upperneg", upneg), ("ident", ident),
                       ("m2T", m2t), ("selcol", selc), ("alog", alogr), ("dvec", dvecr)):
        em.op("gpsimd", lambda g, n=name_, s=sb_: g.dma_start(out=s[:], in_=cfg[n]), incs=s_w.inc(16))
    em.op("gpsimd", lambda g: g.dma_start(out=uhalo[:], in_=urr[:, :, 125:128]), incs=s_w.inc(16))
    if cfg.get("bcast") is not None:
        mg_sb = em.sb("a_mg", [128, 1], F32)
        em.op("gpsimd", lambda g: g.dma_start(out=mg_sb[:], in_=cfg["bcast"]["mgatec"]),
              incs=s_w.inc(16))
        cfg["bcast"]["mgate_sb"] = mg_sb
    wready = s_w.now()
    for buf, val in ((onesrow, 1.0), (onesbf, 1.0)):
        em.op("vector", lambda v, b_=buf, vv=val: v.memset(b_[:], vv), incs=s_dve.inc())
    em.op("vector", lambda v: v.memset(ssb[:], 0.0), incs=s_dve.inc())
    em.op("vector", lambda v: v.memset(dacc[:], 0.0), incs=s_dve.inc())
    em.op("vector", lambda v: v.memset(xcraw[:, :, 0:3], 0.0), incs=s_dve.inc())
    em.op("vector", lambda v: v.memset(ccraw[:, 0:3], 0.0), incs=s_dve.inc())
    em.op("vector", lambda v: v.memset(L2[:], 1.0), incs=s_dve.inc())
    em.op("vector", lambda v: v.memset(R2[:], -1.0), incs=s_dve.inc())
    ms_done = s_dve.now()

    em.op("scalar", lambda s: s.activation(arow[:], alogr[:], AF.Exp),
          waits=wready, incs=s_act.inc())
    em.op("vector", lambda v: v.tensor_scalar_mul(arow[:], arow[:], -1.0),
          waits=s_act.now(), incs=s_dve.inc())
    em.op("tensor", lambda t: t.matmul(pq[0][:, 0:16], onesrow[:], arow[:], start=True, stop=True),
          waits=(s_dve.now(), ms_done), incs=s_pe.inc())
    em.op("scalar", lambda s: s.copy(abc[:], pq[0][:, 0:16]), waits=s_pe.now(), incs=s_act.inc())
    em.op("tensor", lambda t: t.matmul(pq[0][:, 16:32], onesrow[:], dvecr[:], start=True, stop=True),
          waits=s_act.now(), incs=s_pe.inc())
    em.op("scalar", lambda s: s.copy(rbc[:, 0:16], pq[0][:, 16:32]), waits=s_pe.now(), incs=s_act.inc())
    dbc_done = s_act.now()
    for h in range(NH):
        em.op("vector", lambda v, h=h: v.tensor_scalar_mul(
            dial[:, h * 128:(h + 1) * 128], ident[:], rbc[:, h:h + 1]),
            waits=(dbc_done,) if h == 0 else (), incs=s_dve.inc())
    prep_done = s_dve.now()
    em.bank_rel[id(pq[0])] = dbc_done

    for k in range(8):
        for ot in range(17):
            m0, mw = ot * 128, (128 if ot < 16 else 80)
            em.op("tensor", lambda t, ot=ot, k=k, m0=m0, mw=mw: t.matmul(
                pq[1][0:mw, ot * 3:(ot + 1) * 3], wbig[:, k, m0:m0 + mw], uhalo[:, k, :],
                start=(k == 0), stop=(k == 7)),
                waits=(wready, prep_done) if (k == 0 and ot == 0) else (),
                incs=s_pe.inc() if (k == 7 and ot == 16) else ())
    halo_mm = s_pe.now()
    for ot in range(16):
        em.op("scalar", lambda s, ot=ot: s.copy(xcraw[:, ot, 0:3], pq[1][:, ot * 3:(ot + 1) * 3]),
              waits=(halo_mm, ms_done) if ot == 0 else (), incs=s_act.inc())
    em.op("scalar", lambda s: s.copy(xcraw[0:64, 16, 0:3], pq[1][0:64, 48:51]), incs=s_act.inc())
    em.bank_rel[id(pq[1])] = halo_done = s_act.now()

    calast = R2[0:1, :].rearrange("q (h t) -> q h t", t=128)[:, :, 127:128].squeeze(2)

    def cum_pipe(dt_ap, dt_ready, with_L2, cp=None, use_pq_rel=True):
        if cp is None:
            cp = pq[0][:]
        w0 = [dt_ready]
        rel_specs = (SR.get("cum"), em.bank_rel.get(id(pq[0])) if use_pq_rel else None)
        for spec in rel_specs:
            if spec is not None:
                w0.append(spec)
        em.op("tensor", lambda t, cp=cp: t.transpose(cp[:, 128:144], dt_ap, ident[0:16, 0:16]),
              waits=tuple(w0), incs=s_pe.inc())
        em.op("scalar", lambda s, cp=cp: s.copy(dtT[:], cp[:, 128:144]), waits=s_pe.now(), incs=s_act.inc())
        em.op("tensor", lambda t, cp=cp: t.matmul(cp[:, 144:160], triu[:], dtT[:], start=True, stop=True),
              waits=s_act.now(), incs=s_pe.inc())
        em.op("vector", lambda v, cp=cp: v.tensor_mul(catok[:], cp[:, 144:160], abc[:]),
              waits=s_pe.now(), incs=s_dve.inc())
        em.op("tensor", lambda t, cp=cp: t.transpose(cp[0:16, 160:288], catok[:], ident[:]),
              waits=s_dve.now(), incs=s_pe.inc())
        em.op("scalar", lambda s, cp=cp: s.copy(cafeat[:], cp[0:16, 160:288]),
              waits=s_pe.now(), incs=s_act.inc())
        caf_done = s_act.now()
        em.op("scalar", lambda e: e.dma_start(out=cfg["cum_dram"], in_=cafeat[:]),
              waits=caf_done, incs=s_cum.inc(16))
        cd_done = s_cum.now()
        em.op("scalar", lambda e: e.dma_start(
            out=R2[0:1, :], in_=cfg["cum_dram"].rearrange("h t -> (h t)").unsqueeze(0)),
            waits=cd_done, incs=s_cum.inc(16))
        if with_L2:
            em.op("scalar", lambda e: e.dma_start(
                out=L2[1:2, :], in_=cfg["cum_dram"].rearrange("h t -> (h t)").unsqueeze(0)),
                incs=s_cum.inc(16))
        row_done = s_cum.now()
        em.op("tensor", lambda t, cp=cp: t.matmul(cp[:, 288:304], onesrow[:], calast, start=True, stop=True),
              waits=row_done, incs=s_pe.inc())
        em.op("tensor", lambda t, cp=cp: t.matmul(cp[0:64, 304:320], onesrow[:, 0:64], calast, start=True, stop=True),
              incs=s_pe.inc())
        bc_done = s_pe.now()
        em.op("vector", lambda v, cp=cp: v.tensor_sub(edec[:], cp[:, 288:304], catok[:]),
              waits=bc_done, incs=s_dve.inc())
        em.op("scalar", lambda s: s.activation(edec[:], edec[:], AF.Exp),
              waits=s_dve.now(), incs=s_act.inc())
        em.op("scalar", lambda s, cp=cp: s.activation(edl[:], cp[0:64, 304:320], AF.Exp), incs=s_act.inc())
        SR["cum"] = done = s_act.now()
        return done, row_done

    def t_and_update(h, tb, extra_first, tp=None, use_pq_rel=True):
        slot = h % 2
        if tp is None:
            tp = pq[1][:]
        bw = list(extra_first) if extra_first else []
        if SR.get("btok2") is not None:
            bw.append(SR["btok2"])
        em.op("vector", lambda v, h=h, tb=tb: v.tensor_scalar_mul(
            btok2[:], btok[tb][:], edec[:, h:h + 1]),
            waits=tuple(bw), incs=s_dve.inc())
        bpp = s_dve.now()
        tw = [bpp]
        for spec in (SR.get(f"T{slot}"), em.bank_rel.get(id(pq[1])) if use_pq_rel else None):
            if spec is not None:
                tw.append(spec)
        em.op("tensor", lambda t, h=h, tb=tb, slot=slot, tp=tp: t.matmul(
            tp[0:64, slot * 128:(slot + 1) * 128], btok2[:],
            xtok[tb][:, h * 128:(h + 1) * 128], start=True, stop=True),
            waits=tuple(tw), incs=s_pe.inc())
        t_done = s_pe.now()
        em.op("vector", lambda v, h=h, slot=slot, tp=tp: v.scalar_tensor_tensor(
            ssb[:, h * 128:(h + 1) * 128], ssb[:, h * 128:(h + 1) * 128],
            edl[:, h:h + 1], tp[0:64, slot * 128:(slot + 1) * 128],
            OP.mult, OP.add), waits=t_done, incs=s_dve.inc())
        SR[f"T{slot}"] = s_dve.now()
        SR["btok2"] = t_done

    ubf_guards = [None, None]
    conv_guard = None
    xhat_guard = None
    tr_prev = [None, None]
    xtok_rd = [None, None]
    gchunk = 0
    for sc in range(NSC):
        b = sc % 2
        ubf = ubf2[b]
        t0 = 128 + sc * 512
        w = [wready]
        if ubf_guards[b] is not None:
            w.append(ubf_guards[b])
        em.op("gpsimd", lambda g, t0=t0, ubf=ubf: g.dma_start(
            out=ubf[:], in_=urr[:, :, t0:t0 + 512]), waits=tuple(w), incs=s_in.inc(16))
        cast_done = s_in.now()

        if sc > 0:
            em.op("scalar", lambda s: s.copy(xcraw[:, :, 0:3], xcraw[:, :, 512:515]),
                  waits=conv_guard, incs=s_act.inc())
        carry_done = s_act.now()

        def ep_x(ot, pb, pe_spec):
            em.op("scalar", lambda s, ot=ot, pb=pb: s.copy(xcraw[:, ot, 3:515], pb[:]),
                  waits=(pe_spec, carry_done), incs=s_act.inc())
            return s_act.now()
        emit_proj(em, pq, wbig, 8, ubf[:], 16, ep_x, s_pe, first_waits=(cast_done,))
        wts0 = [em.bank_rel[id(pq[0])]]
        for k in range(8):
            em.op("tensor", lambda t, k=k, ubf=ubf: t.matmul(
                pq[0][0:80, 0:512], wbig[:, k, 2048:2128], ubf[:, k, :],
                start=(k == 0), stop=(k == 7)),
                waits=tuple(wts0) if k == 0 else (), incs=s_pe.inc() if k == 7 else ())
        bd_mm = ubf_guards[0] = ubf_guards[1] = s_pe.now()
        em.op("scalar", lambda s: s.copy(xcraw[0:64, 16, 3:515], pq[0][0:64, 0:512]),
              waits=(bd_mm, carry_done), incs=s_act.inc())
        dtg = [xhat_guard] if xhat_guard else []
        em.op("scalar", lambda s: s.copy(dtraw, pq[0][64:80, 0:512]),
              waits=tuple(dtg), incs=s_act.inc())
        em.bank_rel[id(pq[0])] = evac_bd = s_act.now()

        silu_prev = [None, None]
        for tile in range(17):
            rows = 128 if tile < 16 else 64
            dst = acc1[tile % 2][0:rows] if tile < 16 else cacc2
            cwt = [evac_bd]
            if tile < 16 and silu_prev[tile % 2] is not None:
                cwt.append(silu_prev[tile % 2])
            if tile == 0 and xhat_guard is not None:
                cwt.append(xhat_guard)
            em.op("vector", lambda v, tile=tile, dst=dst, rows=rows: v.tensor_scalar_mul(
                dst, xcraw[0:rows, tile, 0:512], cw[0:rows, tile, 0:1]),
                waits=tuple(cwt), incs=s_dve.inc())
            for kk in range(1, 4):
                em.op("vector", lambda v, tile=tile, kk=kk, dst=dst, rows=rows: v.scalar_tensor_tensor(
                    dst, xcraw[0:rows, tile, kk:512 + kk], cw[0:rows, tile, kk:kk + 1],
                    dst, OP.mult, OP.add), incs=s_dve.inc())
            taps = s_dve.now()
            if tile < 16:
                em.op("scalar", lambda s, tile=tile: s.activation(
                    xhat[:, tile, :], acc1[tile % 2], AF.Silu, bias=cb[:, tile:tile + 1]),
                    waits=taps, incs=s_act.inc())
                silu_prev[tile % 2] = s_act.now()
            else:
                bfw = [taps]
                if SR.get("bfeat_rd") is not None:
                    bfw.extend(SR["bfeat_rd"])
                em.op("scalar", lambda s: s.activation(
                    bfeat[:], cacc2, AF.Silu, bias=cb[0:64, 16:17]),
                    waits=tuple(bfw), incs=s_act.inc())
        conv_guard = s_dve.now()
        silu_done = s_act.now()

        em.op("scalar", lambda s, sc=sc: s.dma_start(
            out=cfg["sp_bfeat"][:, sc * 512:(sc + 1) * 512], in_=bfeat[:]),
            waits=silu_done, incs=s_sp.inc(16))

        em.op("scalar", lambda s: s.activation(dtscr[:, 0, :], dtraw, AF.Abs, bias=dtb[:]),
              waits=evac_bd, incs=s_act.inc())
        em.op("scalar", lambda s: s.activation(dtscr[:, 1, :], dtscr[:, 0, :], AF.Exp, scale=-1.0),
              incs=s_act.inc())
        em.op("vector", lambda v: v.tensor_scalar_add(dtscr[:, 1, :], dtscr[:, 1, :], 1.0),
              waits=s_act.now(), incs=s_dve.inc())
        em.op("scalar", lambda s: s.activation(dtscr[:, 2, :], dtscr[:, 1, :], AF.Ln),
              waits=s_dve.now(), incs=s_act.inc())
        em.op("scalar", lambda s: s.activation(dtscr[:, 0, :], dtraw, AF.Relu, bias=dtb[:]),
              incs=s_act.inc())
        dfw = [s_act.now()]
        if SR.get("dtsf_rd") is not None:
            dfw.extend(SR["dtsf_rd"])
        em.op("vector", lambda v: v.tensor_add(
            dtsf[:].rearrange("p a t -> p (a t)"), dtscr[:, 0, :], dtscr[:, 2, :]),
            waits=tuple(dfw), incs=s_dve.inc())
        dt_done = s_dve.now()
        em.op("scalar", lambda s, sc=sc: s.dma_start(
            out=cfg["sp_dt"][:, sc * 512:(sc + 1) * 512],
            in_=dtsf[:].rearrange("p a t -> p (a t)")), waits=dt_done, incs=s_sp.inc(16))

        for cc in range(4):
            tb = gchunk % 2
            trw = [silu_done]
            if tr_prev[tb] is not None:
                trw.append(tr_prev[tb])
            if xtok_rd[tb] is not None:
                trw.append(xtok_rd[tb])
            for ft in range(16):
                em.op("sync", lambda sy, ft=ft, cc=cc, tb=tb: sy.dma_start_transpose(
                    xtok[tb][:, ft * 128:(ft + 1) * 128], xhat[:, ft, cc * 128:(cc + 1) * 128]),
                    waits=tuple(trw) if ft == 0 else (), incs=s_tr.inc(16))
            em.op("sync", lambda sy, cc=cc, tb=tb: sy.dma_start_transpose(
                btok[tb][:], bfeat[:, cc * 128:(cc + 1) * 128]), incs=s_tr.inc(16))
            tr_done = s_tr.now()
            em.op("scalar", lambda s, gc=gchunk, tb=tb: s.dma_start(
                out=cfg["sp_xtok"][gc * 128:(gc + 1) * 128, :], in_=xtok[tb][:]),
                waits=tr_done, incs=s_sp.inc(16))
            em.op("scalar", lambda s, gc=gchunk, tb=tb: s.dma_start(
                out=cfg["sp_btok"][gc * 128:(gc + 1) * 128, :], in_=btok[tb][:]),
                incs=s_sp.inc(16))
            tr_prev[tb] = s_sp.now()
            cum_done, _ = cum_pipe(dtsf[:, cc, :], dt_done, with_L2=False,
                                   cp=pY[:, 0:512], use_pq_rel=False)
            em.op("vector", lambda v: v.tensor_add(
                dacc[:].unsqueeze(2), dacc[:].unsqueeze(2),
                R2[0:1, :].rearrange("q (h t) -> q h t", t=128)[:, :, 127:128]),
                waits=cum_done, incs=s_dve.inc())
            for h in range(NH):
                t_and_update(h, tb, (cum_done, tr_done) if h == 0 else None,
                             tp=pY[:, 512:1024], use_pq_rel=False)
            xtok_rd[tb] = s_pe.now()
            gchunk += 1
        xhat_guard = (s_sp, s_sp.n)
        SR["dtsf_rd"] = [SR["cum"], s_sp.now()]
        SR["bfeat_rd"] = [s_tr.now(), s_sp.now()]

    sfin = s_dve.now()
    em.op("scalar", lambda s: s.dma_start(out=cfg["ag_in"][0:64, :], in_=ssb[:]),
          waits=sfin, incs=s_sp.inc(16))
    em.op("scalar", lambda s: s.dma_start(out=cfg["ag_in"][64:65, 0:16], in_=dacc[:]),
          incs=s_sp.inc(16))
    em.op("gpsimd", lambda g: g.collective_compute(
        "AllGather", OP.bypass, replica_groups=RG8,
        ins=[cfg["ag_in"].opt()], outs=[cfg["ag_out"].opt()]),
        waits=s_sp.now(), incs=s_cc.inc(1))
    ag_done = s_cc.now()

    em.op("sync", lambda sy: sy.dma_start(
        out=dall[:], in_=cfg["ag_out"].rearrange("(j r) h -> j r h", r=65)[:, 64, 0:16]),
        waits=ag_done, incs=s_ld.inc(16))
    em.op("tensor", lambda t: t.matmul(pq[0][0:8, 0:16], m2t[:], dall[:], start=True, stop=True),
          waits=(s_ld.now(), em.bank_rel[id(pq[0])]), incs=s_pe.inc())
    em.op("scalar", lambda s: s.activation(dall[:], pq[0][0:8, 0:16], AF.Exp),
          waits=s_pe.now(), incs=s_act.inc())
    em.bank_rel[id(pq[0])] = s_act.now()
    em.op("vector", lambda v: v.tensor_scalar_mul(dall[:], dall[:], selc[:]),
          waits=s_act.now(), incs=s_dve.inc())
    em.op("scalar", lambda s: s.dma_start(out=cfg["w8d"], in_=dall[:]),
          waits=s_dve.now(), incs=s_sp.inc(16))
    em.op("sync", lambda sy: sy.dma_start(
        out=w8all[:], in_=cfg["w8d"].rearrange("j h -> (j h)").unsqueeze(0)),
        waits=s_sp.now(), incs=s_ld.inc(16))
    w8_done = s_ld.now()
    sinit_rel = None
    for j in range(8):
        em.op("vector", lambda v, j=j: v.tensor_copy(
            w8rep[:].rearrange("q (h t) -> q h t", h=16),
            w8all[:, j * 16:(j + 1) * 16].unsqueeze(2).broadcast_to([1, 16, 128])),
            waits=(w8_done,) if j == 0 else (), incs=s_dve.inc())
        wrep_done = s_dve.now()
        for q in range(4):
            ww = [wrep_done] if q == 0 else []
            if q == 0 and sinit_rel is not None:
                ww.append(sinit_rel)
            em.op("tensor", lambda t, q=q: t.matmul(
                pY[0:64, q * 512:(q + 1) * 512], onesbf[:, 0:64], w8rep[:, q * 512:(q + 1) * 512],
                start=True, stop=True), waits=tuple(ww), incs=s_pe.inc() if q == 3 else ())
        wbc_done = s_pe.now()
        em.op("sync", lambda sy, j=j: sy.dma_start(
            out=sgld[:], in_=cfg["ag_out"][j * 65:j * 65 + 64, :]),
            waits=(ag_done,) if j == 0 else (), incs=s_ld.inc(16))
        ld_done = s_ld.now()
        if j == 0:
            em.op("vector", lambda v: v.tensor_mul(ssb[:], sgld[:], pY[0:64, :]),
                  waits=(wbc_done, ld_done, sfin), incs=s_dve.inc())
        else:
            em.op("vector", lambda v: v.tensor_mul(sgld[:], sgld[:], pY[0:64, :]),
                  waits=(wbc_done, ld_done), incs=s_dve.inc())
            em.op("vector", lambda v: v.tensor_add(ssb[:], ssb[:], sgld[:]), incs=s_dve.inc())
        sinit_rel = s_dve.now()
    em.bank_rel[id(pY)] = sinit_rel

    em.op("gpsimd", lambda g: g.dma_start(
        out=wbig[:, :, 0:2112], in_=cfg["wzc"].rearrange("(k p) o -> p k o", p=128)),
        waits=(s_pe.now(), ag_done), incs=s_w.inc(16))
    wzc_ready = s_w.now()
    for aa in range(8):
        em.op("gpsimd", lambda g, aa=aa: g.dma_start(
            out=cfg["ow_bf"][aa],
            in_=cfg["owT"].rearrange("(k p) o -> p k o", p=128)[:, :, aa * 128:(aa + 1) * 128]),
            incs=s_w.inc(16))
    owbf_ready = s_w.now()
    for k in range(8):
        em.op("tensor", lambda t, k=k: t.matmul(
            pq[1][0:64, 0:3], wbig[:, k, 2048:2112], uhalo[:, k, :],
            start=(k == 0), stop=(k == 7)),
            waits=(wzc_ready, em.bank_rel[id(pq[1])]) if k == 0 else (),
            incs=s_pe.inc() if k == 7 else ())
    em.op("scalar", lambda s: s.copy(ccraw[:, 0:3], pq[1][0:64, 0:3]),
          waits=s_pe.now(), incs=s_act.inc())
    em.bank_rel[id(pq[1])] = s_act.now()

    usb_guards = [None, None]
    gchunk = 0
    xl_prev = [None, None]
    cconv_guard = None
    prev_apply = None
    prev_ln_casts = None
    ow_prev = [None, None]
    SR.pop("cum", None)
    ubf_guards = [s_pe.now(), s_pe.now()]
    bcast = cfg.get("bcast")
    s_bc = em.sem("a_bc")
    s_bcc = em.sem("a_bcc")
    for sc in range(NSC):
        b = sc % 2
        usb = usb2[b]
        ubf = ubf2[b]
        t0 = 128 + sc * 512
        w = [wzc_ready]
        if usb_guards[b] is not None:
            w.extend(usb_guards[b])
        em.op("gpsimd", lambda g, t0=t0, ubf=ubf: g.dma_start(
            out=ubf[:], in_=urr[:, :, t0:t0 + 512]), waits=(ubf_guards[b],), incs=s_in.inc(16))
        cast_done = s_in.now()
        if bcast is not None and sc == NSC - 1:
            em.op("gpsimd", lambda g: g.collective_compute(
                "AllReduce", OP.add, replica_groups=RG8,
                ins=[bcast["ar1_in"].opt()], outs=[bcast["ar1_out"].opt()]),
                waits=s_bc.at(64), incs=s_bcc.inc(1))
        em.op("sync", lambda sy, t0=t0, usb=usb: sy.dma_start(
            out=usb[:], in_=urr[:, :, t0:t0 + 512]), waits=tuple(w), incs=s_inu.inc(16))
        usb_done = s_inu.now()

        def ep_z(ot, pb, pe_spec):
            wz = [pe_spec]
            if prev_apply is not None:
                wz.append(prev_apply)
            em.op("scalar", lambda s, ot=ot, pb=pb: s.activation(
                sigz[:, ot, :], pb[:], AF.Sigmoid), waits=tuple(wz), incs=s_act.inc())
            em.op("vector", lambda v, ot=ot: v.tensor_scalar_mul(
                sigz[:, ot, :], sigz[:, ot, :], rmsw[:, ot:ot + 1]),
                waits=s_act.now(), incs=s_dve.inc())
            return s_dve.now()
        emit_proj(em, pq, wbig, 8, ubf[:], 16, ep_z, s_pe, first_waits=(cast_done,))
        if sc > 0:
            em.op("scalar", lambda s: s.copy(ccraw[:, 0:3], ccraw[:, 512:515]),
                  waits=cconv_guard, incs=s_act.inc())
        ccarry = s_act.now()
        wtc = [em.bank_rel[id(pq[0])]]
        for k in range(8):
            em.op("tensor", lambda t, k=k, ubf=ubf: t.matmul(
                pq[0][0:64, 0:512], wbig[:, k, 2048:2112], ubf[:, k, :],
                start=(k == 0), stop=(k == 7)),
                waits=tuple(wtc) if k == 0 else (), incs=s_pe.inc() if k == 7 else ())
        ubf_guards[0] = ubf_guards[1] = s_pe.now()
        em.op("scalar", lambda s: s.copy(ccraw[:, 3:515], pq[0][0:64, 0:512]),
              waits=(s_pe.now(), ccarry), incs=s_act.inc())
        em.bank_rel[id(pq[0])] = cevac = s_act.now()
        em.op("vector", lambda v: v.tensor_scalar_mul(cacc2, ccraw[:, 0:512], cw[0:64, 17, 0:1]),
              waits=cevac, incs=s_dve.inc())
        for kk in range(1, 4):
            em.op("vector", lambda v, kk=kk: v.scalar_tensor_tensor(
                cacc2, ccraw[:, kk:512 + kk], cw[0:64, 17, kk:kk + 1], cacc2,
                OP.mult, OP.add), incs=s_dve.inc())
        cconv_guard = s_dve.now()
        chw = [cconv_guard]
        if SR.get("chat") is not None:
            chw.append(SR["chat"])
        em.op("scalar", lambda s: s.activation(chat[:], cacc2, AF.Silu, bias=cb[0:64, 17:18]),
              waits=tuple(chw), incs=s_act.inc())
        chat_done = s_act.now()
        em.op("sync", lambda sy, sc=sc: sy.dma_start(
            out=bfeat[:], in_=cfg["sp_bfeat"][:, sc * 512:(sc + 1) * 512]),
            waits=(SR.get("bfeat") or ()) and (SR["bfeat"],), incs=s_ld.inc(16))
        bf_ld = s_ld.now()
        em.op("sync", lambda sy, sc=sc: sy.dma_start(
            out=dtsf[:].rearrange("p a t -> p (a t)"), in_=cfg["sp_dt"][:, sc * 512:(sc + 1) * 512]),
            waits=(SR.get("dtld") or ()) and (SR["dtld"],), incs=s_ld.inc(16))
        dt_ld = s_ld.now()

        for cc in range(4):
            tb = gchunk % 2
            lw = []
            if xl_prev[tb] is not None:
                lw.append(xl_prev[tb])
            em.op("scalar", lambda s, gc=gchunk, tb=tb: s.dma_start(
                out=xtok[tb][:], in_=cfg["sp_xtok"][gc * 128:(gc + 1) * 128, :]),
                waits=tuple(lw), incs=s_ld.inc(16))
            em.op("scalar", lambda s, gc=gchunk, tb=tb: s.dma_start(
                out=btok[tb][:], in_=cfg["sp_btok"][gc * 128:(gc + 1) * 128, :]),
                incs=s_ld.inc(16))
            ld_done = s_ld.now()
            cum_done, _ = cum_pipe(dtsf[:, cc, :], dt_ld, with_L2=True)
            ew1 = [cum_done]
            if SR.get("e1") is not None:
                ew1.append(SR["e1"])
            em.op("scalar", lambda s: s.activation(e1row[:], R2[0:1, :], AF.Exp),
                  waits=tuple(ew1), incs=s_act.inc())
            e1_done = s_act.now()
            scw = [SR["Supd"]] if SR.get("Supd") else []
            em.op("scalar", lambda s: s.copy(sbf[:], ssb[:]), waits=tuple(scw), incs=s_act.inc())
            scast_done = s_act.now()
            gw_ = [chat_done, bf_ld]
            for spec in (SR.get("G2"), SR.get("RBC"), em.bank_rel.get(id(pq[0]))):
                if spec is not None:
                    gw_.append(spec)
            em.op("tensor", lambda t, cc=cc: t.matmul(
                pq[0][:, 0:128], bfeat[:, cc * 128:(cc + 1) * 128], chat[:, cc * 128:(cc + 1) * 128],
                start=True, stop=True), waits=tuple(gw_), incs=s_pe.inc())
            em.op("scalar", lambda s: s.copy(g2sb[:], pq[0][:, 0:128]),
                  waits=s_pe.now(), incs=s_act.inc())
            SR["G2"] = g2_done = s_act.now()
            for g in range(4):
                dw_ = [cum_done]
                for spec in (SR.get(f"D{g % 2}"), em.bank_rel.get(id(pD[g % 2]))):
                    if spec is not None:
                        dw_.append(spec)
                for hh in range(4):
                    h = g * 4 + hh
                    em.op("tensor", lambda t, g=g, hh=hh, h=h: t.matmul(
                        pD[g % 2][:, hh * 128:(hh + 1) * 128],
                        L2[:, h * 128:(h + 1) * 128], R2[:, h * 128:(h + 1) * 128],
                        start=True, stop=True),
                        waits=tuple(dw_) if hh == 0 else (),
                        incs=s_pe.inc() if hh == 3 else ())
                dmm = s_pe.now()
                em.op("vector", lambda v, g=g: v.tensor_tensor(
                    emtmp[:].rearrange("p (a t) -> p a t", a=4),
                    pD[g % 2][:].rearrange("p (a t) -> p a t", a=4),
                    upneg[:].unsqueeze(1).broadcast_to([128, 4, 128]), OP.add),
                    waits=dmm, incs=s_dve.inc())
                em.op("scalar", lambda s, g=g: s.activation(emask[:], emtmp[:], AF.Exp),
                      waits=s_dve.now(), incs=s_act.inc())
                em.op("vector", lambda v, g=g: v.tensor_tensor(
                    g2all[:, g * 512:(g + 1) * 512].rearrange("p (a t) -> p a t", a=4),
                    emask[:].rearrange("p (a t) -> p a t", a=4),
                    g2sb[:].unsqueeze(1).broadcast_to([128, 4, 128]), OP.mult),
                    waits=(s_act.now(), g2_done), incs=s_dve.inc())
                SR[f"D{g % 2}"] = s_dve.now()
            g2a_done = s_dve.now()
            yw = [g2a_done, scast_done, ld_done, e1_done]
            for spec in (SR.get("Y"), em.bank_rel.get(id(pY))):
                if spec is not None:
                    yw.append(spec)
            em.bank_rel[id(pY)] = None
            for h in range(NH):
                slot = h % 2
                ew = [e1_done]
                for spec in (SR.get(f"EB{slot}"), em.bank_rel.get(id(pq[1]))):
                    if spec is not None:
                        ew.append(spec)
                em.op("tensor", lambda t, h=h, slot=slot: t.matmul(
                    pq[1][0:64, 256 + slot * 128:384 + slot * 128],
                    onesbf[:, 0:64], e1row[:, h * 128:(h + 1) * 128], start=True, stop=True),
                    waits=tuple(ew), incs=s_pe.inc())
                eb_done = s_pe.now()
                ccw = [eb_done]
                if SR.get(f"YH{slot}") is not None:
                    ccw.append(SR[f"YH{slot}"])
                em.op("vector", lambda v, h=h, cc=cc, slot=slot: v.tensor_mul(
                    cc2[slot][:], chat[:, cc * 128:(cc + 1) * 128],
                    pq[1][0:64, 256 + slot * 128:384 + slot * 128]),
                    waits=tuple(ccw), incs=s_dve.inc())
                SR[f"EB{slot}"] = cc2_done = s_dve.now()
                em.op("tensor", lambda t, h=h, tb=tb: t.matmul(
                    pY[:, h * 128:(h + 1) * 128], xtok[tb][:, h * 128:(h + 1) * 128],
                    g2all[:, h * 128:(h + 1) * 128], start=True, stop=False),
                    waits=tuple(yw) if h == 0 else ())
                em.op("tensor", lambda t, h=h, slot=slot: t.matmul(
                    pY[:, h * 128:(h + 1) * 128], sbf[:, h * 128:(h + 1) * 128],
                    cc2[slot][:], start=False, stop=False), waits=cc2_done)
                em.op("tensor", lambda t, h=h, tb=tb: t.matmul(
                    pY[:, h * 128:(h + 1) * 128], xtok[tb][:, h * 128:(h + 1) * 128],
                    dial[:, h * 128:(h + 1) * 128], start=False, stop=True),
                    incs=s_pe.inc())
                SR[f"YH{slot}"] = s_pe.now()
                t_and_update(h, tb, (cum_done, ld_done) if h == 0 else None)
            SR["Supd"] = SR["T1"]
            y_mm_done = s_pe.now()
            xl_prev[tb] = y_mm_done
            em.op("scalar", lambda s: s.activation(y2bf[:], pY[:], AF.Square),
                  waits=y_mm_done, incs=s_act.inc())
            y2_done = s_act.now()
            mw_ = [y2_done, ln.init_done]
            if SR.get("MU") is not None:
                mw_.append(SR["MU"])
            for k in range(16):
                em.op("tensor", lambda t, k=k: t.matmul(
                    pq[0][0:1, 320:448], ln.ones128[:], y2bf[:, k * 128:(k + 1) * 128],
                    start=(k == 0), stop=(k == 15)),
                    waits=tuple(mw_) if k == 0 else (), incs=s_pe.inc() if k == 15 else ())
            mu_done = s_pe.now()
            em.op("vector", lambda v: v.tensor_scalar_mul(rstdr[:, 0, :], pq[0][0:1, 320:448], 1.0 / 2048),
                  waits=mu_done, incs=s_dve.inc())
            SR["MU"] = s_dve.now()
            em.op("vector", lambda v: v.tensor_scalar_add(rstdr[:, 1, :], rstdr[:, 0, :], 1e-5),
                  incs=s_dve.inc())
            em.op("scalar", lambda s: s.activation(rstdr[:, 2, :], rstdr[:, 1, :], AF.Ln),
                  waits=s_dve.now(), incs=s_act.inc())
            em.op("scalar", lambda s: s.activation(rstdr[:, 1, :], rstdr[:, 2, :], AF.Exp, scale=-0.5),
                  incs=s_act.inc())
            rstd_done = s_act.now()
            rw_ = [rstd_done, g2a_done]
            for spec in (SR.get("RBC"), SR.get("G2")):
                if spec is not None:
                    rw_.append(spec)
            em.op("tensor", lambda t: t.matmul(pq[0][:, 0:128], onesrow[:], rstdr[:, 1, :],
                                               start=True, stop=True),
                  waits=tuple(rw_), incs=s_pe.inc())
            em.op("scalar", lambda s: s.copy(rbc[:], pq[0][:, 0:128]),
                  waits=s_pe.now(), incs=s_act.inc())
            SR["RBC"] = rbc_done = s_act.now()
            em.op("vector", lambda v, cc=cc: v.tensor_tensor(
                szp[:], sigz[:, :, cc * 128:(cc + 1) * 128],
                rbc[:].unsqueeze(1).broadcast_to([128, 16, 128]), OP.mult),
                waits=(rbc_done,), incs=s_dve.inc())
            ygw = []
            if cc == 0 and ln.prev_sums is not None:
                ygw.append(ln.prev_sums)
            em.op("vector", lambda v, cc=cc: v.tensor_tensor(
                ygat[:, :, cc * 128:(cc + 1) * 128], szp[:],
                pY[:].rearrange("p (a t) -> p a t", a=16), OP.mult),
                waits=tuple(ygw), incs=s_dve.inc())
            SR["Y"] = s_dve.now()
            gchunk += 1
        yg_done = s_dve.now()
        em.bank_rel[id(pq[0])] = s_dve.now()
        em.bank_rel[id(pq[1])] = s_dve.now()

        for ot in range(8):
            oww = []
            if ow_prev[ot % 2] is not None:
                oww.append(ow_prev[ot % 2])
            oww.append(owbf_ready)
            em.op("sync", lambda sy, ot=ot: sy.dma_start(
                out=owsb[ot % 2][:], in_=cfg["ow_bf"][ot]),
                waits=tuple(oww), incs=s_w.inc(16))
            ow_ld = s_w.now()
            pb = pq[ot % 2]
            for k in range(16):
                wts = []
                if k == 0:
                    wts = [ow_ld, yg_done]
                    if em.bank_rel.get(id(pb)) is not None:
                        wts.append(em.bank_rel[id(pb)])
                em.op("tensor", lambda t, ot=ot, k=k, pb=pb: t.matmul(
                    pb[:, 0:512], owsb[ot % 2][:, k, :], ygat[:, k, :],
                    start=(k == 0), stop=(k == 15)),
                    waits=tuple(wts), incs=s_pe.inc() if k == 15 else ())
            mm_done = s_pe.now()
            ow_prev[ot % 2] = mm_done
            em.op("vector", lambda v, ot=ot, pb=pb, usb=usb: v.tensor_add(
                sigz[:, ot, :], usb[:, ot, :], pb[:, 0:512]),
                waits=(mm_done, usb_done) if ot == 0 else mm_done, incs=s_dve.inc())
            em.bank_rel[id(pb)] = s_dve.now()
        v_done = s_dve.now()
        SR["Y"] = s_pe.now()
        em.bank_rel[id(pD[0])] = SR.get("D0")
        em.bank_rel[id(pD[1])] = SR.get("D1")
        apply_done = ln.emit(sigz[:, 0:8, :], usb[:], v_ready=(v_done,))[0]
        prev_apply = apply_done
        prev_ln_casts = ln.last_casts
        em.op("sync", lambda sy, sc=sc, usb=usb: sy.dma_start(
            out=cfg["x1"].rearrange("(k p) t -> p k t", p=128)[:, :, sc * 512:(sc + 1) * 512],
            in_=usb[:]), waits=apply_done, incs=s_st.inc(16))
        usb_guards[0] = usb_guards[1] = [s_st.now()]
        if bcast is not None and sc < 4:
            em.op("vector", lambda v, usb=usb: v.tensor_scalar_mul(
                sigz[:, 0:8, :].rearrange("p a t -> p (a t)"),
                usb[:].rearrange("p a t -> p (a t)"), bcast["mgate_sb"][:]),
                waits=apply_done, incs=s_dve.inc())
            em.op("scalar", lambda s, sc=sc: s.dma_start(
                out=bcast["ar1_in"].rearrange("(k p) t -> p k t", p=128)[:, :, sc * 512:(sc + 1) * 512],
                in_=sigz[:, 0:8, :]), waits=s_dve.now(), incs=s_bc.inc(16))
            usb_guards[0] = usb_guards[1] = usb_guards[0] + [s_bc.now()]
            prev_apply = s_bc.now()
        SR["bfeat"] = s_pe.now()
        SR["dtld"] = SR["cum"]
        SR["e1"] = s_pe.now()
        SR["chat"] = s_dve.now()
    em.op("sync", lambda sy: sy.nop(), waits=s_st.now())
    em.bcast_done = s_bcc.now() if bcast is not None else None


def host_mamba_inputs(u_slice_fm_with_halo, in_proj_w, conv_w, conv_b, dt_bias,
                      A_log, D, rms_w, out_w, core_id, consts):
    W = np.asarray(in_proj_w, np.float32)
    wT = W.T
    wxbd = np.ascontiguousarray(np.concatenate(
        [wT[:, 2048:4096], wT[:, 4096:4160], wT[:, 4224:4240]], axis=1))
    wzc = np.ascontiguousarray(np.concatenate(
        [wT[:, 0:2048], wT[:, 4160:4224]], axis=1))
    cwf = np.asarray(conv_w, np.float32)
    cbf = np.asarray(conv_b, np.float32)
    cw18 = np.zeros((128, 18, 4), np.float32)
    cb18 = np.zeros((128, 18), np.float32)
    cw18[:, 0:16, :] = cwf[0:2048].reshape(16, 128, 4).transpose(1, 0, 2)
    cb18[:, 0:16] = cbf[0:2048].reshape(16, 128).T
    cw18[0:64, 16, :] = cwf[2048:2112]
    cb18[0:64, 16] = cbf[2048:2112]
    cw18[0:64, 17, :] = cwf[2112:2176]
    cb18[0:64, 17] = cbf[2112:2176]
    slot = core_id % 4
    m2 = np.zeros((8, 8), np.float32)
    sel = np.zeros((8, 1), np.float32)
    base = (core_id // 4) * 4
    for j in range(base, base + slot):
        sel[j, 0] = 1.0
        for i in range(j + 1, base + slot):
            m2[j, i] = 1.0
    return {
        "u": np.ascontiguousarray(u_slice_fm_with_halo, np.float32),
        "wxbd": wxbd, "wzc": wzc,
        "owT": host_wT(out_w),
        "cwcol": cw18, "cbcol": cb18,
        "dtbcol": np.asarray(dt_bias, np.float32).reshape(16, 1),
        "alog": np.asarray(A_log, np.float32).reshape(1, 16),
        "dvec": np.asarray(D, np.float32).reshape(1, 16),
        "rmspk": np.ascontiguousarray(np.asarray(rms_w, np.float32).reshape(16, 128).T),
        "triuinc": consts["triuinc"], "upperneg": consts["upperneg"],
        "ident": consts["ident"],
        "m2T": np.ascontiguousarray(m2.T), "selcol": sel,
    }



RG8 = [[0, 1, 2, 3, 4, 5, 6, 7]]


def build_moe():
    nc = bass.Bass("TRN2", target_bir_lowering=False, debug=False)
    aps = {}
    aps["x1m"] = nc.dram_tensor("x1m", [1024, 2048], F32, kind="ExternalInput").ap()
    for w in ("gwT", "uwT", "dwT"):
        aps[w] = nc.dram_tensor(w, [1024, 1024], F32, kind="ExternalInput").ap()
    for bname in ("gbpk", "ubpk", "dbpk"):
        aps[bname] = nc.dram_tensor(bname, [128, 8], F32, kind="ExternalInput").ap()
    aps["ridx"] = nc.dram_tensor("ridx", [128, 16, 2], I32, kind="ExternalInput").ap()
    aps["rwcol"] = nc.dram_tensor("rwcol", [128, 16], F32, kind="ExternalInput").ap()
    aps["ecol"] = nc.dram_tensor("ecol", [128, 1], F32, kind="ExternalInput").ap()
    aps["moe"] = nc.dram_tensor("moe", [1024, 2048], F32, kind="ExternalOutput").ap()
    aps["ar_in"] = nc.dram_tensor("ar_in", [1024, 2048], F32).ap()
    aps["m_dram"] = nc.dram_tensor("m_dram", [1, 2048], F32).ap()
    aps["ar_out"] = nc.dram_tensor("ar_out", [1024, 2048], F32).ap()
    es = ExitStack()
    with es:
        em = Emit(nc, es)
        emit_moe(em, aps)
        em.finish()
    return nc


def emit_moe(em, cfg):
    x1m = cfg.get("x1m")
    gw = em.sb("b_gw", [128, 8, 1024], BF16)
    uw = em.sb("b_uw", [128, 8, 1024], BF16)
    dw = em.sb("b_dw", [128, 8, 1024], BF16)
    gb = em.sb("b_gb", [128, 8], F32)
    ub = em.sb("b_ub", [128, 8], F32)
    db = em.sb("b_db", [128, 8], F32)
    ridxf = em.sb("b_ridxf", [128, 16, 2], F32)
    ridxi = em.sb("b_ridxi", [128, 16, 2], I32)
    rwcol = em.sb("b_rwcol", [128, 16], F32)
    ecol = em.sb("b_ecol", [128, 1], F32)
    eq = em.sb("b_eq", [128, 16, 2], F32)
    msb = em.sb("b_m", [128, 16], F32)
    mrow = em.sb("b_mrow", [1, 2048], F32)
    Mb = em.sb("b_Mb", [128, 4, 512], F32)
    onesrow = em.sb("b_onesrow", [1, 128], F32)
    x1sb = [em.sb(f"b_x1_{i}", [128, 8, 512], F32) for i in range(2)]
    x1bf = em.sb("b_x1b", [128, 8, 512], BF16)
    g7 = em.sb("b_g7", [128, 8, 512], F32)
    up7 = em.sb("b_up7", [128, 8, 512], F32)
    sg = em.sb("b_sg", [128, 8, 512], F32)
    glu = em.sb("b_glu", [128, 8, 512], F32)
    inter = em.sb("b_inter", [128, 8, 512], BF16)
    part = em.sb("b_part", [128, 8, 512], F32)

    pq = [em.ps(f"b_pq{i}", [128, 512]) for i in range(2)]
    pM = em.ps("b_pM", [128, 512])

    s_in = em.sem("b_in")
    s_w = em.sem("b_w")
    s_pe = em.sem("b_pe")
    s_dve = em.sem("b_dve")
    s_act = em.sem("b_act")
    s_ps = em.sem("b_ps")
    s_cc = em.sem("b_cc")
    s_st = em.sem("b_st")
    s_mr = em.sem("b_mr")

    em._moe_cc = s_cc
    if x1m is None:
        x1m = cfg["ar1_out"]
        em.pending["sync"] = list(em.pending.get("sync", [])) + [cfg["bcast_done"]]

    for w, sb in (("gwT", gw), ("uwT", uw), ("dwT", dw)):
        em.op("gpsimd", lambda g, w=w, sb=sb: g.dma_start(
            out=sb[:], in_=cfg[w].rearrange("(k p) o -> p k o", p=128)), incs=s_w.inc(16))
    for bn, sb in (("gbpk", gb), ("ubpk", ub), ("dbpk", db)):
        em.op("gpsimd", lambda g, bn=bn, sb=sb: g.dma_start(out=sb[:], in_=cfg[bn]), incs=s_w.inc(16))
    em.op("gpsimd", lambda g: g.dma_start(out=ridxi[:], in_=cfg["ridx"]), incs=s_w.inc(16))
    em.op("gpsimd", lambda g: g.dma_start(out=rwcol[:], in_=cfg["rwcol"]), incs=s_w.inc(16))
    em.op("gpsimd", lambda g: g.dma_start(out=ecol[:], in_=cfg["ecol"]), incs=s_w.inc(16))
    wready = s_w.now()

    em.op("vector", lambda v: v.memset(onesrow[:], 1.0), incs=s_dve.inc())
    em.op("vector", lambda v: v.tensor_copy(ridxf[:], ridxi[:]), waits=wready, incs=s_dve.inc())
    em.op("vector", lambda v: v.tensor_single_scalar(eq[:], ridxf[:], ecol[:], OP.is_equal),
          incs=s_dve.inc())
    em.op("vector", lambda v: v.tensor_add(msb[:], eq[:, :, 0], eq[:, :, 1]), incs=s_dve.inc())
    em.op("vector", lambda v: v.tensor_mul(msb[:], msb[:], rwcol[:]), incs=s_dve.inc())
    em.op("sync", lambda sy: sy.dma_start(
        out=cfg["m_dram"].rearrange("q (p a) -> (q p) a", p=128), in_=msb[:]),
        waits=s_dve.now(), incs=s_mr.inc(16))
    em.op("sync", lambda sy: sy.dma_start(out=mrow[:], in_=cfg["m_dram"]),
          waits=s_mr.now(), incs=s_mr.inc(16))
    for q in range(4):
        em.op("tensor", lambda t, q=q: t.matmul(pM[:], onesrow[:], mrow[:, q * 512:(q + 1) * 512],
                                                start=True, stop=True),
              waits=(s_mr.now(), s_dve.now()) if q == 0 else (s_act.now(),), incs=s_pe.inc())
        em.op("scalar", lambda s, q=q: s.copy(Mb[:, q, :], pM[:]), waits=s_pe.now(), incs=s_act.inc())
    m_done = s_act.now()

    freed = [None, None]
    prev_down = None
    for sc in range(4):
        b = sc % 2
        t0 = sc * 512
        w = [wready]
        if freed[b] is not None:
            w.append(freed[b])
        em.op("sync", lambda sy, t0=t0, b=b: sy.dma_start(
            out=x1sb[b][:], in_=x1m.rearrange("(k p) t -> p k t", p=128)[:, :, t0:t0 + 512]),
            waits=tuple(w), incs=s_in.inc(16))
        cw = [s_in.now()]
        if prev_down is not None:
            cw.append(prev_down)
        em.op("scalar", lambda s, b=b: s.copy(
            x1bf[:].rearrange("p a t -> p (a t)"), x1sb[b][:].rearrange("p a t -> p (a t)")),
            waits=tuple(cw), incs=s_act.inc())
        cast_done = s_act.now()

        def ep_gate(ot, pb, pe_spec):
            em.op("vector", lambda v, ot=ot, pb=pb: v.tensor_scalar(
                g7[:, ot, :], pb[:], gb[:, ot:ot + 1], 7.0, OP.add, OP.min),
                waits=pe_spec, incs=s_dve.inc())
            return s_dve.now()
        emit_proj(em, pq, gw, 8, x1bf[:], 8, ep_gate, s_pe,
                  first_waits=(cast_done, m_done))

        def ep_up(ot, pb, pe_spec):
            em.op("vector", lambda v, ot=ot, pb=pb: v.tensor_scalar(
                up7[:, ot, :], pb[:], ub[:, ot:ot + 1], 7.0, OP.add, OP.min),
                waits=pe_spec, incs=s_dve.inc())
            em.op("vector", lambda v, ot=ot: v.tensor_scalar(
                up7[:, ot, :], up7[:, ot, :], -7.0, None, OP.max),
                incs=s_dve.inc())
            return s_dve.now()
        emit_proj(em, pq, uw, 8, x1bf[:], 8, ep_up, s_pe)
        gu_done = s_dve.now()

        em.op("scalar", lambda s: s.activation(
            sg[:].rearrange("p a t -> p (a t)"), g7[:].rearrange("p a t -> p (a t)"),
            AF.Sigmoid, scale=1.702), waits=gu_done, incs=s_act.inc())
        em.op("vector", lambda v: v.tensor_mul(
            glu[:].rearrange("p a t -> p (a t)"), g7[:].rearrange("p a t -> p (a t)"),
            sg[:].rearrange("p a t -> p (a t)")), waits=s_act.now(), incs=s_dve.inc())
        em.op("vector", lambda v: v.scalar_tensor_tensor(
            inter[:].rearrange("p a t -> p (a t)"), up7[:].rearrange("p a t -> p (a t)"),
            1.0, glu[:].rearrange("p a t -> p (a t)"), OP.add, OP.mult),
            incs=s_dve.inc())
        inter_done = s_dve.now()

        def ep_down(ot, pb, pe_spec):
            em.op("vector", lambda v, ot=ot, pb=pb, sc=sc: v.scalar_tensor_tensor(
                part[:, ot, :], pb[:], db[:, ot:ot + 1], Mb[:, sc, :], OP.add, OP.mult),
                waits=pe_spec, incs=s_dve.inc())
            return s_dve.now()
        emit_proj(em, pq, dw, 8, inter[:], 8, ep_down, s_pe,
                  first_waits=(inter_done,))
        down_done = prev_down = s_dve.now()
        freed[b] = down_done
        em.op("scalar", lambda s, t0=t0: s.dma_start(
            out=cfg["ar_in"].rearrange("(k p) t -> p k t", p=128)[:, :, t0:t0 + 512],
            in_=part[:]), waits=down_done, incs=s_ps.inc(16))

    em.op("gpsimd", lambda g: g.collective_compute(
        "AllReduce", OP.add, replica_groups=RG8,
        ins=[cfg["ar_in"].opt()], outs=[cfg["ar_out"].opt()]),
        waits=s_ps.now(), incs=s_cc.inc(1))
    if "moe" in cfg:
        em.op("sync", lambda sy: sy.dma_start(out=cfg["moe"], in_=cfg["ar_out"]),
              waits=s_cc.now(), incs=s_st.inc(16))
        em.op("sync", lambda sy: sy.nop(), waits=s_st.now())
    return s_cc.now()


def host_moe_inputs(x1m_fm, expert_id, gate_up_proj, gate_up_bias, down_proj,
                    down_bias, router_indices, routing_weights):
    e = expert_id
    gup = np.asarray(gate_up_proj[e], np.float32)
    gW = gup[:, 0::2]
    uW = gup[:, 1::2]
    gub = np.asarray(gate_up_bias[e], np.float32)
    gb_ = gub[0::2]
    ub_ = gub[1::2]
    dW = np.asarray(down_proj[e], np.float32)
    db_ = np.asarray(down_bias[e], np.float32)
    r = {
        "gwT": np.ascontiguousarray(gW),
        "uwT": np.ascontiguousarray(uW),
        "dwT": np.ascontiguousarray(dW),
        "gbpk": np.ascontiguousarray(gb_.reshape(8, 128).T),
        "ubpk": np.ascontiguousarray(ub_.reshape(8, 128).T),
        "dbpk": np.ascontiguousarray(db_.reshape(8, 128).T),
        "ridx": np.ascontiguousarray(np.asarray(router_indices, np.int32).reshape(128, 16, 2)),
        "rwcol": np.ascontiguousarray(np.asarray(routing_weights, np.float32)[:, e].reshape(128, 16)),
        "ecol": np.full((128, 1), float(e), np.float32),
    }
    if x1m_fm is not None:
        r["x1m"] = np.ascontiguousarray(x1m_fm, np.float32)
    return r




def build_l1attn(T=4096, with_madd=False):
    nc = bass.Bass("TRN2", target_bir_lowering=False, debug=False)
    aps = {}
    aps["x2"] = nc.dram_tensor("x2", [1024, T], F32, kind="ExternalInput").ap()
    if with_madd:
        aps["madd"] = nc.dram_tensor("madd", [1024, T], F32, kind="ExternalInput").ap()
    for w in ("wqT", "wkT", "wvT", "wgT", "woT"):
        aps[w] = nc.dram_tensor(w, [1024, 1024], F32, kind="ExternalInput").ap()
    aps["ld0"] = nc.dram_tensor("ld0", [1, 1], F32, kind="ExternalInput").ap()
    aps["diffblk"] = nc.dram_tensor("diffblk", [128, 128], F32, kind="ExternalInput").ap()
    aps["offneg"] = nc.dram_tensor("offneg", [128, 128], F32, kind="ExternalInput").ap()
    aps["out"] = nc.dram_tensor("out", [1024, T], F32, kind="ExternalOutput").ap()
    aps["T"] = T
    es = ExitStack()
    with es:
        em = Emit(nc, es)
        emit_l1attn(em, aps)
        em.finish()
    return nc


def emit_l1attn(em, cfg):
    T = cfg["T"]
    NSC = T // 512
    x2, out = cfg["x2"], cfg["out"]

    wq = em.sb("c_wq", [128, 8, 1024], BF16)
    wk = em.sb("c_wk", [128, 8, 1024], BF16)
    wv = em.sb("c_wv", [128, 8, 1024], BF16)
    wg = em.sb("c_wg", [128, 8, 1024], BF16)
    wo = em.sb("c_wo", [128, 8, 1024], BF16)
    diffblk = em.sb("c_diffblk", [128, 128], F32)
    offneg = em.sb("c_offneg", [128, 128], F32)
    ld0t = em.sb("c_ld0", [1, 1], F32)
    mst = em.sb("c_mst", [1, 2], F32)
    lndc = em.sb("c_lndc", [128, 1], F32)
    m32 = em.sb("c_m32", [128, 128], F32)

    x2sb = [em.sb(f"c_x2_{i}", [128, 8, 512], F32) for i in range(2)]
    x2bf = em.sb("c_x2b", [128, 8, 512], BF16)
    qbf = em.sb("c_qbf", [128, 8, 512], BF16)
    kbf = em.sb("c_kbf", [128, 8, 512], BF16)
    vbf_ = em.sb("c_vbf", [128, 8, 512], BF16)
    gate = em.sb("c_gate", [128, 8, 512], F32)
    vtok = em.sb("c_vtok", [128, 2, 1024], BF16)
    s2b = [em.sb(f"c_s2b_{i}", [128, 128], BF16) for i in range(2)]
    gob = x2bf
    vsb = em.sb("c_v", [128, 8, 512], F32)

    pq = [em.ps(f"c_pq{i}", [128, 512]) for i in range(2)]
    pO = em.ps("c_pO", [128, 1024])
    ln = cfg.get("ln") or LNKit(em, "c_ln")

    s_in = em.sem("c_in")
    s_w = em.sem("c_w")
    s_pe = em.sem("c_pe")
    s_dve = em.sem("c_dve")
    s_act = em.sem("c_act")
    s_tr = em.sem("c_tr")
    s_st = em.sem("c_st")

    if "w_bf" in cfg:
        for qn, w, sb in (("sync", "wqT", wq), ("scalar", "wkT", wk), ("sync", "wvT", wv),
                          ("scalar", "wgT", wg), ("sync", "woT", wo)):
            em.op(qn, lambda e, w=w, sb=sb: e.dma_start(
                out=sb[:], in_=cfg["w_bf"][w].rearrange("(k p) o -> p k o", p=128)),
                waits=(cfg["w_bf_done"],), incs=s_w.inc(16))
    else:
        for w, sb in (("wqT", wq), ("wkT", wk), ("wvT", wv), ("wgT", wg), ("woT", wo)):
            em.op("gpsimd", lambda g, w=w, sb=sb: g.dma_start(
                out=sb[:], in_=cfg[w].rearrange("(k p) o -> p k o", p=128)), incs=s_w.inc(16))
    em.op("scalar", lambda e: e.dma_start(out=diffblk[:], in_=cfg["diffblk"]), incs=s_w.inc(16))
    em.op("scalar", lambda e: e.dma_start(out=offneg[:], in_=cfg["offneg"]), incs=s_w.inc(16))
    em.op("scalar", lambda e: e.dma_start(out=ld0t[:], in_=cfg["ld0"]), incs=s_w.inc(16))
    wready = s_w.now()

    em.op("scalar", lambda s: s.activation(mst[:, 0:1], ld0t[:], AF.Sigmoid),
          waits=wready, incs=s_act.inc())
    em.op("scalar", lambda s: s.activation(mst[:, 1:2], mst[:, 0:1], AF.Ln),
          incs=s_act.inc())
    em.op("tensor", lambda t: t.matmul(pq[0][:, 0:1], ln.onesrow[:], mst[:, 1:2], start=True, stop=True),
          waits=(s_act.now(), ln.init_done), incs=s_pe.inc())
    em.op("scalar", lambda s: s.copy(lndc[:], pq[0][:, 0:1]), waits=s_pe.now(), incs=s_act.inc())
    em.bank_rel[id(pq[0])] = s_act.now()
    em.op("vector", lambda v: v.tensor_scalar_mul(m32[:], diffblk[:], lndc[:]),
          waits=s_act.now(), incs=s_dve.inc())
    em.op("vector", lambda v: v.tensor_add(m32[:], m32[:], offneg[:]), incs=s_dve.inc())
    em.op("scalar", lambda s: s.activation(m32[:], m32[:], AF.Exp), waits=s_dve.now(),
          incs=s_act.inc())
    mask_done = s_act.now()

    freed = [None, None]
    osb_freed = [None, None]
    prev_wo = None
    prev_ln_casts = None
    prev_vsb_free = None
    sc_order = list(range(NSC))
    if "moesum" in cfg:
        sc_order = [s_ for s_ in range(NSC) if s_ >= 4] + [s_ for s_ in range(NSC) if s_ < 4]
    for idx_, sc in enumerate(sc_order):
        b = idx_ % 2
        t0 = sc * 512
        w = [wready]
        if freed[b] is not None:
            w.append(freed[b])
        em.op("sync", lambda sy, t0=t0, b=b: sy.dma_start(
            out=x2sb[b][:], in_=x2.rearrange("(k p) t -> p k t", p=128)[:, :, t0:t0 + 512]),
            waits=tuple(w), incs=s_in.inc(16))
        if "madd" in cfg:
            mw = [s_in.now()]
            if prev_vsb_free is not None:
                mw.append(prev_vsb_free)
            em.op("sync", lambda sy, t0=t0: sy.dma_start(
                out=vsb[:], in_=cfg["madd"].rearrange("(k p) t -> p k t", p=128)[:, :, t0:t0 + 512]),
                waits=tuple(mw), incs=s_in.inc(16))
            em.op("vector", lambda v, b=b: v.tensor_add(
                vsb[:].rearrange("p a t -> p (a t)"), vsb[:].rearrange("p a t -> p (a t)"),
                x2sb[b][:].rearrange("p a t -> p (a t)")),
                waits=s_in.now(), incs=s_dve.inc())
            ln.emit(vsb[:], x2sb[b][:], v_ready=(s_dve.now(),))
        elif "moesum" in cfg:
            if sc < 4:
                mw = [s_in.now(), cfg["moe_done"]]
                if prev_vsb_free is not None:
                    mw.append(prev_vsb_free)
                em.op("sync", lambda sy, t0_=sc * 512: sy.dma_start(
                    out=vsb[:], in_=cfg["moesum"].rearrange("(k p) t -> p k t", p=128)[:, :, t0_:t0_ + 512]),
                    waits=tuple(mw), incs=s_in.inc(16))
                em.op("vector", lambda v, b=b: v.scalar_tensor_tensor(
                    vsb[:].rearrange("p a t -> p (a t)"), vsb[:].rearrange("p a t -> p (a t)"),
                    cfg["mgate_sb"][:], x2sb[b][:].rearrange("p a t -> p (a t)"),
                    OP.mult, OP.add), waits=s_in.now(), incs=s_dve.inc())
                ln.emit(vsb[:], x2sb[b][:], v_ready=(s_dve.now(),))
            else:
                ln.emit(x2sb[b][:], x2sb[b][:], v_ready=(s_in.now(),))
        cw = [s_in.now()]
        if prev_wo is not None:
            cw.append(prev_wo)
        if "madd" in cfg or "moesum" in cfg:
            cw.append(ln.s_dve.now())
        em.op("scalar", lambda s, b=b: s.copy(
            x2bf[:].rearrange("p a t -> p (a t)"), x2sb[b][:].rearrange("p a t -> p (a t)")),
            waits=tuple(cw), incs=s_act.inc())
        cast_done = s_act.now()

        def mk_ep(dst, act_fn):
            def ep(ot, pb, pe_spec):
                em.op("scalar", lambda s, ot=ot, pb=pb: act_fn(s, dst, ot, pb),
                      waits=pe_spec, incs=s_act.inc())
                return s_act.now()
            return ep
        emit_proj(em, pq, wq, 8, x2bf[:], 8,
                  mk_ep(qbf, lambda s, d, ot, pb: s.copy(d[:, ot, :], pb[:])),
                  s_pe, first_waits=(cast_done, mask_done))
        emit_proj(em, pq, wk, 8, x2bf[:], 8,
                  mk_ep(kbf, lambda s, d, ot, pb: s.copy(d[:, ot, :], pb[:])), s_pe)
        emit_proj(em, pq, wv, 8, x2bf[:], 8,
                  mk_ep(vbf_, lambda s, d, ot, pb: s.copy(d[:, ot, :], pb[:])), s_pe)
        emit_proj(em, pq, wg, 8, x2bf[:], 8,
                  mk_ep(gate, lambda s, d, ot, pb: s.activation(d[:, ot, :], pb[:], AF.Sigmoid)), s_pe)
        qkvg_done = s_act.now()

        prev_gate_tt = None
        gate_tts = [None, None]
        for c in range(4):
            cs = slice(c * 128, (c + 1) * 128)
            sp = pq[c % 2]
            trw = [qkvg_done]
            if gate_tts[c % 2] is not None:
                trw.append(gate_tts[c % 2])
            for ft in range(8):
                em.op("sync", lambda sy, c=c, ft=ft: sy.dma_start_transpose(
                    vtok[:, c % 2, ft * 128:(ft + 1) * 128], vbf_[:, ft, c * 128:(c + 1) * 128]),
                    waits=tuple(trw) if ft == 0 else (), incs=s_tr.inc(16))
            tr_done = s_tr.now()
            for k in range(8):
                wts0 = [qkvg_done]
                if em.bank_rel.get(id(sp)) is not None:
                    wts0.append(em.bank_rel[id(sp)])
                em.op("tensor", lambda t, k=k, c=c, sp=sp, cs=cs, b=b: t.matmul(
                    sp[:, 0:128], kbf[:, k, cs], qbf[:, k, cs],
                    start=(k == 0), stop=(k == 7)),
                    waits=tuple(wts0) if k == 0 else (),
                    incs=s_pe.inc() if k == 7 else ())
            em.op("vector", lambda v, c=c, sp=sp: v.tensor_tensor(
                s2b[c % 2][:], sp[:, 0:128], m32[:], OP.mult),
                waits=s_pe.now(), incs=s_dve.inc())
            s2_done = s_dve.now()
            em.bank_rel[id(sp)] = s2_done
            for ft in range(8):
                wts = []
                if ft == 0:
                    wts = [s2_done, tr_done]
                    if prev_gate_tt is not None:
                        wts.append(prev_gate_tt)
                em.op("tensor", lambda t, c=c, ft=ft: t.matmul(
                    pO[:, ft * 128:(ft + 1) * 128], vtok[:, c % 2, ft * 128:(ft + 1) * 128],
                    s2b[c % 2][:], start=True, stop=True),
                    waits=tuple(wts), incs=s_pe.inc() if ft == 7 else ())
            em.op("vector", lambda v, c=c, cs=cs: v.tensor_tensor(
                gob[:, :, cs], gate[:, :, cs],
                pO[:].rearrange("p (a t) -> p a t", a=8), OP.mult),
                waits=s_pe.now(), incs=s_dve.inc())
            prev_gate_tt = gate_tts[c % 2] = s_dve.now()

        pln = prev_ln_casts
        def ep_wo(ot, pb, pe_spec):
            wts = [pe_spec]
            if pln is not None and ot == 0:
                wts.append(pln)
            if ("madd" in cfg or "moesum" in cfg) and ot == 0:
                wts.append(ln.s_dve.now())
            em.op("vector", lambda v, ot=ot, pb=pb, b=b: v.scalar_tensor_tensor(
                vsb[:, ot, :], x2sb[b][:, ot, :], 2.0, pb[:], OP.mult, OP.add),
                waits=tuple(wts), incs=s_dve.inc())
            return s_dve.now()
        emit_proj(em, pq, wo, 8, gob[:], 8, ep_wo, s_pe,
                  first_waits=(prev_gate_tt,))
        v_done = s_dve.now()
        prev_wo = s_pe.now()

        ow = [] if osb_freed[b] is None else [osb_freed[b]]
        apply_done = ln.emit(vsb[:], x2sb[b][:], v_ready=(v_done,), out_waits=tuple(ow))[0]
        prev_ln_casts = ln.last_casts
        prev_vsb_free = apply_done
        freed[b] = apply_done
        em.op("sync", lambda sy, t0=t0, b=b: sy.dma_start(
            out=out.rearrange("(k p) t -> p k t", p=128)[:, :, t0:t0 + 512], in_=x2sb[b][:]),
            waits=apply_done, incs=s_st.inc(16))
        osb_freed[b] = s_st.now()
    em.op("sync", lambda sy: sy.nop(), waits=s_st.now())


def host_l1attn_inputs(x2_fm, W_q, W_k, W_v, W_g, W_o, log_decay, consts):
    return {
        "x2": np.ascontiguousarray(x2_fm, np.float32),
        "wqT": host_wT(W_q), "wkT": host_wT(W_k), "wvT": host_wT(W_v),
        "wgT": host_wT(W_g), "woT": host_wT(W_o),
        "ld0": np.asarray(log_decay[0], np.float32).reshape(1, 1),
        "diffblk": consts["diffblk"], "offneg": consts["offneg"],
    }




def build_l1ffn(T=4096):
    nc = bass.Bass("TRN2", target_bir_lowering=False, debug=False)
    x3 = nc.dram_tensor("x3", [1024, T], F32, kind="ExternalInput").ap()
    wT = nc.dram_tensor("wT", [1024, 1024], F32, kind="ExternalInput").ap()
    bpk = nc.dram_tensor("bpk", [128, 8], F32, kind="ExternalInput").ap()
    ldT = nc.dram_tensor("ldT", [100, 4], F32, kind="ExternalInput").ap()
    luT = nc.dram_tensor("luT", [4, 50], F32, kind="ExternalInput").ap()
    uaT = nc.dram_tensor("uaT", [50, 1024], F32, kind="ExternalInput").ap()
    da = nc.dram_tensor("da", [100, 1024], F32, kind="ExternalInput").ap()
    out = nc.dram_tensor("out", [1024, T], F32, kind="ExternalOutput").ap()
    es = ExitStack()
    with es:
        em = Emit(nc, es)
        emit_l1ffn(em, dict(x3=x3, wT=wT, bpk=bpk, ldT=ldT, luT=luT, uaT=uaT,
                            da=da, out=out, T=T))
        em.finish()
    return nc


def emit_l1ffn(em, cfg):
    nc = em.nc
    T = cfg["T"]
    NSC = T // 512
    x3, wT, bpk, out = cfg["x3"], cfg["wT"], cfg["bpk"], cfg["out"]

    wTf = em.sb("d_wTf", [128, 8, 1024], F32)
    wc = em.sb("d_wc", [128, 8, 1024], BF16)
    bcol = em.sb("d_bcol", [128, 8], F32)
    ones128 = em.sb("d_ones128", [128, 1], BF16)
    onesrow = em.sb("d_onesrow", [1, 128], F32)
    t1b = em.sb("d_t1b", [4, 1024], BF16)
    t2b = em.sb("d_t2b", [50, 1024], BF16)
    ldTb = em.sb("d_ldTb", [100, 4], BF16)
    luTb = em.sb("d_luTb", [4, 50], BF16)
    uaTb = em.sb("d_uaTb", [50, 1024], BF16)
    dab = em.sb("d_dab", [100, 1024], BF16)
    x3sb = [em.sb(f"d_x3_{i}", [128, 8, 512], F32) for i in range(2)]
    x3bf = [em.sb(f"d_x3b_{i}", [128, 8, 512], BF16) for i in range(2)]
    vsb = [em.sb(f"d_v_{i}", [128, 8, 512], F32) for i in range(2)]
    osb = [em.sb(f"d_o_{i}", [128, 8, 512], F32) for i in range(2)]

    pp = [em.ps(f"d_pp{i}", [128, 512]) for i in range(2)]
    pL = em.ps("d_pL", [128, 1024])
    ln = cfg.get("ln") or LNKit(em, "d_ln")

    s_in = em.sem("d_in")
    s_w = em.sem("d_w")
    s_pe = em.sem("d_pe")
    s_dve = em.sem("d_dve")
    s_act = em.sem("d_act")
    s_st = em.sem("d_st")

    em.op("gpsimd", lambda g: g.dma_start(out=wTf[:], in_=wT.rearrange("(k p) o -> p k o", p=128)), incs=s_w.inc(16))
    em.op("gpsimd", lambda g: g.dma_start(out=ldTb[:], in_=cfg["ldT"]), incs=s_w.inc(16))
    em.op("gpsimd", lambda g: g.dma_start(out=luTb[:], in_=cfg["luT"]), incs=s_w.inc(16))
    em.op("gpsimd", lambda g: g.dma_start(out=uaTb[:], in_=cfg["uaT"]), incs=s_w.inc(16))
    em.op("gpsimd", lambda g: g.dma_start(out=dab[:], in_=cfg["da"]), incs=s_w.inc(16))
    em.op("gpsimd", lambda g: g.dma_start(out=bcol[:], in_=bpk), incs=s_w.inc(16))
    wready = s_w.now()

    for h in range(2):
        em.op("tensor", lambda t, h=h: t.matmul(pL[0:4, h * 512:(h + 1) * 512], ldTb[:], dab[:, h * 512:(h + 1) * 512], start=True, stop=True),
              waits=wready, incs=s_pe.inc())
    em.op("scalar", lambda s: s.copy(t1b[:], pL[0:4, :]), waits=s_pe.now(), incs=s_act.inc())
    for h in range(2):
        em.op("tensor", lambda t, h=h: t.matmul(pL[0:50, h * 512:(h + 1) * 512], luTb[:], t1b[:, h * 512:(h + 1) * 512], start=True, stop=True),
              waits=s_act.now(), incs=s_pe.inc())
    em.op("scalar", lambda s: s.copy(t2b[:], pL[0:50, :]), waits=s_pe.now(), incs=s_act.inc())
    t2_done = s_act.now()
    for m in range(8):
        for h in range(2):
            em.op("tensor", lambda t, m=m, h=h: t.matmul(
                pL[:, h * 512:(h + 1) * 512], t2b[:, m * 128:(m + 1) * 128], uaTb[:, h * 512:(h + 1) * 512], start=True, stop=True),
                waits=(t2_done if m == 0 else s_dve.now()) if h == 0 else (), incs=s_pe.inc() if h == 1 else ())
        em.op("vector", lambda v, m=m: v.tensor_add(wc[:, m, :], wTf[:, m, :], pL[:, :]),
              waits=s_pe.now(), incs=s_dve.inc())

    freed = [None, None]
    osb_freed = [None, None]
    prev_sums = None
    for sc in range(NSC):
        b = sc % 2
        t0 = sc * 512
        w = [wready]
        if freed[b] is not None:
            w.append(freed[b])
        em.op("sync", lambda sy, t0=t0, b=b: sy.dma_start(
            out=x3sb[b][:], in_=x3.rearrange("(k p) t -> p k t", p=128)[:, :, t0:t0 + 512]),
            waits=tuple(w), incs=s_in.inc(16))
        ld_here = s_in.now()
        em.op("scalar", lambda s, b=b: s.copy(
            x3bf[b][:].rearrange("p a t -> p (a t)"), x3sb[b][:].rearrange("p a t -> p (a t)")),
            waits=ld_here, incs=s_act.inc())
        cast_done = s_act.now()
        for ot in range(8):
            pb = pp[ot % 2]
            for k in range(8):
                wts = ()
                if k == 0:
                    wts = (cast_done, s_dve.now())
                em.op("tensor", lambda t, b=b, ot=ot, k=k, pb=pb: t.matmul(
                    pb[:], wc[:, k, ot * 128:(ot + 1) * 128], x3bf[b][:, k, :],
                    start=(k == 0), stop=(k == 7)),
                    waits=wts, incs=s_pe.inc() if k == 7 else ())
            em.op("vector", lambda v, b=b, ot=ot, pb=pb: v.scalar_tensor_tensor(
                vsb[b][:, ot, :], pb[:], bcol[:, ot:ot + 1], x3sb[b][:, ot, :], OP.add, OP.add),
                waits=s_pe.now(), incs=s_dve.inc())
        v_done = s_dve.now()
        ow = [] if osb_freed[b] is None else [osb_freed[b]]
        apply_done = ln.emit(vsb[b][:], osb[b][:], v_ready=(v_done,), out_waits=tuple(ow))[0]
        freed[b] = apply_done
        em.op("sync", lambda sy, t0=t0, b=b: sy.dma_start(
            out=out.rearrange("(k p) t -> p k t", p=128)[:, :, t0:t0 + 512], in_=osb[b][:]),
            waits=apply_done, incs=s_st.inc(16))
        osb_freed[b] = s_st.now()
    em.op("sync", lambda sy: sy.nop(), waits=s_st.now())


def host_l1ffn_inputs(x3_fm, W, b, lora_down, lora_up, down_aux, up_aux):
    return {
        "x3": np.ascontiguousarray(x3_fm, dtype=np.float32),
        "wT": host_wT(W),
        "bpk": np.ascontiguousarray(np.asarray(b, np.float32).reshape(8, 128).T),
        "ldT": host_wT(lora_down),
        "luT": host_wT(lora_up),
        "uaT": host_wT(up_aux),
        "da": np.ascontiguousarray(np.asarray(down_aux, np.float32)),
        "out": None,
    }








def kernel_unfused(**inputs):
    T = 4096
    consts = host_consts()
    u = np.asarray(inputs["L0_attn_u"], np.float32)

    common_a = host_mamba_inputs_common(
        inputs["L0_attn_in_proj_w"], inputs["L0_attn_conv_w"], inputs["L0_attn_conv_b"],
        inputs["L0_attn_dt_bias"], inputs["L0_attn_A_log"], inputs["L0_attn_D"],
        inputs["L0_attn_rms_w"], inputs["L0_attn_out_w"], consts)
    in_maps = []
    for c in range(8):
        bb, t0 = c // 4, (c % 4) * T
        if t0 == 0:
            uh = np.concatenate([np.zeros((128, 1024), np.float32), u[bb, 0:T]], axis=0)
        else:
            uh = u[bb, t0 - 128:t0 + T]
        m = dict(common_a)
        m["u"] = np.ascontiguousarray(uh.T)
        m["m2T"], m["selcol"] = host_core_masks(c)
        in_maps.append(m)
    nc_a = build_mamba(T)
    res_a = run_bass_kernel_spmd(nc_a, in_maps, core_ids=list(range(8)))
    x1 = [res_a.results[c]["x1"] for c in range(8)]

    x1m = np.ascontiguousarray(x1[0][:, 0:2048])
    in_maps = [host_moe_inputs(
        x1m, e, inputs["L0_ffn_gate_up_proj"], inputs["L0_ffn_gate_up_proj_bias"],
        inputs["L0_ffn_down_proj"], inputs["L0_ffn_down_proj_bias"],
        inputs["L0_ffn_router_indices"], inputs["L0_ffn_routing_weights"]) for e in range(8)]
    nc_b = build_moe()
    res_b = run_bass_kernel_spmd(nc_b, in_maps, core_ids=list(range(8)))
    moe = res_b.results[0]["moe"]

    common_c = host_l1attn_inputs_common(
        inputs["L1_attn_W_q"], inputs["L1_attn_W_k"], inputs["L1_attn_W_v"],
        inputs["L1_attn_W_g"], inputs["L1_attn_W_o"], inputs["L1_attn_log_decay"], consts)
    in_maps = []
    for c in range(8):
        m = dict(common_c)
        m["x2"] = x1[c]
        madd = np.zeros((1024, T), np.float32)
        if c == 0:
            madd[:, 0:2048] = moe
        m["madd"] = madd
        in_maps.append(m)
    nc_c = build_l1attn(T, with_madd=True)
    res_c = run_bass_kernel_spmd(nc_c, in_maps, core_ids=list(range(8)))
    x3 = [res_c.results[c]["out"] for c in range(8)]

    common_d = host_l1ffn_inputs_common(
        inputs["L1_ffn_W"], inputs["L1_ffn_b"], inputs["L1_ffn_lora_down"],
        inputs["L1_ffn_lora_up"], inputs["L1_ffn_down_aux"], inputs["L1_ffn_up_aux"])
    in_maps = []
    for c in range(8):
        m = dict(common_d)
        m["x3"] = x3[c]
        in_maps.append(m)
    nc_d = build_l1ffn(T)
    res_d = run_bass_kernel_spmd(nc_d, in_maps, core_ids=list(range(8)))

    out = np.empty((2, 16384, 1024), np.float32)
    for c in range(8):
        bb, t0 = c // 4, (c % 4) * T
        out[bb, t0:t0 + T] = res_d.results[c]["out"].T
    return out


def host_mamba_inputs_common(in_proj_w, conv_w, conv_b, dt_bias, A_log, D, rms_w,
                             out_w, consts):
    W = np.asarray(in_proj_w, np.float32)
    wT = W.T
    wxbd = np.ascontiguousarray(np.concatenate(
        [wT[:, 2048:4096], wT[:, 4096:4160], wT[:, 4224:4240]], axis=1))
    wzc = np.ascontiguousarray(np.concatenate(
        [wT[:, 0:2048], wT[:, 4160:4224]], axis=1))
    cwf = np.asarray(conv_w, np.float32)
    cbf = np.asarray(conv_b, np.float32)
    cw18 = np.zeros((128, 18, 4), np.float32)
    cb18 = np.zeros((128, 18), np.float32)
    cw18[:, 0:16, :] = cwf[0:2048].reshape(16, 128, 4).transpose(1, 0, 2)
    cb18[:, 0:16] = cbf[0:2048].reshape(16, 128).T
    cw18[0:64, 16, :] = cwf[2048:2112]
    cb18[0:64, 16] = cbf[2048:2112]
    cw18[0:64, 17, :] = cwf[2112:2176]
    cb18[0:64, 17] = cbf[2112:2176]
    return {
        "wxbd": wxbd, "wzc": wzc, "owT": host_wT(out_w),
        "cwcol": cw18, "cbcol": cb18,
        "dtbcol": np.asarray(dt_bias, np.float32).reshape(16, 1),
        "alog": np.asarray(A_log, np.float32).reshape(1, 16),
        "dvec": np.asarray(D, np.float32).reshape(1, 16),
        "rmspk": np.ascontiguousarray(np.asarray(rms_w, np.float32).reshape(16, 128).T),
        "triuinc": consts["triuinc"], "upperneg": consts["upperneg"],
        "ident": consts["ident"],
    }


def host_core_masks(core_id):
    slot = core_id % 4
    m2 = np.zeros((8, 8), np.float32)
    sel = np.zeros((8, 1), np.float32)
    base = (core_id // 4) * 4
    for j in range(base, base + slot):
        sel[j, 0] = 1.0
        for i in range(j + 1, base + slot):
            m2[j, i] = 1.0
    return np.ascontiguousarray(m2.T), sel


def host_l1attn_inputs_common(W_q, W_k, W_v, W_g, W_o, log_decay, consts):
    return {
        "wqT": host_wT(W_q), "wkT": host_wT(W_k), "wvT": host_wT(W_v),
        "wgT": host_wT(W_g), "woT": host_wT(W_o),
        "ld0": np.ascontiguousarray(np.asarray(log_decay, np.float32)[0:1].reshape(1, 1)),
        "diffblk": consts["diffblk"], "offneg": consts["offneg"],
    }


def host_l1ffn_inputs_common(W, b, lora_down, lora_up, down_aux, up_aux):
    return {
        "wT": host_wT(W),
        "bpk": np.ascontiguousarray(np.asarray(b, np.float32).reshape(8, 128).T),
        "ldT": host_wT(lora_down),
        "luT": host_wT(lora_up),
        "uaT": host_wT(up_aux),
        "da": np.ascontiguousarray(np.asarray(down_aux, np.float32)),
    }









def build_fused(T=4096):
    nc = bass.Bass("TRN2", target_bir_lowering=False, debug=False)
    a = {"T": T}
    a["u"] = nc.dram_tensor("u", [1024, 128 + T], F32, kind="ExternalInput").ap()
    a["wxbd"] = nc.dram_tensor("wxbd", [1024, 2128], F32, kind="ExternalInput").ap()
    a["wzc"] = nc.dram_tensor("wzc", [1024, 2112], F32, kind="ExternalInput").ap()
    a["owT"] = nc.dram_tensor("owT", [2048, 1024], F32, kind="ExternalInput").ap()
    a["cwcol"] = nc.dram_tensor("cwcol", [128, 18, 4], F32, kind="ExternalInput").ap()
    a["cbcol"] = nc.dram_tensor("cbcol", [128, 18], F32, kind="ExternalInput").ap()
    a["dtbcol"] = nc.dram_tensor("dtbcol", [16, 1], F32, kind="ExternalInput").ap()
    a["alog"] = nc.dram_tensor("alog", [1, 16], F32, kind="ExternalInput").ap()
    a["dvec"] = nc.dram_tensor("dvec", [1, 16], F32, kind="ExternalInput").ap()
    a["rmspk"] = nc.dram_tensor("rmspk", [128, 16], F32, kind="ExternalInput").ap()
    a["triuinc"] = nc.dram_tensor("triuinc", [128, 128], F32, kind="ExternalInput").ap()
    a["upperneg"] = nc.dram_tensor("upperneg", [128, 128], F32, kind="ExternalInput").ap()
    a["ident"] = nc.dram_tensor("ident", [128, 128], F32, kind="ExternalInput").ap()
    a["m2T"] = nc.dram_tensor("m2T", [8, 8], F32, kind="ExternalInput").ap()
    a["selcol"] = nc.dram_tensor("selcol", [8, 1], F32, kind="ExternalInput").ap()
    a["x1"] = nc.dram_tensor("x1", [1024, T], F32).ap()
    a["sp_xtok"] = nc.dram_tensor("sp_xtok", [T, 2048], BF16).ap()
    a["sp_btok"] = nc.dram_tensor("sp_btok", [T, 64], BF16).ap()
    a["sp_bfeat"] = nc.dram_tensor("sp_bfeat", [64, T], BF16).ap()
    a["sp_dt"] = nc.dram_tensor("sp_dt", [16, T], F32).ap()
    a["w8d"] = nc.dram_tensor("w8d", [8, 16], F32).ap()
    a["cum_dram"] = nc.dram_tensor("cum_dram", [16, 128], F32).ap()
    a["ow_bf"] = nc.dram_tensor("ow_bf", [8, 128, 16, 128], BF16).ap()
    a["ag_in"] = nc.dram_tensor("ag_in", [65, 2048], F32).ap()
    a["ag_out"] = nc.dram_tensor("ag_out", [520, 2048], F32).ap()
    b = {}
    for w in ("gwT", "uwT", "dwT"):
        b[w] = nc.dram_tensor(w, [1024, 1024], F32, kind="ExternalInput").ap()
    for bn in ("gbpk", "ubpk", "dbpk"):
        b[bn] = nc.dram_tensor(bn, [128, 8], F32, kind="ExternalInput").ap()
    b["ridx"] = nc.dram_tensor("ridx", [128, 16, 2], I32, kind="ExternalInput").ap()
    b["rwcol"] = nc.dram_tensor("rwcol", [128, 16], F32, kind="ExternalInput").ap()
    b["ecol"] = nc.dram_tensor("ecol", [128, 1], F32, kind="ExternalInput").ap()
    b["mgatec"] = nc.dram_tensor("mgatec", [128, 1], F32, kind="ExternalInput").ap()
    b["x1sc"] = a["x1"]
    b["ar1_in"] = nc.dram_tensor("ar1_in", [1024, 2048], F32).ap()
    b["ar1_out"] = nc.dram_tensor("ar1_out", [1024, 2048], F32).ap()
    b["m_dram"] = nc.dram_tensor("m_dram", [1, 2048], F32).ap()
    b["ar_in"] = nc.dram_tensor("ar_in", [1024, 2048], F32).ap()
    b["ar_out"] = nc.dram_tensor("ar_out", [1024, 2048], F32).ap()
    c = {"T": T}
    c["x2"] = a["x1"]
    for w in ("wqT", "wkT", "wvT", "wgT", "woT"):
        c[w] = nc.dram_tensor(w, [1024, 1024], F32, kind="ExternalInput").ap()
    c["ld0"] = nc.dram_tensor("ld0", [1, 1], F32, kind="ExternalInput").ap()
    c["diffblk"] = nc.dram_tensor("diffblk", [128, 128], F32, kind="ExternalInput").ap()
    c["offneg"] = nc.dram_tensor("offneg", [128, 128], F32, kind="ExternalInput").ap()
    c["moesum"] = b["ar_out"]
    c["out"] = nc.dram_tensor("x3", [1024, T], F32).ap()
    d = {"T": T}
    d["x3"] = c["out"]
    d["wT"] = nc.dram_tensor("wT", [1024, 1024], F32, kind="ExternalInput").ap()
    d["bpk"] = nc.dram_tensor("bpk", [128, 8], F32, kind="ExternalInput").ap()
    d["ldT"] = nc.dram_tensor("ldT", [100, 4], F32, kind="ExternalInput").ap()
    d["luT"] = nc.dram_tensor("luT", [4, 50], F32, kind="ExternalInput").ap()
    d["uaT"] = nc.dram_tensor("uaT", [50, 1024], F32, kind="ExternalInput").ap()
    d["da"] = nc.dram_tensor("da", [100, 1024], F32, kind="ExternalInput").ap()
    d["out"] = nc.dram_tensor("out", [1024, T], F32, kind="ExternalOutput").ap()

    es = ExitStack()
    with es:
        em = Emit(nc, es)
        a["bcast"] = dict(ar1_in=b["ar1_in"], ar1_out=b["ar1_out"], mgatec=b["mgatec"])
        with ExitStack() as esA:
            em.cur_es = esA
            emit_mamba(em, a)
        s_cw = em.sem("f_cw")
        c["w_bf"] = {}
        for w in ("wqT", "wkT", "wvT", "wgT", "woT"):
            dst = nc.dram_tensor(w + "_bf", [1024, 1024], BF16).ap()
            c["w_bf"][w] = dst
            em.op("gpsimd", lambda g, dst=dst, src=c[w]: g.dma_start(out=dst, in_=src),
                  incs=s_cw.inc(16))
        c["w_bf_done"] = s_cw.now()
        specsA = [s.now() for s in em.all_sems if s.n > 0 and s is not s_cw]
        for e in ENGS:
            em.pending[e] = list(specsA)
        em.bank_rel = {}
        b["bcast_done"] = em.bcast_done
        with ExitStack() as esB:
            em.cur_es = esB
            moe_done = emit_moe(em, b)
        cc_sem = em._moe_cc
        specs = [s.now() for s in em.all_sems if s.n > 0 and s is not cc_sem]
        for e in ENGS:
            em.pending[e] = list(specs)
        em.bank_rel = {}
        c["moe_done"] = moe_done
        with ExitStack() as esC:
            em.cur_es = esC
            mg = em.sb("c_mgate", [128, 1], F32)
            sw = em.sem("c_mg")
            em.op("gpsimd", lambda g: g.dma_start(out=mg[:], in_=b["mgatec"]), incs=sw.inc(16))
            em.pending["vector"] = list(em.pending.get("vector", [])) + [sw.now()]
            c["mgate_sb"] = mg
            emit_l1attn(em, c)
        em.barrier()
        with ExitStack() as esD:
            em.cur_es = esD
            emit_l1ffn(em, d)
        em.cur_es = es
        em.finish()
    return nc


def kernel(**inputs):
    T = 4096
    consts = host_consts()
    u = np.asarray(inputs["L0_attn_u"], np.float32)

    common_a = host_mamba_inputs_common(
        inputs["L0_attn_in_proj_w"], inputs["L0_attn_conv_w"], inputs["L0_attn_conv_b"],
        inputs["L0_attn_dt_bias"], inputs["L0_attn_A_log"], inputs["L0_attn_D"],
        inputs["L0_attn_rms_w"], inputs["L0_attn_out_w"], consts)
    common_c = host_l1attn_inputs_common(
        inputs["L1_attn_W_q"], inputs["L1_attn_W_k"], inputs["L1_attn_W_v"],
        inputs["L1_attn_W_g"], inputs["L1_attn_W_o"], inputs["L1_attn_log_decay"], consts)
    common_d = host_l1ffn_inputs_common(
        inputs["L1_ffn_W"], inputs["L1_ffn_b"], inputs["L1_ffn_lora_down"],
        inputs["L1_ffn_lora_up"], inputs["L1_ffn_down_aux"], inputs["L1_ffn_up_aux"])

    in_maps = []
    for cid in range(8):
        bb, t0 = cid // 4, (cid % 4) * T
        if t0 == 0:
            uh = np.concatenate([np.zeros((128, 1024), np.float32), u[bb, 0:T]], axis=0)
        else:
            uh = u[bb, t0 - 128:t0 + T]
        m = dict(common_a)
        m.update(common_c)
        m.update(common_d)
        m["u"] = np.ascontiguousarray(uh.T)
        m["m2T"], m["selcol"] = host_core_masks(cid)
        m.update(host_moe_inputs(
            None, cid, inputs["L0_ffn_gate_up_proj"], inputs["L0_ffn_gate_up_proj_bias"],
            inputs["L0_ffn_down_proj"], inputs["L0_ffn_down_proj_bias"],
            inputs["L0_ffn_router_indices"], inputs["L0_ffn_routing_weights"]))
        m["mgatec"] = np.full((128, 1), 1.0 if cid == 0 else 0.0, np.float32)
        in_maps.append(m)

    nc = build_fused(T)
    res = run_bass_kernel_spmd(nc, in_maps, core_ids=list(range(8)))
    out = np.empty((2, 16384, 1024), np.float32)
    for cid in range(8):
        bb, t0 = cid // 4, (cid % 4) * T
        out[bb, t0:t0 + T] = res.results[cid]["out"].T
    return out


# version 07
# speedup vs baseline: 1.1780x; 1.1780x over previous
import numpy as np
from contextlib import ExitStack
import concourse.bass as bass
import concourse.mybir as mybir
from concourse.bass_utils import run_bass_kernel_spmd

F32 = mybir.dt.float32
BF16 = mybir.dt.bfloat16
I32 = mybir.dt.int32
AF = mybir.ActivationFunctionType
OP = mybir.AluOpType
AX = mybir.AxisListType

ENGS = ("tensor", "vector", "scalar", "gpsimd", "sync")


class Sem:
    __slots__ = ("h", "n", "name")

    def __init__(self, h, name):
        self.h = h
        self.n = 0
        self.name = name

    def inc(self, k=1):
        self.n += k
        return (self, k)

    def now(self):
        return (self, self.n)

    def at(self, v):
        return (self, v)


class Emit:

    def __init__(self, nc, es):
        self.nc = nc
        self.es = es
        self.cur_es = es
        self.ops = {e: [] for e in ENGS}
        self.last_wait = {}
        self.bank_rel = {}
        self.all_sems = []
        self.pending = {}

    def sem(self, name):
        h = self.es.enter_context(self.nc.semaphore(name))
        s = Sem(h, name)
        self.all_sems.append(s)
        return s

    def sb(self, name, shape, dtype):
        return self.cur_es.enter_context(self.nc.sbuf_tensor(name, shape, dtype))

    def ps(self, name, shape, dtype=F32):
        return self.cur_es.enter_context(self.nc.psum_tensor(name, shape, dtype))

    def barrier(self):
        specs = [s.now() for s in self.all_sems if s.n > 0]
        for e in ENGS:
            self.pending[e] = list(specs)
        self.bank_rel = {}

    AUTO_EDGE_ENGS = ("vector", "scalar", "gpsimd")
    _ASYNC_NAMES = ("dma_start", "dma_start_transpose", "collective_compute",
                    "indirect_dma_start", "wait_ge", "nop")

    def _auto_sem(self, eng):
        key = f"__auto_{eng}"
        if not hasattr(self, "_autos"):
            self._autos = {}
        if eng not in self._autos:
            self._autos[eng] = self.sem(key)
        return self._autos[eng]

    def op(self, eng, fn, waits=(), incs=()):
        if waits and isinstance(waits[0], Sem):
            waits = (waits,)
        if incs and isinstance(incs[0], Sem):
            incs = (incs,)
        waits = tuple(waits)
        if self.pending.get(eng):
            waits = tuple(self.pending.pop(eng)) + waits
        names = fn.__code__.co_names
        is_async = any(nm in names for nm in self._ASYNC_NAMES)
        auto_inc = None
        if eng in self.AUTO_EDGE_ENGS and not is_async:
            edge = getattr(self, "_edge", {}).get(eng)
            if edge is not None:
                waits = waits + (edge,)
            if not incs:
                a = self._auto_sem(eng)
                auto_inc = (a, 1, a.inc(1) and a.n)
            if not hasattr(self, "_edge"):
                self._edge = {}
            if incs:
                sem, k = incs[-1]
                self._edge[eng] = (sem, sem.n)
            else:
                self._edge[eng] = (self._autos[eng], self._autos[eng].n)
        incs = tuple((sem, k, sem.n) for (sem, k) in incs)
        if auto_inc is not None:
            incs = incs + ((auto_inc[0], 1, auto_inc[2]),)
        self.ops[eng].append((waits, fn, incs))

    def replay(self, eng, engine_obj):
        for waits, fn, incs in self.ops[eng]:
            for (sem, val) in waits:
                key = (eng, id(sem))
                if val > 0 and self.last_wait.get(key, -1) < val:
                    engine_obj.wait_ge(sem.h, val)
                    self.last_wait[key] = val
            ins = fn(engine_obj)
            for (sem, k, val_after) in incs:
                ins = ins.then_inc(sem.h, k)

    def finish(self):
        with self.nc.Block() as block:
            block.tensor(lambda t: self.replay("tensor", t))
            block.vector(lambda v: self.replay("vector", v))
            block.scalar(lambda s: self.replay("scalar", s))
            block.gpsimd(lambda g: self.replay("gpsimd", g))
            block.sync(lambda sy: self.replay("sync", sy))
        self.ops = {e: [] for e in ENGS}


def ceil_div(a, b):
    return (a + b - 1) // b



def host_wT(w):
    return np.ascontiguousarray(np.asarray(w, dtype=np.float32).T)


def host_fm(x2d):
    return np.ascontiguousarray(np.asarray(x2d, dtype=np.float32).T)


CONST_TRIL_INC = None


def host_consts():
    ident = np.eye(128, dtype=np.float32)
    trilinc = np.tril(np.ones((128, 128), np.float32))
    triuinc = np.triu(np.ones((128, 128), np.float32))
    upperneg = np.where(np.arange(128)[:, None] > np.arange(128)[None, :],
                        np.float32(-1e30), np.float32(0.0))
    tp = np.arange(128)
    same_blk = (tp[:, None] // 32) == (tp[None, :] // 32)
    diff = tp[None, :] - tp[:, None]
    diffblk = np.where(same_blk & (diff >= 0), diff.astype(np.float32), np.float32(0.0))
    offneg = np.where(same_blk & (diff >= 0), np.float32(0.0), np.float32(-1e30))
    return dict(ident=ident, trilinc=trilinc.astype(np.float32), triuinc=triuinc,
                upperneg=upperneg, diffblk=diffblk, offneg=offneg)




def emit_proj(em, psum_pair, w_sb, ktiles, act3, otiles, epilogue, s_pe,
              first_waits=(), extra_k0_waits=None, N=512, start_acc=True):
    rel = [em.bank_rel.get(id(psum_pair[0])), em.bank_rel.get(id(psum_pair[1]))]
    for ot in range(otiles):
        pb = psum_pair[ot % 2]
        for k in range(ktiles):
            wts = []
            if k == 0:
                if ot < 2:
                    wts.extend(first_waits)
                if rel[ot % 2] is not None:
                    wts.append(rel[ot % 2])
                if extra_k0_waits is not None:
                    wts.extend(extra_k0_waits(ot))
            em.op("tensor", lambda t, ot=ot, k=k, pb=pb: t.matmul(
                pb[:, 0:N], w_sb[:, k, ot * 128:(ot + 1) * 128], act3[:, k, :],
                start=(k == 0 and start_acc), stop=(k == ktiles - 1)),
                waits=tuple(wts), incs=s_pe.inc() if k == ktiles - 1 else ())
        rel[ot % 2] = epilogue(ot, pb, s_pe.now())
    em.bank_rel[id(psum_pair[0])] = rel[0]
    em.bank_rel[id(psum_pair[1])] = rel[1]


class LNKit:

    def __init__(self, em, name, N=512, psums=None, bufs=None):
        self.em = em
        self.N = N
        if bufs is None:
            self.vbf = em.sb(f"{name}_vbf", [128, 8, N], BF16)[:]
            self.v2bf = em.sb(f"{name}_v2bf", [128, 8, N], BF16)[:]
        else:
            self.vbf, self.v2bf = bufs
        self.stat = em.sb(f"{name}_stat", [1, 4, N], F32)
        self.ones128 = em.sb(f"{name}_o128", [128, 1], BF16)
        self.onesrow = em.sb(f"{name}_orow", [1, 128], F32)
        if psums is None:
            self.pstA = em.ps(f"{name}_pstA", [1, N])[:]
            self.pstB = em.ps(f"{name}_pstB", [1, N])[:]
            pbc = em.ps(f"{name}_pbc", [128, 2 * N])
            self.pbc_mean = pbc[:, 0:N]
            self.pbc_rstd = pbc[:, N:2 * N]
            self.rel_keys = ()
        else:
            self.pstA, self.pstB, self.pbc_mean, self.pbc_rstd, self.rel_keys = psums
        self.s_pe = em.sem(f"{name}_pe")
        self.s_dve = em.sem(f"{name}_dve")
        self.s_act = em.sem(f"{name}_act")
        em.op("vector", lambda v: v.memset(self.ones128[:], 1.0), incs=self.s_dve.inc())
        em.op("vector", lambda v: v.memset(self.onesrow[:], 1.0), incs=self.s_dve.inc())
        self.init_done = self.s_dve.now()
        self.prev_sums = None
        self.prev_bc_readers = None

    def emit(self, vsb, out_f32, out_bf=None, v_ready=(), out_waits=()):
        em, N = self.em, self.N
        cw = list(v_ready)
        if self.prev_sums is not None:
            cw.append(self.prev_sums)
        em.op("scalar", lambda s: s.copy(
            self.vbf.rearrange("p a t -> p (a t)"), vsb.rearrange("p a t -> p (a t)")),
            waits=tuple(cw), incs=self.s_act.inc())
        em.op("scalar", lambda s: s.activation(
            self.v2bf.rearrange("p a t -> p (a t)"), vsb.rearrange("p a t -> p (a t)"), AF.Square),
            incs=self.s_act.inc())
        casts_done = self.last_casts = self.s_act.now()
        k0w = [casts_done, self.s_dve.now()]
        for key in self.rel_keys:
            if em.bank_rel.get(id(key)) is not None:
                k0w.append(em.bank_rel[id(key)])
        for k in range(8):
            em.op("tensor", lambda t, k=k: t.matmul(self.pstA, self.ones128[:], self.vbf[:, k, :],
                                                    start=(k == 0), stop=(k == 7)),
                  waits=tuple(k0w) if k == 0 else ())
        for k in range(8):
            em.op("tensor", lambda t, k=k: t.matmul(self.pstB, self.ones128[:], self.v2bf[:, k, :],
                                                    start=(k == 0), stop=(k == 7)),
                  incs=self.s_pe.inc() if k == 7 else ())
        self.prev_sums = sums_done = self.s_pe.now()
        st = self.stat
        em.op("vector", lambda v: v.tensor_scalar_mul(st[:, 0, :], self.pstA, 1.0 / 1024),
              waits=sums_done, incs=self.s_dve.inc())
        em.op("vector", lambda v: v.tensor_scalar_mul(st[:, 1, :], self.pstB, 1.0 / 1024),
              incs=self.s_dve.inc())
        em.op("vector", lambda v: v.tensor_mul(st[:, 2, :], st[:, 0, :], st[:, 0, :]),
              incs=self.s_dve.inc())
        em.op("vector", lambda v: v.scalar_tensor_tensor(st[:, 2, :], st[:, 1, :], 1e-5, st[:, 2, :], OP.add, OP.subtract),
              incs=self.s_dve.inc())
        em.op("scalar", lambda s: s.activation(st[:, 3, :], st[:, 2, :], AF.Ln),
              waits=self.s_dve.now(), incs=self.s_act.inc())
        em.op("scalar", lambda s: s.activation(st[:, 1, :], st[:, 3, :], AF.Exp, scale=-0.5),
              incs=self.s_act.inc())
        rstd_done = self.s_act.now()
        bw = [rstd_done]
        if self.prev_bc_readers is not None:
            bw.append(self.prev_bc_readers)
        em.op("tensor", lambda t: t.matmul(self.pbc_mean, self.onesrow[:], st[:, 0, :], start=True, stop=True),
              waits=tuple(bw))
        em.op("tensor", lambda t: t.matmul(self.pbc_rstd, self.onesrow[:], st[:, 1, :], start=True, stop=True),
              incs=self.s_pe.inc())
        bc_done = self.s_pe.now()
        tgt = out_f32 if out_f32 is not None else out_bf
        for ot in range(8):
            em.op("vector", lambda v, ot=ot: v.tensor_sub(tgt[:, ot, :], vsb[:, ot, :], self.pbc_mean),
                  waits=(bc_done,) + tuple(out_waits) if ot == 0 else (), incs=self.s_dve.inc())
            em.op("vector", lambda v, ot=ot: v.tensor_mul(tgt[:, ot, :], tgt[:, ot, :], self.pbc_rstd),
                  incs=self.s_dve.inc())
        done = self.s_dve.now()
        for key in self.rel_keys:
            em.bank_rel[id(key)] = done
        ret = [done]
        if out_bf is not None and out_f32 is not None:
            em.op("scalar", lambda s: s.copy(
                out_bf.rearrange("p a t -> p (a t)"), out_f32.rearrange("p a t -> p (a t)")),
                waits=done, incs=self.s_act.inc())
            ret.append(self.s_act.now())
        self.prev_bc_readers = done
        return tuple(ret)



RG8 = [[0, 1, 2, 3, 4, 5, 6, 7]]
NH = 16


def build_mamba(T=4096):
    nc = bass.Bass("TRN2", target_bir_lowering=False, debug=False)
    aps = {"T": T}
    aps["u"] = nc.dram_tensor("u", [1024, 128 + T], F32, kind="ExternalInput").ap()
    aps["wxbd"] = nc.dram_tensor("wxbd", [1024, 2128], F32, kind="ExternalInput").ap()
    aps["wzc"] = nc.dram_tensor("wzc", [1024, 2112], F32, kind="ExternalInput").ap()
    aps["owT"] = nc.dram_tensor("owT", [2048, 1024], F32, kind="ExternalInput").ap()
    aps["cwcol"] = nc.dram_tensor("cwcol", [128, 18, 4], F32, kind="ExternalInput").ap()
    aps["cbcol"] = nc.dram_tensor("cbcol", [128, 18], F32, kind="ExternalInput").ap()
    aps["dtbcol"] = nc.dram_tensor("dtbcol", [16, 1], F32, kind="ExternalInput").ap()
    aps["alog"] = nc.dram_tensor("alog", [1, 16], F32, kind="ExternalInput").ap()
    aps["dvec"] = nc.dram_tensor("dvec", [1, 16], F32, kind="ExternalInput").ap()
    aps["rmspk"] = nc.dram_tensor("rmspk", [128, 16], F32, kind="ExternalInput").ap()
    aps["triuinc"] = nc.dram_tensor("triuinc", [128, 128], F32, kind="ExternalInput").ap()
    aps["upperneg"] = nc.dram_tensor("upperneg", [128, 128], F32, kind="ExternalInput").ap()
    aps["ident"] = nc.dram_tensor("ident", [128, 128], F32, kind="ExternalInput").ap()
    aps["m2T"] = nc.dram_tensor("m2T", [8, 8], F32, kind="ExternalInput").ap()
    aps["selcol"] = nc.dram_tensor("selcol", [8, 1], F32, kind="ExternalInput").ap()
    aps["x1"] = nc.dram_tensor("x1", [1024, T], F32, kind="ExternalOutput").ap()
    aps["sp_xtok"] = nc.dram_tensor("sp_xtok", [T, 2048], BF16).ap()
    aps["sp_btok"] = nc.dram_tensor("sp_btok", [T, 64], BF16).ap()
    aps["sp_bfeat"] = nc.dram_tensor("sp_bfeat", [64, T], BF16).ap()
    aps["sp_dt"] = nc.dram_tensor("sp_dt", [16, T], F32).ap()
    aps["w8d"] = nc.dram_tensor("w8d", [8, 16], F32).ap()
    aps["cum_dram"] = nc.dram_tensor("cum_dram", [16, 128], F32).ap()
    aps["ow_bf"] = nc.dram_tensor("ow_bf", [8, 128, 16, 128], BF16).ap()
    aps["ag_in"] = nc.dram_tensor("ag_in", [65, 2048], F32).ap()
    aps["ag_out"] = nc.dram_tensor("ag_out", [520, 2048], F32).ap()
    es = ExitStack()
    with es:
        em = Emit(nc, es)
        emit_mamba(em, aps)
        em.finish()
    return nc


def emit_mamba(em, cfg):
    T = cfg["T"]
    NSC = T // 512
    u = cfg["u"]
    urr = u.rearrange("(k p) t -> p k t", p=128)

    wbig = em.sb("a_wbig", [128, 8, 2176], BF16)
    cw = em.sb("a_cw", [128, 18, 4], F32)
    cb = em.sb("a_cb", [128, 18], F32)
    dtb = em.sb("a_dtb", [16, 1], F32)
    rmsw = em.sb("a_rmsw", [128, 16], F32)
    triu = em.sb("a_triu", [128, 128], F32)
    upneg = em.sb("a_upneg", [128, 128], F32)
    ident = em.sb("a_ident", [128, 128], F32)
    alogr = em.sb("a_alogr", [1, 16], F32)
    dvecr = em.sb("a_dvecr", [1, 16], F32)
    arow = em.sb("a_arow", [1, 16], F32)
    abc = em.sb("a_abc", [128, 16], F32)
    dial = em.sb("a_dial", [128, 16 * 128], BF16)
    usb1 = em.sb("a_u", [128, 8, 512], F32)
    usb2 = [usb1, usb1]
    ubf1 = em.sb("a_ub", [128, 8, 512], BF16)
    ubf2 = [ubf1, ubf1]
    uhalo = em.sb("a_uhalo", [128, 8, 3], BF16)
    scr = em.sb("a_scr", [128, 10304], F32)
    xcraw = scr[:, 0:8755].rearrange("p (a t) -> p a t", a=17)
    acc1 = [scr[:, 8768:9280], scr[:, 9280:9792]]
    cacc2 = scr[0:64, 9792:10304]
    sigz = scr[:, 0:8192].rearrange("p (a t) -> p a t", a=16)
    szp = scr[:, 8192:10240].rearrange("p (a t) -> p a t", a=16)
    sgld = scr[0:64, 0:2048]
    w8rep = scr[0:1, 2048:3072].bitcast(BF16)
    un2 = em.sb("a_un2", [128, 8192], BF16)
    xhat = un2[:].rearrange("p (a t) -> p a t", a=16)
    ygat = un2[:].rearrange("p (a t) -> p a t", a=16)
    lnvbf = un2[:, 0:4096].rearrange("p (a t) -> p a t", a=8)
    lnv2bf = un2[:, 4096:8192].rearrange("p (a t) -> p a t", a=8)
    un3 = em.sb("a_un3", [128, 4096], BF16)
    dtscr = un3[0:16, 0:3072].bitcast(F32).rearrange("p (a t) -> p a t", a=3)
    owsb = [un3[:, 0:2048].rearrange("p (a t) -> p a t", a=16),
            un3[:, 2048:4096].rearrange("p (a t) -> p a t", a=16)]
    ccraw = em.sb("a_ccraw", [64, 515], F32)
    bfeat = em.sb("a_bfeat", [64, 512], BF16)
    chat = em.sb("a_chat", [64, 512], BF16)
    dtraw = un3[0:16, 3072:4096].bitcast(F32)
    dtsf = em.sb("a_dtsf", [16, 4, 128], F32)
    xtok = [em.sb(f"a_xtok_{i}", [128, 2048], BF16) for i in range(2)]
    btok = [em.sb(f"a_btok_{i}", [128, 64], BF16) for i in range(2)]
    btok2 = em.sb("a_btok2", [128, 64], BF16)
    dtT = em.sb("a_dtT", [128, 16], F32)
    catok = em.sb("a_catok", [128, 16], F32)
    cafeat = em.sb("a_cafeat", [16, 128], F32)
    L2 = em.sb("a_L2", [2, 2048], F32)
    R2 = em.sb("a_R2", [2, 2048], F32)
    e1row = em.sb("a_e1row", [1, 2048], BF16)
    edec = em.sb("a_edec", [128, 16], F32)
    edl = em.sb("a_edl", [64, 16], F32)
    dacc = em.sb("a_dacc", [1, 16], F32)
    ssb = em.sb("a_ssb", [64, 16 * 128], F32)
    sbf = em.sb("a_sbf", [64, 16 * 128], BF16)
    g2sb = em.sb("a_g2sb", [128, 128], F32)
    emtmp = em.sb("a_emtmp", [128, 512], F32)
    emask = em.sb("a_emask", [128, 512], F32)
    g2all = em.sb("a_g2all", [128, 2048], BF16)
    cc2 = [em.sb(f"a_cc2_{i}", [64, 128], BF16) for i in range(2)]
    y2bf = em.sb("a_y2bf", [128, 2048], BF16)
    rstdr = em.sb("a_rstdr", [1, 3, 128], F32)
    rbc = em.sb("a_rbc", [128, 128], F32)
    w8all = em.sb("a_w8all", [1, 128], F32)
    dall = em.sb("a_dall", [8, 16], F32)
    m2t = em.sb("a_m2t", [8, 8], F32)
    selc = em.sb("a_selc", [8, 1], F32)
    onesrow = em.sb("a_onesrow", [1, 128], F32)
    onesbf = em.sb("a_onesbf", [1, 128], BF16)

    pY = em.ps("a_pY", [128, 2048])
    pD = [em.ps(f"a_pD{i}", [128, 512]) for i in range(2)]
    pq = [em.ps(f"a_pq{i}", [128, 512]) for i in range(2)]

    ln = LNKit(em, "a_ln", psums=(pD[0][0:1, 0:512], pD[1][0:1, 0:512],
                                  pD[0][:, 0:512], pD[1][:, 0:512],
                                  (pD[0], pD[1])), bufs=(lnvbf, lnv2bf))

    s_in = em.sem("a_in")
    s_w = em.sem("a_w")
    s_pe = em.sem("a_pe")
    s_dve = em.sem("a_dve")
    s_act = em.sem("a_act")
    s_tr = em.sem("a_tr")
    s_cum = em.sem("a_cum")
    s_inu = em.sem("a_inu")
    s_sp = em.sem("a_sp")
    s_ld = em.sem("a_ld")
    s_cc = em.sem("a_cc")
    s_st = em.sem("a_st")

    SR = {}

    em.op("gpsimd", lambda g: g.dma_start(
        out=wbig[:, :, 0:2128], in_=cfg["wxbd"].rearrange("(k p) o -> p k o", p=128)), incs=s_w.inc(16))
    for name_, sb_ in (("cwcol", cw), ("cbcol", cb), ("dtbcol", dtb), ("rmspk", rmsw),
                       ("triuinc", triu), ("upperneg", upneg), ("ident", ident),
                       ("m2T", m2t), ("selcol", selc), ("alog", alogr), ("dvec", dvecr)):
        em.op("gpsimd", lambda g, n=name_, s=sb_: g.dma_start(out=s[:], in_=cfg[n]), incs=s_w.inc(16))
    em.op("gpsimd", lambda g: g.dma_start(out=uhalo[:], in_=urr[:, :, 125:128]), incs=s_w.inc(16))
    if cfg.get("bcast") is not None:
        mg_sb = em.sb("a_mg", [128, 1], F32)
        em.op("gpsimd", lambda g: g.dma_start(out=mg_sb[:], in_=cfg["bcast"]["mgatec"]),
              incs=s_w.inc(16))
        cfg["bcast"]["mgate_sb"] = mg_sb
    wready = s_w.now()
    for buf, val in ((onesrow, 1.0), (onesbf, 1.0)):
        em.op("vector", lambda v, b_=buf, vv=val: v.memset(b_[:], vv), incs=s_dve.inc())
    em.op("vector", lambda v: v.memset(ssb[:], 0.0), incs=s_dve.inc())
    em.op("vector", lambda v: v.memset(dacc[:], 0.0), incs=s_dve.inc())
    em.op("vector", lambda v: v.memset(xcraw[:, :, 0:3], 0.0), incs=s_dve.inc())
    em.op("vector", lambda v: v.memset(ccraw[:, 0:3], 0.0), incs=s_dve.inc())
    em.op("vector", lambda v: v.memset(L2[:], 1.0), incs=s_dve.inc())
    em.op("vector", lambda v: v.memset(R2[:], -1.0), incs=s_dve.inc())
    ms_done = s_dve.now()

    em.op("scalar", lambda s: s.activation(arow[:], alogr[:], AF.Exp),
          waits=wready, incs=s_act.inc())
    em.op("vector", lambda v: v.tensor_scalar_mul(arow[:], arow[:], -1.0),
          waits=s_act.now(), incs=s_dve.inc())
    em.op("tensor", lambda t: t.matmul(pq[0][:, 0:16], onesrow[:], arow[:], start=True, stop=True),
          waits=(s_dve.now(), ms_done), incs=s_pe.inc())
    em.op("scalar", lambda s: s.copy(abc[:], pq[0][:, 0:16]), waits=s_pe.now(), incs=s_act.inc())
    em.op("tensor", lambda t: t.matmul(pq[0][:, 16:32], onesrow[:], dvecr[:], start=True, stop=True),
          waits=s_act.now(), incs=s_pe.inc())
    em.op("scalar", lambda s: s.copy(rbc[:, 0:16], pq[0][:, 16:32]), waits=s_pe.now(), incs=s_act.inc())
    dbc_done = s_act.now()
    for h in range(NH):
        em.op("vector", lambda v, h=h: v.tensor_scalar_mul(
            dial[:, h * 128:(h + 1) * 128], ident[:], rbc[:, h:h + 1]),
            waits=(dbc_done,) if h == 0 else (), incs=s_dve.inc())
    prep_done = s_dve.now()
    em.bank_rel[id(pq[0])] = dbc_done

    for k in range(8):
        for ot in range(17):
            m0, mw = ot * 128, (128 if ot < 16 else 80)
            em.op("tensor", lambda t, ot=ot, k=k, m0=m0, mw=mw: t.matmul(
                pq[1][0:mw, ot * 3:(ot + 1) * 3], wbig[:, k, m0:m0 + mw], uhalo[:, k, :],
                start=(k == 0), stop=(k == 7)),
                waits=(wready, prep_done) if (k == 0 and ot == 0) else (),
                incs=s_pe.inc() if (k == 7 and ot == 16) else ())
    halo_mm = s_pe.now()
    for ot in range(16):
        em.op("scalar", lambda s, ot=ot: s.copy(xcraw[:, ot, 0:3], pq[1][:, ot * 3:(ot + 1) * 3]),
              waits=(halo_mm, ms_done) if ot == 0 else (), incs=s_act.inc())
    em.op("scalar", lambda s: s.copy(xcraw[0:64, 16, 0:3], pq[1][0:64, 48:51]), incs=s_act.inc())
    em.bank_rel[id(pq[1])] = halo_done = s_act.now()

    calast = R2[0:1, :].rearrange("q (h t) -> q h t", t=128)[:, :, 127:128].squeeze(2)

    def cum_pipe(dt_ap, dt_ready, with_L2, cp=None, use_pq_rel=True):
        if cp is None:
            cp = pq[0][:]
        w0 = [dt_ready]
        rel_specs = (SR.get("cum"), em.bank_rel.get(id(pq[0])) if use_pq_rel else None)
        for spec in rel_specs:
            if spec is not None:
                w0.append(spec)
        em.op("tensor", lambda t, cp=cp: t.transpose(cp[:, 128:144], dt_ap, ident[0:16, 0:16]),
              waits=tuple(w0), incs=s_pe.inc())
        em.op("scalar", lambda s, cp=cp: s.copy(dtT[:], cp[:, 128:144]), waits=s_pe.now(), incs=s_act.inc())
        em.op("tensor", lambda t, cp=cp: t.matmul(cp[:, 144:160], triu[:], dtT[:], start=True, stop=True),
              waits=s_act.now(), incs=s_pe.inc())
        em.op("vector", lambda v, cp=cp: v.tensor_mul(catok[:], cp[:, 144:160], abc[:]),
              waits=s_pe.now(), incs=s_dve.inc())
        em.op("tensor", lambda t, cp=cp: t.transpose(cp[0:16, 160:288], catok[:], ident[:]),
              waits=s_dve.now(), incs=s_pe.inc())
        em.op("scalar", lambda s, cp=cp: s.copy(cafeat[:], cp[0:16, 160:288]),
              waits=s_pe.now(), incs=s_act.inc())
        caf_done = s_act.now()
        em.op("scalar", lambda e: e.dma_start(out=cfg["cum_dram"], in_=cafeat[:]),
              waits=caf_done, incs=s_cum.inc(16))
        cd_done = s_cum.now()
        em.op("scalar", lambda e: e.dma_start(
            out=R2[0:1, :], in_=cfg["cum_dram"].rearrange("h t -> (h t)").unsqueeze(0)),
            waits=cd_done, incs=s_cum.inc(16))
        if with_L2:
            em.op("scalar", lambda e: e.dma_start(
                out=L2[1:2, :], in_=cfg["cum_dram"].rearrange("h t -> (h t)").unsqueeze(0)),
                incs=s_cum.inc(16))
        row_done = s_cum.now()
        em.op("tensor", lambda t, cp=cp: t.matmul(cp[:, 288:304], onesrow[:], calast, start=True, stop=True),
              waits=row_done, incs=s_pe.inc())
        em.op("tensor", lambda t, cp=cp: t.matmul(cp[0:64, 304:320], onesrow[:, 0:64], calast, start=True, stop=True),
              incs=s_pe.inc())
        bc_done = s_pe.now()
        em.op("vector", lambda v, cp=cp: v.tensor_sub(edec[:], cp[:, 288:304], catok[:]),
              waits=bc_done, incs=s_dve.inc())
        em.op("scalar", lambda s: s.activation(edec[:], edec[:], AF.Exp),
              waits=s_dve.now(), incs=s_act.inc())
        em.op("scalar", lambda s, cp=cp: s.activation(edl[:], cp[0:64, 304:320], AF.Exp), incs=s_act.inc())
        SR["cum"] = done = s_act.now()
        return done, row_done

    def t_and_update(h, tb, extra_first, tp=None, use_pq_rel=True):
        slot = h % 2
        if tp is None:
            tp = pq[1][:]
        bw = list(extra_first) if extra_first else []
        if SR.get("btok2") is not None:
            bw.append(SR["btok2"])
        em.op("vector", lambda v, h=h, tb=tb: v.tensor_scalar_mul(
            btok2[:], btok[tb][:], edec[:, h:h + 1]),
            waits=tuple(bw), incs=s_dve.inc())
        bpp = s_dve.now()
        tw = [bpp]
        for spec in (SR.get(f"T{slot}"), em.bank_rel.get(id(pq[1])) if use_pq_rel else None):
            if spec is not None:
                tw.append(spec)
        em.op("tensor", lambda t, h=h, tb=tb, slot=slot, tp=tp: t.matmul(
            tp[0:64, slot * 128:(slot + 1) * 128], btok2[:],
            xtok[tb][:, h * 128:(h + 1) * 128], start=True, stop=True),
            waits=tuple(tw), incs=s_pe.inc())
        t_done = s_pe.now()
        em.op("vector", lambda v, h=h, slot=slot, tp=tp: v.scalar_tensor_tensor(
            ssb[:, h * 128:(h + 1) * 128], ssb[:, h * 128:(h + 1) * 128],
            edl[:, h:h + 1], tp[0:64, slot * 128:(slot + 1) * 128],
            OP.mult, OP.add), waits=t_done, incs=s_dve.inc())
        SR[f"T{slot}"] = s_dve.now()
        SR["btok2"] = t_done

    ubf_guards = [None, None]
    conv_guard = None
    xhat_guard = None
    tr_prev = [None, None]
    xtok_rd = [None, None]
    gchunk = 0
    for sc in range(NSC):
        b = sc % 2
        ubf = ubf2[b]
        t0 = 128 + sc * 512
        w = [wready]
        if ubf_guards[b] is not None:
            w.append(ubf_guards[b])
        em.op("gpsimd", lambda g, t0=t0, ubf=ubf: g.dma_start(
            out=ubf[:], in_=urr[:, :, t0:t0 + 512]), waits=tuple(w), incs=s_in.inc(16))
        cast_done = s_in.now()

        if sc > 0:
            em.op("scalar", lambda s: s.copy(xcraw[:, :, 0:3], xcraw[:, :, 512:515]),
                  waits=conv_guard, incs=s_act.inc())
        carry_done = s_act.now()

        def ep_x(ot, pb, pe_spec):
            em.op("scalar", lambda s, ot=ot, pb=pb: s.copy(xcraw[:, ot, 3:515], pb[:]),
                  waits=(pe_spec, carry_done), incs=s_act.inc())
            return s_act.now()
        emit_proj(em, pq, wbig, 8, ubf[:], 16, ep_x, s_pe, first_waits=(cast_done,))
        wts0 = [em.bank_rel[id(pq[0])]]
        for k in range(8):
            em.op("tensor", lambda t, k=k, ubf=ubf: t.matmul(
                pq[0][0:80, 0:512], wbig[:, k, 2048:2128], ubf[:, k, :],
                start=(k == 0), stop=(k == 7)),
                waits=tuple(wts0) if k == 0 else (), incs=s_pe.inc() if k == 7 else ())
        bd_mm = ubf_guards[0] = ubf_guards[1] = s_pe.now()
        em.op("scalar", lambda s: s.copy(xcraw[0:64, 16, 3:515], pq[0][0:64, 0:512]),
              waits=(bd_mm, carry_done), incs=s_act.inc())
        dtg = [xhat_guard] if xhat_guard else []
        em.op("scalar", lambda s: s.copy(dtraw, pq[0][64:80, 0:512]),
              waits=tuple(dtg), incs=s_act.inc())
        em.bank_rel[id(pq[0])] = evac_bd = s_act.now()

        silu_prev = [None, None]
        for tile in range(17):
            rows = 128 if tile < 16 else 64
            dst = acc1[tile % 2][0:rows] if tile < 16 else cacc2
            cwt = [evac_bd]
            if tile < 16 and silu_prev[tile % 2] is not None:
                cwt.append(silu_prev[tile % 2])
            if tile == 0 and xhat_guard is not None:
                cwt.append(xhat_guard)
            em.op("vector", lambda v, tile=tile, dst=dst, rows=rows: v.tensor_scalar_mul(
                dst, xcraw[0:rows, tile, 0:512], cw[0:rows, tile, 0:1]),
                waits=tuple(cwt), incs=s_dve.inc())
            for kk in range(1, 4):
                em.op("vector", lambda v, tile=tile, kk=kk, dst=dst, rows=rows: v.scalar_tensor_tensor(
                    dst, xcraw[0:rows, tile, kk:512 + kk], cw[0:rows, tile, kk:kk + 1],
                    dst, OP.mult, OP.add), incs=s_dve.inc())
            taps = s_dve.now()
            if tile < 16:
                em.op("scalar", lambda s, tile=tile: s.activation(
                    xhat[:, tile, :], acc1[tile % 2], AF.Silu, bias=cb[:, tile:tile + 1]),
                    waits=taps, incs=s_act.inc())
                silu_prev[tile % 2] = s_act.now()
            else:
                bfw = [taps]
                if SR.get("bfeat_rd") is not None:
                    bfw.extend(SR["bfeat_rd"])
                em.op("scalar", lambda s: s.activation(
                    bfeat[:], cacc2, AF.Silu, bias=cb[0:64, 16:17]),
                    waits=tuple(bfw), incs=s_act.inc())
        conv_guard = s_dve.now()
        silu_done = s_act.now()

        em.op("scalar", lambda s, sc=sc: s.dma_start(
            out=cfg["sp_bfeat"][:, sc * 512:(sc + 1) * 512], in_=bfeat[:]),
            waits=silu_done, incs=s_sp.inc(16))

        em.op("scalar", lambda s: s.activation(dtscr[:, 0, :], dtraw, AF.Abs, bias=dtb[:]),
              waits=evac_bd, incs=s_act.inc())
        em.op("scalar", lambda s: s.activation(dtscr[:, 1, :], dtscr[:, 0, :], AF.Exp, scale=-1.0),
              incs=s_act.inc())
        em.op("vector", lambda v: v.tensor_scalar_add(dtscr[:, 1, :], dtscr[:, 1, :], 1.0),
              waits=s_act.now(), incs=s_dve.inc())
        em.op("scalar", lambda s: s.activation(dtscr[:, 2, :], dtscr[:, 1, :], AF.Ln),
              waits=s_dve.now(), incs=s_act.inc())
        em.op("scalar", lambda s: s.activation(dtscr[:, 0, :], dtraw, AF.Relu, bias=dtb[:]),
              incs=s_act.inc())
        dfw = [s_act.now()]
        if SR.get("dtsf_rd") is not None:
            dfw.extend(SR["dtsf_rd"])
        em.op("vector", lambda v: v.tensor_add(
            dtsf[:].rearrange("p a t -> p (a t)"), dtscr[:, 0, :], dtscr[:, 2, :]),
            waits=tuple(dfw), incs=s_dve.inc())
        dt_done = s_dve.now()
        em.op("scalar", lambda s, sc=sc: s.dma_start(
            out=cfg["sp_dt"][:, sc * 512:(sc + 1) * 512],
            in_=dtsf[:].rearrange("p a t -> p (a t)")), waits=dt_done, incs=s_sp.inc(16))

        for cc in range(4):
            tb = gchunk % 2
            trw = [silu_done]
            if tr_prev[tb] is not None:
                trw.append(tr_prev[tb])
            if xtok_rd[tb] is not None:
                trw.append(xtok_rd[tb])
            for ft in range(16):
                em.op("sync", lambda sy, ft=ft, cc=cc, tb=tb: sy.dma_start_transpose(
                    xtok[tb][:, ft * 128:(ft + 1) * 128], xhat[:, ft, cc * 128:(cc + 1) * 128]),
                    waits=tuple(trw) if ft == 0 else (), incs=s_tr.inc(16))
            em.op("sync", lambda sy, cc=cc, tb=tb: sy.dma_start_transpose(
                btok[tb][:], bfeat[:, cc * 128:(cc + 1) * 128]), incs=s_tr.inc(16))
            tr_done = s_tr.now()
            em.op("scalar", lambda s, gc=gchunk, tb=tb: s.dma_start(
                out=cfg["sp_xtok"][gc * 128:(gc + 1) * 128, :], in_=xtok[tb][:]),
                waits=tr_done, incs=s_sp.inc(16))
            em.op("scalar", lambda s, gc=gchunk, tb=tb: s.dma_start(
                out=cfg["sp_btok"][gc * 128:(gc + 1) * 128, :], in_=btok[tb][:]),
                incs=s_sp.inc(16))
            tr_prev[tb] = s_sp.now()
            cum_done, _ = cum_pipe(dtsf[:, cc, :], dt_done, with_L2=False,
                                   cp=pY[:, 0:512], use_pq_rel=False)
            em.op("vector", lambda v: v.tensor_add(
                dacc[:].unsqueeze(2), dacc[:].unsqueeze(2),
                R2[0:1, :].rearrange("q (h t) -> q h t", t=128)[:, :, 127:128]),
                waits=cum_done, incs=s_dve.inc())
            for h in range(NH):
                t_and_update(h, tb, (cum_done, tr_done) if h == 0 else None,
                             tp=pY[:, 512:1024], use_pq_rel=False)
            xtok_rd[tb] = s_pe.now()
            gchunk += 1
        xhat_guard = (s_sp, s_sp.n)
        SR["dtsf_rd"] = [SR["cum"], s_sp.now()]
        SR["bfeat_rd"] = [s_tr.now(), s_sp.now()]

    sfin = s_dve.now()
    em.op("scalar", lambda s: s.dma_start(out=cfg["ag_in"][0:64, :], in_=ssb[:]),
          waits=sfin, incs=s_sp.inc(16))
    em.op("scalar", lambda s: s.dma_start(out=cfg["ag_in"][64:65, 0:16], in_=dacc[:]),
          incs=s_sp.inc(16))
    em.op("gpsimd", lambda g: g.collective_compute(
        "AllGather", OP.bypass, replica_groups=RG8,
        ins=[cfg["ag_in"].opt()], outs=[cfg["ag_out"].opt()]),
        waits=s_sp.now(), incs=s_cc.inc(1))
    ag_done = s_cc.now()

    em.op("sync", lambda sy: sy.dma_start(
        out=dall[:], in_=cfg["ag_out"].rearrange("(j r) h -> j r h", r=65)[:, 64, 0:16]),
        waits=ag_done, incs=s_ld.inc(16))
    em.op("tensor", lambda t: t.matmul(pq[0][0:8, 0:16], m2t[:], dall[:], start=True, stop=True),
          waits=(s_ld.now(), em.bank_rel[id(pq[0])]), incs=s_pe.inc())
    em.op("scalar", lambda s: s.activation(dall[:], pq[0][0:8, 0:16], AF.Exp),
          waits=s_pe.now(), incs=s_act.inc())
    em.bank_rel[id(pq[0])] = s_act.now()
    em.op("vector", lambda v: v.tensor_scalar_mul(dall[:], dall[:], selc[:]),
          waits=s_act.now(), incs=s_dve.inc())
    em.op("scalar", lambda s: s.dma_start(out=cfg["w8d"], in_=dall[:]),
          waits=s_dve.now(), incs=s_sp.inc(16))
    em.op("sync", lambda sy: sy.dma_start(
        out=w8all[:], in_=cfg["w8d"].rearrange("j h -> (j h)").unsqueeze(0)),
        waits=s_sp.now(), incs=s_ld.inc(16))
    w8_done = s_ld.now()
    sinit_rel = None
    for j in range(8):
        em.op("vector", lambda v, j=j: v.tensor_copy(
            w8rep[:].rearrange("q (h t) -> q h t", h=16),
            w8all[:, j * 16:(j + 1) * 16].unsqueeze(2).broadcast_to([1, 16, 128])),
            waits=(w8_done,) if j == 0 else (), incs=s_dve.inc())
        wrep_done = s_dve.now()
        for q in range(4):
            ww = [wrep_done] if q == 0 else []
            if q == 0 and sinit_rel is not None:
                ww.append(sinit_rel)
            em.op("tensor", lambda t, q=q: t.matmul(
                pY[0:64, q * 512:(q + 1) * 512], onesbf[:, 0:64], w8rep[:, q * 512:(q + 1) * 512],
                start=True, stop=True), waits=tuple(ww), incs=s_pe.inc() if q == 3 else ())
        wbc_done = s_pe.now()
        em.op("sync", lambda sy, j=j: sy.dma_start(
            out=sgld[:], in_=cfg["ag_out"][j * 65:j * 65 + 64, :]),
            waits=(ag_done,) if j == 0 else (), incs=s_ld.inc(16))
        ld_done = s_ld.now()
        if j == 0:
            em.op("vector", lambda v: v.tensor_mul(ssb[:], sgld[:], pY[0:64, :]),
                  waits=(wbc_done, ld_done, sfin), incs=s_dve.inc())
        else:
            em.op("vector", lambda v: v.tensor_mul(sgld[:], sgld[:], pY[0:64, :]),
                  waits=(wbc_done, ld_done), incs=s_dve.inc())
            em.op("vector", lambda v: v.tensor_add(ssb[:], ssb[:], sgld[:]), incs=s_dve.inc())
        sinit_rel = s_dve.now()
    em.bank_rel[id(pY)] = sinit_rel

    em.op("gpsimd", lambda g: g.dma_start(
        out=wbig[:, :, 0:2112], in_=cfg["wzc"].rearrange("(k p) o -> p k o", p=128)),
        waits=(s_pe.now(), ag_done), incs=s_w.inc(16))
    wzc_ready = s_w.now()
    for aa in range(8):
        em.op("gpsimd", lambda g, aa=aa: g.dma_start(
            out=cfg["ow_bf"][aa],
            in_=cfg["owT"].rearrange("(k p) o -> p k o", p=128)[:, :, aa * 128:(aa + 1) * 128]),
            incs=s_w.inc(16))
    owbf_ready = s_w.now()
    for k in range(8):
        em.op("tensor", lambda t, k=k: t.matmul(
            pq[1][0:64, 0:3], wbig[:, k, 2048:2112], uhalo[:, k, :],
            start=(k == 0), stop=(k == 7)),
            waits=(wzc_ready, em.bank_rel[id(pq[1])]) if k == 0 else (),
            incs=s_pe.inc() if k == 7 else ())
    em.op("scalar", lambda s: s.copy(ccraw[:, 0:3], pq[1][0:64, 0:3]),
          waits=s_pe.now(), incs=s_act.inc())
    em.bank_rel[id(pq[1])] = s_act.now()

    usb_guards = [None, None]
    gchunk = 0
    xl_prev = [None, None]
    cconv_guard = None
    prev_apply = None
    prev_ln_casts = None
    ow_prev = [None, None]
    SR.pop("cum", None)
    ubf_guards = [s_pe.now(), s_pe.now()]
    bcast = cfg.get("bcast")
    s_bc = em.sem("a_bc")
    s_bcc = em.sem("a_bcc")
    for sc in range(NSC):
        b = sc % 2
        usb = usb2[b]
        ubf = ubf2[b]
        t0 = 128 + sc * 512
        w = [wzc_ready]
        if usb_guards[b] is not None:
            w.extend(usb_guards[b])
        em.op("gpsimd", lambda g, t0=t0, ubf=ubf: g.dma_start(
            out=ubf[:], in_=urr[:, :, t0:t0 + 512]), waits=(ubf_guards[b],), incs=s_in.inc(16))
        cast_done = s_in.now()
        if bcast is not None and sc == NSC - 1:
            em.op("gpsimd", lambda g: g.collective_compute(
                "AllReduce", OP.add, replica_groups=RG8,
                ins=[bcast["ar1_in"].opt()], outs=[bcast["ar1_out"].opt()]),
                waits=s_bc.at(64), incs=s_bcc.inc(1))
        em.op("sync", lambda sy, t0=t0, usb=usb: sy.dma_start(
            out=usb[:], in_=urr[:, :, t0:t0 + 512]), waits=tuple(w), incs=s_inu.inc(16))
        usb_done = s_inu.now()

        def ep_z(ot, pb, pe_spec):
            wz = [pe_spec]
            if prev_apply is not None:
                wz.append(prev_apply)
            em.op("scalar", lambda s, ot=ot, pb=pb: s.activation(
                sigz[:, ot, :], pb[:], AF.Sigmoid), waits=tuple(wz), incs=s_act.inc())
            em.op("vector", lambda v, ot=ot: v.tensor_scalar_mul(
                sigz[:, ot, :], sigz[:, ot, :], rmsw[:, ot:ot + 1]),
                waits=s_act.now(), incs=s_dve.inc())
            return s_dve.now()
        emit_proj(em, pq, wbig, 8, ubf[:], 16, ep_z, s_pe, first_waits=(cast_done,))
        if sc > 0:
            em.op("scalar", lambda s: s.copy(ccraw[:, 0:3], ccraw[:, 512:515]),
                  waits=cconv_guard, incs=s_act.inc())
        ccarry = s_act.now()
        wtc = [em.bank_rel[id(pq[0])]]
        for k in range(8):
            em.op("tensor", lambda t, k=k, ubf=ubf: t.matmul(
                pq[0][0:64, 0:512], wbig[:, k, 2048:2112], ubf[:, k, :],
                start=(k == 0), stop=(k == 7)),
                waits=tuple(wtc) if k == 0 else (), incs=s_pe.inc() if k == 7 else ())
        ubf_guards[0] = ubf_guards[1] = s_pe.now()
        em.op("scalar", lambda s: s.copy(ccraw[:, 3:515], pq[0][0:64, 0:512]),
              waits=(s_pe.now(), ccarry), incs=s_act.inc())
        em.bank_rel[id(pq[0])] = cevac = s_act.now()
        em.op("vector", lambda v: v.tensor_scalar_mul(cacc2, ccraw[:, 0:512], cw[0:64, 17, 0:1]),
              waits=cevac, incs=s_dve.inc())
        for kk in range(1, 4):
            em.op("vector", lambda v, kk=kk: v.scalar_tensor_tensor(
                cacc2, ccraw[:, kk:512 + kk], cw[0:64, 17, kk:kk + 1], cacc2,
                OP.mult, OP.add), incs=s_dve.inc())
        cconv_guard = s_dve.now()
        chw = [cconv_guard]
        if SR.get("chat") is not None:
            chw.append(SR["chat"])
        em.op("scalar", lambda s: s.activation(chat[:], cacc2, AF.Silu, bias=cb[0:64, 17:18]),
              waits=tuple(chw), incs=s_act.inc())
        chat_done = s_act.now()
        em.op("sync", lambda sy, sc=sc: sy.dma_start(
            out=bfeat[:], in_=cfg["sp_bfeat"][:, sc * 512:(sc + 1) * 512]),
            waits=(SR.get("bfeat") or ()) and (SR["bfeat"],), incs=s_ld.inc(16))
        bf_ld = s_ld.now()
        em.op("sync", lambda sy, sc=sc: sy.dma_start(
            out=dtsf[:].rearrange("p a t -> p (a t)"), in_=cfg["sp_dt"][:, sc * 512:(sc + 1) * 512]),
            waits=(SR.get("dtld") or ()) and (SR["dtld"],), incs=s_ld.inc(16))
        dt_ld = s_ld.now()

        for cc in range(4):
            tb = gchunk % 2
            lw = []
            if xl_prev[tb] is not None:
                lw.append(xl_prev[tb])
            em.op("scalar", lambda s, gc=gchunk, tb=tb: s.dma_start(
                out=xtok[tb][:], in_=cfg["sp_xtok"][gc * 128:(gc + 1) * 128, :]),
                waits=tuple(lw), incs=s_ld.inc(16))
            em.op("scalar", lambda s, gc=gchunk, tb=tb: s.dma_start(
                out=btok[tb][:], in_=cfg["sp_btok"][gc * 128:(gc + 1) * 128, :]),
                incs=s_ld.inc(16))
            ld_done = s_ld.now()
            cum_done, _ = cum_pipe(dtsf[:, cc, :], dt_ld, with_L2=True)
            ew1 = [cum_done]
            if SR.get("e1") is not None:
                ew1.append(SR["e1"])
            em.op("scalar", lambda s: s.activation(e1row[:], R2[0:1, :], AF.Exp),
                  waits=tuple(ew1), incs=s_act.inc())
            e1_done = s_act.now()
            scw = [SR["Supd"]] if SR.get("Supd") else []
            em.op("scalar", lambda s: s.copy(sbf[:], ssb[:]), waits=tuple(scw), incs=s_act.inc())
            scast_done = s_act.now()
            gw_ = [chat_done, bf_ld]
            for spec in (SR.get("G2"), SR.get("RBC"), em.bank_rel.get(id(pq[0]))):
                if spec is not None:
                    gw_.append(spec)
            em.op("tensor", lambda t, cc=cc: t.matmul(
                pq[0][:, 0:128], bfeat[:, cc * 128:(cc + 1) * 128], chat[:, cc * 128:(cc + 1) * 128],
                start=True, stop=True), waits=tuple(gw_), incs=s_pe.inc())
            em.op("scalar", lambda s: s.copy(g2sb[:], pq[0][:, 0:128]),
                  waits=s_pe.now(), incs=s_act.inc())
            SR["G2"] = g2_done = s_act.now()
            for g in range(4):
                dw_ = [cum_done]
                for spec in (SR.get(f"D{g % 2}"), em.bank_rel.get(id(pD[g % 2]))):
                    if spec is not None:
                        dw_.append(spec)
                for hh in range(4):
                    h = g * 4 + hh
                    em.op("tensor", lambda t, g=g, hh=hh, h=h: t.matmul(
                        pD[g % 2][:, hh * 128:(hh + 1) * 128],
                        L2[:, h * 128:(h + 1) * 128], R2[:, h * 128:(h + 1) * 128],
                        start=True, stop=True),
                        waits=tuple(dw_) if hh == 0 else (),
                        incs=s_pe.inc() if hh == 3 else ())
                dmm = s_pe.now()
                em.op("vector", lambda v, g=g: v.tensor_tensor(
                    emtmp[:].rearrange("p (a t) -> p a t", a=4),
                    pD[g % 2][:].rearrange("p (a t) -> p a t", a=4),
                    upneg[:].unsqueeze(1).broadcast_to([128, 4, 128]), OP.add),
                    waits=dmm, incs=s_dve.inc())
                em.op("scalar", lambda s, g=g: s.activation(emask[:], emtmp[:], AF.Exp),
                      waits=s_dve.now(), incs=s_act.inc())
                em.op("vector", lambda v, g=g: v.tensor_tensor(
                    g2all[:, g * 512:(g + 1) * 512].rearrange("p (a t) -> p a t", a=4),
                    emask[:].rearrange("p (a t) -> p a t", a=4),
                    g2sb[:].unsqueeze(1).broadcast_to([128, 4, 128]), OP.mult),
                    waits=(s_act.now(), g2_done), incs=s_dve.inc())
                SR[f"D{g % 2}"] = s_dve.now()
            g2a_done = s_dve.now()
            yw = [g2a_done, scast_done, ld_done, e1_done]
            for spec in (SR.get("Y"), em.bank_rel.get(id(pY))):
                if spec is not None:
                    yw.append(spec)
            em.bank_rel[id(pY)] = None
            for h in range(NH):
                slot = h % 2
                ew = [e1_done]
                for spec in (SR.get(f"EB{slot}"), em.bank_rel.get(id(pq[1]))):
                    if spec is not None:
                        ew.append(spec)
                em.op("tensor", lambda t, h=h, slot=slot: t.matmul(
                    pq[1][0:64, 256 + slot * 128:384 + slot * 128],
                    onesbf[:, 0:64], e1row[:, h * 128:(h + 1) * 128], start=True, stop=True),
                    waits=tuple(ew), incs=s_pe.inc())
                eb_done = s_pe.now()
                ccw = [eb_done]
                if SR.get(f"YH{slot}") is not None:
                    ccw.append(SR[f"YH{slot}"])
                em.op("vector", lambda v, h=h, cc=cc, slot=slot: v.tensor_mul(
                    cc2[slot][:], chat[:, cc * 128:(cc + 1) * 128],
                    pq[1][0:64, 256 + slot * 128:384 + slot * 128]),
                    waits=tuple(ccw), incs=s_dve.inc())
                SR[f"EB{slot}"] = cc2_done = s_dve.now()
                em.op("tensor", lambda t, h=h, tb=tb: t.matmul(
                    pY[:, h * 128:(h + 1) * 128], xtok[tb][:, h * 128:(h + 1) * 128],
                    g2all[:, h * 128:(h + 1) * 128], start=True, stop=False),
                    waits=tuple(yw) if h == 0 else ())
                em.op("tensor", lambda t, h=h, slot=slot: t.matmul(
                    pY[:, h * 128:(h + 1) * 128], sbf[:, h * 128:(h + 1) * 128],
                    cc2[slot][:], start=False, stop=False), waits=cc2_done)
                em.op("tensor", lambda t, h=h, tb=tb: t.matmul(
                    pY[:, h * 128:(h + 1) * 128], xtok[tb][:, h * 128:(h + 1) * 128],
                    dial[:, h * 128:(h + 1) * 128], start=False, stop=True),
                    incs=s_pe.inc())
                SR[f"YH{slot}"] = s_pe.now()
                t_and_update(h, tb, (cum_done, ld_done) if h == 0 else None)
            SR["Supd"] = SR["T1"]
            y_mm_done = s_pe.now()
            xl_prev[tb] = y_mm_done
            em.op("scalar", lambda s: s.activation(y2bf[:], pY[:], AF.Square),
                  waits=y_mm_done, incs=s_act.inc())
            y2_done = s_act.now()
            mw_ = [y2_done, ln.init_done]
            if SR.get("MU") is not None:
                mw_.append(SR["MU"])
            for k in range(16):
                em.op("tensor", lambda t, k=k: t.matmul(
                    pq[0][0:1, 320:448], ln.ones128[:], y2bf[:, k * 128:(k + 1) * 128],
                    start=(k == 0), stop=(k == 15)),
                    waits=tuple(mw_) if k == 0 else (), incs=s_pe.inc() if k == 15 else ())
            mu_done = s_pe.now()
            em.op("vector", lambda v: v.tensor_scalar_mul(rstdr[:, 0, :], pq[0][0:1, 320:448], 1.0 / 2048),
                  waits=mu_done, incs=s_dve.inc())
            SR["MU"] = s_dve.now()
            em.op("vector", lambda v: v.tensor_scalar_add(rstdr[:, 1, :], rstdr[:, 0, :], 1e-5),
                  incs=s_dve.inc())
            em.op("scalar", lambda s: s.activation(rstdr[:, 2, :], rstdr[:, 1, :], AF.Ln),
                  waits=s_dve.now(), incs=s_act.inc())
            em.op("scalar", lambda s: s.activation(rstdr[:, 1, :], rstdr[:, 2, :], AF.Exp, scale=-0.5),
                  incs=s_act.inc())
            rstd_done = s_act.now()
            rw_ = [rstd_done, g2a_done]
            for spec in (SR.get("RBC"), SR.get("G2")):
                if spec is not None:
                    rw_.append(spec)
            em.op("tensor", lambda t: t.matmul(pq[0][:, 0:128], onesrow[:], rstdr[:, 1, :],
                                               start=True, stop=True),
                  waits=tuple(rw_), incs=s_pe.inc())
            em.op("scalar", lambda s: s.copy(rbc[:], pq[0][:, 0:128]),
                  waits=s_pe.now(), incs=s_act.inc())
            SR["RBC"] = rbc_done = s_act.now()
            em.op("vector", lambda v, cc=cc: v.tensor_tensor(
                szp[:], sigz[:, :, cc * 128:(cc + 1) * 128],
                rbc[:].unsqueeze(1).broadcast_to([128, 16, 128]), OP.mult),
                waits=(rbc_done,), incs=s_dve.inc())
            ygw = []
            if cc == 0 and ln.prev_sums is not None:
                ygw.append(ln.prev_sums)
            em.op("vector", lambda v, cc=cc: v.tensor_tensor(
                ygat[:, :, cc * 128:(cc + 1) * 128], szp[:],
                pY[:].rearrange("p (a t) -> p a t", a=16), OP.mult),
                waits=tuple(ygw), incs=s_dve.inc())
            SR["Y"] = s_dve.now()
            gchunk += 1
        yg_done = s_dve.now()
        em.bank_rel[id(pq[0])] = s_dve.now()
        em.bank_rel[id(pq[1])] = s_dve.now()

        for ot in range(8):
            oww = []
            if ow_prev[ot % 2] is not None:
                oww.append(ow_prev[ot % 2])
            oww.append(owbf_ready)
            em.op("sync", lambda sy, ot=ot: sy.dma_start(
                out=owsb[ot % 2][:], in_=cfg["ow_bf"][ot]),
                waits=tuple(oww), incs=s_w.inc(16))
            ow_ld = s_w.now()
            pb = pq[ot % 2]
            for k in range(16):
                wts = []
                if k == 0:
                    wts = [ow_ld, yg_done]
                    if em.bank_rel.get(id(pb)) is not None:
                        wts.append(em.bank_rel[id(pb)])
                em.op("tensor", lambda t, ot=ot, k=k, pb=pb: t.matmul(
                    pb[:, 0:512], owsb[ot % 2][:, k, :], ygat[:, k, :],
                    start=(k == 0), stop=(k == 15)),
                    waits=tuple(wts), incs=s_pe.inc() if k == 15 else ())
            mm_done = s_pe.now()
            ow_prev[ot % 2] = mm_done
            em.op("vector", lambda v, ot=ot, pb=pb, usb=usb: v.tensor_add(
                sigz[:, ot, :], usb[:, ot, :], pb[:, 0:512]),
                waits=(mm_done, usb_done) if ot == 0 else mm_done, incs=s_dve.inc())
            em.bank_rel[id(pb)] = s_dve.now()
        v_done = s_dve.now()
        SR["Y"] = s_pe.now()
        em.bank_rel[id(pD[0])] = SR.get("D0")
        em.bank_rel[id(pD[1])] = SR.get("D1")
        apply_done = ln.emit(sigz[:, 0:8, :], usb[:], v_ready=(v_done,))[0]
        prev_apply = apply_done
        prev_ln_casts = ln.last_casts
        em.op("sync", lambda sy, sc=sc, usb=usb: sy.dma_start(
            out=cfg["x1"].rearrange("(k p) t -> p k t", p=128)[:, :, sc * 512:(sc + 1) * 512],
            in_=usb[:]), waits=apply_done, incs=s_st.inc(16))
        usb_guards[0] = usb_guards[1] = [s_st.now()]
        if bcast is not None and sc < 4:
            em.op("vector", lambda v, usb=usb: v.tensor_scalar_mul(
                sigz[:, 0:8, :].rearrange("p a t -> p (a t)"),
                usb[:].rearrange("p a t -> p (a t)"), bcast["mgate_sb"][:]),
                waits=apply_done, incs=s_dve.inc())
            em.op("scalar", lambda s, sc=sc: s.dma_start(
                out=bcast["ar1_in"].rearrange("(k p) t -> p k t", p=128)[:, :, sc * 512:(sc + 1) * 512],
                in_=sigz[:, 0:8, :]), waits=s_dve.now(), incs=s_bc.inc(16))
            usb_guards[0] = usb_guards[1] = usb_guards[0] + [s_bc.now()]
            prev_apply = s_bc.now()
        SR["bfeat"] = s_pe.now()
        SR["dtld"] = SR["cum"]
        SR["e1"] = s_pe.now()
        SR["chat"] = s_dve.now()
    em.op("sync", lambda sy: sy.nop(), waits=s_st.now())
    em.bcast_done = s_bcc.now() if bcast is not None else None


def host_mamba_inputs(u_slice_fm_with_halo, in_proj_w, conv_w, conv_b, dt_bias,
                      A_log, D, rms_w, out_w, core_id, consts):
    W = np.asarray(in_proj_w, np.float32)
    wT = W.T
    wxbd = np.ascontiguousarray(np.concatenate(
        [wT[:, 2048:4096], wT[:, 4096:4160], wT[:, 4224:4240]], axis=1))
    wzc = np.ascontiguousarray(np.concatenate(
        [wT[:, 0:2048], wT[:, 4160:4224]], axis=1))
    cwf = np.asarray(conv_w, np.float32)
    cbf = np.asarray(conv_b, np.float32)
    cw18 = np.zeros((128, 18, 4), np.float32)
    cb18 = np.zeros((128, 18), np.float32)
    cw18[:, 0:16, :] = cwf[0:2048].reshape(16, 128, 4).transpose(1, 0, 2)
    cb18[:, 0:16] = cbf[0:2048].reshape(16, 128).T
    cw18[0:64, 16, :] = cwf[2048:2112]
    cb18[0:64, 16] = cbf[2048:2112]
    cw18[0:64, 17, :] = cwf[2112:2176]
    cb18[0:64, 17] = cbf[2112:2176]
    slot = core_id % 4
    m2 = np.zeros((8, 8), np.float32)
    sel = np.zeros((8, 1), np.float32)
    base = (core_id // 4) * 4
    for j in range(base, base + slot):
        sel[j, 0] = 1.0
        for i in range(j + 1, base + slot):
            m2[j, i] = 1.0
    return {
        "u": np.ascontiguousarray(u_slice_fm_with_halo, np.float32),
        "wxbd": wxbd, "wzc": wzc,
        "owT": host_wT(out_w),
        "cwcol": cw18, "cbcol": cb18,
        "dtbcol": np.asarray(dt_bias, np.float32).reshape(16, 1),
        "alog": np.asarray(A_log, np.float32).reshape(1, 16),
        "dvec": np.asarray(D, np.float32).reshape(1, 16),
        "rmspk": np.ascontiguousarray(np.asarray(rms_w, np.float32).reshape(16, 128).T),
        "triuinc": consts["triuinc"], "upperneg": consts["upperneg"],
        "ident": consts["ident"],
        "m2T": np.ascontiguousarray(m2.T), "selcol": sel,
    }



RG8 = [[0, 1, 2, 3, 4, 5, 6, 7]]


def build_moe():
    nc = bass.Bass("TRN2", target_bir_lowering=False, debug=False)
    aps = {}
    aps["x1m"] = nc.dram_tensor("x1m", [1024, 2048], F32, kind="ExternalInput").ap()
    for w in ("gwT", "uwT", "dwT"):
        aps[w] = nc.dram_tensor(w, [1024, 1024], F32, kind="ExternalInput").ap()
    for bname in ("gbpk", "ubpk", "dbpk"):
        aps[bname] = nc.dram_tensor(bname, [128, 8], F32, kind="ExternalInput").ap()
    aps["ridx"] = nc.dram_tensor("ridx", [128, 16, 2], I32, kind="ExternalInput").ap()
    aps["rwcol"] = nc.dram_tensor("rwcol", [128, 16], F32, kind="ExternalInput").ap()
    aps["ecol"] = nc.dram_tensor("ecol", [128, 1], F32, kind="ExternalInput").ap()
    aps["moe"] = nc.dram_tensor("moe", [1024, 2048], F32, kind="ExternalOutput").ap()
    aps["ar_in"] = nc.dram_tensor("ar_in", [1024, 2048], F32).ap()
    aps["m_dram"] = nc.dram_tensor("m_dram", [1, 2048], F32).ap()
    aps["ar_out"] = nc.dram_tensor("ar_out", [1024, 2048], F32).ap()
    es = ExitStack()
    with es:
        em = Emit(nc, es)
        emit_moe(em, aps)
        em.finish()
    return nc


def emit_moe(em, cfg):
    x1m = cfg.get("x1m")
    gw = em.sb("b_gw", [128, 8, 1024], BF16)
    uw = em.sb("b_uw", [128, 8, 1024], BF16)
    dw = em.sb("b_dw", [128, 8, 1024], BF16)
    gb = em.sb("b_gb", [128, 8], F32)
    ub = em.sb("b_ub", [128, 8], F32)
    db = em.sb("b_db", [128, 8], F32)
    ridxf = em.sb("b_ridxf", [128, 16, 2], F32)
    ridxi = em.sb("b_ridxi", [128, 16, 2], I32)
    rwcol = em.sb("b_rwcol", [128, 16], F32)
    ecol = em.sb("b_ecol", [128, 1], F32)
    eq = em.sb("b_eq", [128, 16, 2], F32)
    msb = em.sb("b_m", [128, 16], F32)
    mrow = em.sb("b_mrow", [1, 2048], F32)
    Mb = em.sb("b_Mb", [128, 4, 512], F32)
    onesrow = em.sb("b_onesrow", [1, 128], F32)
    x1sb = [em.sb(f"b_x1_{i}", [128, 8, 512], F32) for i in range(2)]
    x1bf = em.sb("b_x1b", [128, 8, 512], BF16)
    g7 = em.sb("b_g7", [128, 8, 512], F32)
    up7 = em.sb("b_up7", [128, 8, 512], F32)
    sg = em.sb("b_sg", [128, 8, 512], F32)
    glu = g7
    inter = em.sb("b_inter", [128, 8, 512], BF16)
    part = up7

    pq = [em.ps(f"b_pq{i}", [128, 512]) for i in range(2)]
    pM = em.ps("b_pM", [128, 512])

    s_in = em.sem("b_in")
    s_w = em.sem("b_w")
    s_pe = em.sem("b_pe")
    s_dve = em.sem("b_dve")
    s_act = em.sem("b_act")
    s_ps = em.sem("b_ps")
    s_cc = em.sem("b_cc")
    s_st = em.sem("b_st")
    s_mr = em.sem("b_mr")

    em._moe_cc = s_cc
    if x1m is None:
        x1m = cfg["ar1_out"]
        em.pending["sync"] = list(em.pending.get("sync", [])) + [cfg["bcast_done"]]

    for w, sb in (("gwT", gw), ("uwT", uw), ("dwT", dw)):
        em.op("gpsimd", lambda g, w=w, sb=sb: g.dma_start(
            out=sb[:], in_=cfg[w].rearrange("(k p) o -> p k o", p=128)), incs=s_w.inc(16))
    for bn, sb in (("gbpk", gb), ("ubpk", ub), ("dbpk", db)):
        em.op("gpsimd", lambda g, bn=bn, sb=sb: g.dma_start(out=sb[:], in_=cfg[bn]), incs=s_w.inc(16))
    em.op("gpsimd", lambda g: g.dma_start(out=ridxi[:], in_=cfg["ridx"]), incs=s_w.inc(16))
    em.op("gpsimd", lambda g: g.dma_start(out=rwcol[:], in_=cfg["rwcol"]), incs=s_w.inc(16))
    em.op("gpsimd", lambda g: g.dma_start(out=ecol[:], in_=cfg["ecol"]), incs=s_w.inc(16))
    wready = s_w.now()

    em.op("vector", lambda v: v.memset(onesrow[:], 1.0), incs=s_dve.inc())
    em.op("vector", lambda v: v.tensor_copy(ridxf[:], ridxi[:]), waits=wready, incs=s_dve.inc())
    em.op("vector", lambda v: v.tensor_single_scalar(eq[:], ridxf[:], ecol[:], OP.is_equal),
          incs=s_dve.inc())
    em.op("vector", lambda v: v.tensor_add(msb[:], eq[:, :, 0], eq[:, :, 1]), incs=s_dve.inc())
    em.op("vector", lambda v: v.tensor_mul(msb[:], msb[:], rwcol[:]), incs=s_dve.inc())
    em.op("sync", lambda sy: sy.dma_start(
        out=cfg["m_dram"].rearrange("q (p a) -> (q p) a", p=128), in_=msb[:]),
        waits=s_dve.now(), incs=s_mr.inc(16))
    em.op("sync", lambda sy: sy.dma_start(out=mrow[:], in_=cfg["m_dram"]),
          waits=s_mr.now(), incs=s_mr.inc(16))
    for q in range(4):
        em.op("tensor", lambda t, q=q: t.matmul(pM[:], onesrow[:], mrow[:, q * 512:(q + 1) * 512],
                                                start=True, stop=True),
              waits=(s_mr.now(), s_dve.now()) if q == 0 else (s_act.now(),), incs=s_pe.inc())
        em.op("scalar", lambda s, q=q: s.copy(Mb[:, q, :], pM[:]), waits=s_pe.now(), incs=s_act.inc())
    m_done = s_act.now()

    freed = [None, None]
    prev_down = None
    for sc in range(4):
        b = sc % 2
        t0 = sc * 512
        w = [wready]
        if freed[b] is not None:
            w.append(freed[b])
        em.op("sync", lambda sy, t0=t0, b=b: sy.dma_start(
            out=x1sb[b][:], in_=x1m.rearrange("(k p) t -> p k t", p=128)[:, :, t0:t0 + 512]),
            waits=tuple(w), incs=s_in.inc(16))
        cw = [s_in.now()]
        if prev_down is not None:
            cw.append(prev_down)
        em.op("scalar", lambda s, b=b: s.copy(
            x1bf[:].rearrange("p a t -> p (a t)"), x1sb[b][:].rearrange("p a t -> p (a t)")),
            waits=tuple(cw), incs=s_act.inc())
        cast_done = s_act.now()

        def ep_gate(ot, pb, pe_spec):
            em.op("vector", lambda v, ot=ot, pb=pb: v.tensor_scalar(
                g7[:, ot, :], pb[:], gb[:, ot:ot + 1], 7.0, OP.add, OP.min),
                waits=pe_spec, incs=s_dve.inc())
            return s_dve.now()
        emit_proj(em, pq, gw, 8, x1bf[:], 8, ep_gate, s_pe,
                  first_waits=(cast_done, m_done))

        def ep_up(ot, pb, pe_spec):
            uw_ = [pe_spec]
            if ot == 0 and sc > 0:
                uw_.append(s_ps.now())
            em.op("vector", lambda v, ot=ot, pb=pb: v.tensor_scalar(
                up7[:, ot, :], pb[:], ub[:, ot:ot + 1], 7.0, OP.add, OP.min),
                waits=tuple(uw_), incs=s_dve.inc())
            em.op("vector", lambda v, ot=ot: v.tensor_scalar(
                up7[:, ot, :], up7[:, ot, :], -7.0, None, OP.max),
                incs=s_dve.inc())
            return s_dve.now()
        emit_proj(em, pq, uw, 8, x1bf[:], 8, ep_up, s_pe)
        gu_done = s_dve.now()

        em.op("scalar", lambda s: s.activation(
            sg[:].rearrange("p a t -> p (a t)"), g7[:].rearrange("p a t -> p (a t)"),
            AF.Sigmoid, scale=1.702), waits=gu_done, incs=s_act.inc())
        em.op("vector", lambda v: v.tensor_mul(
            glu[:].rearrange("p a t -> p (a t)"), g7[:].rearrange("p a t -> p (a t)"),
            sg[:].rearrange("p a t -> p (a t)")), waits=s_act.now(), incs=s_dve.inc())
        em.op("vector", lambda v: v.scalar_tensor_tensor(
            inter[:].rearrange("p a t -> p (a t)"), up7[:].rearrange("p a t -> p (a t)"),
            1.0, glu[:].rearrange("p a t -> p (a t)"), OP.add, OP.mult),
            incs=s_dve.inc())
        inter_done = s_dve.now()

        def ep_down(ot, pb, pe_spec):
            em.op("vector", lambda v, ot=ot, pb=pb, sc=sc: v.scalar_tensor_tensor(
                part[:, ot, :], pb[:], db[:, ot:ot + 1], Mb[:, sc, :], OP.add, OP.mult),
                waits=pe_spec, incs=s_dve.inc())
            return s_dve.now()
        emit_proj(em, pq, dw, 8, inter[:], 8, ep_down, s_pe,
                  first_waits=(inter_done,))
        down_done = prev_down = s_dve.now()
        freed[b] = down_done
        em.op("scalar", lambda s, t0=t0: s.dma_start(
            out=cfg["ar_in"].rearrange("(k p) t -> p k t", p=128)[:, :, t0:t0 + 512],
            in_=part[:]), waits=down_done, incs=s_ps.inc(16))

    em.op("gpsimd", lambda g: g.collective_compute(
        "AllReduce", OP.add, replica_groups=RG8,
        ins=[cfg["ar_in"].opt()], outs=[cfg["ar_out"].opt()]),
        waits=s_ps.now(), incs=s_cc.inc(1))
    if "moe" in cfg:
        em.op("sync", lambda sy: sy.dma_start(out=cfg["moe"], in_=cfg["ar_out"]),
              waits=s_cc.now(), incs=s_st.inc(16))
        em.op("sync", lambda sy: sy.nop(), waits=s_st.now())
    return s_cc.now()


def host_moe_inputs(x1m_fm, expert_id, gate_up_proj, gate_up_bias, down_proj,
                    down_bias, router_indices, routing_weights):
    e = expert_id
    gup = np.asarray(gate_up_proj[e], np.float32)
    gW = gup[:, 0::2]
    uW = gup[:, 1::2]
    gub = np.asarray(gate_up_bias[e], np.float32)
    gb_ = gub[0::2]
    ub_ = gub[1::2]
    dW = np.asarray(down_proj[e], np.float32)
    db_ = np.asarray(down_bias[e], np.float32)
    r = {
        "gwT": np.ascontiguousarray(gW),
        "uwT": np.ascontiguousarray(uW),
        "dwT": np.ascontiguousarray(dW),
        "gbpk": np.ascontiguousarray(gb_.reshape(8, 128).T),
        "ubpk": np.ascontiguousarray(ub_.reshape(8, 128).T),
        "dbpk": np.ascontiguousarray(db_.reshape(8, 128).T),
        "ridx": np.ascontiguousarray(np.asarray(router_indices, np.int32).reshape(128, 16, 2)),
        "rwcol": np.ascontiguousarray(np.asarray(routing_weights, np.float32)[:, e].reshape(128, 16)),
        "ecol": np.full((128, 1), float(e), np.float32),
    }
    if x1m_fm is not None:
        r["x1m"] = np.ascontiguousarray(x1m_fm, np.float32)
    return r

def emit_wc_prep(em, cfg):
    wtmp = em.sb("w_wtmp", [128, 1024], F32)
    wcsb = em.sb("w_wc", [128, 8, 1024], BF16)
    t1b = em.sb("w_t1b", [4, 1024], BF16)
    t2b = em.sb("w_t2b", [50, 1024], BF16)
    ldTb = em.sb("w_ldTb", [100, 4], BF16)
    luTb = em.sb("w_luTb", [4, 50], BF16)
    uaTb = em.sb("w_uaTb", [50, 1024], BF16)
    dab = em.sb("w_dab", [100, 1024], BF16)
    pL = em.ps("w_pL", [128, 1024])
    s_w = em.sem("w_w")
    s_pe = em.sem("w_pe")
    s_act = em.sem("w_act")
    s_dve = em.sem("w_dve")
    s_ld = em.sem("w_ld")
    s_st = em.sem("w_st")
    for n, sb in (("ldT", ldTb), ("luT", luTb), ("uaT", uaTb), ("da", dab)):
        em.op("gpsimd", lambda g, n=n, sb=sb: g.dma_start(out=sb[:], in_=cfg[n]),
              incs=s_w.inc(16))
    wready = s_w.now()
    for h in range(2):
        em.op("tensor", lambda t, h=h: t.matmul(
            pL[0:4, h * 512:(h + 1) * 512], ldTb[:], dab[:, h * 512:(h + 1) * 512],
            start=True, stop=True), waits=wready, incs=s_pe.inc())
    em.op("scalar", lambda s: s.copy(t1b[:], pL[0:4, :]), waits=s_pe.now(), incs=s_act.inc())
    for h in range(2):
        em.op("tensor", lambda t, h=h: t.matmul(
            pL[0:50, h * 512:(h + 1) * 512], luTb[:], t1b[:, h * 512:(h + 1) * 512],
            start=True, stop=True), waits=s_act.now(), incs=s_pe.inc())
    em.op("scalar", lambda s: s.copy(t2b[:], pL[0:50, :]), waits=s_pe.now(), incs=s_act.inc())
    t2_done = s_act.now()
    for mm in range(8):
        lw = [t2_done] if mm == 0 else [s_dve.now()]
        em.op("sync", lambda sy, mm=mm: sy.dma_start(
            out=wtmp[:], in_=cfg["wT"].rearrange("(k p) o -> p k o", p=128)[:, mm, :]),
            waits=tuple(lw) if mm > 0 else (), incs=s_ld.inc(16))
        ld_here = s_ld.now()
        for h in range(2):
            em.op("tensor", lambda t, mm=mm, h=h: t.matmul(
                pL[:, h * 512:(h + 1) * 512], t2b[:, mm * 128:(mm + 1) * 128],
                uaTb[:, h * 512:(h + 1) * 512], start=True, stop=True),
                waits=tuple(lw) if h == 0 else (), incs=s_pe.inc() if h == 1 else ())
        em.op("vector", lambda v, mm=mm: v.tensor_add(wcsb[:, mm, :], wtmp[:], pL[:, :]),
              waits=(s_pe.now(), ld_here), incs=s_dve.inc())
    em.op("scalar", lambda s: s.dma_start(out=cfg["wc_dram"], in_=wcsb[:]),
          waits=s_dve.now(), incs=s_st.inc(16))
    return s_st.now()





def build_l1attn(T=4096, with_madd=False):
    nc = bass.Bass("TRN2", target_bir_lowering=False, debug=False)
    aps = {}
    aps["x2"] = nc.dram_tensor("x2", [1024, T], F32, kind="ExternalInput").ap()
    if with_madd:
        aps["madd"] = nc.dram_tensor("madd", [1024, T], F32, kind="ExternalInput").ap()
    for w in ("wqT", "wkT", "wvT", "wgT", "woT"):
        aps[w] = nc.dram_tensor(w, [1024, 1024], F32, kind="ExternalInput").ap()
    aps["ld0"] = nc.dram_tensor("ld0", [1, 1], F32, kind="ExternalInput").ap()
    aps["diffblk"] = nc.dram_tensor("diffblk", [128, 128], F32, kind="ExternalInput").ap()
    aps["offneg"] = nc.dram_tensor("offneg", [128, 128], F32, kind="ExternalInput").ap()
    aps["out"] = nc.dram_tensor("out", [1024, T], F32, kind="ExternalOutput").ap()
    aps["T"] = T
    es = ExitStack()
    with es:
        em = Emit(nc, es)
        emit_l1attn(em, aps)
        em.finish()
    return nc


def emit_l1attn(em, cfg):
    T = cfg["T"]
    NSC = T // 512
    x2, out = cfg["x2"], cfg["out"]

    wq = em.sb("c_wq", [128, 8, 1024], BF16)
    wk = em.sb("c_wk", [128, 8, 1024], BF16)
    wv = em.sb("c_wv", [128, 8, 1024], BF16)
    wg = em.sb("c_wg", [128, 8, 1024], BF16)
    wo = em.sb("c_wo", [128, 8, 1024], BF16)
    diffblk = em.sb("c_diffblk", [128, 128], F32)
    offneg = em.sb("c_offneg", [128, 128], F32)
    ld0t = em.sb("c_ld0", [1, 1], F32)
    mst = em.sb("c_mst", [1, 2], F32)
    lndc = em.sb("c_lndc", [128, 1], F32)
    m32 = em.sb("c_m32", [128, 128], F32)

    x2sb = [em.sb(f"c_x2_{i}", [128, 8, 512], F32) for i in range(2)]
    x2bf = em.sb("c_x2b", [128, 8, 512], BF16)
    qbf = em.sb("c_qbf", [128, 8, 512], BF16)
    kbf = em.sb("c_kbf", [128, 8, 512], BF16)
    vbf_ = em.sb("c_vbf", [128, 8, 512], BF16)
    gate = em.sb("c_gate", [128, 8, 512], F32)
    vtok = em.sb("c_vtok", [128, 2, 1024], BF16)
    s2b = [em.sb(f"c_s2b_{i}", [128, 128], BF16) for i in range(2)]
    gob = x2bf
    vsb = em.sb("c_v", [128, 8, 512], F32)
    dcfg = cfg.get("d")
    if dcfg is not None:
        wc = em.sb("c_wcd", [128, 8, 1024], BF16)
        bcol = em.sb("c_bcol", [128, 8], F32)

    pq = [em.ps(f"c_pq{i}", [128, 512]) for i in range(2)]
    pO = em.ps("c_pO", [128, 1024])
    if dcfg is not None:
        ln = LNKit(em, "c_ln", bufs=(x2bf[:], kbf[:]))
    else:
        ln = cfg.get("ln") or LNKit(em, "c_ln")

    s_in = em.sem("c_in")
    s_w = em.sem("c_w")
    s_pe = em.sem("c_pe")
    s_dve = em.sem("c_dve")
    s_act = em.sem("c_act")
    s_tr = em.sem("c_tr")
    s_st = em.sem("c_st")

    if "w_bf" in cfg:
        for qn, w, sb in (("sync", "wqT", wq), ("scalar", "wkT", wk), ("sync", "wvT", wv),
                          ("scalar", "wgT", wg), ("sync", "woT", wo)):
            em.op(qn, lambda e, w=w, sb=sb: e.dma_start(
                out=sb[:], in_=cfg["w_bf"][w].rearrange("(k p) o -> p k o", p=128)),
                waits=(cfg["w_bf_done"],), incs=s_w.inc(16))
    else:
        for w, sb in (("wqT", wq), ("wkT", wk), ("wvT", wv), ("wgT", wg), ("woT", wo)):
            em.op("gpsimd", lambda g, w=w, sb=sb: g.dma_start(
                out=sb[:], in_=cfg[w].rearrange("(k p) o -> p k o", p=128)), incs=s_w.inc(16))
    em.op("scalar", lambda e: e.dma_start(out=diffblk[:], in_=cfg["diffblk"]), incs=s_w.inc(16))
    em.op("scalar", lambda e: e.dma_start(out=offneg[:], in_=cfg["offneg"]), incs=s_w.inc(16))
    em.op("scalar", lambda e: e.dma_start(out=ld0t[:], in_=cfg["ld0"]), incs=s_w.inc(16))
    if dcfg is not None:
        em.op("sync", lambda sy: sy.dma_start(out=wc[:], in_=dcfg["wc_dram"]),
              waits=(dcfg["wc_done"],), incs=s_w.inc(16))
        em.op("sync", lambda sy: sy.dma_start(out=bcol[:], in_=dcfg["bpk"]), incs=s_w.inc(16))
    wready = s_w.now()

    em.op("scalar", lambda s: s.activation(mst[:, 0:1], ld0t[:], AF.Sigmoid),
          waits=wready, incs=s_act.inc())
    em.op("scalar", lambda s: s.activation(mst[:, 1:2], mst[:, 0:1], AF.Ln),
          incs=s_act.inc())
    em.op("tensor", lambda t: t.matmul(pq[0][:, 0:1], ln.onesrow[:], mst[:, 1:2], start=True, stop=True),
          waits=(s_act.now(), ln.init_done), incs=s_pe.inc())
    em.op("scalar", lambda s: s.copy(lndc[:], pq[0][:, 0:1]), waits=s_pe.now(), incs=s_act.inc())
    em.bank_rel[id(pq[0])] = s_act.now()
    em.op("vector", lambda v: v.tensor_scalar_mul(m32[:], diffblk[:], lndc[:]),
          waits=s_act.now(), incs=s_dve.inc())
    em.op("vector", lambda v: v.tensor_add(m32[:], m32[:], offneg[:]), incs=s_dve.inc())
    em.op("scalar", lambda s: s.activation(m32[:], m32[:], AF.Exp), waits=s_dve.now(),
          incs=s_act.inc())
    mask_done = s_act.now()

    freed = [None, None]
    osb_freed = [None, None]
    prev_wo = None
    prev_ln_casts = None
    prev_vsb_free = None
    sc_order = list(range(NSC))
    if "moesum" in cfg:
        sc_order = [s_ for s_ in range(NSC) if s_ >= 4] + [s_ for s_ in range(NSC) if s_ < 4]
    for idx_, sc in enumerate(sc_order):
        b = idx_ % 2
        t0 = sc * 512
        w = [wready]
        if freed[b] is not None:
            w.append(freed[b])
        em.op("sync", lambda sy, t0=t0, b=b: sy.dma_start(
            out=x2sb[b][:], in_=x2.rearrange("(k p) t -> p k t", p=128)[:, :, t0:t0 + 512]),
            waits=tuple(w), incs=s_in.inc(16))
        if "madd" in cfg:
            mw = [s_in.now()]
            if prev_vsb_free is not None:
                mw.append(prev_vsb_free)
            em.op("sync", lambda sy, t0=t0: sy.dma_start(
                out=vsb[:], in_=cfg["madd"].rearrange("(k p) t -> p k t", p=128)[:, :, t0:t0 + 512]),
                waits=tuple(mw), incs=s_in.inc(16))
            em.op("vector", lambda v, b=b: v.tensor_add(
                vsb[:].rearrange("p a t -> p (a t)"), vsb[:].rearrange("p a t -> p (a t)"),
                x2sb[b][:].rearrange("p a t -> p (a t)")),
                waits=s_in.now(), incs=s_dve.inc())
            ln.emit(vsb[:], x2sb[b][:], v_ready=(s_dve.now(),))
        elif "moesum" in cfg:
            if sc < 4:
                mw = [s_in.now(), cfg["moe_done"]]
                if prev_vsb_free is not None:
                    mw.append(prev_vsb_free)
                em.op("sync", lambda sy, t0_=sc * 512: sy.dma_start(
                    out=vsb[:], in_=cfg["moesum"].rearrange("(k p) t -> p k t", p=128)[:, :, t0_:t0_ + 512]),
                    waits=tuple(mw), incs=s_in.inc(16))
                em.op("vector", lambda v, b=b: v.scalar_tensor_tensor(
                    vsb[:].rearrange("p a t -> p (a t)"), vsb[:].rearrange("p a t -> p (a t)"),
                    cfg["mgate_sb"][:], x2sb[b][:].rearrange("p a t -> p (a t)"),
                    OP.mult, OP.add), waits=s_in.now(), incs=s_dve.inc())
                ln.emit(vsb[:], x2sb[b][:], v_ready=(s_dve.now(),))
            else:
                ln.emit(x2sb[b][:], x2sb[b][:], v_ready=(s_in.now(),))
        cw = [s_in.now()]
        if prev_wo is not None:
            cw.append(prev_wo)
        if "madd" in cfg or "moesum" in cfg:
            cw.append(ln.s_dve.now())
        em.op("scalar", lambda s, b=b: s.copy(
            x2bf[:].rearrange("p a t -> p (a t)"), x2sb[b][:].rearrange("p a t -> p (a t)")),
            waits=tuple(cw), incs=s_act.inc())
        cast_done = s_act.now()

        def mk_ep(dst, act_fn):
            def ep(ot, pb, pe_spec):
                em.op("scalar", lambda s, ot=ot, pb=pb: act_fn(s, dst, ot, pb),
                      waits=pe_spec, incs=s_act.inc())
                return s_act.now()
            return ep
        emit_proj(em, pq, wq, 8, x2bf[:], 8,
                  mk_ep(qbf, lambda s, d, ot, pb: s.copy(d[:, ot, :], pb[:])),
                  s_pe, first_waits=(cast_done, mask_done))
        emit_proj(em, pq, wk, 8, x2bf[:], 8,
                  mk_ep(kbf, lambda s, d, ot, pb: s.copy(d[:, ot, :], pb[:])), s_pe)
        emit_proj(em, pq, wv, 8, x2bf[:], 8,
                  mk_ep(vbf_, lambda s, d, ot, pb: s.copy(d[:, ot, :], pb[:])), s_pe)
        emit_proj(em, pq, wg, 8, x2bf[:], 8,
                  mk_ep(gate, lambda s, d, ot, pb: s.activation(d[:, ot, :], pb[:], AF.Sigmoid)), s_pe)
        qkvg_done = s_act.now()

        prev_gate_tt = None
        gate_tts = [None, None]
        for c in range(4):
            cs = slice(c * 128, (c + 1) * 128)
            sp = pq[c % 2]
            trw = [qkvg_done]
            if gate_tts[c % 2] is not None:
                trw.append(gate_tts[c % 2])
            for ft in range(8):
                em.op("sync", lambda sy, c=c, ft=ft: sy.dma_start_transpose(
                    vtok[:, c % 2, ft * 128:(ft + 1) * 128], vbf_[:, ft, c * 128:(c + 1) * 128]),
                    waits=tuple(trw) if ft == 0 else (), incs=s_tr.inc(16))
            tr_done = s_tr.now()
            for k in range(8):
                wts0 = [qkvg_done]
                if em.bank_rel.get(id(sp)) is not None:
                    wts0.append(em.bank_rel[id(sp)])
                em.op("tensor", lambda t, k=k, c=c, sp=sp, cs=cs, b=b: t.matmul(
                    sp[:, 0:128], kbf[:, k, cs], qbf[:, k, cs],
                    start=(k == 0), stop=(k == 7)),
                    waits=tuple(wts0) if k == 0 else (),
                    incs=s_pe.inc() if k == 7 else ())
            em.op("vector", lambda v, c=c, sp=sp: v.tensor_tensor(
                s2b[c % 2][:], sp[:, 0:128], m32[:], OP.mult),
                waits=s_pe.now(), incs=s_dve.inc())
            s2_done = s_dve.now()
            em.bank_rel[id(sp)] = s2_done
            for ft in range(8):
                wts = []
                if ft == 0:
                    wts = [s2_done, tr_done]
                    if prev_gate_tt is not None:
                        wts.append(prev_gate_tt)
                em.op("tensor", lambda t, c=c, ft=ft: t.matmul(
                    pO[:, ft * 128:(ft + 1) * 128], vtok[:, c % 2, ft * 128:(ft + 1) * 128],
                    s2b[c % 2][:], start=True, stop=True),
                    waits=tuple(wts), incs=s_pe.inc() if ft == 7 else ())
            em.op("vector", lambda v, c=c, cs=cs: v.tensor_tensor(
                gob[:, :, cs], gate[:, :, cs],
                pO[:].rearrange("p (a t) -> p a t", a=8), OP.mult),
                waits=s_pe.now(), incs=s_dve.inc())
            prev_gate_tt = gate_tts[c % 2] = s_dve.now()

        pln = prev_ln_casts
        pvf = prev_vsb_free
        def ep_wo(ot, pb, pe_spec):
            wts = [pe_spec]
            if pln is not None and ot == 0:
                wts.append(pln)
            if pvf is not None and ot == 0:
                wts.append(pvf)
            if ("madd" in cfg or "moesum" in cfg) and ot == 0:
                wts.append(ln.s_dve.now())
            em.op("vector", lambda v, ot=ot, pb=pb, b=b: v.scalar_tensor_tensor(
                vsb[:, ot, :], x2sb[b][:, ot, :], 2.0, pb[:], OP.mult, OP.add),
                waits=tuple(wts), incs=s_dve.inc())
            return s_dve.now()
        emit_proj(em, pq, wo, 8, gob[:], 8, ep_wo, s_pe,
                  first_waits=(prev_gate_tt,))
        v_done = s_dve.now()
        prev_wo = s_pe.now()

        if dcfg is None:
            ow = [] if osb_freed[b] is None else [osb_freed[b]]
            apply_done = ln.emit(vsb[:], x2sb[b][:], v_ready=(v_done,), out_waits=tuple(ow))[0]
            prev_ln_casts = ln.last_casts
            prev_vsb_free = apply_done
            freed[b] = apply_done
            em.op("sync", lambda sy, t0=t0, b=b: sy.dma_start(
                out=out.rearrange("(k p) t -> p k t", p=128)[:, :, t0:t0 + 512], in_=x2sb[b][:]),
                waits=apply_done, incs=s_st.inc(16))
            osb_freed[b] = s_st.now()
        else:
            apply_done = ln.emit(vsb[:], x2sb[b][:], v_ready=(v_done,))[0]
            prev_ln_casts = ln.last_casts
            x3bf = qbf
            em.op("scalar", lambda s, b=b: s.copy(
                x3bf[:].rearrange("p a t -> p (a t)"), x2sb[b][:].rearrange("p a t -> p (a t)")),
                waits=apply_done, incs=s_act.inc())
            cast_d = s_act.now()
            def ep_d(ot, pb, pe_spec):
                em.op("vector", lambda v, ot=ot, pb=pb, b=b: v.scalar_tensor_tensor(
                    gate[:, ot, :], pb[:], bcol[:, ot:ot + 1], x2sb[b][:, ot, :],
                    OP.add, OP.add), waits=pe_spec, incs=s_dve.inc())
                return s_dve.now()
            emit_proj(em, pq, wc, 8, x3bf[:], 8, ep_d, s_pe, first_waits=(cast_d,))
            v4_done = s_dve.now()
            freed[b] = v4_done
            ow = [] if osb_freed[0] is None else [osb_freed[0]]
            dln = ln.emit(gate[:], vsb[:], v_ready=(v4_done,), out_waits=tuple(ow))
            prev_ln_casts = ln.last_casts
            d_gate_guard = dln[0]
            em.op("sync", lambda sy, t0=t0: sy.dma_start(
                out=out.rearrange("(k p) t -> p k t", p=128)[:, :, t0:t0 + 512], in_=vsb[:]),
                waits=dln[0], incs=s_st.inc(16))
            osb_freed[0] = prev_vsb_free = s_st.now()
    em.op("sync", lambda sy: sy.nop(), waits=s_st.now())


def host_l1attn_inputs(x2_fm, W_q, W_k, W_v, W_g, W_o, log_decay, consts):
    return {
        "x2": np.ascontiguousarray(x2_fm, np.float32),
        "wqT": host_wT(W_q), "wkT": host_wT(W_k), "wvT": host_wT(W_v),
        "wgT": host_wT(W_g), "woT": host_wT(W_o),
        "ld0": np.asarray(log_decay[0], np.float32).reshape(1, 1),
        "diffblk": consts["diffblk"], "offneg": consts["offneg"],
    }




def build_l1ffn(T=4096):
    nc = bass.Bass("TRN2", target_bir_lowering=False, debug=False)
    x3 = nc.dram_tensor("x3", [1024, T], F32, kind="ExternalInput").ap()
    wT = nc.dram_tensor("wT", [1024, 1024], F32, kind="ExternalInput").ap()
    bpk = nc.dram_tensor("bpk", [128, 8], F32, kind="ExternalInput").ap()
    ldT = nc.dram_tensor("ldT", [100, 4], F32, kind="ExternalInput").ap()
    luT = nc.dram_tensor("luT", [4, 50], F32, kind="ExternalInput").ap()
    uaT = nc.dram_tensor("uaT", [50, 1024], F32, kind="ExternalInput").ap()
    da = nc.dram_tensor("da", [100, 1024], F32, kind="ExternalInput").ap()
    out = nc.dram_tensor("out", [1024, T], F32, kind="ExternalOutput").ap()
    es = ExitStack()
    with es:
        em = Emit(nc, es)
        emit_l1ffn(em, dict(x3=x3, wT=wT, bpk=bpk, ldT=ldT, luT=luT, uaT=uaT,
                            da=da, out=out, T=T))
        em.finish()
    return nc


def emit_l1ffn(em, cfg):
    nc = em.nc
    T = cfg["T"]
    NSC = T // 512
    x3, wT, bpk, out = cfg["x3"], cfg["wT"], cfg["bpk"], cfg["out"]

    wTf = em.sb("d_wTf", [128, 8, 1024], F32)
    wc = em.sb("d_wc", [128, 8, 1024], BF16)
    bcol = em.sb("d_bcol", [128, 8], F32)
    ones128 = em.sb("d_ones128", [128, 1], BF16)
    onesrow = em.sb("d_onesrow", [1, 128], F32)
    t1b = em.sb("d_t1b", [4, 1024], BF16)
    t2b = em.sb("d_t2b", [50, 1024], BF16)
    ldTb = em.sb("d_ldTb", [100, 4], BF16)
    luTb = em.sb("d_luTb", [4, 50], BF16)
    uaTb = em.sb("d_uaTb", [50, 1024], BF16)
    dab = em.sb("d_dab", [100, 1024], BF16)
    x3sb = [em.sb(f"d_x3_{i}", [128, 8, 512], F32) for i in range(2)]
    x3bf = [em.sb(f"d_x3b_{i}", [128, 8, 512], BF16) for i in range(2)]
    vsb = [em.sb(f"d_v_{i}", [128, 8, 512], F32) for i in range(2)]
    osb = [em.sb(f"d_o_{i}", [128, 8, 512], F32) for i in range(2)]

    pp = [em.ps(f"d_pp{i}", [128, 512]) for i in range(2)]
    pL = em.ps("d_pL", [128, 1024])
    ln = cfg.get("ln") or LNKit(em, "d_ln")

    s_in = em.sem("d_in")
    s_w = em.sem("d_w")
    s_pe = em.sem("d_pe")
    s_dve = em.sem("d_dve")
    s_act = em.sem("d_act")
    s_st = em.sem("d_st")

    em.op("gpsimd", lambda g: g.dma_start(out=wTf[:], in_=wT.rearrange("(k p) o -> p k o", p=128)), incs=s_w.inc(16))
    em.op("gpsimd", lambda g: g.dma_start(out=ldTb[:], in_=cfg["ldT"]), incs=s_w.inc(16))
    em.op("gpsimd", lambda g: g.dma_start(out=luTb[:], in_=cfg["luT"]), incs=s_w.inc(16))
    em.op("gpsimd", lambda g: g.dma_start(out=uaTb[:], in_=cfg["uaT"]), incs=s_w.inc(16))
    em.op("gpsimd", lambda g: g.dma_start(out=dab[:], in_=cfg["da"]), incs=s_w.inc(16))
    em.op("gpsimd", lambda g: g.dma_start(out=bcol[:], in_=bpk), incs=s_w.inc(16))
    wready = s_w.now()

    for h in range(2):
        em.op("tensor", lambda t, h=h: t.matmul(pL[0:4, h * 512:(h + 1) * 512], ldTb[:], dab[:, h * 512:(h + 1) * 512], start=True, stop=True),
              waits=wready, incs=s_pe.inc())
    em.op("scalar", lambda s: s.copy(t1b[:], pL[0:4, :]), waits=s_pe.now(), incs=s_act.inc())
    for h in range(2):
        em.op("tensor", lambda t, h=h: t.matmul(pL[0:50, h * 512:(h + 1) * 512], luTb[:], t1b[:, h * 512:(h + 1) * 512], start=True, stop=True),
              waits=s_act.now(), incs=s_pe.inc())
    em.op("scalar", lambda s: s.copy(t2b[:], pL[0:50, :]), waits=s_pe.now(), incs=s_act.inc())
    t2_done = s_act.now()
    for m in range(8):
        for h in range(2):
            em.op("tensor", lambda t, m=m, h=h: t.matmul(
                pL[:, h * 512:(h + 1) * 512], t2b[:, m * 128:(m + 1) * 128], uaTb[:, h * 512:(h + 1) * 512], start=True, stop=True),
                waits=(t2_done if m == 0 else s_dve.now()) if h == 0 else (), incs=s_pe.inc() if h == 1 else ())
        em.op("vector", lambda v, m=m: v.tensor_add(wc[:, m, :], wTf[:, m, :], pL[:, :]),
              waits=s_pe.now(), incs=s_dve.inc())

    freed = [None, None]
    osb_freed = [None, None]
    prev_sums = None
    for sc in range(NSC):
        b = sc % 2
        t0 = sc * 512
        w = [wready]
        if freed[b] is not None:
            w.append(freed[b])
        em.op("sync", lambda sy, t0=t0, b=b: sy.dma_start(
            out=x3sb[b][:], in_=x3.rearrange("(k p) t -> p k t", p=128)[:, :, t0:t0 + 512]),
            waits=tuple(w), incs=s_in.inc(16))
        ld_here = s_in.now()
        em.op("scalar", lambda s, b=b: s.copy(
            x3bf[b][:].rearrange("p a t -> p (a t)"), x3sb[b][:].rearrange("p a t -> p (a t)")),
            waits=ld_here, incs=s_act.inc())
        cast_done = s_act.now()
        for ot in range(8):
            pb = pp[ot % 2]
            for k in range(8):
                wts = ()
                if k == 0:
                    wts = (cast_done, s_dve.now())
                em.op("tensor", lambda t, b=b, ot=ot, k=k, pb=pb: t.matmul(
                    pb[:], wc[:, k, ot * 128:(ot + 1) * 128], x3bf[b][:, k, :],
                    start=(k == 0), stop=(k == 7)),
                    waits=wts, incs=s_pe.inc() if k == 7 else ())
            em.op("vector", lambda v, b=b, ot=ot, pb=pb: v.scalar_tensor_tensor(
                vsb[b][:, ot, :], pb[:], bcol[:, ot:ot + 1], x3sb[b][:, ot, :], OP.add, OP.add),
                waits=s_pe.now(), incs=s_dve.inc())
        v_done = s_dve.now()
        ow = [] if osb_freed[b] is None else [osb_freed[b]]
        apply_done = ln.emit(vsb[b][:], osb[b][:], v_ready=(v_done,), out_waits=tuple(ow))[0]
        freed[b] = apply_done
        em.op("sync", lambda sy, t0=t0, b=b: sy.dma_start(
            out=out.rearrange("(k p) t -> p k t", p=128)[:, :, t0:t0 + 512], in_=osb[b][:]),
            waits=apply_done, incs=s_st.inc(16))
        osb_freed[b] = s_st.now()
    em.op("sync", lambda sy: sy.nop(), waits=s_st.now())


def host_l1ffn_inputs(x3_fm, W, b, lora_down, lora_up, down_aux, up_aux):
    return {
        "x3": np.ascontiguousarray(x3_fm, dtype=np.float32),
        "wT": host_wT(W),
        "bpk": np.ascontiguousarray(np.asarray(b, np.float32).reshape(8, 128).T),
        "ldT": host_wT(lora_down),
        "luT": host_wT(lora_up),
        "uaT": host_wT(up_aux),
        "da": np.ascontiguousarray(np.asarray(down_aux, np.float32)),
        "out": None,
    }








def kernel_unfused(**inputs):
    T = 4096
    consts = host_consts()
    u = np.asarray(inputs["L0_attn_u"], np.float32)

    common_a = host_mamba_inputs_common(
        inputs["L0_attn_in_proj_w"], inputs["L0_attn_conv_w"], inputs["L0_attn_conv_b"],
        inputs["L0_attn_dt_bias"], inputs["L0_attn_A_log"], inputs["L0_attn_D"],
        inputs["L0_attn_rms_w"], inputs["L0_attn_out_w"], consts)
    in_maps = []
    for c in range(8):
        bb, t0 = c // 4, (c % 4) * T
        if t0 == 0:
            uh = np.concatenate([np.zeros((128, 1024), np.float32), u[bb, 0:T]], axis=0)
        else:
            uh = u[bb, t0 - 128:t0 + T]
        m = dict(common_a)
        m["u"] = np.ascontiguousarray(uh.T)
        m["m2T"], m["selcol"] = host_core_masks(c)
        in_maps.append(m)
    nc_a = build_mamba(T)
    res_a = run_bass_kernel_spmd(nc_a, in_maps, core_ids=list(range(8)))
    x1 = [res_a.results[c]["x1"] for c in range(8)]

    x1m = np.ascontiguousarray(x1[0][:, 0:2048])
    in_maps = [host_moe_inputs(
        x1m, e, inputs["L0_ffn_gate_up_proj"], inputs["L0_ffn_gate_up_proj_bias"],
        inputs["L0_ffn_down_proj"], inputs["L0_ffn_down_proj_bias"],
        inputs["L0_ffn_router_indices"], inputs["L0_ffn_routing_weights"]) for e in range(8)]
    nc_b = build_moe()
    res_b = run_bass_kernel_spmd(nc_b, in_maps, core_ids=list(range(8)))
    moe = res_b.results[0]["moe"]

    common_c = host_l1attn_inputs_common(
        inputs["L1_attn_W_q"], inputs["L1_attn_W_k"], inputs["L1_attn_W_v"],
        inputs["L1_attn_W_g"], inputs["L1_attn_W_o"], inputs["L1_attn_log_decay"], consts)
    in_maps = []
    for c in range(8):
        m = dict(common_c)
        m["x2"] = x1[c]
        madd = np.zeros((1024, T), np.float32)
        if c == 0:
            madd[:, 0:2048] = moe
        m["madd"] = madd
        in_maps.append(m)
    nc_c = build_l1attn(T, with_madd=True)
    res_c = run_bass_kernel_spmd(nc_c, in_maps, core_ids=list(range(8)))
    x3 = [res_c.results[c]["out"] for c in range(8)]

    common_d = host_l1ffn_inputs_common(
        inputs["L1_ffn_W"], inputs["L1_ffn_b"], inputs["L1_ffn_lora_down"],
        inputs["L1_ffn_lora_up"], inputs["L1_ffn_down_aux"], inputs["L1_ffn_up_aux"])
    in_maps = []
    for c in range(8):
        m = dict(common_d)
        m["x3"] = x3[c]
        in_maps.append(m)
    nc_d = build_l1ffn(T)
    res_d = run_bass_kernel_spmd(nc_d, in_maps, core_ids=list(range(8)))

    out = np.empty((2, 16384, 1024), np.float32)
    for c in range(8):
        bb, t0 = c // 4, (c % 4) * T
        out[bb, t0:t0 + T] = res_d.results[c]["out"].T
    return out


def host_mamba_inputs_common(in_proj_w, conv_w, conv_b, dt_bias, A_log, D, rms_w,
                             out_w, consts):
    W = np.asarray(in_proj_w, np.float32)
    wT = W.T
    wxbd = np.ascontiguousarray(np.concatenate(
        [wT[:, 2048:4096], wT[:, 4096:4160], wT[:, 4224:4240]], axis=1))
    wzc = np.ascontiguousarray(np.concatenate(
        [wT[:, 0:2048], wT[:, 4160:4224]], axis=1))
    cwf = np.asarray(conv_w, np.float32)
    cbf = np.asarray(conv_b, np.float32)
    cw18 = np.zeros((128, 18, 4), np.float32)
    cb18 = np.zeros((128, 18), np.float32)
    cw18[:, 0:16, :] = cwf[0:2048].reshape(16, 128, 4).transpose(1, 0, 2)
    cb18[:, 0:16] = cbf[0:2048].reshape(16, 128).T
    cw18[0:64, 16, :] = cwf[2048:2112]
    cb18[0:64, 16] = cbf[2048:2112]
    cw18[0:64, 17, :] = cwf[2112:2176]
    cb18[0:64, 17] = cbf[2112:2176]
    return {
        "wxbd": wxbd, "wzc": wzc, "owT": host_wT(out_w),
        "cwcol": cw18, "cbcol": cb18,
        "dtbcol": np.asarray(dt_bias, np.float32).reshape(16, 1),
        "alog": np.asarray(A_log, np.float32).reshape(1, 16),
        "dvec": np.asarray(D, np.float32).reshape(1, 16),
        "rmspk": np.ascontiguousarray(np.asarray(rms_w, np.float32).reshape(16, 128).T),
        "triuinc": consts["triuinc"], "upperneg": consts["upperneg"],
        "ident": consts["ident"],
    }


def host_core_masks(core_id):
    slot = core_id % 4
    m2 = np.zeros((8, 8), np.float32)
    sel = np.zeros((8, 1), np.float32)
    base = (core_id // 4) * 4
    for j in range(base, base + slot):
        sel[j, 0] = 1.0
        for i in range(j + 1, base + slot):
            m2[j, i] = 1.0
    return np.ascontiguousarray(m2.T), sel


def host_l1attn_inputs_common(W_q, W_k, W_v, W_g, W_o, log_decay, consts):
    return {
        "wqT": host_wT(W_q), "wkT": host_wT(W_k), "wvT": host_wT(W_v),
        "wgT": host_wT(W_g), "woT": host_wT(W_o),
        "ld0": np.ascontiguousarray(np.asarray(log_decay, np.float32)[0:1].reshape(1, 1)),
        "diffblk": consts["diffblk"], "offneg": consts["offneg"],
    }


def host_l1ffn_inputs_common(W, b, lora_down, lora_up, down_aux, up_aux):
    return {
        "wT": host_wT(W),
        "bpk": np.ascontiguousarray(np.asarray(b, np.float32).reshape(8, 128).T),
        "ldT": host_wT(lora_down),
        "luT": host_wT(lora_up),
        "uaT": host_wT(up_aux),
        "da": np.ascontiguousarray(np.asarray(down_aux, np.float32)),
    }









def build_fused(T=4096):
    nc = bass.Bass("TRN2", target_bir_lowering=False, debug=False)
    a = {"T": T}
    a["u"] = nc.dram_tensor("u", [1024, 128 + T], F32, kind="ExternalInput").ap()
    a["wxbd"] = nc.dram_tensor("wxbd", [1024, 2128], F32, kind="ExternalInput").ap()
    a["wzc"] = nc.dram_tensor("wzc", [1024, 2112], F32, kind="ExternalInput").ap()
    a["owT"] = nc.dram_tensor("owT", [2048, 1024], F32, kind="ExternalInput").ap()
    a["cwcol"] = nc.dram_tensor("cwcol", [128, 18, 4], F32, kind="ExternalInput").ap()
    a["cbcol"] = nc.dram_tensor("cbcol", [128, 18], F32, kind="ExternalInput").ap()
    a["dtbcol"] = nc.dram_tensor("dtbcol", [16, 1], F32, kind="ExternalInput").ap()
    a["alog"] = nc.dram_tensor("alog", [1, 16], F32, kind="ExternalInput").ap()
    a["dvec"] = nc.dram_tensor("dvec", [1, 16], F32, kind="ExternalInput").ap()
    a["rmspk"] = nc.dram_tensor("rmspk", [128, 16], F32, kind="ExternalInput").ap()
    a["triuinc"] = nc.dram_tensor("triuinc", [128, 128], F32, kind="ExternalInput").ap()
    a["upperneg"] = nc.dram_tensor("upperneg", [128, 128], F32, kind="ExternalInput").ap()
    a["ident"] = nc.dram_tensor("ident", [128, 128], F32, kind="ExternalInput").ap()
    a["m2T"] = nc.dram_tensor("m2T", [8, 8], F32, kind="ExternalInput").ap()
    a["selcol"] = nc.dram_tensor("selcol", [8, 1], F32, kind="ExternalInput").ap()
    a["x1"] = nc.dram_tensor("x1", [1024, T], F32).ap()
    a["sp_xtok"] = nc.dram_tensor("sp_xtok", [T, 2048], BF16).ap()
    a["sp_btok"] = nc.dram_tensor("sp_btok", [T, 64], BF16).ap()
    a["sp_bfeat"] = nc.dram_tensor("sp_bfeat", [64, T], BF16).ap()
    a["sp_dt"] = nc.dram_tensor("sp_dt", [16, T], F32).ap()
    a["w8d"] = nc.dram_tensor("w8d", [8, 16], F32).ap()
    a["cum_dram"] = nc.dram_tensor("cum_dram", [16, 128], F32).ap()
    a["ow_bf"] = nc.dram_tensor("ow_bf", [8, 128, 16, 128], BF16).ap()
    a["ag_in"] = nc.dram_tensor("ag_in", [65, 2048], F32).ap()
    a["ag_out"] = nc.dram_tensor("ag_out", [520, 2048], F32).ap()
    b = {}
    for w in ("gwT", "uwT", "dwT"):
        b[w] = nc.dram_tensor(w, [1024, 1024], F32, kind="ExternalInput").ap()
    for bn in ("gbpk", "ubpk", "dbpk"):
        b[bn] = nc.dram_tensor(bn, [128, 8], F32, kind="ExternalInput").ap()
    b["ridx"] = nc.dram_tensor("ridx", [128, 16, 2], I32, kind="ExternalInput").ap()
    b["rwcol"] = nc.dram_tensor("rwcol", [128, 16], F32, kind="ExternalInput").ap()
    b["ecol"] = nc.dram_tensor("ecol", [128, 1], F32, kind="ExternalInput").ap()
    b["mgatec"] = nc.dram_tensor("mgatec", [128, 1], F32, kind="ExternalInput").ap()
    b["x1sc"] = a["x1"]
    b["ar1_in"] = nc.dram_tensor("ar1_in", [1024, 2048], F32).ap()
    b["ar1_out"] = nc.dram_tensor("ar1_out", [1024, 2048], F32).ap()
    b["m_dram"] = nc.dram_tensor("m_dram", [1, 2048], F32).ap()
    b["ar_in"] = nc.dram_tensor("ar_in", [1024, 2048], F32).ap()
    b["ar_out"] = nc.dram_tensor("ar_out", [1024, 2048], F32).ap()
    c = {"T": T}
    c["x2"] = a["x1"]
    for w in ("wqT", "wkT", "wvT", "wgT", "woT"):
        c[w] = nc.dram_tensor(w, [1024, 1024], F32, kind="ExternalInput").ap()
    c["ld0"] = nc.dram_tensor("ld0", [1, 1], F32, kind="ExternalInput").ap()
    c["diffblk"] = nc.dram_tensor("diffblk", [128, 128], F32, kind="ExternalInput").ap()
    c["offneg"] = nc.dram_tensor("offneg", [128, 128], F32, kind="ExternalInput").ap()
    c["moesum"] = b["ar_out"]
    c["out"] = nc.dram_tensor("out", [1024, T], F32, kind="ExternalOutput").ap()
    d = {"T": T}
    d["wT"] = nc.dram_tensor("wT", [1024, 1024], F32, kind="ExternalInput").ap()
    d["bpk"] = nc.dram_tensor("bpk", [128, 8], F32, kind="ExternalInput").ap()
    d["ldT"] = nc.dram_tensor("ldT", [100, 4], F32, kind="ExternalInput").ap()
    d["luT"] = nc.dram_tensor("luT", [4, 50], F32, kind="ExternalInput").ap()
    d["uaT"] = nc.dram_tensor("uaT", [50, 1024], F32, kind="ExternalInput").ap()
    d["da"] = nc.dram_tensor("da", [100, 1024], F32, kind="ExternalInput").ap()
    d["wc_dram"] = nc.dram_tensor("wc_bf", [128, 8, 1024], BF16).ap()

    es = ExitStack()
    with es:
        em = Emit(nc, es)
        a["bcast"] = dict(ar1_in=b["ar1_in"], ar1_out=b["ar1_out"], mgatec=b["mgatec"])
        with ExitStack() as esA:
            em.cur_es = esA
            emit_mamba(em, a)
        s_cw = em.sem("f_cw")
        c["w_bf"] = {}
        for w in ("wqT", "wkT", "wvT", "wgT", "woT"):
            dst = nc.dram_tensor(w + "_bf", [1024, 1024], BF16).ap()
            c["w_bf"][w] = dst
            em.op("gpsimd", lambda g, dst=dst, src=c[w]: g.dma_start(out=dst, in_=src),
                  incs=s_cw.inc(16))
        c["w_bf_done"] = s_cw.now()
        specsA = [s.now() for s in em.all_sems if s.n > 0 and s is not s_cw]
        for e in ENGS:
            em.pending[e] = list(specsA)
        em.bank_rel = {}
        b["bcast_done"] = em.bcast_done
        with ExitStack() as esB:
            em.cur_es = esB
            d["wc_done"] = emit_wc_prep(em, d)
            moe_done = emit_moe(em, b)
        cc_sem = em._moe_cc
        specs = [s.now() for s in em.all_sems if s.n > 0 and s is not cc_sem]
        for e in ENGS:
            em.pending[e] = list(specs)
        em.bank_rel = {}
        c["moe_done"] = moe_done
        with ExitStack() as esC:
            em.cur_es = esC
            mg = em.sb("c_mgate", [128, 1], F32)
            sw = em.sem("c_mg")
            em.op("gpsimd", lambda g: g.dma_start(out=mg[:], in_=b["mgatec"]), incs=sw.inc(16))
            em.pending["vector"] = list(em.pending.get("vector", [])) + [sw.now()]
            c["mgate_sb"] = mg
            c["d"] = d
            emit_l1attn(em, c)
        em.cur_es = es
        em.finish()
    return nc


def kernel(**inputs):
    T = 4096
    consts = host_consts()
    u = np.asarray(inputs["L0_attn_u"], np.float32)

    common_a = host_mamba_inputs_common(
        inputs["L0_attn_in_proj_w"], inputs["L0_attn_conv_w"], inputs["L0_attn_conv_b"],
        inputs["L0_attn_dt_bias"], inputs["L0_attn_A_log"], inputs["L0_attn_D"],
        inputs["L0_attn_rms_w"], inputs["L0_attn_out_w"], consts)
    common_c = host_l1attn_inputs_common(
        inputs["L1_attn_W_q"], inputs["L1_attn_W_k"], inputs["L1_attn_W_v"],
        inputs["L1_attn_W_g"], inputs["L1_attn_W_o"], inputs["L1_attn_log_decay"], consts)
    common_d = host_l1ffn_inputs_common(
        inputs["L1_ffn_W"], inputs["L1_ffn_b"], inputs["L1_ffn_lora_down"],
        inputs["L1_ffn_lora_up"], inputs["L1_ffn_down_aux"], inputs["L1_ffn_up_aux"])

    in_maps = []
    for cid in range(8):
        bb, t0 = cid // 4, (cid % 4) * T
        if t0 == 0:
            uh = np.concatenate([np.zeros((128, 1024), np.float32), u[bb, 0:T]], axis=0)
        else:
            uh = u[bb, t0 - 128:t0 + T]
        m = dict(common_a)
        m.update(common_c)
        m.update(common_d)
        m["u"] = np.ascontiguousarray(uh.T)
        m["m2T"], m["selcol"] = host_core_masks(cid)
        m.update(host_moe_inputs(
            None, cid, inputs["L0_ffn_gate_up_proj"], inputs["L0_ffn_gate_up_proj_bias"],
            inputs["L0_ffn_down_proj"], inputs["L0_ffn_down_proj_bias"],
            inputs["L0_ffn_router_indices"], inputs["L0_ffn_routing_weights"]))
        m["mgatec"] = np.full((128, 1), 1.0 if cid == 0 else 0.0, np.float32)
        in_maps.append(m)

    nc = build_fused(T)
    res = run_bass_kernel_spmd(nc, in_maps, core_ids=list(range(8)))
    out = np.empty((2, 16384, 1024), np.float32)
    for cid in range(8):
        bb, t0 = cid // 4, (cid % 4) * T
        out[bb, t0:t0 + T] = res.results[cid]["out"].T
    return out


# version 08
# speedup vs baseline: 1.1868x; 1.0075x over previous
import numpy as np
from contextlib import ExitStack
import concourse.bass as bass
import concourse.mybir as mybir
from concourse.bass_utils import run_bass_kernel_spmd

F32 = mybir.dt.float32
BF16 = mybir.dt.bfloat16
I32 = mybir.dt.int32
AF = mybir.ActivationFunctionType
OP = mybir.AluOpType
AX = mybir.AxisListType

ENGS = ("tensor", "vector", "scalar", "gpsimd", "sync")


class Sem:
    __slots__ = ("h", "n", "name")

    def __init__(self, h, name):
        self.h = h
        self.n = 0
        self.name = name

    def inc(self, k=1):
        self.n += k
        return (self, k)

    def now(self):
        return (self, self.n)

    def at(self, v):
        return (self, v)


class Emit:

    def __init__(self, nc, es):
        self.nc = nc
        self.es = es
        self.cur_es = es
        self.ops = {e: [] for e in ENGS}
        self.last_wait = {}
        self.bank_rel = {}
        self.all_sems = []
        self.pending = {}

    def sem(self, name):
        h = self.es.enter_context(self.nc.semaphore(name))
        s = Sem(h, name)
        self.all_sems.append(s)
        return s

    def sb(self, name, shape, dtype):
        return self.cur_es.enter_context(self.nc.sbuf_tensor(name, shape, dtype))

    def ps(self, name, shape, dtype=F32):
        return self.cur_es.enter_context(self.nc.psum_tensor(name, shape, dtype))

    def barrier(self):
        specs = [s.now() for s in self.all_sems if s.n > 0]
        for e in ENGS:
            self.pending[e] = list(specs)
        self.bank_rel = {}

    AUTO_EDGE_ENGS = ("vector", "scalar", "gpsimd")
    _ASYNC_NAMES = ("dma_start", "dma_start_transpose", "collective_compute",
                    "indirect_dma_start", "wait_ge", "nop")

    def _auto_sem(self, eng):
        key = f"__auto_{eng}"
        if not hasattr(self, "_autos"):
            self._autos = {}
        if eng not in self._autos:
            self._autos[eng] = self.sem(key)
        return self._autos[eng]

    def op(self, eng, fn, waits=(), incs=()):
        if waits and isinstance(waits[0], Sem):
            waits = (waits,)
        if incs and isinstance(incs[0], Sem):
            incs = (incs,)
        waits = tuple(waits)
        if self.pending.get(eng):
            waits = tuple(self.pending.pop(eng)) + waits
        names = fn.__code__.co_names
        is_async = any(nm in names for nm in self._ASYNC_NAMES)
        auto_inc = None
        if eng in self.AUTO_EDGE_ENGS and not is_async:
            edge = getattr(self, "_edge", {}).get(eng)
            if edge is not None:
                waits = waits + (edge,)
            if not incs:
                a = self._auto_sem(eng)
                auto_inc = (a, 1, a.inc(1) and a.n)
            if not hasattr(self, "_edge"):
                self._edge = {}
            if incs:
                sem, k = incs[-1]
                self._edge[eng] = (sem, sem.n)
            else:
                self._edge[eng] = (self._autos[eng], self._autos[eng].n)
        incs = tuple((sem, k, sem.n) for (sem, k) in incs)
        if auto_inc is not None:
            incs = incs + ((auto_inc[0], 1, auto_inc[2]),)
        self.ops[eng].append((waits, fn, incs))

    def replay(self, eng, engine_obj):
        for waits, fn, incs in self.ops[eng]:
            for (sem, val) in waits:
                key = (eng, id(sem))
                if val > 0 and self.last_wait.get(key, -1) < val:
                    engine_obj.wait_ge(sem.h, val)
                    self.last_wait[key] = val
            ins = fn(engine_obj)
            for (sem, k, val_after) in incs:
                ins = ins.then_inc(sem.h, k)

    def finish(self):
        with self.nc.Block() as block:
            block.tensor(lambda t: self.replay("tensor", t))
            block.vector(lambda v: self.replay("vector", v))
            block.scalar(lambda s: self.replay("scalar", s))
            block.gpsimd(lambda g: self.replay("gpsimd", g))
            block.sync(lambda sy: self.replay("sync", sy))
        self.ops = {e: [] for e in ENGS}


def ceil_div(a, b):
    return (a + b - 1) // b



def host_wT(w):
    return np.ascontiguousarray(np.asarray(w, dtype=np.float32).T)


def host_fm(x2d):
    return np.ascontiguousarray(np.asarray(x2d, dtype=np.float32).T)


CONST_TRIL_INC = None


def host_consts():
    ident = np.eye(128, dtype=np.float32)
    trilinc = np.tril(np.ones((128, 128), np.float32))
    triuinc = np.triu(np.ones((128, 128), np.float32))
    upperneg = np.where(np.arange(128)[:, None] > np.arange(128)[None, :],
                        np.float32(-1e30), np.float32(0.0))
    tp = np.arange(128)
    same_blk = (tp[:, None] // 32) == (tp[None, :] // 32)
    diff = tp[None, :] - tp[:, None]
    diffblk = np.where(same_blk & (diff >= 0), diff.astype(np.float32), np.float32(0.0))
    offneg = np.where(same_blk & (diff >= 0), np.float32(0.0), np.float32(-1e30))
    return dict(ident=ident, trilinc=trilinc.astype(np.float32), triuinc=triuinc,
                upperneg=upperneg, diffblk=diffblk, offneg=offneg)




def emit_proj(em, psum_pair, w_sb, ktiles, act3, otiles, epilogue, s_pe,
              first_waits=(), extra_k0_waits=None, N=512, start_acc=True):
    rel = [em.bank_rel.get(id(psum_pair[0])), em.bank_rel.get(id(psum_pair[1]))]
    for ot in range(otiles):
        pb = psum_pair[ot % 2]
        for k in range(ktiles):
            wts = []
            if k == 0:
                if ot < 2:
                    wts.extend(first_waits)
                if rel[ot % 2] is not None:
                    wts.append(rel[ot % 2])
                if extra_k0_waits is not None:
                    wts.extend(extra_k0_waits(ot))
            em.op("tensor", lambda t, ot=ot, k=k, pb=pb: t.matmul(
                pb[:, 0:N], w_sb[:, k, ot * 128:(ot + 1) * 128], act3[:, k, :],
                start=(k == 0 and start_acc), stop=(k == ktiles - 1)),
                waits=tuple(wts), incs=s_pe.inc() if k == ktiles - 1 else ())
        rel[ot % 2] = epilogue(ot, pb, s_pe.now())
    em.bank_rel[id(psum_pair[0])] = rel[0]
    em.bank_rel[id(psum_pair[1])] = rel[1]


class LNKit:

    def __init__(self, em, name, N=512, psums=None, bufs=None):
        self.em = em
        self.N = N
        if bufs is None:
            self.vbf = em.sb(f"{name}_vbf", [128, 8, N], BF16)[:]
            self.v2bf = em.sb(f"{name}_v2bf", [128, 8, N], BF16)[:]
        else:
            self.vbf, self.v2bf = bufs
        self.stat = em.sb(f"{name}_stat", [1, 4, N], F32)
        self.ones128 = em.sb(f"{name}_o128", [128, 1], BF16)
        self.onesrow = em.sb(f"{name}_orow", [1, 128], F32)
        if psums is None:
            self.pstA = em.ps(f"{name}_pstA", [1, N])[:]
            self.pstB = em.ps(f"{name}_pstB", [1, N])[:]
            pbc = em.ps(f"{name}_pbc", [128, 2 * N])
            self.pbc_mean = pbc[:, 0:N]
            self.pbc_rstd = pbc[:, N:2 * N]
            self.rel_keys = ()
        else:
            self.pstA, self.pstB, self.pbc_mean, self.pbc_rstd, self.rel_keys = psums
        self.s_pe = em.sem(f"{name}_pe")
        self.s_dve = em.sem(f"{name}_dve")
        self.s_act = em.sem(f"{name}_act")
        em.op("vector", lambda v: v.memset(self.ones128[:], 1.0), incs=self.s_dve.inc())
        em.op("vector", lambda v: v.memset(self.onesrow[:], 1.0), incs=self.s_dve.inc())
        self.init_done = self.s_dve.now()
        self.prev_sums = None
        self.prev_bc_readers = None

    def emit(self, vsb, out_f32, out_bf=None, v_ready=(), out_waits=()):
        em, N = self.em, self.N
        cw = list(v_ready)
        if self.prev_sums is not None:
            cw.append(self.prev_sums)
        em.op("scalar", lambda s: s.copy(
            self.vbf.rearrange("p a t -> p (a t)"), vsb.rearrange("p a t -> p (a t)")),
            waits=tuple(cw), incs=self.s_act.inc())
        em.op("scalar", lambda s: s.activation(
            self.v2bf.rearrange("p a t -> p (a t)"), vsb.rearrange("p a t -> p (a t)"), AF.Square),
            incs=self.s_act.inc())
        casts_done = self.last_casts = self.s_act.now()
        k0w = [casts_done, self.s_dve.now()]
        for key in self.rel_keys:
            if em.bank_rel.get(id(key)) is not None:
                k0w.append(em.bank_rel[id(key)])
        for k in range(8):
            em.op("tensor", lambda t, k=k: t.matmul(self.pstA, self.ones128[:], self.vbf[:, k, :],
                                                    start=(k == 0), stop=(k == 7)),
                  waits=tuple(k0w) if k == 0 else ())
        for k in range(8):
            em.op("tensor", lambda t, k=k: t.matmul(self.pstB, self.ones128[:], self.v2bf[:, k, :],
                                                    start=(k == 0), stop=(k == 7)),
                  incs=self.s_pe.inc() if k == 7 else ())
        self.prev_sums = sums_done = self.s_pe.now()
        st = self.stat
        em.op("vector", lambda v: v.tensor_scalar_mul(st[:, 0, :], self.pstA, 1.0 / 1024),
              waits=sums_done, incs=self.s_dve.inc())
        em.op("vector", lambda v: v.tensor_scalar_mul(st[:, 1, :], self.pstB, 1.0 / 1024),
              incs=self.s_dve.inc())
        em.op("vector", lambda v: v.tensor_mul(st[:, 2, :], st[:, 0, :], st[:, 0, :]),
              incs=self.s_dve.inc())
        em.op("vector", lambda v: v.scalar_tensor_tensor(st[:, 2, :], st[:, 1, :], 1e-5, st[:, 2, :], OP.add, OP.subtract),
              incs=self.s_dve.inc())
        em.op("scalar", lambda s: s.activation(st[:, 3, :], st[:, 2, :], AF.Ln),
              waits=self.s_dve.now(), incs=self.s_act.inc())
        em.op("scalar", lambda s: s.activation(st[:, 1, :], st[:, 3, :], AF.Exp, scale=-0.5),
              incs=self.s_act.inc())
        rstd_done = self.s_act.now()
        bw = [rstd_done]
        if self.prev_bc_readers is not None:
            bw.append(self.prev_bc_readers)
        em.op("tensor", lambda t: t.matmul(self.pbc_mean, self.onesrow[:], st[:, 0, :], start=True, stop=True),
              waits=tuple(bw))
        em.op("tensor", lambda t: t.matmul(self.pbc_rstd, self.onesrow[:], st[:, 1, :], start=True, stop=True),
              incs=self.s_pe.inc())
        bc_done = self.s_pe.now()
        tgt = out_f32 if out_f32 is not None else out_bf
        for ot in range(8):
            em.op("vector", lambda v, ot=ot: v.tensor_sub(tgt[:, ot, :], vsb[:, ot, :], self.pbc_mean),
                  waits=(bc_done,) + tuple(out_waits) if ot == 0 else (), incs=self.s_dve.inc())
            em.op("vector", lambda v, ot=ot: v.tensor_mul(tgt[:, ot, :], tgt[:, ot, :], self.pbc_rstd),
                  incs=self.s_dve.inc())
        done = self.s_dve.now()
        for key in self.rel_keys:
            em.bank_rel[id(key)] = done
        ret = [done]
        if out_bf is not None and out_f32 is not None:
            em.op("scalar", lambda s: s.copy(
                out_bf.rearrange("p a t -> p (a t)"), out_f32.rearrange("p a t -> p (a t)")),
                waits=done, incs=self.s_act.inc())
            ret.append(self.s_act.now())
        self.prev_bc_readers = done
        return tuple(ret)



RG8 = [[0, 1, 2, 3, 4, 5, 6, 7]]
NH = 16


def build_mamba(T=4096):
    nc = bass.Bass("TRN2", target_bir_lowering=False, debug=False)
    aps = {"T": T}
    aps["u"] = nc.dram_tensor("u", [1024, 128 + T], F32, kind="ExternalInput").ap()
    aps["wxbd"] = nc.dram_tensor("wxbd", [1024, 2128], F32, kind="ExternalInput").ap()
    aps["wzc"] = nc.dram_tensor("wzc", [1024, 2112], F32, kind="ExternalInput").ap()
    aps["owT"] = nc.dram_tensor("owT", [2048, 1024], F32, kind="ExternalInput").ap()
    aps["cwcol"] = nc.dram_tensor("cwcol", [128, 18, 4], F32, kind="ExternalInput").ap()
    aps["cbcol"] = nc.dram_tensor("cbcol", [128, 18], F32, kind="ExternalInput").ap()
    aps["dtbcol"] = nc.dram_tensor("dtbcol", [16, 1], F32, kind="ExternalInput").ap()
    aps["alog"] = nc.dram_tensor("alog", [1, 16], F32, kind="ExternalInput").ap()
    aps["dvec"] = nc.dram_tensor("dvec", [1, 16], F32, kind="ExternalInput").ap()
    aps["rmspk"] = nc.dram_tensor("rmspk", [128, 16], F32, kind="ExternalInput").ap()
    aps["triuinc"] = nc.dram_tensor("triuinc", [128, 128], F32, kind="ExternalInput").ap()
    aps["upperneg"] = nc.dram_tensor("upperneg", [128, 128], F32, kind="ExternalInput").ap()
    aps["ident"] = nc.dram_tensor("ident", [128, 128], F32, kind="ExternalInput").ap()
    aps["m2T"] = nc.dram_tensor("m2T", [8, 8], F32, kind="ExternalInput").ap()
    aps["selcol"] = nc.dram_tensor("selcol", [8, 1], F32, kind="ExternalInput").ap()
    aps["x1"] = nc.dram_tensor("x1", [1024, T], F32, kind="ExternalOutput").ap()
    aps["sp_xtok"] = nc.dram_tensor("sp_xtok", [T, 2048], BF16).ap()
    aps["sp_btok"] = nc.dram_tensor("sp_btok", [T, 64], BF16).ap()
    aps["sp_bfeat"] = nc.dram_tensor("sp_bfeat", [64, T], BF16).ap()
    aps["sp_dt"] = nc.dram_tensor("sp_dt", [16, T], F32).ap()
    aps["w8d"] = nc.dram_tensor("w8d", [8, 16], F32).ap()
    aps["cum_dram"] = nc.dram_tensor("cum_dram", [16, 128], F32).ap()
    aps["ow_bf"] = nc.dram_tensor("ow_bf", [8, 128, 16, 128], BF16).ap()
    aps["ag_in"] = nc.dram_tensor("ag_in", [65, 2048], F32).ap()
    aps["ag_out"] = nc.dram_tensor("ag_out", [520, 2048], F32).ap()
    es = ExitStack()
    with es:
        em = Emit(nc, es)
        emit_mamba(em, aps)
        em.finish()
    return nc


def emit_mamba(em, cfg):
    T = cfg["T"]
    NSC = T // 512
    u = cfg["u"]
    urr = u.rearrange("(k p) t -> p k t", p=128)

    wbig = em.sb("a_wbig", [128, 8, 2176], BF16)
    cw = em.sb("a_cw", [128, 18, 4], F32)
    cb = em.sb("a_cb", [128, 18], F32)
    dtb = em.sb("a_dtb", [16, 1], F32)
    rmsw = em.sb("a_rmsw", [128, 16], F32)
    triu = em.sb("a_triu", [128, 128], F32)
    upneg = em.sb("a_upneg", [128, 128], F32)
    ident = em.sb("a_ident", [128, 128], F32)
    alogr = em.sb("a_alogr", [1, 16], F32)
    dvecr = em.sb("a_dvecr", [1, 16], F32)
    arow = em.sb("a_arow", [1, 16], F32)
    abc = em.sb("a_abc", [128, 16], F32)
    dial = em.sb("a_dial", [128, 16 * 128], BF16)
    usb1 = em.sb("a_u", [128, 8, 512], F32)
    usb2 = [usb1, usb1]
    ubf1 = em.sb("a_ub", [128, 8, 512], BF16)
    ubf2 = [ubf1, ubf1]
    uhalo = em.sb("a_uhalo", [128, 8, 3], BF16)
    scr = em.sb("a_scr", [128, 10304], F32)
    xcraw = scr[:, 0:8755].rearrange("p (a t) -> p a t", a=17)
    acc1 = [scr[:, 8768:9280], scr[:, 9280:9792]]
    cacc2 = scr[0:64, 9792:10304]
    sigz = scr[:, 0:8192].rearrange("p (a t) -> p a t", a=16)
    szp = scr[:, 8192:10240].rearrange("p (a t) -> p a t", a=16)
    sgld = scr[0:64, 0:2048]
    w8rep = scr[0:1, 2048:3072].bitcast(BF16)
    un2 = em.sb("a_un2", [128, 8192], BF16)
    xhat = un2[:].rearrange("p (a t) -> p a t", a=16)
    ygat = un2[:].rearrange("p (a t) -> p a t", a=16)
    lnvbf = un2[:, 0:4096].rearrange("p (a t) -> p a t", a=8)
    lnv2bf = un2[:, 4096:8192].rearrange("p (a t) -> p a t", a=8)
    un3 = em.sb("a_un3", [128, 4096], BF16)
    dtscr = un3[0:16, 0:3072].bitcast(F32).rearrange("p (a t) -> p a t", a=3)
    owsb = [un3[:, 0:2048].rearrange("p (a t) -> p a t", a=16),
            un3[:, 2048:4096].rearrange("p (a t) -> p a t", a=16)]
    ccraw = em.sb("a_ccraw", [64, 515], F32)
    bfeat = em.sb("a_bfeat", [64, 512], BF16)
    chat = em.sb("a_chat", [64, 512], BF16)
    dtraw = un3[0:16, 3072:4096].bitcast(F32)
    dtsf = em.sb("a_dtsf", [16, 4, 128], F32)
    xtok = [em.sb(f"a_xtok_{i}", [128, 2048], BF16) for i in range(2)]
    btok = [em.sb(f"a_btok_{i}", [128, 64], BF16) for i in range(2)]
    btok2 = [em.sb(f"a_btok2_{i}", [128, 64], BF16) for i in range(2)]
    onescol = em.sb("a_onescol", [128, 1], F32)
    calsb = em.sb("a_calsb", [1, 16], F32)
    dtT = em.sb("a_dtT", [128, 16], F32)
    catok = em.sb("a_catok", [128, 16], F32)
    cafeat = em.sb("a_cafeat", [16, 128], F32)
    L2 = em.sb("a_L2", [2, 2048], F32)
    R2 = em.sb("a_R2", [2, 2048], F32)
    e1row = em.sb("a_e1row", [1, 2048], BF16)
    edec = em.sb("a_edec", [128, 16], F32)
    edl = em.sb("a_edl", [64, 16], F32)
    dacc = em.sb("a_dacc", [1, 16], F32)
    ssb = em.sb("a_ssb", [64, 16 * 128], F32)
    sbf = em.sb("a_sbf", [64, 16 * 128], BF16)
    g2sb = em.sb("a_g2sb", [128, 128], F32)
    emtmp = em.sb("a_emtmp", [128, 512], F32)
    emask = em.sb("a_emask", [128, 512], F32)
    g2all = em.sb("a_g2all", [128, 2048], BF16)
    cc2 = [em.sb(f"a_cc2_{i}", [64, 128], BF16) for i in range(2)]
    y2bf = em.sb("a_y2bf", [128, 2048], BF16)
    rstdr = em.sb("a_rstdr", [1, 3, 128], F32)
    rbc = em.sb("a_rbc", [128, 128], F32)
    w8all = em.sb("a_w8all", [1, 128], F32)
    dall = em.sb("a_dall", [8, 16], F32)
    m2t = em.sb("a_m2t", [8, 8], F32)
    selc = em.sb("a_selc", [8, 1], F32)
    onesrow = em.sb("a_onesrow", [1, 128], F32)
    onesbf = em.sb("a_onesbf", [1, 128], BF16)

    pY = em.ps("a_pY", [128, 2048])
    pD = [em.ps(f"a_pD{i}", [128, 512]) for i in range(2)]
    pq = [em.ps(f"a_pq{i}", [128, 512]) for i in range(2)]

    ln = LNKit(em, "a_ln", psums=(pD[0][0:1, 0:512], pD[1][0:1, 0:512],
                                  pD[0][:, 0:512], pD[1][:, 0:512],
                                  (pD[0], pD[1])), bufs=(lnvbf, lnv2bf))

    s_in = em.sem("a_in")
    s_w = em.sem("a_w")
    s_pe = em.sem("a_pe")
    s_dve = em.sem("a_dve")
    s_act = em.sem("a_act")
    s_tr = em.sem("a_tr")
    s_cum = em.sem("a_cum")
    s_inu = em.sem("a_inu")
    s_sp = em.sem("a_sp")
    s_ld = em.sem("a_ld")
    s_cc = em.sem("a_cc")
    s_st = em.sem("a_st")

    SR = {}

    em.op("gpsimd", lambda g: g.dma_start(
        out=wbig[:, :, 0:2128], in_=cfg["wxbd"].rearrange("(k p) o -> p k o", p=128)), incs=s_w.inc(16))
    for name_, sb_ in (("cwcol", cw), ("cbcol", cb), ("dtbcol", dtb), ("rmspk", rmsw),
                       ("triuinc", triu), ("upperneg", upneg), ("ident", ident),
                       ("m2T", m2t), ("selcol", selc), ("alog", alogr), ("dvec", dvecr)):
        em.op("gpsimd", lambda g, n=name_, s=sb_: g.dma_start(out=s[:], in_=cfg[n]), incs=s_w.inc(16))
    em.op("gpsimd", lambda g: g.dma_start(out=uhalo[:], in_=urr[:, :, 125:128]), incs=s_w.inc(16))
    if cfg.get("bcast") is not None:
        mg_sb = em.sb("a_mg", [128, 1], F32)
        em.op("gpsimd", lambda g: g.dma_start(out=mg_sb[:], in_=cfg["bcast"]["mgatec"]),
              incs=s_w.inc(16))
        cfg["bcast"]["mgate_sb"] = mg_sb
    wready = s_w.now()
    for buf, val in ((onesrow, 1.0), (onesbf, 1.0), (onescol, 1.0)):
        em.op("vector", lambda v, b_=buf, vv=val: v.memset(b_[:], vv), incs=s_dve.inc())
    em.op("vector", lambda v: v.memset(ssb[:], 0.0), incs=s_dve.inc())
    em.op("vector", lambda v: v.memset(dacc[:], 0.0), incs=s_dve.inc())
    em.op("vector", lambda v: v.memset(xcraw[:, :, 0:3], 0.0), incs=s_dve.inc())
    em.op("vector", lambda v: v.memset(ccraw[:, 0:3], 0.0), incs=s_dve.inc())
    em.op("vector", lambda v: v.memset(L2[:], 1.0), incs=s_dve.inc())
    em.op("vector", lambda v: v.memset(R2[:], -1.0), incs=s_dve.inc())
    ms_done = s_dve.now()

    em.op("scalar", lambda s: s.activation(arow[:], alogr[:], AF.Exp),
          waits=wready, incs=s_act.inc())
    em.op("vector", lambda v: v.tensor_scalar_mul(arow[:], arow[:], -1.0),
          waits=s_act.now(), incs=s_dve.inc())
    em.op("tensor", lambda t: t.matmul(pq[0][:, 0:16], onesrow[:], arow[:], start=True, stop=True),
          waits=(s_dve.now(), ms_done), incs=s_pe.inc())
    em.op("scalar", lambda s: s.copy(abc[:], pq[0][:, 0:16]), waits=s_pe.now(), incs=s_act.inc())
    em.op("tensor", lambda t: t.matmul(pq[0][:, 16:32], onesrow[:], dvecr[:], start=True, stop=True),
          waits=s_act.now(), incs=s_pe.inc())
    em.op("scalar", lambda s: s.copy(rbc[:, 0:16], pq[0][:, 16:32]), waits=s_pe.now(), incs=s_act.inc())
    dbc_done = s_act.now()
    for h in range(NH):
        em.op("vector", lambda v, h=h: v.tensor_scalar_mul(
            dial[:, h * 128:(h + 1) * 128], ident[:], rbc[:, h:h + 1]),
            waits=(dbc_done,) if h == 0 else (), incs=s_dve.inc())
    prep_done = s_dve.now()
    em.bank_rel[id(pq[0])] = dbc_done

    for k in range(8):
        for ot in range(17):
            m0, mw = ot * 128, (128 if ot < 16 else 80)
            em.op("tensor", lambda t, ot=ot, k=k, m0=m0, mw=mw: t.matmul(
                pq[1][0:mw, ot * 3:(ot + 1) * 3], wbig[:, k, m0:m0 + mw], uhalo[:, k, :],
                start=(k == 0), stop=(k == 7)),
                waits=(wready, prep_done) if (k == 0 and ot == 0) else (),
                incs=s_pe.inc() if (k == 7 and ot == 16) else ())
    halo_mm = s_pe.now()
    for ot in range(16):
        em.op("scalar", lambda s, ot=ot: s.copy(xcraw[:, ot, 0:3], pq[1][:, ot * 3:(ot + 1) * 3]),
              waits=(halo_mm, ms_done) if ot == 0 else (), incs=s_act.inc())
    em.op("scalar", lambda s: s.copy(xcraw[0:64, 16, 0:3], pq[1][0:64, 48:51]), incs=s_act.inc())
    em.bank_rel[id(pq[1])] = halo_done = s_act.now()

    calast = R2[0:1, :].rearrange("q (h t) -> q h t", t=128)[:, :, 127:128].squeeze(2)

    def cum_pipe(dt_ap, dt_ready, with_L2, cp=None, use_pq_rel=True):
        if cp is None:
            cp = pq[0][:]
        w0 = [dt_ready]
        rel_specs = (SR.get("cum"), em.bank_rel.get(id(pq[0])) if use_pq_rel else None)
        for spec in rel_specs:
            if spec is not None:
                w0.append(spec)
        em.op("tensor", lambda t, cp=cp: t.transpose(cp[:, 128:144], dt_ap, ident[0:16, 0:16]),
              waits=tuple(w0), incs=s_pe.inc())
        em.op("scalar", lambda s, cp=cp: s.copy(dtT[:], cp[:, 128:144]), waits=s_pe.now(), incs=s_act.inc())
        em.op("tensor", lambda t, cp=cp: t.matmul(cp[:, 144:160], triu[:], dtT[:], start=True, stop=True),
              waits=s_act.now(), incs=s_pe.inc())
        if with_L2:
            em.op("vector", lambda v, cp=cp: v.tensor_mul(catok[:], cp[:, 144:160], abc[:]),
                  waits=s_pe.now(), incs=s_dve.inc())
            em.op("tensor", lambda t, cp=cp: t.transpose(cp[0:16, 160:288], catok[:], ident[:]),
                  waits=s_dve.now(), incs=s_pe.inc())
            em.op("scalar", lambda s, cp=cp: s.copy(cafeat[:], cp[0:16, 160:288]),
                  waits=s_pe.now(), incs=s_act.inc())
            caf_done = s_act.now()
            em.op("scalar", lambda e: e.dma_start(out=cfg["cum_dram"], in_=cafeat[:]),
                  waits=caf_done, incs=s_cum.inc(16))
            cd_done = s_cum.now()
            em.op("scalar", lambda e: e.dma_start(
                out=R2[0:1, :], in_=cfg["cum_dram"].rearrange("h t -> (h t)").unsqueeze(0)),
                waits=cd_done, incs=s_cum.inc(16))
            em.op("scalar", lambda e: e.dma_start(
                out=L2[1:2, :], in_=cfg["cum_dram"].rearrange("h t -> (h t)").unsqueeze(0)),
                incs=s_cum.inc(16))
            row_done = s_cum.now()
            em.op("tensor", lambda t, cp=cp: t.matmul(cp[:, 288:304], onesrow[:], calast, start=True, stop=True),
                  waits=row_done, incs=s_pe.inc())
            em.op("tensor", lambda t, cp=cp: t.matmul(cp[0:64, 304:320], onesrow[:, 0:64], calast, start=True, stop=True),
                  incs=s_pe.inc())
            bc_done = s_pe.now()
        else:
            em.op("tensor", lambda t, cp=cp: t.matmul(
                cp[0:1, 160:176], onescol[:], dtT[:], start=True, stop=True),
                incs=s_pe.inc())
            em.op("vector", lambda v, cp=cp: v.tensor_mul(catok[:], cp[:, 144:160], abc[:]),
                  waits=s_pe.now(), incs=s_dve.inc())
            em.op("vector", lambda v, cp=cp: v.tensor_mul(calsb[:], cp[0:1, 160:176], arow[:]),
                  incs=s_dve.inc())
            cal_done = s_dve.now()
            row_done = None
            em.op("tensor", lambda t, cp=cp: t.matmul(cp[:, 288:304], onesrow[:], calsb[:], start=True, stop=True),
                  waits=cal_done, incs=s_pe.inc())
            em.op("tensor", lambda t, cp=cp: t.matmul(cp[0:64, 304:320], onesrow[:, 0:64], calsb[:], start=True, stop=True),
                  incs=s_pe.inc())
            bc_done = s_pe.now()
        em.op("vector", lambda v, cp=cp: v.tensor_sub(edec[:], cp[:, 288:304], catok[:]),
              waits=bc_done, incs=s_dve.inc())
        em.op("scalar", lambda s: s.activation(edec[:], edec[:], AF.Exp),
              waits=s_dve.now(), incs=s_act.inc())
        em.op("scalar", lambda s, cp=cp: s.activation(edl[:], cp[0:64, 304:320], AF.Exp), incs=s_act.inc())
        SR["cum"] = done = s_act.now()
        return done, row_done

    def t_and_update(h, tb, extra_first, tp=None, use_pq_rel=True):
        slot = h % 2
        if tp is None:
            tp = pq[1][:]
        bw = list(extra_first) if extra_first else []
        if SR.get(f"btok2_{slot}") is not None:
            bw.append(SR[f"btok2_{slot}"])
        em.op("vector", lambda v, h=h, tb=tb, slot=slot: v.tensor_scalar_mul(
            btok2[slot][:], btok[tb][:], edec[:, h:h + 1]),
            waits=tuple(bw), incs=s_dve.inc())
        bpp = s_dve.now()
        tw = [bpp]
        for spec in (SR.get(f"T{slot}"), em.bank_rel.get(id(pq[1])) if use_pq_rel else None):
            if spec is not None:
                tw.append(spec)
        em.op("tensor", lambda t, h=h, tb=tb, slot=slot, tp=tp: t.matmul(
            tp[0:64, slot * 128:(slot + 1) * 128], btok2[slot][:],
            xtok[tb][:, h * 128:(h + 1) * 128], start=True, stop=True),
            waits=tuple(tw), incs=s_pe.inc())
        t_done = s_pe.now()
        em.op("vector", lambda v, h=h, slot=slot, tp=tp: v.scalar_tensor_tensor(
            ssb[:, h * 128:(h + 1) * 128], ssb[:, h * 128:(h + 1) * 128],
            edl[:, h:h + 1], tp[0:64, slot * 128:(slot + 1) * 128],
            OP.mult, OP.add), waits=t_done, incs=s_dve.inc())
        SR[f"T{slot}"] = s_dve.now()
        SR[f"btok2_{slot}"] = t_done

    ubf_guards = [None, None]
    conv_guard = None
    xhat_guard = None
    tr_prev = [None, None]
    xtok_rd = [None, None]
    gchunk = 0
    for sc in range(NSC):
        b = sc % 2
        ubf = ubf2[b]
        t0 = 128 + sc * 512
        w = [wready]
        if ubf_guards[b] is not None:
            w.append(ubf_guards[b])
        em.op("gpsimd", lambda g, t0=t0, ubf=ubf: g.dma_start(
            out=ubf[:], in_=urr[:, :, t0:t0 + 512]), waits=tuple(w), incs=s_in.inc(16))
        cast_done = s_in.now()

        if sc > 0:
            em.op("scalar", lambda s: s.copy(xcraw[:, :, 0:3], xcraw[:, :, 512:515]),
                  waits=conv_guard, incs=s_act.inc())
        carry_done = s_act.now()

        def ep_x(ot, pb, pe_spec):
            em.op("scalar", lambda s, ot=ot, pb=pb: s.copy(xcraw[:, ot, 3:515], pb[:]),
                  waits=(pe_spec, carry_done), incs=s_act.inc())
            return s_act.now()
        emit_proj(em, pq, wbig, 8, ubf[:], 16, ep_x, s_pe, first_waits=(cast_done,))
        wts0 = [em.bank_rel[id(pq[0])]]
        for k in range(8):
            em.op("tensor", lambda t, k=k, ubf=ubf: t.matmul(
                pq[0][0:80, 0:512], wbig[:, k, 2048:2128], ubf[:, k, :],
                start=(k == 0), stop=(k == 7)),
                waits=tuple(wts0) if k == 0 else (), incs=s_pe.inc() if k == 7 else ())
        bd_mm = ubf_guards[0] = ubf_guards[1] = s_pe.now()
        em.op("scalar", lambda s: s.copy(xcraw[0:64, 16, 3:515], pq[0][0:64, 0:512]),
              waits=(bd_mm, carry_done), incs=s_act.inc())
        dtg = [xhat_guard] if xhat_guard else []
        em.op("scalar", lambda s: s.copy(dtraw, pq[0][64:80, 0:512]),
              waits=tuple(dtg), incs=s_act.inc())
        em.bank_rel[id(pq[0])] = evac_bd = s_act.now()

        silu_prev = [None, None]
        for tile in range(17):
            rows = 128 if tile < 16 else 64
            dst = acc1[tile % 2][0:rows] if tile < 16 else cacc2
            cwt = [evac_bd]
            if tile < 16 and silu_prev[tile % 2] is not None:
                cwt.append(silu_prev[tile % 2])
            if tile == 0 and xhat_guard is not None:
                cwt.append(xhat_guard)
            em.op("vector", lambda v, tile=tile, dst=dst, rows=rows: v.tensor_scalar_mul(
                dst, xcraw[0:rows, tile, 0:512], cw[0:rows, tile, 0:1]),
                waits=tuple(cwt), incs=s_dve.inc())
            for kk in range(1, 4):
                em.op("vector", lambda v, tile=tile, kk=kk, dst=dst, rows=rows: v.scalar_tensor_tensor(
                    dst, xcraw[0:rows, tile, kk:512 + kk], cw[0:rows, tile, kk:kk + 1],
                    dst, OP.mult, OP.add), incs=s_dve.inc())
            taps = s_dve.now()
            if tile < 16:
                em.op("scalar", lambda s, tile=tile: s.activation(
                    xhat[:, tile, :], acc1[tile % 2], AF.Silu, bias=cb[:, tile:tile + 1]),
                    waits=taps, incs=s_act.inc())
                silu_prev[tile % 2] = s_act.now()
            else:
                bfw = [taps]
                if SR.get("bfeat_rd") is not None:
                    bfw.extend(SR["bfeat_rd"])
                em.op("scalar", lambda s: s.activation(
                    bfeat[:], cacc2, AF.Silu, bias=cb[0:64, 16:17]),
                    waits=tuple(bfw), incs=s_act.inc())
        conv_guard = s_dve.now()
        silu_done = s_act.now()

        em.op("scalar", lambda s, sc=sc: s.dma_start(
            out=cfg["sp_bfeat"][:, sc * 512:(sc + 1) * 512], in_=bfeat[:]),
            waits=silu_done, incs=s_sp.inc(16))

        em.op("scalar", lambda s: s.activation(dtscr[:, 0, :], dtraw, AF.Abs, bias=dtb[:]),
              waits=evac_bd, incs=s_act.inc())
        em.op("scalar", lambda s: s.activation(dtscr[:, 1, :], dtscr[:, 0, :], AF.Exp, scale=-1.0),
              incs=s_act.inc())
        em.op("vector", lambda v: v.tensor_scalar_add(dtscr[:, 1, :], dtscr[:, 1, :], 1.0),
              waits=s_act.now(), incs=s_dve.inc())
        em.op("scalar", lambda s: s.activation(dtscr[:, 2, :], dtscr[:, 1, :], AF.Ln),
              waits=s_dve.now(), incs=s_act.inc())
        em.op("scalar", lambda s: s.activation(dtscr[:, 0, :], dtraw, AF.Relu, bias=dtb[:]),
              incs=s_act.inc())
        dfw = [s_act.now()]
        if SR.get("dtsf_rd") is not None:
            dfw.extend(SR["dtsf_rd"])
        em.op("vector", lambda v: v.tensor_add(
            dtsf[:].rearrange("p a t -> p (a t)"), dtscr[:, 0, :], dtscr[:, 2, :]),
            waits=tuple(dfw), incs=s_dve.inc())
        dt_done = s_dve.now()
        em.op("scalar", lambda s, sc=sc: s.dma_start(
            out=cfg["sp_dt"][:, sc * 512:(sc + 1) * 512],
            in_=dtsf[:].rearrange("p a t -> p (a t)")), waits=dt_done, incs=s_sp.inc(16))

        for cc in range(4):
            tb = gchunk % 2
            trw = [silu_done]
            if tr_prev[tb] is not None:
                trw.append(tr_prev[tb])
            if xtok_rd[tb] is not None:
                trw.append(xtok_rd[tb])
            for ft in range(16):
                em.op("sync", lambda sy, ft=ft, cc=cc, tb=tb: sy.dma_start_transpose(
                    xtok[tb][:, ft * 128:(ft + 1) * 128], xhat[:, ft, cc * 128:(cc + 1) * 128]),
                    waits=tuple(trw) if ft == 0 else (), incs=s_tr.inc(16))
            em.op("sync", lambda sy, cc=cc, tb=tb: sy.dma_start_transpose(
                btok[tb][:], bfeat[:, cc * 128:(cc + 1) * 128]), incs=s_tr.inc(16))
            tr_done = s_tr.now()
            em.op("scalar", lambda s, gc=gchunk, tb=tb: s.dma_start(
                out=cfg["sp_xtok"][gc * 128:(gc + 1) * 128, :], in_=xtok[tb][:]),
                waits=tr_done, incs=s_sp.inc(16))
            em.op("scalar", lambda s, gc=gchunk, tb=tb: s.dma_start(
                out=cfg["sp_btok"][gc * 128:(gc + 1) * 128, :], in_=btok[tb][:]),
                incs=s_sp.inc(16))
            tr_prev[tb] = s_sp.now()
            cum_done, _ = cum_pipe(dtsf[:, cc, :], dt_done, with_L2=False,
                                   cp=pY[:, 0:512], use_pq_rel=False)
            em.op("vector", lambda v: v.tensor_add(dacc[:], dacc[:], calsb[:]),
                waits=cum_done, incs=s_dve.inc())
            for h in range(NH):
                t_and_update(h, tb, (cum_done, tr_done) if h == 0 else None,
                             tp=pY[:, 512:1024], use_pq_rel=False)
            xtok_rd[tb] = s_pe.now()
            gchunk += 1
        xhat_guard = (s_sp, s_sp.n)
        SR["dtsf_rd"] = [SR["cum"], s_sp.now()]
        SR["bfeat_rd"] = [s_tr.now(), s_sp.now()]

    sfin = s_dve.now()
    em.op("scalar", lambda s: s.dma_start(out=cfg["ag_in"][0:64, :], in_=ssb[:]),
          waits=sfin, incs=s_sp.inc(16))
    em.op("scalar", lambda s: s.dma_start(out=cfg["ag_in"][64:65, 0:16], in_=dacc[:]),
          incs=s_sp.inc(16))
    em.op("gpsimd", lambda g: g.collective_compute(
        "AllGather", OP.bypass, replica_groups=RG8,
        ins=[cfg["ag_in"].opt()], outs=[cfg["ag_out"].opt()]),
        waits=s_sp.now(), incs=s_cc.inc(1))
    ag_done = s_cc.now()

    em.op("sync", lambda sy: sy.dma_start(
        out=dall[:], in_=cfg["ag_out"].rearrange("(j r) h -> j r h", r=65)[:, 64, 0:16]),
        waits=ag_done, incs=s_ld.inc(16))
    em.op("tensor", lambda t: t.matmul(pq[0][0:8, 0:16], m2t[:], dall[:], start=True, stop=True),
          waits=(s_ld.now(), em.bank_rel[id(pq[0])]), incs=s_pe.inc())
    em.op("scalar", lambda s: s.activation(dall[:], pq[0][0:8, 0:16], AF.Exp),
          waits=s_pe.now(), incs=s_act.inc())
    em.bank_rel[id(pq[0])] = s_act.now()
    em.op("vector", lambda v: v.tensor_scalar_mul(dall[:], dall[:], selc[:]),
          waits=s_act.now(), incs=s_dve.inc())
    em.op("scalar", lambda s: s.dma_start(out=cfg["w8d"], in_=dall[:]),
          waits=s_dve.now(), incs=s_sp.inc(16))
    em.op("sync", lambda sy: sy.dma_start(
        out=w8all[:], in_=cfg["w8d"].rearrange("j h -> (j h)").unsqueeze(0)),
        waits=s_sp.now(), incs=s_ld.inc(16))
    w8_done = s_ld.now()
    sinit_rel = None
    for j in range(8):
        em.op("vector", lambda v, j=j: v.tensor_copy(
            w8rep[:].rearrange("q (h t) -> q h t", h=16),
            w8all[:, j * 16:(j + 1) * 16].unsqueeze(2).broadcast_to([1, 16, 128])),
            waits=(w8_done,) if j == 0 else (), incs=s_dve.inc())
        wrep_done = s_dve.now()
        for q in range(4):
            ww = [wrep_done] if q == 0 else []
            if q == 0 and sinit_rel is not None:
                ww.append(sinit_rel)
            em.op("tensor", lambda t, q=q: t.matmul(
                pY[0:64, q * 512:(q + 1) * 512], onesbf[:, 0:64], w8rep[:, q * 512:(q + 1) * 512],
                start=True, stop=True), waits=tuple(ww), incs=s_pe.inc() if q == 3 else ())
        wbc_done = s_pe.now()
        em.op("sync", lambda sy, j=j: sy.dma_start(
            out=sgld[:], in_=cfg["ag_out"][j * 65:j * 65 + 64, :]),
            waits=(ag_done,) if j == 0 else (), incs=s_ld.inc(16))
        ld_done = s_ld.now()
        if j == 0:
            em.op("vector", lambda v: v.tensor_mul(ssb[:], sgld[:], pY[0:64, :]),
                  waits=(wbc_done, ld_done, sfin), incs=s_dve.inc())
        else:
            em.op("vector", lambda v: v.tensor_mul(sgld[:], sgld[:], pY[0:64, :]),
                  waits=(wbc_done, ld_done), incs=s_dve.inc())
            em.op("vector", lambda v: v.tensor_add(ssb[:], ssb[:], sgld[:]), incs=s_dve.inc())
        sinit_rel = s_dve.now()
    em.bank_rel[id(pY)] = sinit_rel

    em.op("gpsimd", lambda g: g.dma_start(
        out=wbig[:, :, 0:2112], in_=cfg["wzc"].rearrange("(k p) o -> p k o", p=128)),
        waits=(s_pe.now(), ag_done), incs=s_w.inc(16))
    wzc_ready = s_w.now()
    for aa in range(8):
        em.op("gpsimd", lambda g, aa=aa: g.dma_start(
            out=cfg["ow_bf"][aa],
            in_=cfg["owT"].rearrange("(k p) o -> p k o", p=128)[:, :, aa * 128:(aa + 1) * 128]),
            incs=s_w.inc(16))
    owbf_ready = s_w.now()
    for k in range(8):
        em.op("tensor", lambda t, k=k: t.matmul(
            pq[1][0:64, 0:3], wbig[:, k, 2048:2112], uhalo[:, k, :],
            start=(k == 0), stop=(k == 7)),
            waits=(wzc_ready, em.bank_rel[id(pq[1])]) if k == 0 else (),
            incs=s_pe.inc() if k == 7 else ())
    em.op("scalar", lambda s: s.copy(ccraw[:, 0:3], pq[1][0:64, 0:3]),
          waits=s_pe.now(), incs=s_act.inc())
    em.bank_rel[id(pq[1])] = s_act.now()

    usb_guards = [None, None]
    gchunk = 0
    xl_prev = [None, None]
    cconv_guard = None
    prev_apply = None
    prev_ln_casts = None
    ow_prev = [None, None]
    SR.pop("cum", None)
    ubf_guards = [s_pe.now(), s_pe.now()]
    bcast = cfg.get("bcast")
    s_bc = em.sem("a_bc")
    s_bcc = em.sem("a_bcc")
    for sc in range(NSC):
        b = sc % 2
        usb = usb2[b]
        ubf = ubf2[b]
        t0 = 128 + sc * 512
        w = [wzc_ready]
        if usb_guards[b] is not None:
            w.extend(usb_guards[b])
        em.op("gpsimd", lambda g, t0=t0, ubf=ubf: g.dma_start(
            out=ubf[:], in_=urr[:, :, t0:t0 + 512]), waits=(ubf_guards[b],), incs=s_in.inc(16))
        cast_done = s_in.now()
        if bcast is not None and sc == NSC - 1:
            em.op("gpsimd", lambda g: g.collective_compute(
                "AllReduce", OP.add, replica_groups=RG8,
                ins=[bcast["ar1_in"].opt()], outs=[bcast["ar1_out"].opt()]),
                waits=s_bc.at(64), incs=s_bcc.inc(1))
        em.op("sync", lambda sy, t0=t0, usb=usb: sy.dma_start(
            out=usb[:], in_=urr[:, :, t0:t0 + 512]), waits=tuple(w), incs=s_inu.inc(16))
        usb_done = s_inu.now()

        def ep_z(ot, pb, pe_spec):
            wz = [pe_spec]
            if prev_apply is not None:
                wz.append(prev_apply)
            em.op("scalar", lambda s, ot=ot, pb=pb: s.activation(
                sigz[:, ot, :], pb[:], AF.Sigmoid), waits=tuple(wz), incs=s_act.inc())
            em.op("vector", lambda v, ot=ot: v.tensor_scalar_mul(
                sigz[:, ot, :], sigz[:, ot, :], rmsw[:, ot:ot + 1]),
                waits=s_act.now(), incs=s_dve.inc())
            return s_dve.now()
        emit_proj(em, pq, wbig, 8, ubf[:], 16, ep_z, s_pe, first_waits=(cast_done,))
        if sc > 0:
            em.op("scalar", lambda s: s.copy(ccraw[:, 0:3], ccraw[:, 512:515]),
                  waits=cconv_guard, incs=s_act.inc())
        ccarry = s_act.now()
        wtc = [em.bank_rel[id(pq[0])]]
        for k in range(8):
            em.op("tensor", lambda t, k=k, ubf=ubf: t.matmul(
                pq[0][0:64, 0:512], wbig[:, k, 2048:2112], ubf[:, k, :],
                start=(k == 0), stop=(k == 7)),
                waits=tuple(wtc) if k == 0 else (), incs=s_pe.inc() if k == 7 else ())
        ubf_guards[0] = ubf_guards[1] = s_pe.now()
        em.op("scalar", lambda s: s.copy(ccraw[:, 3:515], pq[0][0:64, 0:512]),
              waits=(s_pe.now(), ccarry), incs=s_act.inc())
        em.bank_rel[id(pq[0])] = cevac = s_act.now()
        em.op("vector", lambda v: v.tensor_scalar_mul(cacc2, ccraw[:, 0:512], cw[0:64, 17, 0:1]),
              waits=cevac, incs=s_dve.inc())
        for kk in range(1, 4):
            em.op("vector", lambda v, kk=kk: v.scalar_tensor_tensor(
                cacc2, ccraw[:, kk:512 + kk], cw[0:64, 17, kk:kk + 1], cacc2,
                OP.mult, OP.add), incs=s_dve.inc())
        cconv_guard = s_dve.now()
        chw = [cconv_guard]
        if SR.get("chat") is not None:
            chw.append(SR["chat"])
        em.op("scalar", lambda s: s.activation(chat[:], cacc2, AF.Silu, bias=cb[0:64, 17:18]),
              waits=tuple(chw), incs=s_act.inc())
        chat_done = s_act.now()
        em.op("sync", lambda sy, sc=sc: sy.dma_start(
            out=bfeat[:], in_=cfg["sp_bfeat"][:, sc * 512:(sc + 1) * 512]),
            waits=(SR.get("bfeat") or ()) and (SR["bfeat"],), incs=s_ld.inc(16))
        bf_ld = s_ld.now()
        em.op("sync", lambda sy, sc=sc: sy.dma_start(
            out=dtsf[:].rearrange("p a t -> p (a t)"), in_=cfg["sp_dt"][:, sc * 512:(sc + 1) * 512]),
            waits=(SR.get("dtld") or ()) and (SR["dtld"],), incs=s_ld.inc(16))
        dt_ld = s_ld.now()

        for cc in range(4):
            tb = gchunk % 2
            lw = []
            if xl_prev[tb] is not None:
                lw.append(xl_prev[tb])
            em.op("scalar", lambda s, gc=gchunk, tb=tb: s.dma_start(
                out=xtok[tb][:], in_=cfg["sp_xtok"][gc * 128:(gc + 1) * 128, :]),
                waits=tuple(lw), incs=s_ld.inc(16))
            em.op("scalar", lambda s, gc=gchunk, tb=tb: s.dma_start(
                out=btok[tb][:], in_=cfg["sp_btok"][gc * 128:(gc + 1) * 128, :]),
                incs=s_ld.inc(16))
            ld_done = s_ld.now()
            cum_done, _ = cum_pipe(dtsf[:, cc, :], dt_ld, with_L2=True)
            ew1 = [cum_done]
            if SR.get("e1") is not None:
                ew1.append(SR["e1"])
            em.op("scalar", lambda s: s.activation(e1row[:], R2[0:1, :], AF.Exp),
                  waits=tuple(ew1), incs=s_act.inc())
            e1_done = s_act.now()
            scw = [SR["Supd"]] if SR.get("Supd") else []
            em.op("scalar", lambda s: s.copy(sbf[:], ssb[:]), waits=tuple(scw), incs=s_act.inc())
            scast_done = s_act.now()
            gw_ = [chat_done, bf_ld]
            for spec in (SR.get("G2"), SR.get("RBC"), em.bank_rel.get(id(pq[0]))):
                if spec is not None:
                    gw_.append(spec)
            em.op("tensor", lambda t, cc=cc: t.matmul(
                pq[0][:, 0:128], bfeat[:, cc * 128:(cc + 1) * 128], chat[:, cc * 128:(cc + 1) * 128],
                start=True, stop=True), waits=tuple(gw_), incs=s_pe.inc())
            em.op("scalar", lambda s: s.copy(g2sb[:], pq[0][:, 0:128]),
                  waits=s_pe.now(), incs=s_act.inc())
            SR["G2"] = g2_done = s_act.now()
            for g in range(4):
                dw_ = [cum_done]
                for spec in (SR.get(f"D{g % 2}"), em.bank_rel.get(id(pD[g % 2]))):
                    if spec is not None:
                        dw_.append(spec)
                for hh in range(4):
                    h = g * 4 + hh
                    em.op("tensor", lambda t, g=g, hh=hh, h=h: t.matmul(
                        pD[g % 2][:, hh * 128:(hh + 1) * 128],
                        L2[:, h * 128:(h + 1) * 128], R2[:, h * 128:(h + 1) * 128],
                        start=True, stop=True),
                        waits=tuple(dw_) if hh == 0 else (),
                        incs=s_pe.inc() if hh == 3 else ())
                dmm = s_pe.now()
                em.op("vector", lambda v, g=g: v.tensor_tensor(
                    emtmp[:].rearrange("p (a t) -> p a t", a=4),
                    pD[g % 2][:].rearrange("p (a t) -> p a t", a=4),
                    upneg[:].unsqueeze(1).broadcast_to([128, 4, 128]), OP.add),
                    waits=dmm, incs=s_dve.inc())
                em.op("scalar", lambda s, g=g: s.activation(emask[:], emtmp[:], AF.Exp),
                      waits=s_dve.now(), incs=s_act.inc())
                em.op("vector", lambda v, g=g: v.tensor_tensor(
                    g2all[:, g * 512:(g + 1) * 512].rearrange("p (a t) -> p a t", a=4),
                    emask[:].rearrange("p (a t) -> p a t", a=4),
                    g2sb[:].unsqueeze(1).broadcast_to([128, 4, 128]), OP.mult),
                    waits=(s_act.now(), g2_done), incs=s_dve.inc())
                SR[f"D{g % 2}"] = s_dve.now()
            g2a_done = s_dve.now()
            yw = [g2a_done, scast_done, ld_done, e1_done]
            for spec in (SR.get("Y"), em.bank_rel.get(id(pY))):
                if spec is not None:
                    yw.append(spec)
            em.bank_rel[id(pY)] = None
            for h in range(NH):
                slot = h % 2
                ew = [e1_done]
                for spec in (SR.get(f"EB{slot}"), em.bank_rel.get(id(pq[1]))):
                    if spec is not None:
                        ew.append(spec)
                em.op("tensor", lambda t, h=h, slot=slot: t.matmul(
                    pq[1][0:64, 256 + slot * 128:384 + slot * 128],
                    onesbf[:, 0:64], e1row[:, h * 128:(h + 1) * 128], start=True, stop=True),
                    waits=tuple(ew), incs=s_pe.inc())
                eb_done = s_pe.now()
                ccw = [eb_done]
                if SR.get(f"YH{slot}") is not None:
                    ccw.append(SR[f"YH{slot}"])
                em.op("vector", lambda v, h=h, cc=cc, slot=slot: v.tensor_mul(
                    cc2[slot][:], chat[:, cc * 128:(cc + 1) * 128],
                    pq[1][0:64, 256 + slot * 128:384 + slot * 128]),
                    waits=tuple(ccw), incs=s_dve.inc())
                SR[f"EB{slot}"] = cc2_done = s_dve.now()
                em.op("tensor", lambda t, h=h, tb=tb: t.matmul(
                    pY[:, h * 128:(h + 1) * 128], xtok[tb][:, h * 128:(h + 1) * 128],
                    g2all[:, h * 128:(h + 1) * 128], start=True, stop=False),
                    waits=tuple(yw) if h == 0 else ())
                em.op("tensor", lambda t, h=h, slot=slot: t.matmul(
                    pY[:, h * 128:(h + 1) * 128], sbf[:, h * 128:(h + 1) * 128],
                    cc2[slot][:], start=False, stop=False), waits=cc2_done)
                em.op("tensor", lambda t, h=h, tb=tb: t.matmul(
                    pY[:, h * 128:(h + 1) * 128], xtok[tb][:, h * 128:(h + 1) * 128],
                    dial[:, h * 128:(h + 1) * 128], start=False, stop=True),
                    incs=s_pe.inc())
                SR[f"YH{slot}"] = s_pe.now()
                t_and_update(h, tb, (cum_done, ld_done) if h == 0 else None)
            SR["Supd"] = SR["T1"]
            y_mm_done = s_pe.now()
            xl_prev[tb] = y_mm_done
            em.op("scalar", lambda s: s.activation(y2bf[:], pY[:], AF.Square),
                  waits=y_mm_done, incs=s_act.inc())
            y2_done = s_act.now()
            mw_ = [y2_done, ln.init_done]
            if SR.get("MU") is not None:
                mw_.append(SR["MU"])
            for k in range(16):
                em.op("tensor", lambda t, k=k: t.matmul(
                    pq[0][0:1, 320:448], ln.ones128[:], y2bf[:, k * 128:(k + 1) * 128],
                    start=(k == 0), stop=(k == 15)),
                    waits=tuple(mw_) if k == 0 else (), incs=s_pe.inc() if k == 15 else ())
            mu_done = s_pe.now()
            em.op("vector", lambda v: v.tensor_scalar_mul(rstdr[:, 0, :], pq[0][0:1, 320:448], 1.0 / 2048),
                  waits=mu_done, incs=s_dve.inc())
            SR["MU"] = s_dve.now()
            em.op("vector", lambda v: v.tensor_scalar_add(rstdr[:, 1, :], rstdr[:, 0, :], 1e-5),
                  incs=s_dve.inc())
            em.op("scalar", lambda s: s.activation(rstdr[:, 2, :], rstdr[:, 1, :], AF.Ln),
                  waits=s_dve.now(), incs=s_act.inc())
            em.op("scalar", lambda s: s.activation(rstdr[:, 1, :], rstdr[:, 2, :], AF.Exp, scale=-0.5),
                  incs=s_act.inc())
            rstd_done = s_act.now()
            rw_ = [rstd_done, g2a_done]
            for spec in (SR.get("RBC"), SR.get("G2")):
                if spec is not None:
                    rw_.append(spec)
            em.op("tensor", lambda t: t.matmul(pq[0][:, 0:128], onesrow[:], rstdr[:, 1, :],
                                               start=True, stop=True),
                  waits=tuple(rw_), incs=s_pe.inc())
            em.op("scalar", lambda s: s.copy(rbc[:], pq[0][:, 0:128]),
                  waits=s_pe.now(), incs=s_act.inc())
            SR["RBC"] = rbc_done = s_act.now()
            em.op("vector", lambda v, cc=cc: v.tensor_tensor(
                szp[:], sigz[:, :, cc * 128:(cc + 1) * 128],
                rbc[:].unsqueeze(1).broadcast_to([128, 16, 128]), OP.mult),
                waits=(rbc_done,), incs=s_dve.inc())
            ygw = []
            if cc == 0 and ln.prev_sums is not None:
                ygw.append(ln.prev_sums)
            em.op("vector", lambda v, cc=cc: v.tensor_tensor(
                ygat[:, :, cc * 128:(cc + 1) * 128], szp[:],
                pY[:].rearrange("p (a t) -> p a t", a=16), OP.mult),
                waits=tuple(ygw), incs=s_dve.inc())
            SR["Y"] = s_dve.now()
            gchunk += 1
        yg_done = s_dve.now()
        em.bank_rel[id(pq[0])] = s_dve.now()
        em.bank_rel[id(pq[1])] = s_dve.now()

        for ot in range(8):
            oww = []
            if ow_prev[ot % 2] is not None:
                oww.append(ow_prev[ot % 2])
            oww.append(owbf_ready)
            em.op("sync", lambda sy, ot=ot: sy.dma_start(
                out=owsb[ot % 2][:], in_=cfg["ow_bf"][ot]),
                waits=tuple(oww), incs=s_w.inc(16))
            ow_ld = s_w.now()
            pb = pq[ot % 2]
            for k in range(16):
                wts = []
                if k == 0:
                    wts = [ow_ld, yg_done]
                    if em.bank_rel.get(id(pb)) is not None:
                        wts.append(em.bank_rel[id(pb)])
                em.op("tensor", lambda t, ot=ot, k=k, pb=pb: t.matmul(
                    pb[:, 0:512], owsb[ot % 2][:, k, :], ygat[:, k, :],
                    start=(k == 0), stop=(k == 15)),
                    waits=tuple(wts), incs=s_pe.inc() if k == 15 else ())
            mm_done = s_pe.now()
            ow_prev[ot % 2] = mm_done
            em.op("vector", lambda v, ot=ot, pb=pb, usb=usb: v.tensor_add(
                sigz[:, ot, :], usb[:, ot, :], pb[:, 0:512]),
                waits=(mm_done, usb_done) if ot == 0 else mm_done, incs=s_dve.inc())
            em.bank_rel[id(pb)] = s_dve.now()
        v_done = s_dve.now()
        SR["Y"] = s_pe.now()
        em.bank_rel[id(pD[0])] = SR.get("D0")
        em.bank_rel[id(pD[1])] = SR.get("D1")
        apply_done = ln.emit(sigz[:, 0:8, :], usb[:], v_ready=(v_done,))[0]
        prev_apply = apply_done
        prev_ln_casts = ln.last_casts
        em.op("sync", lambda sy, sc=sc, usb=usb: sy.dma_start(
            out=cfg["x1"].rearrange("(k p) t -> p k t", p=128)[:, :, sc * 512:(sc + 1) * 512],
            in_=usb[:]), waits=apply_done, incs=s_st.inc(16))
        usb_guards[0] = usb_guards[1] = [s_st.now()]
        if bcast is not None and sc < 4:
            em.op("vector", lambda v, usb=usb: v.tensor_scalar_mul(
                sigz[:, 0:8, :].rearrange("p a t -> p (a t)"),
                usb[:].rearrange("p a t -> p (a t)"), bcast["mgate_sb"][:]),
                waits=apply_done, incs=s_dve.inc())
            em.op("scalar", lambda s, sc=sc: s.dma_start(
                out=bcast["ar1_in"].rearrange("(k p) t -> p k t", p=128)[:, :, sc * 512:(sc + 1) * 512],
                in_=sigz[:, 0:8, :]), waits=s_dve.now(), incs=s_bc.inc(16))
            usb_guards[0] = usb_guards[1] = usb_guards[0] + [s_bc.now()]
            prev_apply = s_bc.now()
        SR["bfeat"] = s_pe.now()
        SR["dtld"] = SR["cum"]
        SR["e1"] = s_pe.now()
        SR["chat"] = s_dve.now()
    em.op("sync", lambda sy: sy.nop(), waits=s_st.now())
    em.bcast_done = s_bcc.now() if bcast is not None else None


def host_mamba_inputs(u_slice_fm_with_halo, in_proj_w, conv_w, conv_b, dt_bias,
                      A_log, D, rms_w, out_w, core_id, consts):
    W = np.asarray(in_proj_w, np.float32)
    wT = W.T
    wxbd = np.ascontiguousarray(np.concatenate(
        [wT[:, 2048:4096], wT[:, 4096:4160], wT[:, 4224:4240]], axis=1))
    wzc = np.ascontiguousarray(np.concatenate(
        [wT[:, 0:2048], wT[:, 4160:4224]], axis=1))
    cwf = np.asarray(conv_w, np.float32)
    cbf = np.asarray(conv_b, np.float32)
    cw18 = np.zeros((128, 18, 4), np.float32)
    cb18 = np.zeros((128, 18), np.float32)
    cw18[:, 0:16, :] = cwf[0:2048].reshape(16, 128, 4).transpose(1, 0, 2)
    cb18[:, 0:16] = cbf[0:2048].reshape(16, 128).T
    cw18[0:64, 16, :] = cwf[2048:2112]
    cb18[0:64, 16] = cbf[2048:2112]
    cw18[0:64, 17, :] = cwf[2112:2176]
    cb18[0:64, 17] = cbf[2112:2176]
    slot = core_id % 4
    m2 = np.zeros((8, 8), np.float32)
    sel = np.zeros((8, 1), np.float32)
    base = (core_id // 4) * 4
    for j in range(base, base + slot):
        sel[j, 0] = 1.0
        for i in range(j + 1, base + slot):
            m2[j, i] = 1.0
    return {
        "u": np.ascontiguousarray(u_slice_fm_with_halo, np.float32),
        "wxbd": wxbd, "wzc": wzc,
        "owT": host_wT(out_w),
        "cwcol": cw18, "cbcol": cb18,
        "dtbcol": np.asarray(dt_bias, np.float32).reshape(16, 1),
        "alog": np.asarray(A_log, np.float32).reshape(1, 16),
        "dvec": np.asarray(D, np.float32).reshape(1, 16),
        "rmspk": np.ascontiguousarray(np.asarray(rms_w, np.float32).reshape(16, 128).T),
        "triuinc": consts["triuinc"], "upperneg": consts["upperneg"],
        "ident": consts["ident"],
        "m2T": np.ascontiguousarray(m2.T), "selcol": sel,
    }



RG8 = [[0, 1, 2, 3, 4, 5, 6, 7]]


def build_moe():
    nc = bass.Bass("TRN2", target_bir_lowering=False, debug=False)
    aps = {}
    aps["x1m"] = nc.dram_tensor("x1m", [1024, 2048], F32, kind="ExternalInput").ap()
    for w in ("gwT", "uwT", "dwT"):
        aps[w] = nc.dram_tensor(w, [1024, 1024], F32, kind="ExternalInput").ap()
    for bname in ("gbpk", "ubpk", "dbpk"):
        aps[bname] = nc.dram_tensor(bname, [128, 8], F32, kind="ExternalInput").ap()
    aps["ridx"] = nc.dram_tensor("ridx", [128, 16, 2], I32, kind="ExternalInput").ap()
    aps["rwcol"] = nc.dram_tensor("rwcol", [128, 16], F32, kind="ExternalInput").ap()
    aps["ecol"] = nc.dram_tensor("ecol", [128, 1], F32, kind="ExternalInput").ap()
    aps["moe"] = nc.dram_tensor("moe", [1024, 2048], F32, kind="ExternalOutput").ap()
    aps["ar_in"] = nc.dram_tensor("ar_in", [1024, 2048], F32).ap()
    aps["m_dram"] = nc.dram_tensor("m_dram", [1, 2048], F32).ap()
    aps["ar_out"] = nc.dram_tensor("ar_out", [1024, 2048], F32).ap()
    es = ExitStack()
    with es:
        em = Emit(nc, es)
        emit_moe(em, aps)
        em.finish()
    return nc


def emit_moe(em, cfg):
    x1m = cfg.get("x1m")
    gw = em.sb("b_gw", [128, 8, 1024], BF16)
    uw = em.sb("b_uw", [128, 8, 1024], BF16)
    dw = em.sb("b_dw", [128, 8, 1024], BF16)
    gb = em.sb("b_gb", [128, 8], F32)
    ub = em.sb("b_ub", [128, 8], F32)
    db = em.sb("b_db", [128, 8], F32)
    ridxf = em.sb("b_ridxf", [128, 16, 2], F32)
    ridxi = em.sb("b_ridxi", [128, 16, 2], I32)
    rwcol = em.sb("b_rwcol", [128, 16], F32)
    ecol = em.sb("b_ecol", [128, 1], F32)
    eq = em.sb("b_eq", [128, 16, 2], F32)
    msb = em.sb("b_m", [128, 16], F32)
    mrow = em.sb("b_mrow", [1, 2048], F32)
    Mb = em.sb("b_Mb", [128, 4, 512], F32)
    onesrow = em.sb("b_onesrow", [1, 128], F32)
    x1sb = [em.sb(f"b_x1_{i}", [128, 8, 512], F32) for i in range(2)]
    x1bf = em.sb("b_x1b", [128, 8, 512], BF16)
    g7 = em.sb("b_g7", [128, 8, 512], F32)
    up7 = em.sb("b_up7", [128, 8, 512], F32)
    sg = em.sb("b_sg", [128, 8, 512], F32)
    glu = g7
    inter = em.sb("b_inter", [128, 8, 512], BF16)
    part = up7

    pq = [em.ps(f"b_pq{i}", [128, 512]) for i in range(2)]
    pM = em.ps("b_pM", [128, 512])

    s_in = em.sem("b_in")
    s_w = em.sem("b_w")
    s_pe = em.sem("b_pe")
    s_dve = em.sem("b_dve")
    s_act = em.sem("b_act")
    s_ps = em.sem("b_ps")
    s_cc = em.sem("b_cc")
    s_st = em.sem("b_st")
    s_mr = em.sem("b_mr")

    em._moe_cc = s_cc
    if x1m is None:
        x1m = cfg["ar1_out"]
        em.pending["sync"] = list(em.pending.get("sync", [])) + [cfg["bcast_done"]]

    for w, sb in (("gwT", gw), ("uwT", uw), ("dwT", dw)):
        em.op("gpsimd", lambda g, w=w, sb=sb: g.dma_start(
            out=sb[:], in_=cfg[w].rearrange("(k p) o -> p k o", p=128)), incs=s_w.inc(16))
    for bn, sb in (("gbpk", gb), ("ubpk", ub), ("dbpk", db)):
        em.op("gpsimd", lambda g, bn=bn, sb=sb: g.dma_start(out=sb[:], in_=cfg[bn]), incs=s_w.inc(16))
    em.op("gpsimd", lambda g: g.dma_start(out=ridxi[:], in_=cfg["ridx"]), incs=s_w.inc(16))
    em.op("gpsimd", lambda g: g.dma_start(out=rwcol[:], in_=cfg["rwcol"]), incs=s_w.inc(16))
    em.op("gpsimd", lambda g: g.dma_start(out=ecol[:], in_=cfg["ecol"]), incs=s_w.inc(16))
    wready = s_w.now()

    em.op("vector", lambda v: v.memset(onesrow[:], 1.0), incs=s_dve.inc())
    em.op("vector", lambda v: v.tensor_copy(ridxf[:], ridxi[:]), waits=wready, incs=s_dve.inc())
    em.op("vector", lambda v: v.tensor_single_scalar(eq[:], ridxf[:], ecol[:], OP.is_equal),
          incs=s_dve.inc())
    em.op("vector", lambda v: v.tensor_add(msb[:], eq[:, :, 0], eq[:, :, 1]), incs=s_dve.inc())
    em.op("vector", lambda v: v.tensor_mul(msb[:], msb[:], rwcol[:]), incs=s_dve.inc())
    em.op("sync", lambda sy: sy.dma_start(
        out=cfg["m_dram"].rearrange("q (p a) -> (q p) a", p=128), in_=msb[:]),
        waits=s_dve.now(), incs=s_mr.inc(16))
    em.op("sync", lambda sy: sy.dma_start(out=mrow[:], in_=cfg["m_dram"]),
          waits=s_mr.now(), incs=s_mr.inc(16))
    for q in range(4):
        em.op("tensor", lambda t, q=q: t.matmul(pM[:], onesrow[:], mrow[:, q * 512:(q + 1) * 512],
                                                start=True, stop=True),
              waits=(s_mr.now(), s_dve.now()) if q == 0 else (s_act.now(),), incs=s_pe.inc())
        em.op("scalar", lambda s, q=q: s.copy(Mb[:, q, :], pM[:]), waits=s_pe.now(), incs=s_act.inc())
    m_done = s_act.now()

    freed = [None, None]
    prev_down = None
    for sc in range(4):
        b = sc % 2
        t0 = sc * 512
        w = [wready]
        if freed[b] is not None:
            w.append(freed[b])
        em.op("sync", lambda sy, t0=t0, b=b: sy.dma_start(
            out=x1sb[b][:], in_=x1m.rearrange("(k p) t -> p k t", p=128)[:, :, t0:t0 + 512]),
            waits=tuple(w), incs=s_in.inc(16))
        cw = [s_in.now()]
        if prev_down is not None:
            cw.append(prev_down)
        em.op("scalar", lambda s, b=b: s.copy(
            x1bf[:].rearrange("p a t -> p (a t)"), x1sb[b][:].rearrange("p a t -> p (a t)")),
            waits=tuple(cw), incs=s_act.inc())
        cast_done = s_act.now()

        def ep_gate(ot, pb, pe_spec):
            em.op("vector", lambda v, ot=ot, pb=pb: v.tensor_scalar(
                g7[:, ot, :], pb[:], gb[:, ot:ot + 1], 7.0, OP.add, OP.min),
                waits=pe_spec, incs=s_dve.inc())
            return s_dve.now()
        emit_proj(em, pq, gw, 8, x1bf[:], 8, ep_gate, s_pe,
                  first_waits=(cast_done, m_done))

        def ep_up(ot, pb, pe_spec):
            uw_ = [pe_spec]
            if ot == 0 and sc > 0:
                uw_.append(s_ps.now())
            em.op("vector", lambda v, ot=ot, pb=pb: v.tensor_scalar(
                up7[:, ot, :], pb[:], ub[:, ot:ot + 1], 7.0, OP.add, OP.min),
                waits=tuple(uw_), incs=s_dve.inc())
            em.op("vector", lambda v, ot=ot: v.tensor_scalar(
                up7[:, ot, :], up7[:, ot, :], -7.0, None, OP.max),
                incs=s_dve.inc())
            return s_dve.now()
        emit_proj(em, pq, uw, 8, x1bf[:], 8, ep_up, s_pe)
        gu_done = s_dve.now()

        em.op("scalar", lambda s: s.activation(
            sg[:].rearrange("p a t -> p (a t)"), g7[:].rearrange("p a t -> p (a t)"),
            AF.Sigmoid, scale=1.702), waits=gu_done, incs=s_act.inc())
        em.op("vector", lambda v: v.tensor_mul(
            glu[:].rearrange("p a t -> p (a t)"), g7[:].rearrange("p a t -> p (a t)"),
            sg[:].rearrange("p a t -> p (a t)")), waits=s_act.now(), incs=s_dve.inc())
        em.op("vector", lambda v: v.scalar_tensor_tensor(
            inter[:].rearrange("p a t -> p (a t)"), up7[:].rearrange("p a t -> p (a t)"),
            1.0, glu[:].rearrange("p a t -> p (a t)"), OP.add, OP.mult),
            incs=s_dve.inc())
        inter_done = s_dve.now()

        def ep_down(ot, pb, pe_spec):
            em.op("vector", lambda v, ot=ot, pb=pb, sc=sc: v.scalar_tensor_tensor(
                part[:, ot, :], pb[:], db[:, ot:ot + 1], Mb[:, sc, :], OP.add, OP.mult),
                waits=pe_spec, incs=s_dve.inc())
            return s_dve.now()
        emit_proj(em, pq, dw, 8, inter[:], 8, ep_down, s_pe,
                  first_waits=(inter_done,))
        down_done = prev_down = s_dve.now()
        freed[b] = down_done
        em.op("scalar", lambda s, t0=t0: s.dma_start(
            out=cfg["ar_in"].rearrange("(k p) t -> p k t", p=128)[:, :, t0:t0 + 512],
            in_=part[:]), waits=down_done, incs=s_ps.inc(16))

    em.op("gpsimd", lambda g: g.collective_compute(
        "AllReduce", OP.add, replica_groups=RG8,
        ins=[cfg["ar_in"].opt()], outs=[cfg["ar_out"].opt()]),
        waits=s_ps.now(), incs=s_cc.inc(1))
    if "moe" in cfg:
        em.op("sync", lambda sy: sy.dma_start(out=cfg["moe"], in_=cfg["ar_out"]),
              waits=s_cc.now(), incs=s_st.inc(16))
        em.op("sync", lambda sy: sy.nop(), waits=s_st.now())
    return s_cc.now()


def host_moe_inputs(x1m_fm, expert_id, gate_up_proj, gate_up_bias, down_proj,
                    down_bias, router_indices, routing_weights):
    e = expert_id
    gup = np.asarray(gate_up_proj[e], np.float32)
    gW = gup[:, 0::2]
    uW = gup[:, 1::2]
    gub = np.asarray(gate_up_bias[e], np.float32)
    gb_ = gub[0::2]
    ub_ = gub[1::2]
    dW = np.asarray(down_proj[e], np.float32)
    db_ = np.asarray(down_bias[e], np.float32)
    r = {
        "gwT": np.ascontiguousarray(gW),
        "uwT": np.ascontiguousarray(uW),
        "dwT": np.ascontiguousarray(dW),
        "gbpk": np.ascontiguousarray(gb_.reshape(8, 128).T),
        "ubpk": np.ascontiguousarray(ub_.reshape(8, 128).T),
        "dbpk": np.ascontiguousarray(db_.reshape(8, 128).T),
        "ridx": np.ascontiguousarray(np.asarray(router_indices, np.int32).reshape(128, 16, 2)),
        "rwcol": np.ascontiguousarray(np.asarray(routing_weights, np.float32)[:, e].reshape(128, 16)),
        "ecol": np.full((128, 1), float(e), np.float32),
    }
    if x1m_fm is not None:
        r["x1m"] = np.ascontiguousarray(x1m_fm, np.float32)
    return r

def emit_wc_prep(em, cfg):
    wtmp = em.sb("w_wtmp", [128, 1024], F32)
    wcsb = em.sb("w_wc", [128, 8, 1024], BF16)
    t1b = em.sb("w_t1b", [4, 1024], BF16)
    t2b = em.sb("w_t2b", [50, 1024], BF16)
    ldTb = em.sb("w_ldTb", [100, 4], BF16)
    luTb = em.sb("w_luTb", [4, 50], BF16)
    uaTb = em.sb("w_uaTb", [50, 1024], BF16)
    dab = em.sb("w_dab", [100, 1024], BF16)
    pL = em.ps("w_pL", [128, 1024])
    s_w = em.sem("w_w")
    s_pe = em.sem("w_pe")
    s_act = em.sem("w_act")
    s_dve = em.sem("w_dve")
    s_ld = em.sem("w_ld")
    s_st = em.sem("w_st")
    for n, sb in (("ldT", ldTb), ("luT", luTb), ("uaT", uaTb), ("da", dab)):
        em.op("gpsimd", lambda g, n=n, sb=sb: g.dma_start(out=sb[:], in_=cfg[n]),
              incs=s_w.inc(16))
    wready = s_w.now()
    for h in range(2):
        em.op("tensor", lambda t, h=h: t.matmul(
            pL[0:4, h * 512:(h + 1) * 512], ldTb[:], dab[:, h * 512:(h + 1) * 512],
            start=True, stop=True), waits=wready, incs=s_pe.inc())
    em.op("scalar", lambda s: s.copy(t1b[:], pL[0:4, :]), waits=s_pe.now(), incs=s_act.inc())
    for h in range(2):
        em.op("tensor", lambda t, h=h: t.matmul(
            pL[0:50, h * 512:(h + 1) * 512], luTb[:], t1b[:, h * 512:(h + 1) * 512],
            start=True, stop=True), waits=s_act.now(), incs=s_pe.inc())
    em.op("scalar", lambda s: s.copy(t2b[:], pL[0:50, :]), waits=s_pe.now(), incs=s_act.inc())
    t2_done = s_act.now()
    for mm in range(8):
        lw = [t2_done] if mm == 0 else [s_dve.now()]
        em.op("sync", lambda sy, mm=mm: sy.dma_start(
            out=wtmp[:], in_=cfg["wT"].rearrange("(k p) o -> p k o", p=128)[:, mm, :]),
            waits=tuple(lw) if mm > 0 else (), incs=s_ld.inc(16))
        ld_here = s_ld.now()
        for h in range(2):
            em.op("tensor", lambda t, mm=mm, h=h: t.matmul(
                pL[:, h * 512:(h + 1) * 512], t2b[:, mm * 128:(mm + 1) * 128],
                uaTb[:, h * 512:(h + 1) * 512], start=True, stop=True),
                waits=tuple(lw) if h == 0 else (), incs=s_pe.inc() if h == 1 else ())
        em.op("vector", lambda v, mm=mm: v.tensor_add(wcsb[:, mm, :], wtmp[:], pL[:, :]),
              waits=(s_pe.now(), ld_here), incs=s_dve.inc())
    em.op("scalar", lambda s: s.dma_start(out=cfg["wc_dram"], in_=wcsb[:]),
          waits=s_dve.now(), incs=s_st.inc(16))
    return s_st.now()





def build_l1attn(T=4096, with_madd=False):
    nc = bass.Bass("TRN2", target_bir_lowering=False, debug=False)
    aps = {}
    aps["x2"] = nc.dram_tensor("x2", [1024, T], F32, kind="ExternalInput").ap()
    if with_madd:
        aps["madd"] = nc.dram_tensor("madd", [1024, T], F32, kind="ExternalInput").ap()
    for w in ("wqT", "wkT", "wvT", "wgT", "woT"):
        aps[w] = nc.dram_tensor(w, [1024, 1024], F32, kind="ExternalInput").ap()
    aps["ld0"] = nc.dram_tensor("ld0", [1, 1], F32, kind="ExternalInput").ap()
    aps["diffblk"] = nc.dram_tensor("diffblk", [128, 128], F32, kind="ExternalInput").ap()
    aps["offneg"] = nc.dram_tensor("offneg", [128, 128], F32, kind="ExternalInput").ap()
    aps["out"] = nc.dram_tensor("out", [1024, T], F32, kind="ExternalOutput").ap()
    aps["T"] = T
    es = ExitStack()
    with es:
        em = Emit(nc, es)
        emit_l1attn(em, aps)
        em.finish()
    return nc


def emit_l1attn(em, cfg):
    T = cfg["T"]
    NSC = T // 512
    x2, out = cfg["x2"], cfg["out"]

    wq = em.sb("c_wq", [128, 8, 1024], BF16)
    wk = em.sb("c_wk", [128, 8, 1024], BF16)
    wv = em.sb("c_wv", [128, 8, 1024], BF16)
    wg = em.sb("c_wg", [128, 8, 1024], BF16)
    wo = em.sb("c_wo", [128, 8, 1024], BF16)
    diffblk = em.sb("c_diffblk", [128, 128], F32)
    offneg = em.sb("c_offneg", [128, 128], F32)
    ld0t = em.sb("c_ld0", [1, 1], F32)
    mst = em.sb("c_mst", [1, 2], F32)
    lndc = em.sb("c_lndc", [128, 1], F32)
    m32 = em.sb("c_m32", [128, 128], F32)

    x2sb = [em.sb(f"c_x2_{i}", [128, 8, 512], F32) for i in range(2)]
    x2bf = em.sb("c_x2b", [128, 8, 512], BF16)
    qbf = em.sb("c_qbf", [128, 8, 512], BF16)
    kbf = em.sb("c_kbf", [128, 8, 512], BF16)
    vbf_ = em.sb("c_vbf", [128, 8, 512], BF16)
    gate = em.sb("c_gate", [128, 8, 512], F32)
    vtok = em.sb("c_vtok", [128, 2, 1024], BF16)
    s2b = [em.sb(f"c_s2b_{i}", [128, 128], BF16) for i in range(2)]
    gob = x2bf
    vsb = em.sb("c_v", [128, 8, 512], F32)
    dcfg = cfg.get("d")
    if dcfg is not None:
        wc = em.sb("c_wcd", [128, 8, 1024], BF16)
        bcol = em.sb("c_bcol", [128, 8], F32)

    pq = [em.ps(f"c_pq{i}", [128, 512]) for i in range(2)]
    pO = em.ps("c_pO", [128, 1024])
    if dcfg is not None:
        ln = LNKit(em, "c_ln", bufs=(x2bf[:], kbf[:]))
    else:
        ln = cfg.get("ln") or LNKit(em, "c_ln")

    s_in = em.sem("c_in")
    s_w = em.sem("c_w")
    s_pe = em.sem("c_pe")
    s_dve = em.sem("c_dve")
    s_act = em.sem("c_act")
    s_tr = em.sem("c_tr")
    s_st = em.sem("c_st")

    if "w_bf" in cfg:
        for qn, w, sb in (("sync", "wqT", wq), ("scalar", "wkT", wk), ("sync", "wvT", wv),
                          ("scalar", "wgT", wg), ("sync", "woT", wo)):
            em.op(qn, lambda e, w=w, sb=sb: e.dma_start(
                out=sb[:], in_=cfg["w_bf"][w].rearrange("(k p) o -> p k o", p=128)),
                waits=(cfg["w_bf_done"],), incs=s_w.inc(16))
    else:
        for w, sb in (("wqT", wq), ("wkT", wk), ("wvT", wv), ("wgT", wg), ("woT", wo)):
            em.op("gpsimd", lambda g, w=w, sb=sb: g.dma_start(
                out=sb[:], in_=cfg[w].rearrange("(k p) o -> p k o", p=128)), incs=s_w.inc(16))
    em.op("scalar", lambda e: e.dma_start(out=diffblk[:], in_=cfg["diffblk"]), incs=s_w.inc(16))
    em.op("scalar", lambda e: e.dma_start(out=offneg[:], in_=cfg["offneg"]), incs=s_w.inc(16))
    em.op("scalar", lambda e: e.dma_start(out=ld0t[:], in_=cfg["ld0"]), incs=s_w.inc(16))
    if dcfg is not None:
        em.op("sync", lambda sy: sy.dma_start(out=wc[:], in_=dcfg["wc_dram"]),
              waits=(dcfg["wc_done"],), incs=s_w.inc(16))
        em.op("sync", lambda sy: sy.dma_start(out=bcol[:], in_=dcfg["bpk"]), incs=s_w.inc(16))
    wready = s_w.now()

    em.op("scalar", lambda s: s.activation(mst[:, 0:1], ld0t[:], AF.Sigmoid),
          waits=wready, incs=s_act.inc())
    em.op("scalar", lambda s: s.activation(mst[:, 1:2], mst[:, 0:1], AF.Ln),
          incs=s_act.inc())
    em.op("tensor", lambda t: t.matmul(pq[0][:, 0:1], ln.onesrow[:], mst[:, 1:2], start=True, stop=True),
          waits=(s_act.now(), ln.init_done), incs=s_pe.inc())
    em.op("scalar", lambda s: s.copy(lndc[:], pq[0][:, 0:1]), waits=s_pe.now(), incs=s_act.inc())
    em.bank_rel[id(pq[0])] = s_act.now()
    em.op("vector", lambda v: v.tensor_scalar_mul(m32[:], diffblk[:], lndc[:]),
          waits=s_act.now(), incs=s_dve.inc())
    em.op("vector", lambda v: v.tensor_add(m32[:], m32[:], offneg[:]), incs=s_dve.inc())
    em.op("scalar", lambda s: s.activation(m32[:], m32[:], AF.Exp), waits=s_dve.now(),
          incs=s_act.inc())
    mask_done = s_act.now()

    freed = [None, None]
    osb_freed = [None, None]
    prev_wo = None
    prev_ln_casts = None
    prev_vsb_free = None
    sc_order = list(range(NSC))
    if "moesum" in cfg:
        sc_order = [s_ for s_ in range(NSC) if s_ >= 4] + [s_ for s_ in range(NSC) if s_ < 4]
    for idx_, sc in enumerate(sc_order):
        b = idx_ % 2
        t0 = sc * 512
        w = [wready]
        if freed[b] is not None:
            w.append(freed[b])
        em.op("sync", lambda sy, t0=t0, b=b: sy.dma_start(
            out=x2sb[b][:], in_=x2.rearrange("(k p) t -> p k t", p=128)[:, :, t0:t0 + 512]),
            waits=tuple(w), incs=s_in.inc(16))
        if "madd" in cfg:
            mw = [s_in.now()]
            if prev_vsb_free is not None:
                mw.append(prev_vsb_free)
            em.op("sync", lambda sy, t0=t0: sy.dma_start(
                out=vsb[:], in_=cfg["madd"].rearrange("(k p) t -> p k t", p=128)[:, :, t0:t0 + 512]),
                waits=tuple(mw), incs=s_in.inc(16))
            em.op("vector", lambda v, b=b: v.tensor_add(
                vsb[:].rearrange("p a t -> p (a t)"), vsb[:].rearrange("p a t -> p (a t)"),
                x2sb[b][:].rearrange("p a t -> p (a t)")),
                waits=s_in.now(), incs=s_dve.inc())
            ln.emit(vsb[:], x2sb[b][:], v_ready=(s_dve.now(),))
        elif "moesum" in cfg:
            if sc < 4:
                mw = [s_in.now(), cfg["moe_done"]]
                if prev_vsb_free is not None:
                    mw.append(prev_vsb_free)
                em.op("sync", lambda sy, t0_=sc * 512: sy.dma_start(
                    out=vsb[:], in_=cfg["moesum"].rearrange("(k p) t -> p k t", p=128)[:, :, t0_:t0_ + 512]),
                    waits=tuple(mw), incs=s_in.inc(16))
                em.op("vector", lambda v, b=b: v.scalar_tensor_tensor(
                    vsb[:].rearrange("p a t -> p (a t)"), vsb[:].rearrange("p a t -> p (a t)"),
                    cfg["mgate_sb"][:], x2sb[b][:].rearrange("p a t -> p (a t)"),
                    OP.mult, OP.add), waits=s_in.now(), incs=s_dve.inc())
                ln.emit(vsb[:], x2sb[b][:], v_ready=(s_dve.now(),))
            else:
                ln.emit(x2sb[b][:], x2sb[b][:], v_ready=(s_in.now(),))
        cw = [s_in.now()]
        if prev_wo is not None:
            cw.append(prev_wo)
        if "madd" in cfg or "moesum" in cfg:
            cw.append(ln.s_dve.now())
        em.op("scalar", lambda s, b=b: s.copy(
            x2bf[:].rearrange("p a t -> p (a t)"), x2sb[b][:].rearrange("p a t -> p (a t)")),
            waits=tuple(cw), incs=s_act.inc())
        cast_done = s_act.now()

        def mk_ep(dst, act_fn):
            def ep(ot, pb, pe_spec):
                em.op("scalar", lambda s, ot=ot, pb=pb: act_fn(s, dst, ot, pb),
                      waits=pe_spec, incs=s_act.inc())
                return s_act.now()
            return ep
        emit_proj(em, pq, wq, 8, x2bf[:], 8,
                  mk_ep(qbf, lambda s, d, ot, pb: s.copy(d[:, ot, :], pb[:])),
                  s_pe, first_waits=(cast_done, mask_done))
        emit_proj(em, pq, wk, 8, x2bf[:], 8,
                  mk_ep(kbf, lambda s, d, ot, pb: s.copy(d[:, ot, :], pb[:])), s_pe)
        emit_proj(em, pq, wv, 8, x2bf[:], 8,
                  mk_ep(vbf_, lambda s, d, ot, pb: s.copy(d[:, ot, :], pb[:])), s_pe)
        emit_proj(em, pq, wg, 8, x2bf[:], 8,
                  mk_ep(gate, lambda s, d, ot, pb: s.activation(d[:, ot, :], pb[:], AF.Sigmoid)), s_pe)
        qkvg_done = s_act.now()

        prev_gate_tt = None
        gate_tts = [None, None]
        for c in range(4):
            cs = slice(c * 128, (c + 1) * 128)
            sp = pq[c % 2]
            trw = [qkvg_done]
            if gate_tts[c % 2] is not None:
                trw.append(gate_tts[c % 2])
            for ft in range(8):
                em.op("sync", lambda sy, c=c, ft=ft: sy.dma_start_transpose(
                    vtok[:, c % 2, ft * 128:(ft + 1) * 128], vbf_[:, ft, c * 128:(c + 1) * 128]),
                    waits=tuple(trw) if ft == 0 else (), incs=s_tr.inc(16))
            tr_done = s_tr.now()
            for k in range(8):
                wts0 = [qkvg_done]
                if em.bank_rel.get(id(sp)) is not None:
                    wts0.append(em.bank_rel[id(sp)])
                em.op("tensor", lambda t, k=k, c=c, sp=sp, cs=cs, b=b: t.matmul(
                    sp[:, 0:128], kbf[:, k, cs], qbf[:, k, cs],
                    start=(k == 0), stop=(k == 7)),
                    waits=tuple(wts0) if k == 0 else (),
                    incs=s_pe.inc() if k == 7 else ())
            em.op("vector", lambda v, c=c, sp=sp: v.tensor_tensor(
                s2b[c % 2][:], sp[:, 0:128], m32[:], OP.mult),
                waits=s_pe.now(), incs=s_dve.inc())
            s2_done = s_dve.now()
            em.bank_rel[id(sp)] = s2_done
            for ft in range(8):
                wts = []
                if ft == 0:
                    wts = [s2_done, tr_done]
                    if prev_gate_tt is not None:
                        wts.append(prev_gate_tt)
                em.op("tensor", lambda t, c=c, ft=ft: t.matmul(
                    pO[:, ft * 128:(ft + 1) * 128], vtok[:, c % 2, ft * 128:(ft + 1) * 128],
                    s2b[c % 2][:], start=True, stop=True),
                    waits=tuple(wts), incs=s_pe.inc() if ft == 7 else ())
            em.op("vector", lambda v, c=c, cs=cs: v.tensor_tensor(
                gob[:, :, cs], gate[:, :, cs],
                pO[:].rearrange("p (a t) -> p a t", a=8), OP.mult),
                waits=s_pe.now(), incs=s_dve.inc())
            prev_gate_tt = gate_tts[c % 2] = s_dve.now()

        pln = prev_ln_casts
        pvf = prev_vsb_free
        def ep_wo(ot, pb, pe_spec):
            wts = [pe_spec]
            if pln is not None and ot == 0:
                wts.append(pln)
            if pvf is not None and ot == 0:
                wts.append(pvf)
            if ("madd" in cfg or "moesum" in cfg) and ot == 0:
                wts.append(ln.s_dve.now())
            em.op("vector", lambda v, ot=ot, pb=pb, b=b: v.scalar_tensor_tensor(
                vsb[:, ot, :], x2sb[b][:, ot, :], 2.0, pb[:], OP.mult, OP.add),
                waits=tuple(wts), incs=s_dve.inc())
            return s_dve.now()
        emit_proj(em, pq, wo, 8, gob[:], 8, ep_wo, s_pe,
                  first_waits=(prev_gate_tt,))
        v_done = s_dve.now()
        prev_wo = s_pe.now()

        if dcfg is None:
            ow = [] if osb_freed[b] is None else [osb_freed[b]]
            apply_done = ln.emit(vsb[:], x2sb[b][:], v_ready=(v_done,), out_waits=tuple(ow))[0]
            prev_ln_casts = ln.last_casts
            prev_vsb_free = apply_done
            freed[b] = apply_done
            em.op("sync", lambda sy, t0=t0, b=b: sy.dma_start(
                out=out.rearrange("(k p) t -> p k t", p=128)[:, :, t0:t0 + 512], in_=x2sb[b][:]),
                waits=apply_done, incs=s_st.inc(16))
            osb_freed[b] = s_st.now()
        else:
            apply_done = ln.emit(vsb[:], x2sb[b][:], v_ready=(v_done,))[0]
            prev_ln_casts = ln.last_casts
            x3bf = qbf
            em.op("scalar", lambda s, b=b: s.copy(
                x3bf[:].rearrange("p a t -> p (a t)"), x2sb[b][:].rearrange("p a t -> p (a t)")),
                waits=apply_done, incs=s_act.inc())
            cast_d = s_act.now()
            def ep_d(ot, pb, pe_spec):
                em.op("vector", lambda v, ot=ot, pb=pb, b=b: v.scalar_tensor_tensor(
                    gate[:, ot, :], pb[:], bcol[:, ot:ot + 1], x2sb[b][:, ot, :],
                    OP.add, OP.add), waits=pe_spec, incs=s_dve.inc())
                return s_dve.now()
            emit_proj(em, pq, wc, 8, x3bf[:], 8, ep_d, s_pe, first_waits=(cast_d,))
            v4_done = s_dve.now()
            freed[b] = v4_done
            ow = [] if osb_freed[0] is None else [osb_freed[0]]
            dln = ln.emit(gate[:], vsb[:], v_ready=(v4_done,), out_waits=tuple(ow))
            prev_ln_casts = ln.last_casts
            d_gate_guard = dln[0]
            em.op("sync", lambda sy, t0=t0: sy.dma_start(
                out=out.rearrange("(k p) t -> p k t", p=128)[:, :, t0:t0 + 512], in_=vsb[:]),
                waits=dln[0], incs=s_st.inc(16))
            osb_freed[0] = prev_vsb_free = s_st.now()
    em.op("sync", lambda sy: sy.nop(), waits=s_st.now())


def host_l1attn_inputs(x2_fm, W_q, W_k, W_v, W_g, W_o, log_decay, consts):
    return {
        "x2": np.ascontiguousarray(x2_fm, np.float32),
        "wqT": host_wT(W_q), "wkT": host_wT(W_k), "wvT": host_wT(W_v),
        "wgT": host_wT(W_g), "woT": host_wT(W_o),
        "ld0": np.asarray(log_decay[0], np.float32).reshape(1, 1),
        "diffblk": consts["diffblk"], "offneg": consts["offneg"],
    }




def build_l1ffn(T=4096):
    nc = bass.Bass("TRN2", target_bir_lowering=False, debug=False)
    x3 = nc.dram_tensor("x3", [1024, T], F32, kind="ExternalInput").ap()
    wT = nc.dram_tensor("wT", [1024, 1024], F32, kind="ExternalInput").ap()
    bpk = nc.dram_tensor("bpk", [128, 8], F32, kind="ExternalInput").ap()
    ldT = nc.dram_tensor("ldT", [100, 4], F32, kind="ExternalInput").ap()
    luT = nc.dram_tensor("luT", [4, 50], F32, kind="ExternalInput").ap()
    uaT = nc.dram_tensor("uaT", [50, 1024], F32, kind="ExternalInput").ap()
    da = nc.dram_tensor("da", [100, 1024], F32, kind="ExternalInput").ap()
    out = nc.dram_tensor("out", [1024, T], F32, kind="ExternalOutput").ap()
    es = ExitStack()
    with es:
        em = Emit(nc, es)
        emit_l1ffn(em, dict(x3=x3, wT=wT, bpk=bpk, ldT=ldT, luT=luT, uaT=uaT,
                            da=da, out=out, T=T))
        em.finish()
    return nc


def emit_l1ffn(em, cfg):
    nc = em.nc
    T = cfg["T"]
    NSC = T // 512
    x3, wT, bpk, out = cfg["x3"], cfg["wT"], cfg["bpk"], cfg["out"]

    wTf = em.sb("d_wTf", [128, 8, 1024], F32)
    wc = em.sb("d_wc", [128, 8, 1024], BF16)
    bcol = em.sb("d_bcol", [128, 8], F32)
    ones128 = em.sb("d_ones128", [128, 1], BF16)
    onesrow = em.sb("d_onesrow", [1, 128], F32)
    t1b = em.sb("d_t1b", [4, 1024], BF16)
    t2b = em.sb("d_t2b", [50, 1024], BF16)
    ldTb = em.sb("d_ldTb", [100, 4], BF16)
    luTb = em.sb("d_luTb", [4, 50], BF16)
    uaTb = em.sb("d_uaTb", [50, 1024], BF16)
    dab = em.sb("d_dab", [100, 1024], BF16)
    x3sb = [em.sb(f"d_x3_{i}", [128, 8, 512], F32) for i in range(2)]
    x3bf = [em.sb(f"d_x3b_{i}", [128, 8, 512], BF16) for i in range(2)]
    vsb = [em.sb(f"d_v_{i}", [128, 8, 512], F32) for i in range(2)]
    osb = [em.sb(f"d_o_{i}", [128, 8, 512], F32) for i in range(2)]

    pp = [em.ps(f"d_pp{i}", [128, 512]) for i in range(2)]
    pL = em.ps("d_pL", [128, 1024])
    ln = cfg.get("ln") or LNKit(em, "d_ln")

    s_in = em.sem("d_in")
    s_w = em.sem("d_w")
    s_pe = em.sem("d_pe")
    s_dve = em.sem("d_dve")
    s_act = em.sem("d_act")
    s_st = em.sem("d_st")

    em.op("gpsimd", lambda g: g.dma_start(out=wTf[:], in_=wT.rearrange("(k p) o -> p k o", p=128)), incs=s_w.inc(16))
    em.op("gpsimd", lambda g: g.dma_start(out=ldTb[:], in_=cfg["ldT"]), incs=s_w.inc(16))
    em.op("gpsimd", lambda g: g.dma_start(out=luTb[:], in_=cfg["luT"]), incs=s_w.inc(16))
    em.op("gpsimd", lambda g: g.dma_start(out=uaTb[:], in_=cfg["uaT"]), incs=s_w.inc(16))
    em.op("gpsimd", lambda g: g.dma_start(out=dab[:], in_=cfg["da"]), incs=s_w.inc(16))
    em.op("gpsimd", lambda g: g.dma_start(out=bcol[:], in_=bpk), incs=s_w.inc(16))
    wready = s_w.now()

    for h in range(2):
        em.op("tensor", lambda t, h=h: t.matmul(pL[0:4, h * 512:(h + 1) * 512], ldTb[:], dab[:, h * 512:(h + 1) * 512], start=True, stop=True),
              waits=wready, incs=s_pe.inc())
    em.op("scalar", lambda s: s.copy(t1b[:], pL[0:4, :]), waits=s_pe.now(), incs=s_act.inc())
    for h in range(2):
        em.op("tensor", lambda t, h=h: t.matmul(pL[0:50, h * 512:(h + 1) * 512], luTb[:], t1b[:, h * 512:(h + 1) * 512], start=True, stop=True),
              waits=s_act.now(), incs=s_pe.inc())
    em.op("scalar", lambda s: s.copy(t2b[:], pL[0:50, :]), waits=s_pe.now(), incs=s_act.inc())
    t2_done = s_act.now()
    for m in range(8):
        for h in range(2):
            em.op("tensor", lambda t, m=m, h=h: t.matmul(
                pL[:, h * 512:(h + 1) * 512], t2b[:, m * 128:(m + 1) * 128], uaTb[:, h * 512:(h + 1) * 512], start=True, stop=True),
                waits=(t2_done if m == 0 else s_dve.now()) if h == 0 else (), incs=s_pe.inc() if h == 1 else ())
        em.op("vector", lambda v, m=m: v.tensor_add(wc[:, m, :], wTf[:, m, :], pL[:, :]),
              waits=s_pe.now(), incs=s_dve.inc())

    freed = [None, None]
    osb_freed = [None, None]
    prev_sums = None
    for sc in range(NSC):
        b = sc % 2
        t0 = sc * 512
        w = [wready]
        if freed[b] is not None:
            w.append(freed[b])
        em.op("sync", lambda sy, t0=t0, b=b: sy.dma_start(
            out=x3sb[b][:], in_=x3.rearrange("(k p) t -> p k t", p=128)[:, :, t0:t0 + 512]),
            waits=tuple(w), incs=s_in.inc(16))
        ld_here = s_in.now()
        em.op("scalar", lambda s, b=b: s.copy(
            x3bf[b][:].rearrange("p a t -> p (a t)"), x3sb[b][:].rearrange("p a t -> p (a t)")),
            waits=ld_here, incs=s_act.inc())
        cast_done = s_act.now()
        for ot in range(8):
            pb = pp[ot % 2]
            for k in range(8):
                wts = ()
                if k == 0:
                    wts = (cast_done, s_dve.now())
                em.op("tensor", lambda t, b=b, ot=ot, k=k, pb=pb: t.matmul(
                    pb[:], wc[:, k, ot * 128:(ot + 1) * 128], x3bf[b][:, k, :],
                    start=(k == 0), stop=(k == 7)),
                    waits=wts, incs=s_pe.inc() if k == 7 else ())
            em.op("vector", lambda v, b=b, ot=ot, pb=pb: v.scalar_tensor_tensor(
                vsb[b][:, ot, :], pb[:], bcol[:, ot:ot + 1], x3sb[b][:, ot, :], OP.add, OP.add),
                waits=s_pe.now(), incs=s_dve.inc())
        v_done = s_dve.now()
        ow = [] if osb_freed[b] is None else [osb_freed[b]]
        apply_done = ln.emit(vsb[b][:], osb[b][:], v_ready=(v_done,), out_waits=tuple(ow))[0]
        freed[b] = apply_done
        em.op("sync", lambda sy, t0=t0, b=b: sy.dma_start(
            out=out.rearrange("(k p) t -> p k t", p=128)[:, :, t0:t0 + 512], in_=osb[b][:]),
            waits=apply_done, incs=s_st.inc(16))
        osb_freed[b] = s_st.now()
    em.op("sync", lambda sy: sy.nop(), waits=s_st.now())


def host_l1ffn_inputs(x3_fm, W, b, lora_down, lora_up, down_aux, up_aux):
    return {
        "x3": np.ascontiguousarray(x3_fm, dtype=np.float32),
        "wT": host_wT(W),
        "bpk": np.ascontiguousarray(np.asarray(b, np.float32).reshape(8, 128).T),
        "ldT": host_wT(lora_down),
        "luT": host_wT(lora_up),
        "uaT": host_wT(up_aux),
        "da": np.ascontiguousarray(np.asarray(down_aux, np.float32)),
        "out": None,
    }








def kernel_unfused(**inputs):
    T = 4096
    consts = host_consts()
    u = np.asarray(inputs["L0_attn_u"], np.float32)

    common_a = host_mamba_inputs_common(
        inputs["L0_attn_in_proj_w"], inputs["L0_attn_conv_w"], inputs["L0_attn_conv_b"],
        inputs["L0_attn_dt_bias"], inputs["L0_attn_A_log"], inputs["L0_attn_D"],
        inputs["L0_attn_rms_w"], inputs["L0_attn_out_w"], consts)
    in_maps = []
    for c in range(8):
        bb, t0 = c // 4, (c % 4) * T
        if t0 == 0:
            uh = np.concatenate([np.zeros((128, 1024), np.float32), u[bb, 0:T]], axis=0)
        else:
            uh = u[bb, t0 - 128:t0 + T]
        m = dict(common_a)
        m["u"] = np.ascontiguousarray(uh.T)
        m["m2T"], m["selcol"] = host_core_masks(c)
        in_maps.append(m)
    nc_a = build_mamba(T)
    res_a = run_bass_kernel_spmd(nc_a, in_maps, core_ids=list(range(8)))
    x1 = [res_a.results[c]["x1"] for c in range(8)]

    x1m = np.ascontiguousarray(x1[0][:, 0:2048])
    in_maps = [host_moe_inputs(
        x1m, e, inputs["L0_ffn_gate_up_proj"], inputs["L0_ffn_gate_up_proj_bias"],
        inputs["L0_ffn_down_proj"], inputs["L0_ffn_down_proj_bias"],
        inputs["L0_ffn_router_indices"], inputs["L0_ffn_routing_weights"]) for e in range(8)]
    nc_b = build_moe()
    res_b = run_bass_kernel_spmd(nc_b, in_maps, core_ids=list(range(8)))
    moe = res_b.results[0]["moe"]

    common_c = host_l1attn_inputs_common(
        inputs["L1_attn_W_q"], inputs["L1_attn_W_k"], inputs["L1_attn_W_v"],
        inputs["L1_attn_W_g"], inputs["L1_attn_W_o"], inputs["L1_attn_log_decay"], consts)
    in_maps = []
    for c in range(8):
        m = dict(common_c)
        m["x2"] = x1[c]
        madd = np.zeros((1024, T), np.float32)
        if c == 0:
            madd[:, 0:2048] = moe
        m["madd"] = madd
        in_maps.append(m)
    nc_c = build_l1attn(T, with_madd=True)
    res_c = run_bass_kernel_spmd(nc_c, in_maps, core_ids=list(range(8)))
    x3 = [res_c.results[c]["out"] for c in range(8)]

    common_d = host_l1ffn_inputs_common(
        inputs["L1_ffn_W"], inputs["L1_ffn_b"], inputs["L1_ffn_lora_down"],
        inputs["L1_ffn_lora_up"], inputs["L1_ffn_down_aux"], inputs["L1_ffn_up_aux"])
    in_maps = []
    for c in range(8):
        m = dict(common_d)
        m["x3"] = x3[c]
        in_maps.append(m)
    nc_d = build_l1ffn(T)
    res_d = run_bass_kernel_spmd(nc_d, in_maps, core_ids=list(range(8)))

    out = np.empty((2, 16384, 1024), np.float32)
    for c in range(8):
        bb, t0 = c // 4, (c % 4) * T
        out[bb, t0:t0 + T] = res_d.results[c]["out"].T
    return out


def host_mamba_inputs_common(in_proj_w, conv_w, conv_b, dt_bias, A_log, D, rms_w,
                             out_w, consts):
    W = np.asarray(in_proj_w, np.float32)
    wT = W.T
    wxbd = np.ascontiguousarray(np.concatenate(
        [wT[:, 2048:4096], wT[:, 4096:4160], wT[:, 4224:4240]], axis=1))
    wzc = np.ascontiguousarray(np.concatenate(
        [wT[:, 0:2048], wT[:, 4160:4224]], axis=1))
    cwf = np.asarray(conv_w, np.float32)
    cbf = np.asarray(conv_b, np.float32)
    cw18 = np.zeros((128, 18, 4), np.float32)
    cb18 = np.zeros((128, 18), np.float32)
    cw18[:, 0:16, :] = cwf[0:2048].reshape(16, 128, 4).transpose(1, 0, 2)
    cb18[:, 0:16] = cbf[0:2048].reshape(16, 128).T
    cw18[0:64, 16, :] = cwf[2048:2112]
    cb18[0:64, 16] = cbf[2048:2112]
    cw18[0:64, 17, :] = cwf[2112:2176]
    cb18[0:64, 17] = cbf[2112:2176]
    return {
        "wxbd": wxbd, "wzc": wzc, "owT": host_wT(out_w),
        "cwcol": cw18, "cbcol": cb18,
        "dtbcol": np.asarray(dt_bias, np.float32).reshape(16, 1),
        "alog": np.asarray(A_log, np.float32).reshape(1, 16),
        "dvec": np.asarray(D, np.float32).reshape(1, 16),
        "rmspk": np.ascontiguousarray(np.asarray(rms_w, np.float32).reshape(16, 128).T),
        "triuinc": consts["triuinc"], "upperneg": consts["upperneg"],
        "ident": consts["ident"],
    }


def host_core_masks(core_id):
    slot = core_id % 4
    m2 = np.zeros((8, 8), np.float32)
    sel = np.zeros((8, 1), np.float32)
    base = (core_id // 4) * 4
    for j in range(base, base + slot):
        sel[j, 0] = 1.0
        for i in range(j + 1, base + slot):
            m2[j, i] = 1.0
    return np.ascontiguousarray(m2.T), sel


def host_l1attn_inputs_common(W_q, W_k, W_v, W_g, W_o, log_decay, consts):
    return {
        "wqT": host_wT(W_q), "wkT": host_wT(W_k), "wvT": host_wT(W_v),
        "wgT": host_wT(W_g), "woT": host_wT(W_o),
        "ld0": np.ascontiguousarray(np.asarray(log_decay, np.float32)[0:1].reshape(1, 1)),
        "diffblk": consts["diffblk"], "offneg": consts["offneg"],
    }


def host_l1ffn_inputs_common(W, b, lora_down, lora_up, down_aux, up_aux):
    return {
        "wT": host_wT(W),
        "bpk": np.ascontiguousarray(np.asarray(b, np.float32).reshape(8, 128).T),
        "ldT": host_wT(lora_down),
        "luT": host_wT(lora_up),
        "uaT": host_wT(up_aux),
        "da": np.ascontiguousarray(np.asarray(down_aux, np.float32)),
    }









def build_fused(T=4096):
    nc = bass.Bass("TRN2", target_bir_lowering=False, debug=False)
    a = {"T": T}
    a["u"] = nc.dram_tensor("u", [1024, 128 + T], F32, kind="ExternalInput").ap()
    a["wxbd"] = nc.dram_tensor("wxbd", [1024, 2128], F32, kind="ExternalInput").ap()
    a["wzc"] = nc.dram_tensor("wzc", [1024, 2112], F32, kind="ExternalInput").ap()
    a["owT"] = nc.dram_tensor("owT", [2048, 1024], F32, kind="ExternalInput").ap()
    a["cwcol"] = nc.dram_tensor("cwcol", [128, 18, 4], F32, kind="ExternalInput").ap()
    a["cbcol"] = nc.dram_tensor("cbcol", [128, 18], F32, kind="ExternalInput").ap()
    a["dtbcol"] = nc.dram_tensor("dtbcol", [16, 1], F32, kind="ExternalInput").ap()
    a["alog"] = nc.dram_tensor("alog", [1, 16], F32, kind="ExternalInput").ap()
    a["dvec"] = nc.dram_tensor("dvec", [1, 16], F32, kind="ExternalInput").ap()
    a["rmspk"] = nc.dram_tensor("rmspk", [128, 16], F32, kind="ExternalInput").ap()
    a["triuinc"] = nc.dram_tensor("triuinc", [128, 128], F32, kind="ExternalInput").ap()
    a["upperneg"] = nc.dram_tensor("upperneg", [128, 128], F32, kind="ExternalInput").ap()
    a["ident"] = nc.dram_tensor("ident", [128, 128], F32, kind="ExternalInput").ap()
    a["m2T"] = nc.dram_tensor("m2T", [8, 8], F32, kind="ExternalInput").ap()
    a["selcol"] = nc.dram_tensor("selcol", [8, 1], F32, kind="ExternalInput").ap()
    a["x1"] = nc.dram_tensor("x1", [1024, T], F32).ap()
    a["sp_xtok"] = nc.dram_tensor("sp_xtok", [T, 2048], BF16).ap()
    a["sp_btok"] = nc.dram_tensor("sp_btok", [T, 64], BF16).ap()
    a["sp_bfeat"] = nc.dram_tensor("sp_bfeat", [64, T], BF16).ap()
    a["sp_dt"] = nc.dram_tensor("sp_dt", [16, T], F32).ap()
    a["w8d"] = nc.dram_tensor("w8d", [8, 16], F32).ap()
    a["cum_dram"] = nc.dram_tensor("cum_dram", [16, 128], F32).ap()
    a["ow_bf"] = nc.dram_tensor("ow_bf", [8, 128, 16, 128], BF16).ap()
    a["ag_in"] = nc.dram_tensor("ag_in", [65, 2048], F32).ap()
    a["ag_out"] = nc.dram_tensor("ag_out", [520, 2048], F32).ap()
    b = {}
    for w in ("gwT", "uwT", "dwT"):
        b[w] = nc.dram_tensor(w, [1024, 1024], F32, kind="ExternalInput").ap()
    for bn in ("gbpk", "ubpk", "dbpk"):
        b[bn] = nc.dram_tensor(bn, [128, 8], F32, kind="ExternalInput").ap()
    b["ridx"] = nc.dram_tensor("ridx", [128, 16, 2], I32, kind="ExternalInput").ap()
    b["rwcol"] = nc.dram_tensor("rwcol", [128, 16], F32, kind="ExternalInput").ap()
    b["ecol"] = nc.dram_tensor("ecol", [128, 1], F32, kind="ExternalInput").ap()
    b["mgatec"] = nc.dram_tensor("mgatec", [128, 1], F32, kind="ExternalInput").ap()
    b["x1sc"] = a["x1"]
    b["ar1_in"] = nc.dram_tensor("ar1_in", [1024, 2048], F32).ap()
    b["ar1_out"] = nc.dram_tensor("ar1_out", [1024, 2048], F32).ap()
    b["m_dram"] = nc.dram_tensor("m_dram", [1, 2048], F32).ap()
    b["ar_in"] = nc.dram_tensor("ar_in", [1024, 2048], F32).ap()
    b["ar_out"] = nc.dram_tensor("ar_out", [1024, 2048], F32).ap()
    c = {"T": T}
    c["x2"] = a["x1"]
    for w in ("wqT", "wkT", "wvT", "wgT", "woT"):
        c[w] = nc.dram_tensor(w, [1024, 1024], F32, kind="ExternalInput").ap()
    c["ld0"] = nc.dram_tensor("ld0", [1, 1], F32, kind="ExternalInput").ap()
    c["diffblk"] = nc.dram_tensor("diffblk", [128, 128], F32, kind="ExternalInput").ap()
    c["offneg"] = nc.dram_tensor("offneg", [128, 128], F32, kind="ExternalInput").ap()
    c["moesum"] = b["ar_out"]
    c["out"] = nc.dram_tensor("out", [1024, T], F32, kind="ExternalOutput").ap()
    d = {"T": T}
    d["wT"] = nc.dram_tensor("wT", [1024, 1024], F32, kind="ExternalInput").ap()
    d["bpk"] = nc.dram_tensor("bpk", [128, 8], F32, kind="ExternalInput").ap()
    d["ldT"] = nc.dram_tensor("ldT", [100, 4], F32, kind="ExternalInput").ap()
    d["luT"] = nc.dram_tensor("luT", [4, 50], F32, kind="ExternalInput").ap()
    d["uaT"] = nc.dram_tensor("uaT", [50, 1024], F32, kind="ExternalInput").ap()
    d["da"] = nc.dram_tensor("da", [100, 1024], F32, kind="ExternalInput").ap()
    d["wc_dram"] = nc.dram_tensor("wc_bf", [128, 8, 1024], BF16).ap()

    es = ExitStack()
    with es:
        em = Emit(nc, es)
        a["bcast"] = dict(ar1_in=b["ar1_in"], ar1_out=b["ar1_out"], mgatec=b["mgatec"])
        with ExitStack() as esA:
            em.cur_es = esA
            emit_mamba(em, a)
        s_cw = em.sem("f_cw")
        c["w_bf"] = {}
        for w in ("wqT", "wkT", "wvT", "wgT", "woT"):
            dst = nc.dram_tensor(w + "_bf", [1024, 1024], BF16).ap()
            c["w_bf"][w] = dst
            em.op("gpsimd", lambda g, dst=dst, src=c[w]: g.dma_start(out=dst, in_=src),
                  incs=s_cw.inc(16))
        c["w_bf_done"] = s_cw.now()
        specsA = [s.now() for s in em.all_sems if s.n > 0 and s is not s_cw]
        for e in ENGS:
            em.pending[e] = list(specsA)
        em.bank_rel = {}
        b["bcast_done"] = em.bcast_done
        with ExitStack() as esB:
            em.cur_es = esB
            d["wc_done"] = emit_wc_prep(em, d)
            moe_done = emit_moe(em, b)
        cc_sem = em._moe_cc
        specs = [s.now() for s in em.all_sems if s.n > 0 and s is not cc_sem]
        for e in ENGS:
            em.pending[e] = list(specs)
        em.bank_rel = {}
        c["moe_done"] = moe_done
        with ExitStack() as esC:
            em.cur_es = esC
            mg = em.sb("c_mgate", [128, 1], F32)
            sw = em.sem("c_mg")
            em.op("gpsimd", lambda g: g.dma_start(out=mg[:], in_=b["mgatec"]), incs=sw.inc(16))
            em.pending["vector"] = list(em.pending.get("vector", [])) + [sw.now()]
            c["mgate_sb"] = mg
            c["d"] = d
            emit_l1attn(em, c)
        em.cur_es = es
        em.finish()
    return nc


def kernel(**inputs):
    T = 4096
    consts = host_consts()
    u = np.asarray(inputs["L0_attn_u"], np.float32)

    common_a = host_mamba_inputs_common(
        inputs["L0_attn_in_proj_w"], inputs["L0_attn_conv_w"], inputs["L0_attn_conv_b"],
        inputs["L0_attn_dt_bias"], inputs["L0_attn_A_log"], inputs["L0_attn_D"],
        inputs["L0_attn_rms_w"], inputs["L0_attn_out_w"], consts)
    common_c = host_l1attn_inputs_common(
        inputs["L1_attn_W_q"], inputs["L1_attn_W_k"], inputs["L1_attn_W_v"],
        inputs["L1_attn_W_g"], inputs["L1_attn_W_o"], inputs["L1_attn_log_decay"], consts)
    common_d = host_l1ffn_inputs_common(
        inputs["L1_ffn_W"], inputs["L1_ffn_b"], inputs["L1_ffn_lora_down"],
        inputs["L1_ffn_lora_up"], inputs["L1_ffn_down_aux"], inputs["L1_ffn_up_aux"])

    in_maps = []
    for cid in range(8):
        bb, t0 = cid // 4, (cid % 4) * T
        if t0 == 0:
            uh = np.concatenate([np.zeros((128, 1024), np.float32), u[bb, 0:T]], axis=0)
        else:
            uh = u[bb, t0 - 128:t0 + T]
        m = dict(common_a)
        m.update(common_c)
        m.update(common_d)
        m["u"] = np.ascontiguousarray(uh.T)
        m["m2T"], m["selcol"] = host_core_masks(cid)
        m.update(host_moe_inputs(
            None, cid, inputs["L0_ffn_gate_up_proj"], inputs["L0_ffn_gate_up_proj_bias"],
            inputs["L0_ffn_down_proj"], inputs["L0_ffn_down_proj_bias"],
            inputs["L0_ffn_router_indices"], inputs["L0_ffn_routing_weights"]))
        m["mgatec"] = np.full((128, 1), 1.0 if cid == 0 else 0.0, np.float32)
        in_maps.append(m)

    nc = build_fused(T)
    res = run_bass_kernel_spmd(nc, in_maps, core_ids=list(range(8)))
    out = np.empty((2, 16384, 1024), np.float32)
    for cid in range(8):
        bb, t0 = cid // 4, (cid % 4) * T
        out[bb, t0:t0 + T] = res.results[cid]["out"].T
    return out


# version 09 (final)
# speedup vs baseline: 1.1975x; 1.0090x over previous
import numpy as np
from contextlib import ExitStack
import concourse.bass as bass
import concourse.mybir as mybir
from concourse.bass_utils import run_bass_kernel_spmd

F32 = mybir.dt.float32
BF16 = mybir.dt.bfloat16
I32 = mybir.dt.int32
AF = mybir.ActivationFunctionType
OP = mybir.AluOpType
AX = mybir.AxisListType

ENGS = ("tensor", "vector", "scalar", "gpsimd", "sync")


class Sem:
    __slots__ = ("h", "n", "name")

    def __init__(self, h, name):
        self.h = h
        self.n = 0
        self.name = name

    def inc(self, k=1):
        self.n += k
        return (self, k)

    def now(self):
        return (self, self.n)

    def at(self, v):
        return (self, v)


class Emit:

    def __init__(self, nc, es):
        self.nc = nc
        self.es = es
        self.cur_es = es
        self.ops = {e: [] for e in ENGS}
        self.last_wait = {}
        self.bank_rel = {}
        self.all_sems = []
        self.pending = {}

    def sem(self, name):
        h = self.es.enter_context(self.nc.semaphore(name))
        s = Sem(h, name)
        self.all_sems.append(s)
        return s

    def sb(self, name, shape, dtype):
        return self.cur_es.enter_context(self.nc.sbuf_tensor(name, shape, dtype))

    def ps(self, name, shape, dtype=F32):
        return self.cur_es.enter_context(self.nc.psum_tensor(name, shape, dtype))

    def barrier(self):
        specs = [s.now() for s in self.all_sems if s.n > 0]
        for e in ENGS:
            self.pending[e] = list(specs)
        self.bank_rel = {}

    AUTO_EDGE_ENGS = ("vector", "scalar", "gpsimd")
    _ASYNC_NAMES = ("dma_start", "dma_start_transpose", "collective_compute",
                    "indirect_dma_start", "wait_ge", "nop")

    def _auto_sem(self, eng):
        key = f"__auto_{eng}"
        if not hasattr(self, "_autos"):
            self._autos = {}
        if eng not in self._autos:
            self._autos[eng] = self.sem(key)
        return self._autos[eng]

    def op(self, eng, fn, waits=(), incs=()):
        if waits and isinstance(waits[0], Sem):
            waits = (waits,)
        if incs and isinstance(incs[0], Sem):
            incs = (incs,)
        waits = tuple(waits)
        if self.pending.get(eng):
            waits = tuple(self.pending.pop(eng)) + waits
        names = fn.__code__.co_names
        is_async = any(nm in names for nm in self._ASYNC_NAMES)
        auto_inc = None
        if eng in self.AUTO_EDGE_ENGS and not is_async:
            edge = getattr(self, "_edge", {}).get(eng)
            if edge is not None:
                waits = waits + (edge,)
            if not incs:
                a = self._auto_sem(eng)
                auto_inc = (a, 1, a.inc(1) and a.n)
            if not hasattr(self, "_edge"):
                self._edge = {}
            if incs:
                sem, k = incs[-1]
                self._edge[eng] = (sem, sem.n)
            else:
                self._edge[eng] = (self._autos[eng], self._autos[eng].n)
        incs = tuple((sem, k, sem.n) for (sem, k) in incs)
        if auto_inc is not None:
            incs = incs + ((auto_inc[0], 1, auto_inc[2]),)
        self.ops[eng].append((waits, fn, incs))

    def replay(self, eng, engine_obj):
        for waits, fn, incs in self.ops[eng]:
            for (sem, val) in waits:
                key = (eng, id(sem))
                if val > 0 and self.last_wait.get(key, -1) < val:
                    engine_obj.wait_ge(sem.h, val)
                    self.last_wait[key] = val
            ins = fn(engine_obj)
            for (sem, k, val_after) in incs:
                ins = ins.then_inc(sem.h, k)

    def finish(self):
        with self.nc.Block() as block:
            block.tensor(lambda t: self.replay("tensor", t))
            block.vector(lambda v: self.replay("vector", v))
            block.scalar(lambda s: self.replay("scalar", s))
            block.gpsimd(lambda g: self.replay("gpsimd", g))
            block.sync(lambda sy: self.replay("sync", sy))
        self.ops = {e: [] for e in ENGS}


def ceil_div(a, b):
    return (a + b - 1) // b



def host_wT(w):
    return np.ascontiguousarray(np.asarray(w, dtype=np.float32).T)


def host_fm(x2d):
    return np.ascontiguousarray(np.asarray(x2d, dtype=np.float32).T)


CONST_TRIL_INC = None


def host_consts():
    ident = np.eye(128, dtype=np.float32)
    trilinc = np.tril(np.ones((128, 128), np.float32))
    triuinc = np.triu(np.ones((128, 128), np.float32))
    upperneg = np.where(np.arange(128)[:, None] > np.arange(128)[None, :],
                        np.float32(-1e30), np.float32(0.0))
    tp = np.arange(128)
    same_blk = (tp[:, None] // 32) == (tp[None, :] // 32)
    diff = tp[None, :] - tp[:, None]
    diffblk = np.where(same_blk & (diff >= 0), diff.astype(np.float32), np.float32(0.0))
    offneg = np.where(same_blk & (diff >= 0), np.float32(0.0), np.float32(-1e30))
    return dict(ident=ident, trilinc=trilinc.astype(np.float32), triuinc=triuinc,
                upperneg=upperneg, diffblk=diffblk, offneg=offneg)




def emit_proj(em, psum_pair, w_sb, ktiles, act3, otiles, epilogue, s_pe,
              first_waits=(), extra_k0_waits=None, N=512, start_acc=True):
    rel = [em.bank_rel.get(id(psum_pair[0])), em.bank_rel.get(id(psum_pair[1]))]
    for ot in range(otiles):
        pb = psum_pair[ot % 2]
        for k in range(ktiles):
            wts = []
            if k == 0:
                if ot < 2:
                    wts.extend(first_waits)
                if rel[ot % 2] is not None:
                    wts.append(rel[ot % 2])
                if extra_k0_waits is not None:
                    wts.extend(extra_k0_waits(ot))
            em.op("tensor", lambda t, ot=ot, k=k, pb=pb: t.matmul(
                pb[:, 0:N], w_sb[:, k, ot * 128:(ot + 1) * 128], act3[:, k, :],
                start=(k == 0 and start_acc), stop=(k == ktiles - 1)),
                waits=tuple(wts), incs=s_pe.inc() if k == ktiles - 1 else ())
        rel[ot % 2] = epilogue(ot, pb, s_pe.now())
    em.bank_rel[id(psum_pair[0])] = rel[0]
    em.bank_rel[id(psum_pair[1])] = rel[1]


class LNKit:

    def __init__(self, em, name, N=512, psums=None, bufs=None):
        self.em = em
        self.N = N
        if bufs is None:
            self.vbf = em.sb(f"{name}_vbf", [128, 8, N], BF16)[:]
            self.v2bf = em.sb(f"{name}_v2bf", [128, 8, N], BF16)[:]
        else:
            self.vbf, self.v2bf = bufs
        self.stat = em.sb(f"{name}_stat", [1, 4, N], F32)
        self.ones128 = em.sb(f"{name}_o128", [128, 1], BF16)
        self.onesrow = em.sb(f"{name}_orow", [1, 128], F32)
        if psums is None:
            self.pstA = em.ps(f"{name}_pstA", [1, N])[:]
            self.pstB = em.ps(f"{name}_pstB", [1, N])[:]
            pbc = em.ps(f"{name}_pbc", [128, 2 * N])
            self.pbc_mean = pbc[:, 0:N]
            self.pbc_rstd = pbc[:, N:2 * N]
            self.rel_keys = ()
        else:
            self.pstA, self.pstB, self.pbc_mean, self.pbc_rstd, self.rel_keys = psums
        self.s_pe = em.sem(f"{name}_pe")
        self.s_dve = em.sem(f"{name}_dve")
        self.s_act = em.sem(f"{name}_act")
        em.op("vector", lambda v: v.memset(self.ones128[:], 1.0), incs=self.s_dve.inc())
        em.op("vector", lambda v: v.memset(self.onesrow[:], 1.0), incs=self.s_dve.inc())
        self.init_done = self.s_dve.now()
        self.prev_sums = None
        self.prev_bc_readers = None

    def emit(self, vsb, out_f32, out_bf=None, v_ready=(), out_waits=()):
        em, N = self.em, self.N
        cw = list(v_ready)
        if self.prev_sums is not None:
            cw.append(self.prev_sums)
        em.op("scalar", lambda s: s.copy(
            self.vbf.rearrange("p a t -> p (a t)"), vsb.rearrange("p a t -> p (a t)")),
            waits=tuple(cw), incs=self.s_act.inc())
        em.op("scalar", lambda s: s.activation(
            self.v2bf.rearrange("p a t -> p (a t)"), vsb.rearrange("p a t -> p (a t)"), AF.Square),
            incs=self.s_act.inc())
        casts_done = self.last_casts = self.s_act.now()
        k0w = [casts_done, self.s_dve.now()]
        for key in self.rel_keys:
            if em.bank_rel.get(id(key)) is not None:
                k0w.append(em.bank_rel[id(key)])
        for k in range(8):
            em.op("tensor", lambda t, k=k: t.matmul(self.pstA, self.ones128[:], self.vbf[:, k, :],
                                                    start=(k == 0), stop=(k == 7)),
                  waits=tuple(k0w) if k == 0 else ())
        for k in range(8):
            em.op("tensor", lambda t, k=k: t.matmul(self.pstB, self.ones128[:], self.v2bf[:, k, :],
                                                    start=(k == 0), stop=(k == 7)),
                  incs=self.s_pe.inc() if k == 7 else ())
        self.prev_sums = sums_done = self.s_pe.now()
        st = self.stat
        em.op("vector", lambda v: v.tensor_scalar_mul(st[:, 0, :], self.pstA, 1.0 / 1024),
              waits=sums_done, incs=self.s_dve.inc())
        em.op("vector", lambda v: v.tensor_scalar_mul(st[:, 1, :], self.pstB, 1.0 / 1024),
              incs=self.s_dve.inc())
        em.op("vector", lambda v: v.tensor_mul(st[:, 2, :], st[:, 0, :], st[:, 0, :]),
              incs=self.s_dve.inc())
        em.op("vector", lambda v: v.scalar_tensor_tensor(st[:, 2, :], st[:, 1, :], 1e-5, st[:, 2, :], OP.add, OP.subtract),
              incs=self.s_dve.inc())
        em.op("scalar", lambda s: s.activation(st[:, 3, :], st[:, 2, :], AF.Ln),
              waits=self.s_dve.now(), incs=self.s_act.inc())
        em.op("scalar", lambda s: s.activation(st[:, 1, :], st[:, 3, :], AF.Exp, scale=-0.5),
              incs=self.s_act.inc())
        rstd_done = self.s_act.now()
        bw = [rstd_done]
        if self.prev_bc_readers is not None:
            bw.append(self.prev_bc_readers)
        em.op("tensor", lambda t: t.matmul(self.pbc_mean, self.onesrow[:], st[:, 0, :], start=True, stop=True),
              waits=tuple(bw))
        em.op("tensor", lambda t: t.matmul(self.pbc_rstd, self.onesrow[:], st[:, 1, :], start=True, stop=True),
              incs=self.s_pe.inc())
        bc_done = self.s_pe.now()
        tgt = out_f32 if out_f32 is not None else out_bf
        for ot in range(8):
            em.op("vector", lambda v, ot=ot: v.tensor_sub(tgt[:, ot, :], vsb[:, ot, :], self.pbc_mean),
                  waits=(bc_done,) + tuple(out_waits) if ot == 0 else (), incs=self.s_dve.inc())
            em.op("vector", lambda v, ot=ot: v.tensor_mul(tgt[:, ot, :], tgt[:, ot, :], self.pbc_rstd),
                  incs=self.s_dve.inc())
        done = self.s_dve.now()
        for key in self.rel_keys:
            em.bank_rel[id(key)] = done
        ret = [done]
        if out_bf is not None and out_f32 is not None:
            em.op("scalar", lambda s: s.copy(
                out_bf.rearrange("p a t -> p (a t)"), out_f32.rearrange("p a t -> p (a t)")),
                waits=done, incs=self.s_act.inc())
            ret.append(self.s_act.now())
        self.prev_bc_readers = done
        return tuple(ret)



RG8 = [[0, 1, 2, 3, 4, 5, 6, 7]]
NH = 16


def build_mamba(T=4096):
    nc = bass.Bass("TRN2", target_bir_lowering=False, debug=False)
    aps = {"T": T}
    aps["u"] = nc.dram_tensor("u", [1024, 128 + T], F32, kind="ExternalInput").ap()
    aps["wxbd"] = nc.dram_tensor("wxbd", [1024, 2128], F32, kind="ExternalInput").ap()
    aps["wzc"] = nc.dram_tensor("wzc", [1024, 2112], F32, kind="ExternalInput").ap()
    aps["owT"] = nc.dram_tensor("owT", [2048, 1024], F32, kind="ExternalInput").ap()
    aps["cwcol"] = nc.dram_tensor("cwcol", [128, 18, 4], F32, kind="ExternalInput").ap()
    aps["cbcol"] = nc.dram_tensor("cbcol", [128, 18], F32, kind="ExternalInput").ap()
    aps["dtbcol"] = nc.dram_tensor("dtbcol", [16, 1], F32, kind="ExternalInput").ap()
    aps["alog"] = nc.dram_tensor("alog", [1, 16], F32, kind="ExternalInput").ap()
    aps["dvec"] = nc.dram_tensor("dvec", [1, 16], F32, kind="ExternalInput").ap()
    aps["rmspk"] = nc.dram_tensor("rmspk", [128, 16], F32, kind="ExternalInput").ap()
    aps["triuinc"] = nc.dram_tensor("triuinc", [128, 128], F32, kind="ExternalInput").ap()
    aps["upperneg"] = nc.dram_tensor("upperneg", [128, 128], F32, kind="ExternalInput").ap()
    aps["ident"] = nc.dram_tensor("ident", [128, 128], F32, kind="ExternalInput").ap()
    aps["m2T"] = nc.dram_tensor("m2T", [8, 8], F32, kind="ExternalInput").ap()
    aps["selcol"] = nc.dram_tensor("selcol", [8, 1], F32, kind="ExternalInput").ap()
    aps["x1"] = nc.dram_tensor("x1", [1024, T], F32, kind="ExternalOutput").ap()
    aps["sp_xtok"] = nc.dram_tensor("sp_xtok", [T, 2048], BF16).ap()
    aps["sp_btok"] = nc.dram_tensor("sp_btok", [T, 64], BF16).ap()
    aps["sp_bfeat"] = nc.dram_tensor("sp_bfeat", [64, T], BF16).ap()
    aps["sp_dt"] = nc.dram_tensor("sp_dt", [16, T], F32).ap()
    aps["w8d"] = nc.dram_tensor("w8d", [8, 16], F32).ap()
    aps["cum_dram"] = nc.dram_tensor("cum_dram", [16, 128], F32).ap()
    aps["ow_bf"] = nc.dram_tensor("ow_bf", [8, 128, 16, 128], BF16).ap()
    aps["ag_in"] = nc.dram_tensor("ag_in", [65, 2048], F32).ap()
    aps["ag_out"] = nc.dram_tensor("ag_out", [520, 2048], F32).ap()
    es = ExitStack()
    with es:
        em = Emit(nc, es)
        emit_mamba(em, aps)
        em.finish()
    return nc


def emit_mamba(em, cfg):
    T = cfg["T"]
    NSC = T // 512
    u = cfg["u"]
    urr = u.rearrange("(k p) t -> p k t", p=128)

    wbig = em.sb("a_wbig", [128, 8, 2176], BF16)
    cw = em.sb("a_cw", [128, 18, 4], F32)
    cb = em.sb("a_cb", [128, 18], F32)
    dtb = em.sb("a_dtb", [16, 1], F32)
    rmsw = em.sb("a_rmsw", [128, 16], F32)
    triu = em.sb("a_triu", [128, 128], F32)
    upneg = em.sb("a_upneg", [128, 128], F32)
    ident = em.sb("a_ident", [128, 128], F32)
    alogr = em.sb("a_alogr", [1, 16], F32)
    dvecr = em.sb("a_dvecr", [1, 16], F32)
    arow = em.sb("a_arow", [1, 16], F32)
    abc = em.sb("a_abc", [128, 16], F32)
    dial = em.sb("a_dial", [128, 16 * 128], BF16)
    usb1 = em.sb("a_u", [128, 8, 512], F32)
    usb2 = [usb1, usb1]
    ubf1 = em.sb("a_ub", [128, 8, 512], BF16)
    ubf2 = [ubf1, ubf1]
    uhalo = em.sb("a_uhalo", [128, 8, 3], BF16)
    scr = em.sb("a_scr", [128, 10304], F32)
    xcraw = scr[:, 0:8755].rearrange("p (a t) -> p a t", a=17)
    acc1 = [scr[:, 8768:9280], scr[:, 9280:9792]]
    cacc2 = scr[0:64, 9792:10304]
    sigz = scr[:, 0:8192].rearrange("p (a t) -> p a t", a=16)
    szp = scr[:, 8192:10240].rearrange("p (a t) -> p a t", a=16)
    sgld = scr[0:64, 0:2048]
    w8rep = scr[0:1, 2048:3072].bitcast(BF16)
    un2 = em.sb("a_un2", [128, 8192], BF16)
    xhat = un2[:].rearrange("p (a t) -> p a t", a=16)
    ygat = un2[:].rearrange("p (a t) -> p a t", a=16)
    lnvbf = un2[:, 0:4096].rearrange("p (a t) -> p a t", a=8)
    lnv2bf = un2[:, 4096:8192].rearrange("p (a t) -> p a t", a=8)
    un3 = em.sb("a_un3", [128, 4096], BF16)
    dtscr = un3[0:16, 0:3072].bitcast(F32).rearrange("p (a t) -> p a t", a=3)
    owsb = [un3[:, 0:2048].rearrange("p (a t) -> p a t", a=16),
            un3[:, 2048:4096].rearrange("p (a t) -> p a t", a=16)]
    ccraw = em.sb("a_ccraw", [64, 515], F32)
    bfeat = em.sb("a_bfeat", [64, 512], BF16)
    chat = em.sb("a_chat", [64, 512], BF16)
    dtraw = un3[0:16, 3072:4096].bitcast(F32)
    dtsf = em.sb("a_dtsf", [16, 4, 128], F32)
    xtok = [em.sb(f"a_xtok_{i}", [128, 2048], BF16) for i in range(2)]
    btok = [em.sb(f"a_btok_{i}", [128, 64], BF16) for i in range(2)]
    btok2 = [em.sb(f"a_btok2_{i}", [128, 64], BF16) for i in range(2)]
    onescol = em.sb("a_onescol", [128, 1], F32)
    calsb = em.sb("a_calsb", [1, 16], F32)
    dtT = em.sb("a_dtT", [128, 16], F32)
    catok = em.sb("a_catok", [128, 16], F32)
    cafeat = em.sb("a_cafeat", [16, 128], F32)
    L2 = em.sb("a_L2", [2, 2048], F32)
    R2 = em.sb("a_R2", [2, 2048], F32)
    e1row = em.sb("a_e1row", [1, 2048], BF16)
    edec = em.sb("a_edec", [128, 16], F32)
    edl = em.sb("a_edl", [64, 16], F32)
    dacc = em.sb("a_dacc", [1, 16], F32)
    ssb = em.sb("a_ssb", [64, 16 * 128], F32)
    sbf = em.sb("a_sbf", [64, 16 * 128], BF16)
    g2sb = em.sb("a_g2sb", [128, 128], F32)
    emtmp = em.sb("a_emtmp", [128, 512], F32)
    emask = em.sb("a_emask", [128, 512], F32)
    g2all = em.sb("a_g2all", [128, 2048], BF16)
    cc2 = [em.sb(f"a_cc2_{i}", [64, 128], BF16) for i in range(2)]
    y2bf = em.sb("a_y2bf", [128, 2048], BF16)
    rstdr = em.sb("a_rstdr", [1, 3, 128], F32)
    rbc = em.sb("a_rbc", [128, 128], F32)
    w8all = em.sb("a_w8all", [1, 128], F32)
    dall = em.sb("a_dall", [8, 16], F32)
    m2t = em.sb("a_m2t", [8, 8], F32)
    selc = em.sb("a_selc", [8, 1], F32)
    onesrow = em.sb("a_onesrow", [1, 128], F32)
    onesbf = em.sb("a_onesbf", [1, 128], BF16)

    pY = em.ps("a_pY", [128, 2048])
    pD = [em.ps(f"a_pD{i}", [128, 512]) for i in range(2)]
    pq = [em.ps(f"a_pq{i}", [128, 512]) for i in range(2)]

    ln = LNKit(em, "a_ln", psums=(pD[0][0:1, 0:512], pD[1][0:1, 0:512],
                                  pD[0][:, 0:512], pD[1][:, 0:512],
                                  (pD[0], pD[1])), bufs=(lnvbf, lnv2bf))

    s_in = em.sem("a_in")
    s_w = em.sem("a_w")
    s_pe = em.sem("a_pe")
    s_dve = em.sem("a_dve")
    s_act = em.sem("a_act")
    s_tr = em.sem("a_tr")
    s_cum = em.sem("a_cum")
    s_inu = em.sem("a_inu")
    s_sp = em.sem("a_sp")
    s_ld = em.sem("a_ld")
    s_cc = em.sem("a_cc")
    s_st = em.sem("a_st")

    SR = {}

    em.op("gpsimd", lambda g: g.dma_start(
        out=wbig[:, :, 0:2128], in_=cfg["wxbd"].rearrange("(k p) o -> p k o", p=128)), incs=s_w.inc(16))
    for name_, sb_ in (("cwcol", cw), ("cbcol", cb), ("dtbcol", dtb), ("rmspk", rmsw),
                       ("triuinc", triu), ("upperneg", upneg), ("ident", ident),
                       ("m2T", m2t), ("selcol", selc), ("alog", alogr), ("dvec", dvecr)):
        em.op("gpsimd", lambda g, n=name_, s=sb_: g.dma_start(out=s[:], in_=cfg[n]), incs=s_w.inc(16))
    em.op("gpsimd", lambda g: g.dma_start(out=uhalo[:], in_=urr[:, :, 125:128]), incs=s_w.inc(16))
    if cfg.get("bcast") is not None:
        mg_sb = em.sb("a_mg", [128, 1], F32)
        em.op("gpsimd", lambda g: g.dma_start(out=mg_sb[:], in_=cfg["bcast"]["mgatec"]),
              incs=s_w.inc(16))
        cfg["bcast"]["mgate_sb"] = mg_sb
    wready = s_w.now()
    for buf, val in ((onesrow, 1.0), (onesbf, 1.0), (onescol, 1.0)):
        em.op("vector", lambda v, b_=buf, vv=val: v.memset(b_[:], vv), incs=s_dve.inc())
    em.op("vector", lambda v: v.memset(ssb[:], 0.0), incs=s_dve.inc())
    em.op("vector", lambda v: v.memset(dacc[:], 0.0), incs=s_dve.inc())
    em.op("vector", lambda v: v.memset(xcraw[:, :, 0:3], 0.0), incs=s_dve.inc())
    em.op("vector", lambda v: v.memset(ccraw[:, 0:3], 0.0), incs=s_dve.inc())
    em.op("vector", lambda v: v.memset(L2[:], 1.0), incs=s_dve.inc())
    em.op("vector", lambda v: v.memset(R2[:], -1.0), incs=s_dve.inc())
    ms_done = s_dve.now()

    em.op("scalar", lambda s: s.activation(arow[:], alogr[:], AF.Exp),
          waits=wready, incs=s_act.inc())
    em.op("vector", lambda v: v.tensor_scalar_mul(arow[:], arow[:], -1.0),
          waits=s_act.now(), incs=s_dve.inc())
    em.op("tensor", lambda t: t.matmul(pq[0][:, 0:16], onesrow[:], arow[:], start=True, stop=True),
          waits=(s_dve.now(), ms_done), incs=s_pe.inc())
    em.op("scalar", lambda s: s.copy(abc[:], pq[0][:, 0:16]), waits=s_pe.now(), incs=s_act.inc())
    em.op("tensor", lambda t: t.matmul(pq[0][:, 16:32], onesrow[:], dvecr[:], start=True, stop=True),
          waits=s_act.now(), incs=s_pe.inc())
    em.op("scalar", lambda s: s.copy(rbc[:, 0:16], pq[0][:, 16:32]), waits=s_pe.now(), incs=s_act.inc())
    dbc_done = s_act.now()
    for h in range(NH):
        em.op("vector", lambda v, h=h: v.tensor_scalar_mul(
            dial[:, h * 128:(h + 1) * 128], ident[:], rbc[:, h:h + 1]),
            waits=(dbc_done,) if h == 0 else (), incs=s_dve.inc())
    prep_done = s_dve.now()
    em.bank_rel[id(pq[0])] = dbc_done

    for k in range(8):
        for ot in range(17):
            m0, mw = ot * 128, (128 if ot < 16 else 80)
            em.op("tensor", lambda t, ot=ot, k=k, m0=m0, mw=mw: t.matmul(
                pq[1][0:mw, ot * 3:(ot + 1) * 3], wbig[:, k, m0:m0 + mw], uhalo[:, k, :],
                start=(k == 0), stop=(k == 7)),
                waits=(wready, prep_done) if (k == 0 and ot == 0) else (),
                incs=s_pe.inc() if (k == 7 and ot == 16) else ())
    halo_mm = s_pe.now()
    for ot in range(16):
        em.op("scalar", lambda s, ot=ot: s.copy(xcraw[:, ot, 0:3], pq[1][:, ot * 3:(ot + 1) * 3]),
              waits=(halo_mm, ms_done) if ot == 0 else (), incs=s_act.inc())
    em.op("scalar", lambda s: s.copy(xcraw[0:64, 16, 0:3], pq[1][0:64, 48:51]), incs=s_act.inc())
    em.bank_rel[id(pq[1])] = halo_done = s_act.now()

    calast = R2[0:1, :].rearrange("q (h t) -> q h t", t=128)[:, :, 127:128].squeeze(2)

    def cum_pipe(dt_ap, dt_ready, with_L2, cp=None, use_pq_rel=True):
        if cp is None:
            cp = pq[0][:]
        w0 = [dt_ready]
        rel_specs = (SR.get("cum"), em.bank_rel.get(id(pq[0])) if use_pq_rel else None)
        for spec in rel_specs:
            if spec is not None:
                w0.append(spec)
        em.op("tensor", lambda t, cp=cp: t.transpose(cp[:, 128:144], dt_ap, ident[0:16, 0:16]),
              waits=tuple(w0), incs=s_pe.inc())
        em.op("scalar", lambda s, cp=cp: s.copy(dtT[:], cp[:, 128:144]), waits=s_pe.now(), incs=s_act.inc())
        em.op("tensor", lambda t, cp=cp: t.matmul(cp[:, 144:160], triu[:], dtT[:], start=True, stop=True),
              waits=s_act.now(), incs=s_pe.inc())
        if with_L2:
            em.op("vector", lambda v, cp=cp: v.tensor_mul(catok[:], cp[:, 144:160], abc[:]),
                  waits=s_pe.now(), incs=s_dve.inc())
            em.op("tensor", lambda t, cp=cp: t.transpose(cp[0:16, 160:288], catok[:], ident[:]),
                  waits=s_dve.now(), incs=s_pe.inc())
            em.op("scalar", lambda s, cp=cp: s.copy(cafeat[:], cp[0:16, 160:288]),
                  waits=s_pe.now(), incs=s_act.inc())
            caf_done = s_act.now()
            em.op("scalar", lambda e: e.dma_start(out=cfg["cum_dram"], in_=cafeat[:]),
                  waits=caf_done, incs=s_cum.inc(16))
            cd_done = s_cum.now()
            em.op("scalar", lambda e: e.dma_start(
                out=R2[0:1, :], in_=cfg["cum_dram"].rearrange("h t -> (h t)").unsqueeze(0)),
                waits=cd_done, incs=s_cum.inc(16))
            em.op("scalar", lambda e: e.dma_start(
                out=L2[1:2, :], in_=cfg["cum_dram"].rearrange("h t -> (h t)").unsqueeze(0)),
                incs=s_cum.inc(16))
            row_done = s_cum.now()
            em.op("tensor", lambda t, cp=cp: t.matmul(cp[:, 288:304], onesrow[:], calast, start=True, stop=True),
                  waits=row_done, incs=s_pe.inc())
            em.op("tensor", lambda t, cp=cp: t.matmul(cp[0:64, 304:320], onesrow[:, 0:64], calast, start=True, stop=True),
                  incs=s_pe.inc())
            bc_done = s_pe.now()
        else:
            em.op("tensor", lambda t, cp=cp: t.matmul(
                cp[0:1, 160:176], onescol[:], dtT[:], start=True, stop=True),
                incs=s_pe.inc())
            em.op("vector", lambda v, cp=cp: v.tensor_mul(catok[:], cp[:, 144:160], abc[:]),
                  waits=s_pe.now(), incs=s_dve.inc())
            em.op("vector", lambda v, cp=cp: v.tensor_mul(calsb[:], cp[0:1, 160:176], arow[:]),
                  incs=s_dve.inc())
            cal_done = s_dve.now()
            row_done = None
            em.op("tensor", lambda t, cp=cp: t.matmul(cp[:, 288:304], onesrow[:], calsb[:], start=True, stop=True),
                  waits=cal_done, incs=s_pe.inc())
            em.op("tensor", lambda t, cp=cp: t.matmul(cp[0:64, 304:320], onesrow[:, 0:64], calsb[:], start=True, stop=True),
                  incs=s_pe.inc())
            bc_done = s_pe.now()
        em.op("vector", lambda v, cp=cp: v.tensor_sub(edec[:], cp[:, 288:304], catok[:]),
              waits=bc_done, incs=s_dve.inc())
        em.op("scalar", lambda s: s.activation(edec[:], edec[:], AF.Exp),
              waits=s_dve.now(), incs=s_act.inc())
        em.op("scalar", lambda s, cp=cp: s.activation(edl[:], cp[0:64, 304:320], AF.Exp), incs=s_act.inc())
        SR["cum"] = done = s_act.now()
        return done, row_done

    def t_and_update(h, tb, extra_first, tp=None, use_pq_rel=True):
        slot = h % 2
        if tp is None:
            tp = pq[1][:]
        bw = list(extra_first) if extra_first else []
        if SR.get(f"btok2_{slot}") is not None:
            bw.append(SR[f"btok2_{slot}"])
        em.op("vector", lambda v, h=h, tb=tb, slot=slot: v.tensor_scalar_mul(
            btok2[slot][:], btok[tb][:], edec[:, h:h + 1]),
            waits=tuple(bw), incs=s_dve.inc())
        bpp = s_dve.now()
        tw = [bpp]
        for spec in (SR.get(f"T{slot}"), em.bank_rel.get(id(pq[1])) if use_pq_rel else None):
            if spec is not None:
                tw.append(spec)
        em.op("tensor", lambda t, h=h, tb=tb, slot=slot, tp=tp: t.matmul(
            tp[0:64, slot * 128:(slot + 1) * 128], btok2[slot][:],
            xtok[tb][:, h * 128:(h + 1) * 128], start=True, stop=True),
            waits=tuple(tw), incs=s_pe.inc())
        t_done = s_pe.now()
        em.op("vector", lambda v, h=h, slot=slot, tp=tp: v.scalar_tensor_tensor(
            ssb[:, h * 128:(h + 1) * 128], ssb[:, h * 128:(h + 1) * 128],
            edl[:, h:h + 1], tp[0:64, slot * 128:(slot + 1) * 128],
            OP.mult, OP.add), waits=t_done, incs=s_dve.inc())
        SR[f"T{slot}"] = s_dve.now()
        SR[f"btok2_{slot}"] = t_done

    ubf_guards = [None, None]
    conv_tile_prev = [None] * 17
    xhat_guard = None
    tr_prev = [None, None]
    xtok_rd = [None, None]
    gchunk = 0
    for sc in range(NSC):
        b = sc % 2
        ubf = ubf2[b]
        t0 = 128 + sc * 512
        w = [wready]
        if ubf_guards[b] is not None:
            w.append(ubf_guards[b])
        em.op("gpsimd", lambda g, t0=t0, ubf=ubf: g.dma_start(
            out=ubf[:], in_=urr[:, :, t0:t0 + 512]), waits=tuple(w), incs=s_in.inc(16))
        cast_done = s_in.now()

        evac_done = [None] * 17

        def ep_x(ot, pb, pe_spec):
            if sc > 0:
                em.op("scalar", lambda s, ot=ot: s.copy(
                    xcraw[:, ot, 0:3], xcraw[:, ot, 512:515]),
                    waits=(conv_tile_prev[ot],), incs=s_act.inc())
            em.op("scalar", lambda s, ot=ot, pb=pb: s.copy(xcraw[:, ot, 3:515], pb[:]),
                  waits=(pe_spec,), incs=s_act.inc())
            evac_done[ot] = s_act.now()
            return evac_done[ot]
        emit_proj(em, pq, wbig, 8, ubf[:], 16, ep_x, s_pe, first_waits=(cast_done,))
        wts0 = [em.bank_rel[id(pq[0])]]
        for k in range(8):
            em.op("tensor", lambda t, k=k, ubf=ubf: t.matmul(
                pq[0][0:80, 0:512], wbig[:, k, 2048:2128], ubf[:, k, :],
                start=(k == 0), stop=(k == 7)),
                waits=tuple(wts0) if k == 0 else (), incs=s_pe.inc() if k == 7 else ())
        bd_mm = ubf_guards[0] = ubf_guards[1] = s_pe.now()
        if sc > 0:
            em.op("scalar", lambda s: s.copy(xcraw[0:64, 16, 0:3], xcraw[0:64, 16, 512:515]),
                  waits=(conv_tile_prev[16],), incs=s_act.inc())
        em.op("scalar", lambda s: s.copy(xcraw[0:64, 16, 3:515], pq[0][0:64, 0:512]),
              waits=(bd_mm,), incs=s_act.inc())
        evac_done[16] = s_act.now()
        dtg = [xhat_guard] if xhat_guard else []
        em.op("scalar", lambda s: s.copy(dtraw, pq[0][64:80, 0:512]),
              waits=tuple(dtg), incs=s_act.inc())
        em.bank_rel[id(pq[0])] = evac_bd = s_act.now()

        silu_prev = [None, None]
        conv_tile_done = [None] * 17
        for tile in range(17):
            rows = 128 if tile < 16 else 64
            dst = acc1[tile % 2][0:rows] if tile < 16 else cacc2
            cwt = [evac_done[tile]]
            if tile < 16 and silu_prev[tile % 2] is not None:
                cwt.append(silu_prev[tile % 2])
            if tile == 0 and xhat_guard is not None:
                cwt.append(xhat_guard)
            em.op("vector", lambda v, tile=tile, dst=dst, rows=rows: v.tensor_scalar_mul(
                dst, xcraw[0:rows, tile, 0:512], cw[0:rows, tile, 0:1]),
                waits=tuple(cwt), incs=s_dve.inc())
            for kk in range(1, 4):
                em.op("vector", lambda v, tile=tile, kk=kk, dst=dst, rows=rows: v.scalar_tensor_tensor(
                    dst, xcraw[0:rows, tile, kk:512 + kk], cw[0:rows, tile, kk:kk + 1],
                    dst, OP.mult, OP.add), incs=s_dve.inc())
            taps = conv_tile_done[tile] = s_dve.now()
            if tile < 16:
                em.op("scalar", lambda s, tile=tile: s.activation(
                    xhat[:, tile, :], acc1[tile % 2], AF.Silu, bias=cb[:, tile:tile + 1]),
                    waits=taps, incs=s_act.inc())
                silu_prev[tile % 2] = s_act.now()
            else:
                bfw = [taps]
                if SR.get("bfeat_rd") is not None:
                    bfw.extend(SR["bfeat_rd"])
                em.op("scalar", lambda s: s.activation(
                    bfeat[:], cacc2, AF.Silu, bias=cb[0:64, 16:17]),
                    waits=tuple(bfw), incs=s_act.inc())
        conv_tile_prev = conv_tile_done
        silu_done = s_act.now()

        em.op("scalar", lambda s, sc=sc: s.dma_start(
            out=cfg["sp_bfeat"][:, sc * 512:(sc + 1) * 512], in_=bfeat[:]),
            waits=silu_done, incs=s_sp.inc(16))

        em.op("scalar", lambda s: s.activation(dtscr[:, 0, :], dtraw, AF.Abs, bias=dtb[:]),
              waits=evac_bd, incs=s_act.inc())
        em.op("scalar", lambda s: s.activation(dtscr[:, 1, :], dtscr[:, 0, :], AF.Exp, scale=-1.0),
              incs=s_act.inc())
        em.op("vector", lambda v: v.tensor_scalar_add(dtscr[:, 1, :], dtscr[:, 1, :], 1.0),
              waits=s_act.now(), incs=s_dve.inc())
        em.op("scalar", lambda s: s.activation(dtscr[:, 2, :], dtscr[:, 1, :], AF.Ln),
              waits=s_dve.now(), incs=s_act.inc())
        em.op("scalar", lambda s: s.activation(dtscr[:, 0, :], dtraw, AF.Relu, bias=dtb[:]),
              incs=s_act.inc())
        dfw = [s_act.now()]
        if SR.get("dtsf_rd") is not None:
            dfw.extend(SR["dtsf_rd"])
        em.op("vector", lambda v: v.tensor_add(
            dtsf[:].rearrange("p a t -> p (a t)"), dtscr[:, 0, :], dtscr[:, 2, :]),
            waits=tuple(dfw), incs=s_dve.inc())
        dt_done = s_dve.now()
        em.op("scalar", lambda s, sc=sc: s.dma_start(
            out=cfg["sp_dt"][:, sc * 512:(sc + 1) * 512],
            in_=dtsf[:].rearrange("p a t -> p (a t)")), waits=dt_done, incs=s_sp.inc(16))

        for cc in range(4):
            tb = gchunk % 2
            trw = [silu_done]
            if tr_prev[tb] is not None:
                trw.append(tr_prev[tb])
            if xtok_rd[tb] is not None:
                trw.append(xtok_rd[tb])
            for ft in range(16):
                em.op("sync", lambda sy, ft=ft, cc=cc, tb=tb: sy.dma_start_transpose(
                    xtok[tb][:, ft * 128:(ft + 1) * 128], xhat[:, ft, cc * 128:(cc + 1) * 128]),
                    waits=tuple(trw) if ft == 0 else (), incs=s_tr.inc(16))
            em.op("sync", lambda sy, cc=cc, tb=tb: sy.dma_start_transpose(
                btok[tb][:], bfeat[:, cc * 128:(cc + 1) * 128]), incs=s_tr.inc(16))
            tr_done = s_tr.now()
            em.op("scalar", lambda s, gc=gchunk, tb=tb: s.dma_start(
                out=cfg["sp_xtok"][gc * 128:(gc + 1) * 128, :], in_=xtok[tb][:]),
                waits=tr_done, incs=s_sp.inc(16))
            em.op("scalar", lambda s, gc=gchunk, tb=tb: s.dma_start(
                out=cfg["sp_btok"][gc * 128:(gc + 1) * 128, :], in_=btok[tb][:]),
                incs=s_sp.inc(16))
            tr_prev[tb] = s_sp.now()
            cum_done, _ = cum_pipe(dtsf[:, cc, :], dt_done, with_L2=False,
                                   cp=pY[:, 0:512], use_pq_rel=False)
            em.op("vector", lambda v: v.tensor_add(dacc[:], dacc[:], calsb[:]),
                waits=cum_done, incs=s_dve.inc())
            for h in range(NH):
                t_and_update(h, tb, (cum_done, tr_done) if h == 0 else None,
                             tp=pY[:, 512:1024], use_pq_rel=False)
            xtok_rd[tb] = s_pe.now()
            gchunk += 1
        xhat_guard = (s_sp, s_sp.n)
        SR["dtsf_rd"] = [SR["cum"], s_sp.now()]
        SR["bfeat_rd"] = [s_tr.now(), s_sp.now()]

    sfin = s_dve.now()
    em.op("scalar", lambda s: s.dma_start(out=cfg["ag_in"][0:64, :], in_=ssb[:]),
          waits=sfin, incs=s_sp.inc(16))
    em.op("scalar", lambda s: s.dma_start(out=cfg["ag_in"][64:65, 0:16], in_=dacc[:]),
          incs=s_sp.inc(16))
    em.op("gpsimd", lambda g: g.collective_compute(
        "AllGather", OP.bypass, replica_groups=RG8,
        ins=[cfg["ag_in"].opt()], outs=[cfg["ag_out"].opt()]),
        waits=s_sp.now(), incs=s_cc.inc(1))
    ag_done = s_cc.now()

    em.op("sync", lambda sy: sy.dma_start(
        out=dall[:], in_=cfg["ag_out"].rearrange("(j r) h -> j r h", r=65)[:, 64, 0:16]),
        waits=ag_done, incs=s_ld.inc(16))
    em.op("tensor", lambda t: t.matmul(pq[0][0:8, 0:16], m2t[:], dall[:], start=True, stop=True),
          waits=(s_ld.now(), em.bank_rel[id(pq[0])]), incs=s_pe.inc())
    em.op("scalar", lambda s: s.activation(dall[:], pq[0][0:8, 0:16], AF.Exp),
          waits=s_pe.now(), incs=s_act.inc())
    em.bank_rel[id(pq[0])] = s_act.now()
    em.op("vector", lambda v: v.tensor_scalar_mul(dall[:], dall[:], selc[:]),
          waits=s_act.now(), incs=s_dve.inc())
    em.op("scalar", lambda s: s.dma_start(out=cfg["w8d"], in_=dall[:]),
          waits=s_dve.now(), incs=s_sp.inc(16))
    em.op("sync", lambda sy: sy.dma_start(
        out=w8all[:], in_=cfg["w8d"].rearrange("j h -> (j h)").unsqueeze(0)),
        waits=s_sp.now(), incs=s_ld.inc(16))
    w8_done = s_ld.now()
    sinit_rel = None
    for j in range(8):
        em.op("vector", lambda v, j=j: v.tensor_copy(
            w8rep[:].rearrange("q (h t) -> q h t", h=16),
            w8all[:, j * 16:(j + 1) * 16].unsqueeze(2).broadcast_to([1, 16, 128])),
            waits=(w8_done,) if j == 0 else (), incs=s_dve.inc())
        wrep_done = s_dve.now()
        for q in range(4):
            ww = [wrep_done] if q == 0 else []
            if q == 0 and sinit_rel is not None:
                ww.append(sinit_rel)
            em.op("tensor", lambda t, q=q: t.matmul(
                pY[0:64, q * 512:(q + 1) * 512], onesbf[:, 0:64], w8rep[:, q * 512:(q + 1) * 512],
                start=True, stop=True), waits=tuple(ww), incs=s_pe.inc() if q == 3 else ())
        wbc_done = s_pe.now()
        em.op("sync", lambda sy, j=j: sy.dma_start(
            out=sgld[:], in_=cfg["ag_out"][j * 65:j * 65 + 64, :]),
            waits=(ag_done,) if j == 0 else (), incs=s_ld.inc(16))
        ld_done = s_ld.now()
        if j == 0:
            em.op("vector", lambda v: v.tensor_mul(ssb[:], sgld[:], pY[0:64, :]),
                  waits=(wbc_done, ld_done, sfin), incs=s_dve.inc())
        else:
            em.op("vector", lambda v: v.tensor_mul(sgld[:], sgld[:], pY[0:64, :]),
                  waits=(wbc_done, ld_done), incs=s_dve.inc())
            em.op("vector", lambda v: v.tensor_add(ssb[:], ssb[:], sgld[:]), incs=s_dve.inc())
        sinit_rel = s_dve.now()
    em.bank_rel[id(pY)] = sinit_rel

    em.op("gpsimd", lambda g: g.dma_start(
        out=wbig[:, :, 0:2112], in_=cfg["wzc"].rearrange("(k p) o -> p k o", p=128)),
        waits=(s_pe.now(), ag_done), incs=s_w.inc(16))
    wzc_ready = s_w.now()
    for aa in range(8):
        em.op("gpsimd", lambda g, aa=aa: g.dma_start(
            out=cfg["ow_bf"][aa],
            in_=cfg["owT"].rearrange("(k p) o -> p k o", p=128)[:, :, aa * 128:(aa + 1) * 128]),
            incs=s_w.inc(16))
    owbf_ready = s_w.now()
    for k in range(8):
        em.op("tensor", lambda t, k=k: t.matmul(
            pq[1][0:64, 0:3], wbig[:, k, 2048:2112], uhalo[:, k, :],
            start=(k == 0), stop=(k == 7)),
            waits=(wzc_ready, em.bank_rel[id(pq[1])]) if k == 0 else (),
            incs=s_pe.inc() if k == 7 else ())
    em.op("scalar", lambda s: s.copy(ccraw[:, 0:3], pq[1][0:64, 0:3]),
          waits=s_pe.now(), incs=s_act.inc())
    em.bank_rel[id(pq[1])] = s_act.now()

    usb_guards = [None, None]
    gchunk = 0
    xl_prev = [None, None]
    cconv_guard = None
    prev_apply = None
    prev_ln_casts = None
    ow_prev = [None, None]
    SR.pop("cum", None)
    ubf_guards = [s_pe.now(), s_pe.now()]
    bcast = cfg.get("bcast")
    s_bc = em.sem("a_bc")
    s_bcc = em.sem("a_bcc")
    for sc in range(NSC):
        b = sc % 2
        usb = usb2[b]
        ubf = ubf2[b]
        t0 = 128 + sc * 512
        w = [wzc_ready]
        if usb_guards[b] is not None:
            w.extend(usb_guards[b])
        em.op("gpsimd", lambda g, t0=t0, ubf=ubf: g.dma_start(
            out=ubf[:], in_=urr[:, :, t0:t0 + 512]), waits=(ubf_guards[b],), incs=s_in.inc(16))
        cast_done = s_in.now()
        if bcast is not None and sc == NSC - 1:
            em.op("gpsimd", lambda g: g.collective_compute(
                "AllReduce", OP.add, replica_groups=RG8,
                ins=[bcast["ar1_in"].opt()], outs=[bcast["ar1_out"].opt()]),
                waits=s_bc.at(64), incs=s_bcc.inc(1))
        em.op("sync", lambda sy, t0=t0, usb=usb: sy.dma_start(
            out=usb[:], in_=urr[:, :, t0:t0 + 512]), waits=tuple(w), incs=s_inu.inc(16))
        usb_done = s_inu.now()

        def ep_z(ot, pb, pe_spec):
            wz = [pe_spec]
            if prev_apply is not None:
                wz.append(prev_apply)
            em.op("scalar", lambda s, ot=ot, pb=pb: s.activation(
                sigz[:, ot, :], pb[:], AF.Sigmoid), waits=tuple(wz), incs=s_act.inc())
            em.op("vector", lambda v, ot=ot: v.tensor_scalar_mul(
                sigz[:, ot, :], sigz[:, ot, :], rmsw[:, ot:ot + 1]),
                waits=s_act.now(), incs=s_dve.inc())
            return s_dve.now()
        emit_proj(em, pq, wbig, 8, ubf[:], 16, ep_z, s_pe, first_waits=(cast_done,))
        if sc > 0:
            em.op("scalar", lambda s: s.copy(ccraw[:, 0:3], ccraw[:, 512:515]),
                  waits=cconv_guard, incs=s_act.inc())
        ccarry = s_act.now()
        wtc = [em.bank_rel[id(pq[0])]]
        for k in range(8):
            em.op("tensor", lambda t, k=k, ubf=ubf: t.matmul(
                pq[0][0:64, 0:512], wbig[:, k, 2048:2112], ubf[:, k, :],
                start=(k == 0), stop=(k == 7)),
                waits=tuple(wtc) if k == 0 else (), incs=s_pe.inc() if k == 7 else ())
        ubf_guards[0] = ubf_guards[1] = s_pe.now()
        em.op("scalar", lambda s: s.copy(ccraw[:, 3:515], pq[0][0:64, 0:512]),
              waits=(s_pe.now(), ccarry), incs=s_act.inc())
        em.bank_rel[id(pq[0])] = cevac = s_act.now()
        em.op("vector", lambda v: v.tensor_scalar_mul(cacc2, ccraw[:, 0:512], cw[0:64, 17, 0:1]),
              waits=cevac, incs=s_dve.inc())
        for kk in range(1, 4):
            em.op("vector", lambda v, kk=kk: v.scalar_tensor_tensor(
                cacc2, ccraw[:, kk:512 + kk], cw[0:64, 17, kk:kk + 1], cacc2,
                OP.mult, OP.add), incs=s_dve.inc())
        cconv_guard = s_dve.now()
        chw = [cconv_guard]
        if SR.get("chat") is not None:
            chw.append(SR["chat"])
        em.op("scalar", lambda s: s.activation(chat[:], cacc2, AF.Silu, bias=cb[0:64, 17:18]),
              waits=tuple(chw), incs=s_act.inc())
        chat_done = s_act.now()
        em.op("sync", lambda sy, sc=sc: sy.dma_start(
            out=bfeat[:], in_=cfg["sp_bfeat"][:, sc * 512:(sc + 1) * 512]),
            waits=(SR.get("bfeat") or ()) and (SR["bfeat"],), incs=s_ld.inc(16))
        bf_ld = s_ld.now()
        em.op("sync", lambda sy, sc=sc: sy.dma_start(
            out=dtsf[:].rearrange("p a t -> p (a t)"), in_=cfg["sp_dt"][:, sc * 512:(sc + 1) * 512]),
            waits=(SR.get("dtld") or ()) and (SR["dtld"],), incs=s_ld.inc(16))
        dt_ld = s_ld.now()

        for cc in range(4):
            tb = gchunk % 2
            lw = []
            if xl_prev[tb] is not None:
                lw.append(xl_prev[tb])
            em.op("scalar", lambda s, gc=gchunk, tb=tb: s.dma_start(
                out=xtok[tb][:], in_=cfg["sp_xtok"][gc * 128:(gc + 1) * 128, :]),
                waits=tuple(lw), incs=s_ld.inc(16))
            em.op("scalar", lambda s, gc=gchunk, tb=tb: s.dma_start(
                out=btok[tb][:], in_=cfg["sp_btok"][gc * 128:(gc + 1) * 128, :]),
                incs=s_ld.inc(16))
            ld_done = s_ld.now()
            cum_done, _ = cum_pipe(dtsf[:, cc, :], dt_ld, with_L2=True)
            ew1 = [cum_done]
            if SR.get("e1") is not None:
                ew1.append(SR["e1"])
            em.op("scalar", lambda s: s.activation(e1row[:], R2[0:1, :], AF.Exp),
                  waits=tuple(ew1), incs=s_act.inc())
            e1_done = s_act.now()
            scw = [SR["Supd"]] if SR.get("Supd") else []
            em.op("scalar", lambda s: s.copy(sbf[:], ssb[:]), waits=tuple(scw), incs=s_act.inc())
            scast_done = s_act.now()
            gw_ = [chat_done, bf_ld]
            for spec in (SR.get("G2"), SR.get("RBC"), em.bank_rel.get(id(pq[0]))):
                if spec is not None:
                    gw_.append(spec)
            em.op("tensor", lambda t, cc=cc: t.matmul(
                pq[0][:, 0:128], bfeat[:, cc * 128:(cc + 1) * 128], chat[:, cc * 128:(cc + 1) * 128],
                start=True, stop=True), waits=tuple(gw_), incs=s_pe.inc())
            em.op("scalar", lambda s: s.copy(g2sb[:], pq[0][:, 0:128]),
                  waits=s_pe.now(), incs=s_act.inc())
            SR["G2"] = g2_done = s_act.now()
            for g in range(4):
                dw_ = [cum_done]
                for spec in (SR.get(f"D{g % 2}"), em.bank_rel.get(id(pD[g % 2]))):
                    if spec is not None:
                        dw_.append(spec)
                for hh in range(4):
                    h = g * 4 + hh
                    em.op("tensor", lambda t, g=g, hh=hh, h=h: t.matmul(
                        pD[g % 2][:, hh * 128:(hh + 1) * 128],
                        L2[:, h * 128:(h + 1) * 128], R2[:, h * 128:(h + 1) * 128],
                        start=True, stop=True),
                        waits=tuple(dw_) if hh == 0 else (),
                        incs=s_pe.inc() if hh == 3 else ())
                dmm = s_pe.now()
                em.op("vector", lambda v, g=g: v.tensor_tensor(
                    emtmp[:].rearrange("p (a t) -> p a t", a=4),
                    pD[g % 2][:].rearrange("p (a t) -> p a t", a=4),
                    upneg[:].unsqueeze(1).broadcast_to([128, 4, 128]), OP.add),
                    waits=dmm, incs=s_dve.inc())
                em.op("scalar", lambda s, g=g: s.activation(emask[:], emtmp[:], AF.Exp),
                      waits=s_dve.now(), incs=s_act.inc())
                em.op("vector", lambda v, g=g: v.tensor_tensor(
                    g2all[:, g * 512:(g + 1) * 512].rearrange("p (a t) -> p a t", a=4),
                    emask[:].rearrange("p (a t) -> p a t", a=4),
                    g2sb[:].unsqueeze(1).broadcast_to([128, 4, 128]), OP.mult),
                    waits=(s_act.now(), g2_done), incs=s_dve.inc())
                SR[f"D{g % 2}"] = s_dve.now()
            g2a_done = s_dve.now()
            yw = [g2a_done, scast_done, ld_done, e1_done]
            for spec in (SR.get("Y"), em.bank_rel.get(id(pY))):
                if spec is not None:
                    yw.append(spec)
            em.bank_rel[id(pY)] = None
            for h in range(NH):
                slot = h % 2
                ew = [e1_done]
                for spec in (SR.get(f"EB{slot}"), em.bank_rel.get(id(pq[1]))):
                    if spec is not None:
                        ew.append(spec)
                em.op("tensor", lambda t, h=h, slot=slot: t.matmul(
                    pq[1][0:64, 256 + slot * 128:384 + slot * 128],
                    onesbf[:, 0:64], e1row[:, h * 128:(h + 1) * 128], start=True, stop=True),
                    waits=tuple(ew), incs=s_pe.inc())
                eb_done = s_pe.now()
                ccw = [eb_done]
                if SR.get(f"YH{slot}") is not None:
                    ccw.append(SR[f"YH{slot}"])
                em.op("vector", lambda v, h=h, cc=cc, slot=slot: v.tensor_mul(
                    cc2[slot][:], chat[:, cc * 128:(cc + 1) * 128],
                    pq[1][0:64, 256 + slot * 128:384 + slot * 128]),
                    waits=tuple(ccw), incs=s_dve.inc())
                SR[f"EB{slot}"] = cc2_done = s_dve.now()
                em.op("tensor", lambda t, h=h, tb=tb: t.matmul(
                    pY[:, h * 128:(h + 1) * 128], xtok[tb][:, h * 128:(h + 1) * 128],
                    g2all[:, h * 128:(h + 1) * 128], start=True, stop=False),
                    waits=tuple(yw) if h == 0 else ())
                em.op("tensor", lambda t, h=h, slot=slot: t.matmul(
                    pY[:, h * 128:(h + 1) * 128], sbf[:, h * 128:(h + 1) * 128],
                    cc2[slot][:], start=False, stop=False), waits=cc2_done)
                em.op("tensor", lambda t, h=h, tb=tb: t.matmul(
                    pY[:, h * 128:(h + 1) * 128], xtok[tb][:, h * 128:(h + 1) * 128],
                    dial[:, h * 128:(h + 1) * 128], start=False, stop=True),
                    incs=s_pe.inc())
                SR[f"YH{slot}"] = s_pe.now()
                t_and_update(h, tb, (cum_done, ld_done) if h == 0 else None)
            SR["Supd"] = SR["T1"]
            y_mm_done = s_pe.now()
            xl_prev[tb] = y_mm_done
            em.op("scalar", lambda s: s.activation(y2bf[:], pY[:], AF.Square),
                  waits=y_mm_done, incs=s_act.inc())
            y2_done = s_act.now()
            mw_ = [y2_done, ln.init_done]
            if SR.get("MU") is not None:
                mw_.append(SR["MU"])
            for k in range(16):
                em.op("tensor", lambda t, k=k: t.matmul(
                    pq[0][0:1, 320:448], ln.ones128[:], y2bf[:, k * 128:(k + 1) * 128],
                    start=(k == 0), stop=(k == 15)),
                    waits=tuple(mw_) if k == 0 else (), incs=s_pe.inc() if k == 15 else ())
            mu_done = s_pe.now()
            em.op("vector", lambda v: v.tensor_scalar_mul(rstdr[:, 0, :], pq[0][0:1, 320:448], 1.0 / 2048),
                  waits=mu_done, incs=s_dve.inc())
            SR["MU"] = s_dve.now()
            em.op("vector", lambda v: v.tensor_scalar_add(rstdr[:, 1, :], rstdr[:, 0, :], 1e-5),
                  incs=s_dve.inc())
            em.op("scalar", lambda s: s.activation(rstdr[:, 2, :], rstdr[:, 1, :], AF.Ln),
                  waits=s_dve.now(), incs=s_act.inc())
            em.op("scalar", lambda s: s.activation(rstdr[:, 1, :], rstdr[:, 2, :], AF.Exp, scale=-0.5),
                  incs=s_act.inc())
            rstd_done = s_act.now()
            rw_ = [rstd_done, g2a_done]
            for spec in (SR.get("RBC"), SR.get("G2")):
                if spec is not None:
                    rw_.append(spec)
            em.op("tensor", lambda t: t.matmul(pq[0][:, 0:128], onesrow[:], rstdr[:, 1, :],
                                               start=True, stop=True),
                  waits=tuple(rw_), incs=s_pe.inc())
            em.op("scalar", lambda s: s.copy(rbc[:], pq[0][:, 0:128]),
                  waits=s_pe.now(), incs=s_act.inc())
            SR["RBC"] = rbc_done = s_act.now()
            em.op("vector", lambda v, cc=cc: v.tensor_tensor(
                szp[:], sigz[:, :, cc * 128:(cc + 1) * 128],
                rbc[:].unsqueeze(1).broadcast_to([128, 16, 128]), OP.mult),
                waits=(rbc_done,), incs=s_dve.inc())
            ygw = []
            if cc == 0 and ln.prev_sums is not None:
                ygw.append(ln.prev_sums)
            em.op("vector", lambda v, cc=cc: v.tensor_tensor(
                ygat[:, :, cc * 128:(cc + 1) * 128], szp[:],
                pY[:].rearrange("p (a t) -> p a t", a=16), OP.mult),
                waits=tuple(ygw), incs=s_dve.inc())
            SR["Y"] = s_dve.now()
            gchunk += 1
        yg_done = s_dve.now()
        em.bank_rel[id(pq[0])] = s_dve.now()
        em.bank_rel[id(pq[1])] = s_dve.now()

        for ot in range(8):
            oww = []
            if ow_prev[ot % 2] is not None:
                oww.append(ow_prev[ot % 2])
            oww.append(owbf_ready)
            em.op("sync", lambda sy, ot=ot: sy.dma_start(
                out=owsb[ot % 2][:], in_=cfg["ow_bf"][ot]),
                waits=tuple(oww), incs=s_w.inc(16))
            ow_ld = s_w.now()
            pb = pq[ot % 2]
            for k in range(16):
                wts = []
                if k == 0:
                    wts = [ow_ld, yg_done]
                    if em.bank_rel.get(id(pb)) is not None:
                        wts.append(em.bank_rel[id(pb)])
                em.op("tensor", lambda t, ot=ot, k=k, pb=pb: t.matmul(
                    pb[:, 0:512], owsb[ot % 2][:, k, :], ygat[:, k, :],
                    start=(k == 0), stop=(k == 15)),
                    waits=tuple(wts), incs=s_pe.inc() if k == 15 else ())
            mm_done = s_pe.now()
            ow_prev[ot % 2] = mm_done
            em.op("vector", lambda v, ot=ot, pb=pb, usb=usb: v.tensor_add(
                sigz[:, ot, :], usb[:, ot, :], pb[:, 0:512]),
                waits=(mm_done, usb_done) if ot == 0 else mm_done, incs=s_dve.inc())
            em.bank_rel[id(pb)] = s_dve.now()
        v_done = s_dve.now()
        SR["Y"] = s_pe.now()
        em.bank_rel[id(pD[0])] = SR.get("D0")
        em.bank_rel[id(pD[1])] = SR.get("D1")
        apply_done = ln.emit(sigz[:, 0:8, :], usb[:], v_ready=(v_done,))[0]
        prev_apply = apply_done
        prev_ln_casts = ln.last_casts
        em.op("sync", lambda sy, sc=sc, usb=usb: sy.dma_start(
            out=cfg["x1"].rearrange("(k p) t -> p k t", p=128)[:, :, sc * 512:(sc + 1) * 512],
            in_=usb[:]), waits=apply_done, incs=s_st.inc(16))
        usb_guards[0] = usb_guards[1] = [s_st.now()]
        if bcast is not None and sc < 4:
            em.op("vector", lambda v, usb=usb: v.tensor_scalar_mul(
                sigz[:, 0:8, :].rearrange("p a t -> p (a t)"),
                usb[:].rearrange("p a t -> p (a t)"), bcast["mgate_sb"][:]),
                waits=apply_done, incs=s_dve.inc())
            em.op("scalar", lambda s, sc=sc: s.dma_start(
                out=bcast["ar1_in"].rearrange("(k p) t -> p k t", p=128)[:, :, sc * 512:(sc + 1) * 512],
                in_=sigz[:, 0:8, :]), waits=s_dve.now(), incs=s_bc.inc(16))
            usb_guards[0] = usb_guards[1] = usb_guards[0] + [s_bc.now()]
            prev_apply = s_bc.now()
        SR["bfeat"] = s_pe.now()
        SR["dtld"] = SR["cum"]
        SR["e1"] = s_pe.now()
        SR["chat"] = s_dve.now()
    em.op("sync", lambda sy: sy.nop(), waits=s_st.now())
    em.bcast_done = s_bcc.now() if bcast is not None else None


def host_mamba_inputs(u_slice_fm_with_halo, in_proj_w, conv_w, conv_b, dt_bias,
                      A_log, D, rms_w, out_w, core_id, consts):
    W = np.asarray(in_proj_w, np.float32)
    wT = W.T
    wxbd = np.ascontiguousarray(np.concatenate(
        [wT[:, 2048:4096], wT[:, 4096:4160], wT[:, 4224:4240]], axis=1))
    wzc = np.ascontiguousarray(np.concatenate(
        [wT[:, 0:2048], wT[:, 4160:4224]], axis=1))
    cwf = np.asarray(conv_w, np.float32)
    cbf = np.asarray(conv_b, np.float32)
    cw18 = np.zeros((128, 18, 4), np.float32)
    cb18 = np.zeros((128, 18), np.float32)
    cw18[:, 0:16, :] = cwf[0:2048].reshape(16, 128, 4).transpose(1, 0, 2)
    cb18[:, 0:16] = cbf[0:2048].reshape(16, 128).T
    cw18[0:64, 16, :] = cwf[2048:2112]
    cb18[0:64, 16] = cbf[2048:2112]
    cw18[0:64, 17, :] = cwf[2112:2176]
    cb18[0:64, 17] = cbf[2112:2176]
    slot = core_id % 4
    m2 = np.zeros((8, 8), np.float32)
    sel = np.zeros((8, 1), np.float32)
    base = (core_id // 4) * 4
    for j in range(base, base + slot):
        sel[j, 0] = 1.0
        for i in range(j + 1, base + slot):
            m2[j, i] = 1.0
    return {
        "u": np.ascontiguousarray(u_slice_fm_with_halo, np.float32),
        "wxbd": wxbd, "wzc": wzc,
        "owT": host_wT(out_w),
        "cwcol": cw18, "cbcol": cb18,
        "dtbcol": np.asarray(dt_bias, np.float32).reshape(16, 1),
        "alog": np.asarray(A_log, np.float32).reshape(1, 16),
        "dvec": np.asarray(D, np.float32).reshape(1, 16),
        "rmspk": np.ascontiguousarray(np.asarray(rms_w, np.float32).reshape(16, 128).T),
        "triuinc": consts["triuinc"], "upperneg": consts["upperneg"],
        "ident": consts["ident"],
        "m2T": np.ascontiguousarray(m2.T), "selcol": sel,
    }



RG8 = [[0, 1, 2, 3, 4, 5, 6, 7]]


def build_moe():
    nc = bass.Bass("TRN2", target_bir_lowering=False, debug=False)
    aps = {}
    aps["x1m"] = nc.dram_tensor("x1m", [1024, 2048], F32, kind="ExternalInput").ap()
    for w in ("gwT", "uwT", "dwT"):
        aps[w] = nc.dram_tensor(w, [1024, 1024], F32, kind="ExternalInput").ap()
    for bname in ("gbpk", "ubpk", "dbpk"):
        aps[bname] = nc.dram_tensor(bname, [128, 8], F32, kind="ExternalInput").ap()
    aps["ridx"] = nc.dram_tensor("ridx", [128, 16, 2], I32, kind="ExternalInput").ap()
    aps["rwcol"] = nc.dram_tensor("rwcol", [128, 16], F32, kind="ExternalInput").ap()
    aps["ecol"] = nc.dram_tensor("ecol", [128, 1], F32, kind="ExternalInput").ap()
    aps["moe"] = nc.dram_tensor("moe", [1024, 2048], F32, kind="ExternalOutput").ap()
    aps["ar_in"] = nc.dram_tensor("ar_in", [1024, 2048], F32).ap()
    aps["m_dram"] = nc.dram_tensor("m_dram", [1, 2048], F32).ap()
    aps["ar_out"] = nc.dram_tensor("ar_out", [1024, 2048], F32).ap()
    es = ExitStack()
    with es:
        em = Emit(nc, es)
        emit_moe(em, aps)
        em.finish()
    return nc


def emit_moe(em, cfg):
    x1m = cfg.get("x1m")
    gw = em.sb("b_gw", [128, 8, 1024], BF16)
    uw = em.sb("b_uw", [128, 8, 1024], BF16)
    dw = em.sb("b_dw", [128, 8, 1024], BF16)
    gb = em.sb("b_gb", [128, 8], F32)
    ub = em.sb("b_ub", [128, 8], F32)
    db = em.sb("b_db", [128, 8], F32)
    ridxf = em.sb("b_ridxf", [128, 16, 2], F32)
    ridxi = em.sb("b_ridxi", [128, 16, 2], I32)
    rwcol = em.sb("b_rwcol", [128, 16], F32)
    ecol = em.sb("b_ecol", [128, 1], F32)
    eq = em.sb("b_eq", [128, 16, 2], F32)
    msb = em.sb("b_m", [128, 16], F32)
    mrow = em.sb("b_mrow", [1, 2048], F32)
    Mb = em.sb("b_Mb", [128, 4, 512], F32)
    onesrow = em.sb("b_onesrow", [1, 128], F32)
    x1sb = [em.sb(f"b_x1_{i}", [128, 8, 512], F32) for i in range(2)]
    x1bf = em.sb("b_x1b", [128, 8, 512], BF16)
    g7 = em.sb("b_g7", [128, 8, 512], F32)
    up7 = em.sb("b_up7", [128, 8, 512], F32)
    sg = em.sb("b_sg", [128, 8, 512], F32)
    glu = g7
    inter = em.sb("b_inter", [128, 8, 512], BF16)
    part = up7

    pq = [em.ps(f"b_pq{i}", [128, 512]) for i in range(2)]
    pM = em.ps("b_pM", [128, 512])

    s_in = em.sem("b_in")
    s_w = em.sem("b_w")
    s_pe = em.sem("b_pe")
    s_dve = em.sem("b_dve")
    s_act = em.sem("b_act")
    s_ps = em.sem("b_ps")
    s_cc = em.sem("b_cc")
    s_st = em.sem("b_st")
    s_mr = em.sem("b_mr")

    em._moe_cc = s_cc
    if x1m is None:
        x1m = cfg["ar1_out"]
        em.pending["sync"] = list(em.pending.get("sync", [])) + [cfg["bcast_done"]]

    for w, sb in (("gwT", gw), ("uwT", uw), ("dwT", dw)):
        em.op("gpsimd", lambda g, w=w, sb=sb: g.dma_start(
            out=sb[:], in_=cfg[w].rearrange("(k p) o -> p k o", p=128)), incs=s_w.inc(16))
    for bn, sb in (("gbpk", gb), ("ubpk", ub), ("dbpk", db)):
        em.op("gpsimd", lambda g, bn=bn, sb=sb: g.dma_start(out=sb[:], in_=cfg[bn]), incs=s_w.inc(16))
    em.op("gpsimd", lambda g: g.dma_start(out=ridxi[:], in_=cfg["ridx"]), incs=s_w.inc(16))
    em.op("gpsimd", lambda g: g.dma_start(out=rwcol[:], in_=cfg["rwcol"]), incs=s_w.inc(16))
    em.op("gpsimd", lambda g: g.dma_start(out=ecol[:], in_=cfg["ecol"]), incs=s_w.inc(16))
    wready = s_w.now()

    em.op("vector", lambda v: v.memset(onesrow[:], 1.0), incs=s_dve.inc())
    em.op("vector", lambda v: v.tensor_copy(ridxf[:], ridxi[:]), waits=wready, incs=s_dve.inc())
    em.op("vector", lambda v: v.tensor_single_scalar(eq[:], ridxf[:], ecol[:], OP.is_equal),
          incs=s_dve.inc())
    em.op("vector", lambda v: v.tensor_add(msb[:], eq[:, :, 0], eq[:, :, 1]), incs=s_dve.inc())
    em.op("vector", lambda v: v.tensor_mul(msb[:], msb[:], rwcol[:]), incs=s_dve.inc())
    em.op("sync", lambda sy: sy.dma_start(
        out=cfg["m_dram"].rearrange("q (p a) -> (q p) a", p=128), in_=msb[:]),
        waits=s_dve.now(), incs=s_mr.inc(16))
    em.op("sync", lambda sy: sy.dma_start(out=mrow[:], in_=cfg["m_dram"]),
          waits=s_mr.now(), incs=s_mr.inc(16))
    for q in range(4):
        em.op("tensor", lambda t, q=q: t.matmul(pM[:], onesrow[:], mrow[:, q * 512:(q + 1) * 512],
                                                start=True, stop=True),
              waits=(s_mr.now(), s_dve.now()) if q == 0 else (s_act.now(),), incs=s_pe.inc())
        em.op("scalar", lambda s, q=q: s.copy(Mb[:, q, :], pM[:]), waits=s_pe.now(), incs=s_act.inc())
    m_done = s_act.now()

    freed = [None, None]
    prev_down = None
    for sc in range(4):
        b = sc % 2
        t0 = sc * 512
        w = [wready]
        if freed[b] is not None:
            w.append(freed[b])
        em.op("sync", lambda sy, t0=t0, b=b: sy.dma_start(
            out=x1sb[b][:], in_=x1m.rearrange("(k p) t -> p k t", p=128)[:, :, t0:t0 + 512]),
            waits=tuple(w), incs=s_in.inc(16))
        cw = [s_in.now()]
        if prev_down is not None:
            cw.append(prev_down)
        em.op("scalar", lambda s, b=b: s.copy(
            x1bf[:].rearrange("p a t -> p (a t)"), x1sb[b][:].rearrange("p a t -> p (a t)")),
            waits=tuple(cw), incs=s_act.inc())
        cast_done = s_act.now()

        def ep_gate(ot, pb, pe_spec):
            em.op("vector", lambda v, ot=ot, pb=pb: v.tensor_scalar(
                g7[:, ot, :], pb[:], gb[:, ot:ot + 1], 7.0, OP.add, OP.min),
                waits=pe_spec, incs=s_dve.inc())
            return s_dve.now()
        emit_proj(em, pq, gw, 8, x1bf[:], 8, ep_gate, s_pe,
                  first_waits=(cast_done, m_done))

        def ep_up(ot, pb, pe_spec):
            uw_ = [pe_spec]
            if ot == 0 and sc > 0:
                uw_.append(s_ps.now())
            em.op("vector", lambda v, ot=ot, pb=pb: v.tensor_scalar(
                up7[:, ot, :], pb[:], ub[:, ot:ot + 1], 7.0, OP.add, OP.min),
                waits=tuple(uw_), incs=s_dve.inc())
            em.op("vector", lambda v, ot=ot: v.tensor_scalar(
                up7[:, ot, :], up7[:, ot, :], -7.0, None, OP.max),
                incs=s_dve.inc())
            return s_dve.now()
        emit_proj(em, pq, uw, 8, x1bf[:], 8, ep_up, s_pe)
        gu_done = s_dve.now()

        em.op("scalar", lambda s: s.activation(
            sg[:].rearrange("p a t -> p (a t)"), g7[:].rearrange("p a t -> p (a t)"),
            AF.Sigmoid, scale=1.702), waits=gu_done, incs=s_act.inc())
        em.op("vector", lambda v: v.tensor_mul(
            glu[:].rearrange("p a t -> p (a t)"), g7[:].rearrange("p a t -> p (a t)"),
            sg[:].rearrange("p a t -> p (a t)")), waits=s_act.now(), incs=s_dve.inc())
        em.op("vector", lambda v: v.scalar_tensor_tensor(
            inter[:].rearrange("p a t -> p (a t)"), up7[:].rearrange("p a t -> p (a t)"),
            1.0, glu[:].rearrange("p a t -> p (a t)"), OP.add, OP.mult),
            incs=s_dve.inc())
        inter_done = s_dve.now()

        def ep_down(ot, pb, pe_spec):
            em.op("vector", lambda v, ot=ot, pb=pb, sc=sc: v.scalar_tensor_tensor(
                part[:, ot, :], pb[:], db[:, ot:ot + 1], Mb[:, sc, :], OP.add, OP.mult),
                waits=pe_spec, incs=s_dve.inc())
            return s_dve.now()
        emit_proj(em, pq, dw, 8, inter[:], 8, ep_down, s_pe,
                  first_waits=(inter_done,))
        down_done = prev_down = s_dve.now()
        freed[b] = down_done
        em.op("scalar", lambda s, t0=t0: s.dma_start(
            out=cfg["ar_in"].rearrange("(k p) t -> p k t", p=128)[:, :, t0:t0 + 512],
            in_=part[:]), waits=down_done, incs=s_ps.inc(16))

    em.op("gpsimd", lambda g: g.collective_compute(
        "AllReduce", OP.add, replica_groups=RG8,
        ins=[cfg["ar_in"].opt()], outs=[cfg["ar_out"].opt()]),
        waits=s_ps.now(), incs=s_cc.inc(1))
    if "moe" in cfg:
        em.op("sync", lambda sy: sy.dma_start(out=cfg["moe"], in_=cfg["ar_out"]),
              waits=s_cc.now(), incs=s_st.inc(16))
        em.op("sync", lambda sy: sy.nop(), waits=s_st.now())
    return s_cc.now()


def host_moe_inputs(x1m_fm, expert_id, gate_up_proj, gate_up_bias, down_proj,
                    down_bias, router_indices, routing_weights):
    e = expert_id
    gup = np.asarray(gate_up_proj[e], np.float32)
    gW = gup[:, 0::2]
    uW = gup[:, 1::2]
    gub = np.asarray(gate_up_bias[e], np.float32)
    gb_ = gub[0::2]
    ub_ = gub[1::2]
    dW = np.asarray(down_proj[e], np.float32)
    db_ = np.asarray(down_bias[e], np.float32)
    r = {
        "gwT": np.ascontiguousarray(gW),
        "uwT": np.ascontiguousarray(uW),
        "dwT": np.ascontiguousarray(dW),
        "gbpk": np.ascontiguousarray(gb_.reshape(8, 128).T),
        "ubpk": np.ascontiguousarray(ub_.reshape(8, 128).T),
        "dbpk": np.ascontiguousarray(db_.reshape(8, 128).T),
        "ridx": np.ascontiguousarray(np.asarray(router_indices, np.int32).reshape(128, 16, 2)),
        "rwcol": np.ascontiguousarray(np.asarray(routing_weights, np.float32)[:, e].reshape(128, 16)),
        "ecol": np.full((128, 1), float(e), np.float32),
    }
    if x1m_fm is not None:
        r["x1m"] = np.ascontiguousarray(x1m_fm, np.float32)
    return r

def emit_wc_prep(em, cfg):
    wtmp = em.sb("w_wtmp", [128, 1024], F32)
    wcsb = em.sb("w_wc", [128, 8, 1024], BF16)
    t1b = em.sb("w_t1b", [4, 1024], BF16)
    t2b = em.sb("w_t2b", [50, 1024], BF16)
    ldTb = em.sb("w_ldTb", [100, 4], BF16)
    luTb = em.sb("w_luTb", [4, 50], BF16)
    uaTb = em.sb("w_uaTb", [50, 1024], BF16)
    dab = em.sb("w_dab", [100, 1024], BF16)
    pL = em.ps("w_pL", [128, 1024])
    s_w = em.sem("w_w")
    s_pe = em.sem("w_pe")
    s_act = em.sem("w_act")
    s_dve = em.sem("w_dve")
    s_ld = em.sem("w_ld")
    s_st = em.sem("w_st")
    for n, sb in (("ldT", ldTb), ("luT", luTb), ("uaT", uaTb), ("da", dab)):
        em.op("gpsimd", lambda g, n=n, sb=sb: g.dma_start(out=sb[:], in_=cfg[n]),
              incs=s_w.inc(16))
    wready = s_w.now()
    for h in range(2):
        em.op("tensor", lambda t, h=h: t.matmul(
            pL[0:4, h * 512:(h + 1) * 512], ldTb[:], dab[:, h * 512:(h + 1) * 512],
            start=True, stop=True), waits=wready, incs=s_pe.inc())
    em.op("scalar", lambda s: s.copy(t1b[:], pL[0:4, :]), waits=s_pe.now(), incs=s_act.inc())
    for h in range(2):
        em.op("tensor", lambda t, h=h: t.matmul(
            pL[0:50, h * 512:(h + 1) * 512], luTb[:], t1b[:, h * 512:(h + 1) * 512],
            start=True, stop=True), waits=s_act.now(), incs=s_pe.inc())
    em.op("scalar", lambda s: s.copy(t2b[:], pL[0:50, :]), waits=s_pe.now(), incs=s_act.inc())
    t2_done = s_act.now()
    for mm in range(8):
        lw = [t2_done] if mm == 0 else [s_dve.now()]
        em.op("sync", lambda sy, mm=mm: sy.dma_start(
            out=wtmp[:], in_=cfg["wT"].rearrange("(k p) o -> p k o", p=128)[:, mm, :]),
            waits=tuple(lw) if mm > 0 else (), incs=s_ld.inc(16))
        ld_here = s_ld.now()
        for h in range(2):
            em.op("tensor", lambda t, mm=mm, h=h: t.matmul(
                pL[:, h * 512:(h + 1) * 512], t2b[:, mm * 128:(mm + 1) * 128],
                uaTb[:, h * 512:(h + 1) * 512], start=True, stop=True),
                waits=tuple(lw) if h == 0 else (), incs=s_pe.inc() if h == 1 else ())
        em.op("vector", lambda v, mm=mm: v.tensor_add(wcsb[:, mm, :], wtmp[:], pL[:, :]),
              waits=(s_pe.now(), ld_here), incs=s_dve.inc())
    em.op("scalar", lambda s: s.dma_start(out=cfg["wc_dram"], in_=wcsb[:]),
          waits=s_dve.now(), incs=s_st.inc(16))
    return s_st.now()





def build_l1attn(T=4096, with_madd=False):
    nc = bass.Bass("TRN2", target_bir_lowering=False, debug=False)
    aps = {}
    aps["x2"] = nc.dram_tensor("x2", [1024, T], F32, kind="ExternalInput").ap()
    if with_madd:
        aps["madd"] = nc.dram_tensor("madd", [1024, T], F32, kind="ExternalInput").ap()
    for w in ("wqT", "wkT", "wvT", "wgT", "woT"):
        aps[w] = nc.dram_tensor(w, [1024, 1024], F32, kind="ExternalInput").ap()
    aps["ld0"] = nc.dram_tensor("ld0", [1, 1], F32, kind="ExternalInput").ap()
    aps["diffblk"] = nc.dram_tensor("diffblk", [128, 128], F32, kind="ExternalInput").ap()
    aps["offneg"] = nc.dram_tensor("offneg", [128, 128], F32, kind="ExternalInput").ap()
    aps["out"] = nc.dram_tensor("out", [1024, T], F32, kind="ExternalOutput").ap()
    aps["T"] = T
    es = ExitStack()
    with es:
        em = Emit(nc, es)
        emit_l1attn(em, aps)
        em.finish()
    return nc


def emit_l1attn(em, cfg):
    T = cfg["T"]
    NSC = T // 512
    x2, out = cfg["x2"], cfg["out"]

    wq = em.sb("c_wq", [128, 8, 1024], BF16)
    wk = em.sb("c_wk", [128, 8, 1024], BF16)
    wv = em.sb("c_wv", [128, 8, 1024], BF16)
    wg = em.sb("c_wg", [128, 8, 1024], BF16)
    wo = em.sb("c_wo", [128, 8, 1024], BF16)
    diffblk = em.sb("c_diffblk", [128, 128], F32)
    offneg = em.sb("c_offneg", [128, 128], F32)
    ld0t = em.sb("c_ld0", [1, 1], F32)
    mst = em.sb("c_mst", [1, 2], F32)
    lndc = em.sb("c_lndc", [128, 1], F32)
    m32 = em.sb("c_m32", [128, 128], F32)

    x2sb = [em.sb(f"c_x2_{i}", [128, 8, 512], F32) for i in range(2)]
    x2bf = em.sb("c_x2b", [128, 8, 512], BF16)
    qbf = em.sb("c_qbf", [128, 8, 512], BF16)
    kbf = em.sb("c_kbf", [128, 8, 512], BF16)
    vbf_ = em.sb("c_vbf", [128, 8, 512], BF16)
    gate = em.sb("c_gate", [128, 8, 512], F32)
    vtok = em.sb("c_vtok", [128, 2, 1024], BF16)
    s2b = [em.sb(f"c_s2b_{i}", [128, 128], BF16) for i in range(2)]
    gob = x2bf
    vsb = em.sb("c_v", [128, 8, 512], F32)
    dcfg = cfg.get("d")
    if dcfg is not None:
        wc = em.sb("c_wcd", [128, 8, 1024], BF16)
        bcol = em.sb("c_bcol", [128, 8], F32)

    pq = [em.ps(f"c_pq{i}", [128, 512]) for i in range(2)]
    pO = em.ps("c_pO", [128, 1024])
    if dcfg is not None:
        ln = LNKit(em, "c_ln", bufs=(x2bf[:], kbf[:]))
    else:
        ln = cfg.get("ln") or LNKit(em, "c_ln")

    s_in = em.sem("c_in")
    s_w = em.sem("c_w")
    s_pe = em.sem("c_pe")
    s_dve = em.sem("c_dve")
    s_act = em.sem("c_act")
    s_tr = em.sem("c_tr")
    s_st = em.sem("c_st")

    if "w_bf" in cfg:
        for qn, w, sb in (("sync", "wqT", wq), ("scalar", "wkT", wk), ("sync", "wvT", wv),
                          ("scalar", "wgT", wg), ("sync", "woT", wo)):
            em.op(qn, lambda e, w=w, sb=sb: e.dma_start(
                out=sb[:], in_=cfg["w_bf"][w].rearrange("(k p) o -> p k o", p=128)),
                waits=(cfg["w_bf_done"],), incs=s_w.inc(16))
    else:
        for w, sb in (("wqT", wq), ("wkT", wk), ("wvT", wv), ("wgT", wg), ("woT", wo)):
            em.op("gpsimd", lambda g, w=w, sb=sb: g.dma_start(
                out=sb[:], in_=cfg[w].rearrange("(k p) o -> p k o", p=128)), incs=s_w.inc(16))
    em.op("scalar", lambda e: e.dma_start(out=diffblk[:], in_=cfg["diffblk"]), incs=s_w.inc(16))
    em.op("scalar", lambda e: e.dma_start(out=offneg[:], in_=cfg["offneg"]), incs=s_w.inc(16))
    em.op("scalar", lambda e: e.dma_start(out=ld0t[:], in_=cfg["ld0"]), incs=s_w.inc(16))
    if dcfg is not None:
        em.op("sync", lambda sy: sy.dma_start(out=wc[:], in_=dcfg["wc_dram"]),
              waits=(dcfg["wc_done"],), incs=s_w.inc(16))
        em.op("sync", lambda sy: sy.dma_start(out=bcol[:], in_=dcfg["bpk"]), incs=s_w.inc(16))
    wready = s_w.now()

    em.op("scalar", lambda s: s.activation(mst[:, 0:1], ld0t[:], AF.Sigmoid),
          waits=wready, incs=s_act.inc())
    em.op("scalar", lambda s: s.activation(mst[:, 1:2], mst[:, 0:1], AF.Ln),
          incs=s_act.inc())
    em.op("tensor", lambda t: t.matmul(pq[0][:, 0:1], ln.onesrow[:], mst[:, 1:2], start=True, stop=True),
          waits=(s_act.now(), ln.init_done), incs=s_pe.inc())
    em.op("scalar", lambda s: s.copy(lndc[:], pq[0][:, 0:1]), waits=s_pe.now(), incs=s_act.inc())
    em.bank_rel[id(pq[0])] = s_act.now()
    em.op("vector", lambda v: v.tensor_scalar_mul(m32[:], diffblk[:], lndc[:]),
          waits=s_act.now(), incs=s_dve.inc())
    em.op("vector", lambda v: v.tensor_add(m32[:], m32[:], offneg[:]), incs=s_dve.inc())
    em.op("scalar", lambda s: s.activation(m32[:], m32[:], AF.Exp), waits=s_dve.now(),
          incs=s_act.inc())
    mask_done = s_act.now()

    freed = [None, None]
    osb_freed = [None, None]
    prev_wo = None
    prev_ln_casts = None
    prev_vsb_free = None
    sc_order = list(range(NSC))
    if "moesum" in cfg:
        sc_order = [s_ for s_ in range(NSC) if s_ >= 4] + [s_ for s_ in range(NSC) if s_ < 4]
    for idx_, sc in enumerate(sc_order):
        b = idx_ % 2
        t0 = sc * 512
        w = [wready]
        if freed[b] is not None:
            w.append(freed[b])
        em.op("sync", lambda sy, t0=t0, b=b: sy.dma_start(
            out=x2sb[b][:], in_=x2.rearrange("(k p) t -> p k t", p=128)[:, :, t0:t0 + 512]),
            waits=tuple(w), incs=s_in.inc(16))
        if "madd" in cfg:
            mw = [s_in.now()]
            if prev_vsb_free is not None:
                mw.append(prev_vsb_free)
            em.op("sync", lambda sy, t0=t0: sy.dma_start(
                out=vsb[:], in_=cfg["madd"].rearrange("(k p) t -> p k t", p=128)[:, :, t0:t0 + 512]),
                waits=tuple(mw), incs=s_in.inc(16))
            em.op("vector", lambda v, b=b: v.tensor_add(
                vsb[:].rearrange("p a t -> p (a t)"), vsb[:].rearrange("p a t -> p (a t)"),
                x2sb[b][:].rearrange("p a t -> p (a t)")),
                waits=s_in.now(), incs=s_dve.inc())
            ln.emit(vsb[:], x2sb[b][:], v_ready=(s_dve.now(),))
        elif "moesum" in cfg:
            if sc < 4:
                mw = [s_in.now(), cfg["moe_done"]]
                if prev_vsb_free is not None:
                    mw.append(prev_vsb_free)
                em.op("sync", lambda sy, t0_=sc * 512: sy.dma_start(
                    out=vsb[:], in_=cfg["moesum"].rearrange("(k p) t -> p k t", p=128)[:, :, t0_:t0_ + 512]),
                    waits=tuple(mw), incs=s_in.inc(16))
                em.op("vector", lambda v, b=b: v.scalar_tensor_tensor(
                    vsb[:].rearrange("p a t -> p (a t)"), vsb[:].rearrange("p a t -> p (a t)"),
                    cfg["mgate_sb"][:], x2sb[b][:].rearrange("p a t -> p (a t)"),
                    OP.mult, OP.add), waits=s_in.now(), incs=s_dve.inc())
                ln.emit(vsb[:], x2sb[b][:], v_ready=(s_dve.now(),))
            else:
                ln.emit(x2sb[b][:], x2sb[b][:], v_ready=(s_in.now(),))
        cw = [s_in.now()]
        if prev_wo is not None:
            cw.append(prev_wo)
        if "madd" in cfg or "moesum" in cfg:
            cw.append(ln.s_dve.now())
        em.op("scalar", lambda s, b=b: s.copy(
            x2bf[:].rearrange("p a t -> p (a t)"), x2sb[b][:].rearrange("p a t -> p (a t)")),
            waits=tuple(cw), incs=s_act.inc())
        cast_done = s_act.now()

        def mk_ep(dst, act_fn):
            def ep(ot, pb, pe_spec):
                em.op("scalar", lambda s, ot=ot, pb=pb: act_fn(s, dst, ot, pb),
                      waits=pe_spec, incs=s_act.inc())
                return s_act.now()
            return ep
        emit_proj(em, pq, wq, 8, x2bf[:], 8,
                  mk_ep(qbf, lambda s, d, ot, pb: s.copy(d[:, ot, :], pb[:])),
                  s_pe, first_waits=(cast_done, mask_done))
        emit_proj(em, pq, wk, 8, x2bf[:], 8,
                  mk_ep(kbf, lambda s, d, ot, pb: s.copy(d[:, ot, :], pb[:])), s_pe)
        emit_proj(em, pq, wv, 8, x2bf[:], 8,
                  mk_ep(vbf_, lambda s, d, ot, pb: s.copy(d[:, ot, :], pb[:])), s_pe)
        emit_proj(em, pq, wg, 8, x2bf[:], 8,
                  mk_ep(gate, lambda s, d, ot, pb: s.activation(d[:, ot, :], pb[:], AF.Sigmoid)), s_pe)
        qkvg_done = s_act.now()

        prev_gate_tt = None
        gate_tts = [None, None]
        for c in range(4):
            cs = slice(c * 128, (c + 1) * 128)
            sp = pq[c % 2]
            trw = [qkvg_done]
            if gate_tts[c % 2] is not None:
                trw.append(gate_tts[c % 2])
            for ft in range(8):
                em.op("sync", lambda sy, c=c, ft=ft: sy.dma_start_transpose(
                    vtok[:, c % 2, ft * 128:(ft + 1) * 128], vbf_[:, ft, c * 128:(c + 1) * 128]),
                    waits=tuple(trw) if ft == 0 else (), incs=s_tr.inc(16))
            tr_done = s_tr.now()
            for k in range(8):
                wts0 = [qkvg_done]
                if em.bank_rel.get(id(sp)) is not None:
                    wts0.append(em.bank_rel[id(sp)])
                em.op("tensor", lambda t, k=k, c=c, sp=sp, cs=cs, b=b: t.matmul(
                    sp[:, 0:128], kbf[:, k, cs], qbf[:, k, cs],
                    start=(k == 0), stop=(k == 7)),
                    waits=tuple(wts0) if k == 0 else (),
                    incs=s_pe.inc() if k == 7 else ())
            em.op("vector", lambda v, c=c, sp=sp: v.tensor_tensor(
                s2b[c % 2][:], sp[:, 0:128], m32[:], OP.mult),
                waits=s_pe.now(), incs=s_dve.inc())
            s2_done = s_dve.now()
            em.bank_rel[id(sp)] = s2_done
            for ft in range(8):
                wts = []
                if ft == 0:
                    wts = [s2_done, tr_done]
                    if prev_gate_tt is not None:
                        wts.append(prev_gate_tt)
                em.op("tensor", lambda t, c=c, ft=ft: t.matmul(
                    pO[:, ft * 128:(ft + 1) * 128], vtok[:, c % 2, ft * 128:(ft + 1) * 128],
                    s2b[c % 2][:], start=True, stop=True),
                    waits=tuple(wts), incs=s_pe.inc() if ft == 7 else ())
            em.op("vector", lambda v, c=c, cs=cs: v.tensor_tensor(
                gob[:, :, cs], gate[:, :, cs],
                pO[:].rearrange("p (a t) -> p a t", a=8), OP.mult),
                waits=s_pe.now(), incs=s_dve.inc())
            prev_gate_tt = gate_tts[c % 2] = s_dve.now()

        pln = prev_ln_casts
        pvf = prev_vsb_free
        def ep_wo(ot, pb, pe_spec):
            wts = [pe_spec]
            if pln is not None and ot == 0:
                wts.append(pln)
            if pvf is not None and ot == 0:
                wts.append(pvf)
            if ("madd" in cfg or "moesum" in cfg) and ot == 0:
                wts.append(ln.s_dve.now())
            em.op("vector", lambda v, ot=ot, pb=pb, b=b: v.scalar_tensor_tensor(
                vsb[:, ot, :], x2sb[b][:, ot, :], 2.0, pb[:], OP.mult, OP.add),
                waits=tuple(wts), incs=s_dve.inc())
            return s_dve.now()
        emit_proj(em, pq, wo, 8, gob[:], 8, ep_wo, s_pe,
                  first_waits=(prev_gate_tt,))
        v_done = s_dve.now()
        prev_wo = s_pe.now()

        if dcfg is None:
            ow = [] if osb_freed[b] is None else [osb_freed[b]]
            apply_done = ln.emit(vsb[:], x2sb[b][:], v_ready=(v_done,), out_waits=tuple(ow))[0]
            prev_ln_casts = ln.last_casts
            prev_vsb_free = apply_done
            freed[b] = apply_done
            em.op("sync", lambda sy, t0=t0, b=b: sy.dma_start(
                out=out.rearrange("(k p) t -> p k t", p=128)[:, :, t0:t0 + 512], in_=x2sb[b][:]),
                waits=apply_done, incs=s_st.inc(16))
            osb_freed[b] = s_st.now()
        else:
            apply_done = ln.emit(vsb[:], x2sb[b][:], v_ready=(v_done,))[0]
            prev_ln_casts = ln.last_casts
            x3bf = qbf
            em.op("scalar", lambda s, b=b: s.copy(
                x3bf[:].rearrange("p a t -> p (a t)"), x2sb[b][:].rearrange("p a t -> p (a t)")),
                waits=apply_done, incs=s_act.inc())
            cast_d = s_act.now()
            def ep_d(ot, pb, pe_spec):
                em.op("vector", lambda v, ot=ot, pb=pb, b=b: v.scalar_tensor_tensor(
                    gate[:, ot, :], pb[:], bcol[:, ot:ot + 1], x2sb[b][:, ot, :],
                    OP.add, OP.add), waits=pe_spec, incs=s_dve.inc())
                return s_dve.now()
            emit_proj(em, pq, wc, 8, x3bf[:], 8, ep_d, s_pe, first_waits=(cast_d,))
            v4_done = s_dve.now()
            freed[b] = v4_done
            ow = [] if osb_freed[0] is None else [osb_freed[0]]
            dln = ln.emit(gate[:], vsb[:], v_ready=(v4_done,), out_waits=tuple(ow))
            prev_ln_casts = ln.last_casts
            d_gate_guard = dln[0]
            em.op("sync", lambda sy, t0=t0: sy.dma_start(
                out=out.rearrange("(k p) t -> p k t", p=128)[:, :, t0:t0 + 512], in_=vsb[:]),
                waits=dln[0], incs=s_st.inc(16))
            osb_freed[0] = prev_vsb_free = s_st.now()
    em.op("sync", lambda sy: sy.nop(), waits=s_st.now())


def host_l1attn_inputs(x2_fm, W_q, W_k, W_v, W_g, W_o, log_decay, consts):
    return {
        "x2": np.ascontiguousarray(x2_fm, np.float32),
        "wqT": host_wT(W_q), "wkT": host_wT(W_k), "wvT": host_wT(W_v),
        "wgT": host_wT(W_g), "woT": host_wT(W_o),
        "ld0": np.asarray(log_decay[0], np.float32).reshape(1, 1),
        "diffblk": consts["diffblk"], "offneg": consts["offneg"],
    }




def build_l1ffn(T=4096):
    nc = bass.Bass("TRN2", target_bir_lowering=False, debug=False)
    x3 = nc.dram_tensor("x3", [1024, T], F32, kind="ExternalInput").ap()
    wT = nc.dram_tensor("wT", [1024, 1024], F32, kind="ExternalInput").ap()
    bpk = nc.dram_tensor("bpk", [128, 8], F32, kind="ExternalInput").ap()
    ldT = nc.dram_tensor("ldT", [100, 4], F32, kind="ExternalInput").ap()
    luT = nc.dram_tensor("luT", [4, 50], F32, kind="ExternalInput").ap()
    uaT = nc.dram_tensor("uaT", [50, 1024], F32, kind="ExternalInput").ap()
    da = nc.dram_tensor("da", [100, 1024], F32, kind="ExternalInput").ap()
    out = nc.dram_tensor("out", [1024, T], F32, kind="ExternalOutput").ap()
    es = ExitStack()
    with es:
        em = Emit(nc, es)
        emit_l1ffn(em, dict(x3=x3, wT=wT, bpk=bpk, ldT=ldT, luT=luT, uaT=uaT,
                            da=da, out=out, T=T))
        em.finish()
    return nc


def emit_l1ffn(em, cfg):
    nc = em.nc
    T = cfg["T"]
    NSC = T // 512
    x3, wT, bpk, out = cfg["x3"], cfg["wT"], cfg["bpk"], cfg["out"]

    wTf = em.sb("d_wTf", [128, 8, 1024], F32)
    wc = em.sb("d_wc", [128, 8, 1024], BF16)
    bcol = em.sb("d_bcol", [128, 8], F32)
    ones128 = em.sb("d_ones128", [128, 1], BF16)
    onesrow = em.sb("d_onesrow", [1, 128], F32)
    t1b = em.sb("d_t1b", [4, 1024], BF16)
    t2b = em.sb("d_t2b", [50, 1024], BF16)
    ldTb = em.sb("d_ldTb", [100, 4], BF16)
    luTb = em.sb("d_luTb", [4, 50], BF16)
    uaTb = em.sb("d_uaTb", [50, 1024], BF16)
    dab = em.sb("d_dab", [100, 1024], BF16)
    x3sb = [em.sb(f"d_x3_{i}", [128, 8, 512], F32) for i in range(2)]
    x3bf = [em.sb(f"d_x3b_{i}", [128, 8, 512], BF16) for i in range(2)]
    vsb = [em.sb(f"d_v_{i}", [128, 8, 512], F32) for i in range(2)]
    osb = [em.sb(f"d_o_{i}", [128, 8, 512], F32) for i in range(2)]

    pp = [em.ps(f"d_pp{i}", [128, 512]) for i in range(2)]
    pL = em.ps("d_pL", [128, 1024])
    ln = cfg.get("ln") or LNKit(em, "d_ln")

    s_in = em.sem("d_in")
    s_w = em.sem("d_w")
    s_pe = em.sem("d_pe")
    s_dve = em.sem("d_dve")
    s_act = em.sem("d_act")
    s_st = em.sem("d_st")

    em.op("gpsimd", lambda g: g.dma_start(out=wTf[:], in_=wT.rearrange("(k p) o -> p k o", p=128)), incs=s_w.inc(16))
    em.op("gpsimd", lambda g: g.dma_start(out=ldTb[:], in_=cfg["ldT"]), incs=s_w.inc(16))
    em.op("gpsimd", lambda g: g.dma_start(out=luTb[:], in_=cfg["luT"]), incs=s_w.inc(16))
    em.op("gpsimd", lambda g: g.dma_start(out=uaTb[:], in_=cfg["uaT"]), incs=s_w.inc(16))
    em.op("gpsimd", lambda g: g.dma_start(out=dab[:], in_=cfg["da"]), incs=s_w.inc(16))
    em.op("gpsimd", lambda g: g.dma_start(out=bcol[:], in_=bpk), incs=s_w.inc(16))
    wready = s_w.now()

    for h in range(2):
        em.op("tensor", lambda t, h=h: t.matmul(pL[0:4, h * 512:(h + 1) * 512], ldTb[:], dab[:, h * 512:(h + 1) * 512], start=True, stop=True),
              waits=wready, incs=s_pe.inc())
    em.op("scalar", lambda s: s.copy(t1b[:], pL[0:4, :]), waits=s_pe.now(), incs=s_act.inc())
    for h in range(2):
        em.op("tensor", lambda t, h=h: t.matmul(pL[0:50, h * 512:(h + 1) * 512], luTb[:], t1b[:, h * 512:(h + 1) * 512], start=True, stop=True),
              waits=s_act.now(), incs=s_pe.inc())
    em.op("scalar", lambda s: s.copy(t2b[:], pL[0:50, :]), waits=s_pe.now(), incs=s_act.inc())
    t2_done = s_act.now()
    for m in range(8):
        for h in range(2):
            em.op("tensor", lambda t, m=m, h=h: t.matmul(
                pL[:, h * 512:(h + 1) * 512], t2b[:, m * 128:(m + 1) * 128], uaTb[:, h * 512:(h + 1) * 512], start=True, stop=True),
                waits=(t2_done if m == 0 else s_dve.now()) if h == 0 else (), incs=s_pe.inc() if h == 1 else ())
        em.op("vector", lambda v, m=m: v.tensor_add(wc[:, m, :], wTf[:, m, :], pL[:, :]),
              waits=s_pe.now(), incs=s_dve.inc())

    freed = [None, None]
    osb_freed = [None, None]
    prev_sums = None
    for sc in range(NSC):
        b = sc % 2
        t0 = sc * 512
        w = [wready]
        if freed[b] is not None:
            w.append(freed[b])
        em.op("sync", lambda sy, t0=t0, b=b: sy.dma_start(
            out=x3sb[b][:], in_=x3.rearrange("(k p) t -> p k t", p=128)[:, :, t0:t0 + 512]),
            waits=tuple(w), incs=s_in.inc(16))
        ld_here = s_in.now()
        em.op("scalar", lambda s, b=b: s.copy(
            x3bf[b][:].rearrange("p a t -> p (a t)"), x3sb[b][:].rearrange("p a t -> p (a t)")),
            waits=ld_here, incs=s_act.inc())
        cast_done = s_act.now()
        for ot in range(8):
            pb = pp[ot % 2]
            for k in range(8):
                wts = ()
                if k == 0:
                    wts = (cast_done, s_dve.now())
                em.op("tensor", lambda t, b=b, ot=ot, k=k, pb=pb: t.matmul(
                    pb[:], wc[:, k, ot * 128:(ot + 1) * 128], x3bf[b][:, k, :],
                    start=(k == 0), stop=(k == 7)),
                    waits=wts, incs=s_pe.inc() if k == 7 else ())
            em.op("vector", lambda v, b=b, ot=ot, pb=pb: v.scalar_tensor_tensor(
                vsb[b][:, ot, :], pb[:], bcol[:, ot:ot + 1], x3sb[b][:, ot, :], OP.add, OP.add),
                waits=s_pe.now(), incs=s_dve.inc())
        v_done = s_dve.now()
        ow = [] if osb_freed[b] is None else [osb_freed[b]]
        apply_done = ln.emit(vsb[b][:], osb[b][:], v_ready=(v_done,), out_waits=tuple(ow))[0]
        freed[b] = apply_done
        em.op("sync", lambda sy, t0=t0, b=b: sy.dma_start(
            out=out.rearrange("(k p) t -> p k t", p=128)[:, :, t0:t0 + 512], in_=osb[b][:]),
            waits=apply_done, incs=s_st.inc(16))
        osb_freed[b] = s_st.now()
    em.op("sync", lambda sy: sy.nop(), waits=s_st.now())


def host_l1ffn_inputs(x3_fm, W, b, lora_down, lora_up, down_aux, up_aux):
    return {
        "x3": np.ascontiguousarray(x3_fm, dtype=np.float32),
        "wT": host_wT(W),
        "bpk": np.ascontiguousarray(np.asarray(b, np.float32).reshape(8, 128).T),
        "ldT": host_wT(lora_down),
        "luT": host_wT(lora_up),
        "uaT": host_wT(up_aux),
        "da": np.ascontiguousarray(np.asarray(down_aux, np.float32)),
        "out": None,
    }








def kernel_unfused(**inputs):
    T = 4096
    consts = host_consts()
    u = np.asarray(inputs["L0_attn_u"], np.float32)

    common_a = host_mamba_inputs_common(
        inputs["L0_attn_in_proj_w"], inputs["L0_attn_conv_w"], inputs["L0_attn_conv_b"],
        inputs["L0_attn_dt_bias"], inputs["L0_attn_A_log"], inputs["L0_attn_D"],
        inputs["L0_attn_rms_w"], inputs["L0_attn_out_w"], consts)
    in_maps = []
    for c in range(8):
        bb, t0 = c // 4, (c % 4) * T
        if t0 == 0:
            uh = np.concatenate([np.zeros((128, 1024), np.float32), u[bb, 0:T]], axis=0)
        else:
            uh = u[bb, t0 - 128:t0 + T]
        m = dict(common_a)
        m["u"] = np.ascontiguousarray(uh.T)
        m["m2T"], m["selcol"] = host_core_masks(c)
        in_maps.append(m)
    nc_a = build_mamba(T)
    res_a = run_bass_kernel_spmd(nc_a, in_maps, core_ids=list(range(8)))
    x1 = [res_a.results[c]["x1"] for c in range(8)]

    x1m = np.ascontiguousarray(x1[0][:, 0:2048])
    in_maps = [host_moe_inputs(
        x1m, e, inputs["L0_ffn_gate_up_proj"], inputs["L0_ffn_gate_up_proj_bias"],
        inputs["L0_ffn_down_proj"], inputs["L0_ffn_down_proj_bias"],
        inputs["L0_ffn_router_indices"], inputs["L0_ffn_routing_weights"]) for e in range(8)]
    nc_b = build_moe()
    res_b = run_bass_kernel_spmd(nc_b, in_maps, core_ids=list(range(8)))
    moe = res_b.results[0]["moe"]

    common_c = host_l1attn_inputs_common(
        inputs["L1_attn_W_q"], inputs["L1_attn_W_k"], inputs["L1_attn_W_v"],
        inputs["L1_attn_W_g"], inputs["L1_attn_W_o"], inputs["L1_attn_log_decay"], consts)
    in_maps = []
    for c in range(8):
        m = dict(common_c)
        m["x2"] = x1[c]
        madd = np.zeros((1024, T), np.float32)
        if c == 0:
            madd[:, 0:2048] = moe
        m["madd"] = madd
        in_maps.append(m)
    nc_c = build_l1attn(T, with_madd=True)
    res_c = run_bass_kernel_spmd(nc_c, in_maps, core_ids=list(range(8)))
    x3 = [res_c.results[c]["out"] for c in range(8)]

    common_d = host_l1ffn_inputs_common(
        inputs["L1_ffn_W"], inputs["L1_ffn_b"], inputs["L1_ffn_lora_down"],
        inputs["L1_ffn_lora_up"], inputs["L1_ffn_down_aux"], inputs["L1_ffn_up_aux"])
    in_maps = []
    for c in range(8):
        m = dict(common_d)
        m["x3"] = x3[c]
        in_maps.append(m)
    nc_d = build_l1ffn(T)
    res_d = run_bass_kernel_spmd(nc_d, in_maps, core_ids=list(range(8)))

    out = np.empty((2, 16384, 1024), np.float32)
    for c in range(8):
        bb, t0 = c // 4, (c % 4) * T
        out[bb, t0:t0 + T] = res_d.results[c]["out"].T
    return out


def host_mamba_inputs_common(in_proj_w, conv_w, conv_b, dt_bias, A_log, D, rms_w,
                             out_w, consts):
    W = np.asarray(in_proj_w, np.float32)
    wT = W.T
    wxbd = np.ascontiguousarray(np.concatenate(
        [wT[:, 2048:4096], wT[:, 4096:4160], wT[:, 4224:4240]], axis=1))
    wzc = np.ascontiguousarray(np.concatenate(
        [wT[:, 0:2048], wT[:, 4160:4224]], axis=1))
    cwf = np.asarray(conv_w, np.float32)
    cbf = np.asarray(conv_b, np.float32)
    cw18 = np.zeros((128, 18, 4), np.float32)
    cb18 = np.zeros((128, 18), np.float32)
    cw18[:, 0:16, :] = cwf[0:2048].reshape(16, 128, 4).transpose(1, 0, 2)
    cb18[:, 0:16] = cbf[0:2048].reshape(16, 128).T
    cw18[0:64, 16, :] = cwf[2048:2112]
    cb18[0:64, 16] = cbf[2048:2112]
    cw18[0:64, 17, :] = cwf[2112:2176]
    cb18[0:64, 17] = cbf[2112:2176]
    return {
        "wxbd": wxbd, "wzc": wzc, "owT": host_wT(out_w),
        "cwcol": cw18, "cbcol": cb18,
        "dtbcol": np.asarray(dt_bias, np.float32).reshape(16, 1),
        "alog": np.asarray(A_log, np.float32).reshape(1, 16),
        "dvec": np.asarray(D, np.float32).reshape(1, 16),
        "rmspk": np.ascontiguousarray(np.asarray(rms_w, np.float32).reshape(16, 128).T),
        "triuinc": consts["triuinc"], "upperneg": consts["upperneg"],
        "ident": consts["ident"],
    }


def host_core_masks(core_id):
    slot = core_id % 4
    m2 = np.zeros((8, 8), np.float32)
    sel = np.zeros((8, 1), np.float32)
    base = (core_id // 4) * 4
    for j in range(base, base + slot):
        sel[j, 0] = 1.0
        for i in range(j + 1, base + slot):
            m2[j, i] = 1.0
    return np.ascontiguousarray(m2.T), sel


def host_l1attn_inputs_common(W_q, W_k, W_v, W_g, W_o, log_decay, consts):
    return {
        "wqT": host_wT(W_q), "wkT": host_wT(W_k), "wvT": host_wT(W_v),
        "wgT": host_wT(W_g), "woT": host_wT(W_o),
        "ld0": np.ascontiguousarray(np.asarray(log_decay, np.float32)[0:1].reshape(1, 1)),
        "diffblk": consts["diffblk"], "offneg": consts["offneg"],
    }


def host_l1ffn_inputs_common(W, b, lora_down, lora_up, down_aux, up_aux):
    return {
        "wT": host_wT(W),
        "bpk": np.ascontiguousarray(np.asarray(b, np.float32).reshape(8, 128).T),
        "ldT": host_wT(lora_down),
        "luT": host_wT(lora_up),
        "uaT": host_wT(up_aux),
        "da": np.ascontiguousarray(np.asarray(down_aux, np.float32)),
    }









def build_fused(T=4096):
    nc = bass.Bass("TRN2", target_bir_lowering=False, debug=False)
    a = {"T": T}
    a["u"] = nc.dram_tensor("u", [1024, 128 + T], F32, kind="ExternalInput").ap()
    a["wxbd"] = nc.dram_tensor("wxbd", [1024, 2128], F32, kind="ExternalInput").ap()
    a["wzc"] = nc.dram_tensor("wzc", [1024, 2112], F32, kind="ExternalInput").ap()
    a["owT"] = nc.dram_tensor("owT", [2048, 1024], F32, kind="ExternalInput").ap()
    a["cwcol"] = nc.dram_tensor("cwcol", [128, 18, 4], F32, kind="ExternalInput").ap()
    a["cbcol"] = nc.dram_tensor("cbcol", [128, 18], F32, kind="ExternalInput").ap()
    a["dtbcol"] = nc.dram_tensor("dtbcol", [16, 1], F32, kind="ExternalInput").ap()
    a["alog"] = nc.dram_tensor("alog", [1, 16], F32, kind="ExternalInput").ap()
    a["dvec"] = nc.dram_tensor("dvec", [1, 16], F32, kind="ExternalInput").ap()
    a["rmspk"] = nc.dram_tensor("rmspk", [128, 16], F32, kind="ExternalInput").ap()
    a["triuinc"] = nc.dram_tensor("triuinc", [128, 128], F32, kind="ExternalInput").ap()
    a["upperneg"] = nc.dram_tensor("upperneg", [128, 128], F32, kind="ExternalInput").ap()
    a["ident"] = nc.dram_tensor("ident", [128, 128], F32, kind="ExternalInput").ap()
    a["m2T"] = nc.dram_tensor("m2T", [8, 8], F32, kind="ExternalInput").ap()
    a["selcol"] = nc.dram_tensor("selcol", [8, 1], F32, kind="ExternalInput").ap()
    a["x1"] = nc.dram_tensor("x1", [1024, T], F32).ap()
    a["sp_xtok"] = nc.dram_tensor("sp_xtok", [T, 2048], BF16).ap()
    a["sp_btok"] = nc.dram_tensor("sp_btok", [T, 64], BF16).ap()
    a["sp_bfeat"] = nc.dram_tensor("sp_bfeat", [64, T], BF16).ap()
    a["sp_dt"] = nc.dram_tensor("sp_dt", [16, T], F32).ap()
    a["w8d"] = nc.dram_tensor("w8d", [8, 16], F32).ap()
    a["cum_dram"] = nc.dram_tensor("cum_dram", [16, 128], F32).ap()
    a["ow_bf"] = nc.dram_tensor("ow_bf", [8, 128, 16, 128], BF16).ap()
    a["ag_in"] = nc.dram_tensor("ag_in", [65, 2048], F32).ap()
    a["ag_out"] = nc.dram_tensor("ag_out", [520, 2048], F32).ap()
    b = {}
    for w in ("gwT", "uwT", "dwT"):
        b[w] = nc.dram_tensor(w, [1024, 1024], F32, kind="ExternalInput").ap()
    for bn in ("gbpk", "ubpk", "dbpk"):
        b[bn] = nc.dram_tensor(bn, [128, 8], F32, kind="ExternalInput").ap()
    b["ridx"] = nc.dram_tensor("ridx", [128, 16, 2], I32, kind="ExternalInput").ap()
    b["rwcol"] = nc.dram_tensor("rwcol", [128, 16], F32, kind="ExternalInput").ap()
    b["ecol"] = nc.dram_tensor("ecol", [128, 1], F32, kind="ExternalInput").ap()
    b["mgatec"] = nc.dram_tensor("mgatec", [128, 1], F32, kind="ExternalInput").ap()
    b["x1sc"] = a["x1"]
    b["ar1_in"] = nc.dram_tensor("ar1_in", [1024, 2048], F32).ap()
    b["ar1_out"] = nc.dram_tensor("ar1_out", [1024, 2048], F32).ap()
    b["m_dram"] = nc.dram_tensor("m_dram", [1, 2048], F32).ap()
    b["ar_in"] = nc.dram_tensor("ar_in", [1024, 2048], F32).ap()
    b["ar_out"] = nc.dram_tensor("ar_out", [1024, 2048], F32).ap()
    c = {"T": T}
    c["x2"] = a["x1"]
    for w in ("wqT", "wkT", "wvT", "wgT", "woT"):
        c[w] = nc.dram_tensor(w, [1024, 1024], F32, kind="ExternalInput").ap()
    c["ld0"] = nc.dram_tensor("ld0", [1, 1], F32, kind="ExternalInput").ap()
    c["diffblk"] = nc.dram_tensor("diffblk", [128, 128], F32, kind="ExternalInput").ap()
    c["offneg"] = nc.dram_tensor("offneg", [128, 128], F32, kind="ExternalInput").ap()
    c["moesum"] = b["ar_out"]
    c["out"] = nc.dram_tensor("out", [1024, T], F32, kind="ExternalOutput").ap()
    d = {"T": T}
    d["wT"] = nc.dram_tensor("wT", [1024, 1024], F32, kind="ExternalInput").ap()
    d["bpk"] = nc.dram_tensor("bpk", [128, 8], F32, kind="ExternalInput").ap()
    d["ldT"] = nc.dram_tensor("ldT", [100, 4], F32, kind="ExternalInput").ap()
    d["luT"] = nc.dram_tensor("luT", [4, 50], F32, kind="ExternalInput").ap()
    d["uaT"] = nc.dram_tensor("uaT", [50, 1024], F32, kind="ExternalInput").ap()
    d["da"] = nc.dram_tensor("da", [100, 1024], F32, kind="ExternalInput").ap()
    d["wc_dram"] = nc.dram_tensor("wc_bf", [128, 8, 1024], BF16).ap()

    es = ExitStack()
    with es:
        em = Emit(nc, es)
        a["bcast"] = dict(ar1_in=b["ar1_in"], ar1_out=b["ar1_out"], mgatec=b["mgatec"])
        with ExitStack() as esA:
            em.cur_es = esA
            emit_mamba(em, a)
        s_cw = em.sem("f_cw")
        c["w_bf"] = {}
        for w in ("wqT", "wkT", "wvT", "wgT", "woT"):
            dst = nc.dram_tensor(w + "_bf", [1024, 1024], BF16).ap()
            c["w_bf"][w] = dst
            em.op("gpsimd", lambda g, dst=dst, src=c[w]: g.dma_start(out=dst, in_=src),
                  incs=s_cw.inc(16))
        c["w_bf_done"] = s_cw.now()
        specsA = [s.now() for s in em.all_sems if s.n > 0 and s is not s_cw]
        for e in ENGS:
            em.pending[e] = list(specsA)
        em.bank_rel = {}
        b["bcast_done"] = em.bcast_done
        with ExitStack() as esB:
            em.cur_es = esB
            d["wc_done"] = emit_wc_prep(em, d)
            moe_done = emit_moe(em, b)
        cc_sem = em._moe_cc
        specs = [s.now() for s in em.all_sems if s.n > 0 and s is not cc_sem]
        for e in ENGS:
            em.pending[e] = list(specs)
        em.bank_rel = {}
        c["moe_done"] = moe_done
        with ExitStack() as esC:
            em.cur_es = esC
            mg = em.sb("c_mgate", [128, 1], F32)
            sw = em.sem("c_mg")
            em.op("gpsimd", lambda g: g.dma_start(out=mg[:], in_=b["mgatec"]), incs=sw.inc(16))
            em.pending["vector"] = list(em.pending.get("vector", [])) + [sw.now()]
            c["mgate_sb"] = mg
            c["d"] = d
            emit_l1attn(em, c)
        em.cur_es = es
        em.finish()
    return nc


def kernel(**inputs):
    T = 4096
    consts = host_consts()
    u = np.asarray(inputs["L0_attn_u"], np.float32)

    common_a = host_mamba_inputs_common(
        inputs["L0_attn_in_proj_w"], inputs["L0_attn_conv_w"], inputs["L0_attn_conv_b"],
        inputs["L0_attn_dt_bias"], inputs["L0_attn_A_log"], inputs["L0_attn_D"],
        inputs["L0_attn_rms_w"], inputs["L0_attn_out_w"], consts)
    common_c = host_l1attn_inputs_common(
        inputs["L1_attn_W_q"], inputs["L1_attn_W_k"], inputs["L1_attn_W_v"],
        inputs["L1_attn_W_g"], inputs["L1_attn_W_o"], inputs["L1_attn_log_decay"], consts)
    common_d = host_l1ffn_inputs_common(
        inputs["L1_ffn_W"], inputs["L1_ffn_b"], inputs["L1_ffn_lora_down"],
        inputs["L1_ffn_lora_up"], inputs["L1_ffn_down_aux"], inputs["L1_ffn_up_aux"])

    in_maps = []
    for cid in range(8):
        bb, t0 = cid // 4, (cid % 4) * T
        if t0 == 0:
            uh = np.concatenate([np.zeros((128, 1024), np.float32), u[bb, 0:T]], axis=0)
        else:
            uh = u[bb, t0 - 128:t0 + T]
        m = dict(common_a)
        m.update(common_c)
        m.update(common_d)
        m["u"] = np.ascontiguousarray(uh.T)
        m["m2T"], m["selcol"] = host_core_masks(cid)
        m.update(host_moe_inputs(
            None, cid, inputs["L0_ffn_gate_up_proj"], inputs["L0_ffn_gate_up_proj_bias"],
            inputs["L0_ffn_down_proj"], inputs["L0_ffn_down_proj_bias"],
            inputs["L0_ffn_router_indices"], inputs["L0_ffn_routing_weights"]))
        m["mgatec"] = np.full((128, 1), 1.0 if cid == 0 else 0.0, np.float32)
        in_maps.append(m)

    nc = build_fused(T)
    res = run_bass_kernel_spmd(nc, in_maps, core_ids=list(range(8)))
    out = np.empty((2, 16384, 1024), np.float32)
    for cid in range(8):
        bb, t0 = cid // 4, (cid % 4) * T
        out[bb, t0:t0 + T] = res.results[cid]["out"].T
    return out
